# version 02
# baseline (speedup 1.0000x reference)
import numpy as np
from contextlib import ExitStack
import concourse.bass as bass
import concourse.mybir as mybir
from concourse.bass_utils import run_bass_kernel_spmd

F32 = mybir.dt.float32
BF16 = mybir.dt.bfloat16
I32 = mybir.dt.int32
AT = mybir.AluOpType
AF = mybir.ActivationFunctionType
AX = mybir.AxisListType.X

TLOC, D, C, H, K = 2048, 1024, 64, 16, 64
NCH = TLOC // C
NH, HD = 8, 128
GRP = 256
NGRP = TLOC // GRP
CPG = GRP // C
FF0, FF1 = 3072, 4096
MARK = -1.0e20
DEBUG = False
LAST_RES = None
GN_EPS = 0.00064
ENGS = ("sync", "tensor", "vector", "scalar", "gpsimd")


class Builder:

    def __init__(self, nc, es, semnames):
        self.nc = nc
        self.q = {e: [] for e in ENGS}
        self.semh = {n: es.enter_context(nc.semaphore(n)) for n in semnames}
        self.semv = {n: 0 for n in semnames}
        self.accessed = {}

    def _rec(self, eng, fn):
        self.q[eng].append(fn)

    @staticmethod
    def _fp(x):
        t = getattr(x, "tensor", None)
        if t is None:
            return None
        span = 1
        try:
            for st, n in x.ap:
                span += abs(int(st)) * (int(n) - 1)
        except Exception:
            span = 1 << 30
        off = int(x.offset)
        return (t.name, off, off + span)

    @staticmethod
    def _ov(a, bb):
        return a[0] == bb[0] and a[1] < bb[2] and bb[1] < a[2]

    def op(self, eng, meth, *a, inc=None, incn=1, **k):
        semh = self.semh
        import traceback as _tb
        site = _tb.extract_stack(limit=6)

        ENGSEM = {"vector": "dv", "scalar": "sc", "gpsimd": "gp", "tensor": "mm"}
        if (inc is None and eng in ENGSEM and meth not in
                ("sem_inc", "drain", "dma_start", "wait_ge", "collective_compute")):
            inc = ENGSEM[eng]
        if meth == "dma_start" and inc is not None and getattr(self, "_pubd", None):
            v = self._pubd.pop(inc, None)
            if v is not None:
                def swt(e, inc=inc, v=v):
                    e.wait_ge(semh[inc], v)
                self._rec(eng, swt)
        if getattr(self, "_pub", None) and self._pub.get(eng) and meth not in (
                "sem_inc", "drain", "wait_ge"):
            def drp(e):
                e.drain()
            self._rec(eng, drp)
            acc0 = self.accessed.setdefault(eng, {"w": [], "r": []})
            acc0["w"] = []
            acc0["r"] = []
            self._pub[eng] = False
        if eng in ("vector", "scalar", "gpsimd", "tensor") and meth not in (
                "sem_inc", "drain", "dma_start"):
            fps = [f for f in (self._fp(x) for x in list(a) + list(k.values()))
                   if f is not None]
            wr = [f for f in ([self._fp(a[0])] if a and self._fp(a[0]) else [])]
            acc = self.accessed.setdefault(eng, {"w": [], "r": []})
            need = any(self._ov(f, w) for f in fps for w in acc["w"]) or                    any(self._ov(w, r) for w in wr for r in acc["r"])
            if need:
                def dr(e):
                    e.drain()
                self._rec(eng, dr)
                acc["w"] = []
                acc["r"] = []
            acc["w"].extend(wr)
            acc["r"].extend(fps[1:] if len(fps) > 1 else [])

        if inc in ("dv", "sc", "gp", "mm") and meth not in (
                "sem_inc", "drain", "wait_ge"):
            if not hasattr(self, "_lastinc"):
                self._lastinc = {}
            if self._lastinc.get(inc, eng) != eng:
                self.wait(eng, inc)
            self._lastinc[inc] = eng

        def run(e, meth=meth, a=a, k=k, inc=inc, incn=incn, site=site):
            try:
                ins = getattr(e, meth)(*a, **k)
            except Exception:
                print("FAILED OP:", meth)
                for f in site:
                    if "kernel.py" in f.filename and f.name not in ("op", "run"):
                        print("   at", f.lineno, f.name, "|", f.line)
                raise
            if inc is not None:
                ins.then_inc(semh[inc], incn)
        self._rec(eng, run)
        if inc is not None:
            self.semv[inc] += incn
            return self.semv[inc]
        return None

    def wait(self, eng, sem, val=None):
        v = self.semv[sem] if val is None else val
        semh = self.semh
        SEM2ENG = {"dv": "vector", "sc": "scalar", "gp": "gpsimd", "mm": "tensor"}
        if sem in SEM2ENG and SEM2ENG[sem] != eng:
            if not hasattr(self, "_pub"):
                self._pub = {}
            self._pub[SEM2ENG[sem]] = True
        if sem in ("ld", "gld", "st", "sld"):
            if not hasattr(self, "_pubd"):
                self._pubd = {}
            self._pubd[sem] = max(v, self._pubd.get(sem, 0))

        def run(e, sem=sem, v=v):
            e.wait_ge(semh[sem], v)
        self._rec(eng, run)

    def sig(self, eng, sem, n=1):
        semh = self.semh
        self.semv[sem] += n
        v = self.semv[sem]

        def run(e, sem=sem, n=n):
            e.sem_inc(semh[sem], n)
        self._rec(eng, run)
        return v

    def fence(self, engines=("tensor", "vector", "scalar", "gpsimd", "sync")):
        for e in engines:
            for sem in ("mm", "dv", "sc", "gp", "ld", "gld", "sld", "cc"):
                if self.semv[sem]:
                    self.wait(e, sem)

    def replay(self, block):
        for ename in ENGS:
            dec = getattr(block, ename)
            ops = self.q[ename]

            def body(e, ops=ops):
                for fn in ops:
                    fn(e)
            dec(body)



def dma(b, qeng, out, in_, sem=None, slow=False, **kw):
    sem = sem or ("gld" if qeng == "gpsimd" else "ld")
    if slow:
        kw["allow_slow_non_contiguous"] = True
    return b.op(qeng, "dma_start", inc=sem, incn=16, out=out, in_=in_, **kw)


def build_program():
    nc = bass.Bass("TRN2", target_bir_lowering=False, debug=False)

    din = {}

    def IN(name, shape, dt=F32):
        din[name] = nc.dram_tensor(name, list(shape), dt, kind="ExternalInput").ap()
        return din[name]

    xs = IN("xs", (TLOC, D))
    xprev = IN("xprev", (1, D))
    flags = IN("flags", (1, 32))
    wpk = {p: IN(f"w{p}t", (D, D)) for p in "rwkvabg"}
    wot = IN("wot", (D, D))
    mixv = IN("mixv", (128, 8, 8))
    gnwr = IN("gnwr", (1, D))
    gnbr = IN("gnbr", (1, D))
    bor = IN("bor", (128, 8))
    wg_i = IN("wg_i", (D, FF0), I32)
    wu_i = IN("wu_i", (D, FF0), I32)
    wd_i = IN("wd_i", (FF0, D), I32)
    encq = IN("encq", (NH, D, HD))
    encv = IN("encv", (NH, D, HD))
    mw1 = IN("mw1", (D, 512))
    mb1 = IN("mb1", (128, 4))
    mw2 = IN("mw2", (512, D))
    mb2 = IN("mb2", (128, 8))
    basem = IN("basem", (128, 8))
    hut = IN("hut", (NH, 32, HD))
    hvv = IN("hvv", (NH, 32, HD))
    decw = IN("decw", (D, D))
    f1w = IN("f1w", (D, FF1))
    f1b = IN("f1b", (128, 32))
    f2w = IN("f2w", (FF1, D))
    f2b = IN("f2b", (128, 8))
    identf = IN("identf", (128, 128))
    identb = IN("identb", (128, 128), BF16)
    mask4 = IN("mask4", (128, 256))
    masklo = IN("masklo", (128, 64))
    idd2 = IN("idd2", (128, 64))
    bandm = IN("bandm", (128, 192))
    bandb = IN("bandb", (128, 192))
    ones2 = IN("ones2", (128, 2))
    sel2 = IN("sel2", (2, 128))
    onesr = IN("onesr", (1, 128))
    onescol = IN("onescol", (128, 1))

    out_h = nc.dram_tensor("out", [TLOC, D], F32, kind="ExternalOutput")
    out = out_h.ap()

    def DT(name, shape, dt=F32):
        skip = ("cc1_in", "cc2_in", "cc1_out", "cc2_out")
        if DEBUG and name not in skip:
            return nc.dram_tensor(name, list(shape), dt, kind="ExternalOutput").ap()
        return nc.dram_tensor(name, list(shape), dt).ap()

    sc_wproj = DT("sc_wproj", (7, D, D), BF16)
    sc_wg = DT("sc_wg", (D, FF0), BF16)
    sc_wu = DT("sc_wu", (D, FF0), BF16)
    sc_wd = DT("sc_wd", (FF0, D), BF16)
    sc_psi = DT("sc_psi", (NCH, H, K, C), BF16)
    sc_p = DT("sc_p", (NCH, H, K, K), BF16)
    sc_oloc = DT("sc_oloc", (TLOC, D), BF16)
    sc_g = DT("sc_g", (TLOC, D), BF16)
    sc_dbg = DT("sc_dbg", (128, 8, K), F32)
    cc1_in = DT("cc1_in", (H, 128, K))
    cc1_out = DT("cc1_out", (8 * H, 128, K))
    cc2_in = DT("cc2_in", (80, D))
    cc2_out = DT("cc2_out", (8 * 80, D))
    sc_wf1 = DT("sc_wf1", (D, FF1), BF16)
    sc_wf2 = DT("sc_wf2", (FF1, D), BF16)
    sc_dec = DT("sc_dec", (D, D), BF16)
    sc_osp = DT("sc_osp", (NH, 128, TLOC), BF16)
    sc_encq = DT("sc_encq", (NH, D, HD), BF16)

    es = ExitStack()

    def SB(name, shape, dt=F32):
        return es.enter_context(nc.sbuf_tensor(name, list(shape), dt))

    with es:
        g = {}
        g["xres"] = SB("xres", (128, 8, TLOC))
        g["stats"] = SB("stats", (128, 16, 2))
        g["stats2"] = SB("stats2", (128, 16, 2))
        g["mixs"] = SB("mixs", (128, 8, 8))
        g["xnprevc"] = SB("xnprevc", (128, 8, 1))
        g["idf"] = SB("idf", (128, 128))
        g["idb"] = SB("idb", (128, 128), BF16)
        g["msk4"] = SB("msk4", (128, 256))
        g["msklo"] = SB("msklo", (128, 64))
        g["idd2"] = SB("idd2s", (128, 64))
        g["bndm"] = SB("bndm", (128, 192))
        g["bndb"] = SB("bndb", (128, 192))
        g["one2"] = SB("one2", (128, 2))
        g["sel2"] = SB("sel2s", (2, 128))
        g["oner"] = SB("oner", (1, 128))
        g["onec"] = SB("onec", (128, 1))
        g["flgb"] = SB("flgb", (128, 32))
        g["rowscr"] = SB("rowscr", (1, 512))
        g["colsc"] = SB("colsc", (128, 24))
        g["dsb"] = SB("dsb", (128, 8, K))
        g["dbf"] = SB("dbf", (128, 8, K), BF16)
        g["scr"] = SB("scr", (128, 4, GRP))
        g["scr2"] = SB("scr2", (128, 4, GRP))
        g["scrb"] = SB("scrb", (128, 2, 1024), BF16)
        g["tokst"] = SB("tokst", (128, 1024))
        g["tokst2"] = SB("tokst2", (128, 1024))
        ARENA_COLS = 49792
        g["arena"] = SB("arena", (128, ARENA_COLS), BF16)

        g["psA"] = es.enter_context(nc.psum_tensor("psA", [128, 512], F32))
        g["psB"] = es.enter_context(nc.psum_tensor("psB", [128, 512], F32))
        g["psST"] = es.enter_context(nc.psum_tensor("psST", [128, 8, 256], F32))
        g["psCH"] = es.enter_context(nc.psum_tensor("psCH", [128, 8, 128], F32))

        class Arena:
            def __init__(self, t, cols):
                self.t = t
                self.cols = cols
                self.off = 0

            def take(self, shape, dt=F32):
                inner = 1
                for s in shape[1:]:
                    inner *= s
                n = inner * (2 if dt in (F32, I32) else 1)
                n = (n + 63) // 64 * 64
                assert self.off + n <= self.cols, f"arena overflow {self.off}+{n}"
                sl = self.t[:, self.off:self.off + n]
                self.off += n
                flat = (sl if dt == BF16 else sl.bitcast(dt))[:, 0:inner]
                v = flat
                if len(shape) == 3:
                    v = flat.rearrange("p (a c) -> p a c", a=shape[1])
                return v

            def reg(self, a, name, shape, dt=F32):
                inner = 1
                for s in shape[1:]:
                    inner *= s
                n = inner * (2 if dt in (F32, I32) else 1)
                n = (n + 63) // 64 * 64
                assert self.off + n <= self.cols, f"arena overflow {self.off}+{n}"
                sl = self.t[:, self.off:self.off + n]
                self.off += n
                flat = (sl if dt == BF16 else sl.bitcast(dt))[:, 0:inner]
                v = flat
                if len(shape) == 3:
                    v = flat.rearrange("p (a c) -> p a c", a=shape[1])
                a[name] = v
                a[name + "_f"] = flat
                return v

        def seta(ar):
            a = {}
            ar.reg(a, "wbuf", (128, 8, 1024), BF16)
            ar.reg(a, "s_r", (128, 8, GRP), BF16)
            ar.reg(a, "s_k", (128, 8, GRP), BF16)
            ar.reg(a, "s_a", (128, 8, GRP), BF16)
            ar.reg(a, "s_b", (128, 8, GRP), BF16)
            ar.reg(a, "wcum", (128, 8, GRP))
            ar.reg(a, "xng", (128, 8, GRP), BF16)
            ar.reg(a, "xxg", (128, 8, GRP), BF16)
            ar.reg(a, "xmx", (128, 8, GRP), BF16)
            ar.reg(a, "xmv", (128, 8, GRP), BF16)
            ar.reg(a, "wcc", (128, 8, 2 * C))
            ar.reg(a, "ftmp", (128, 8, 3 * C))
            ar.reg(a, "fcol", (128, 8, 4))
            ar.reg(a, "lhsbk", (128, 8, 2 * C), BF16)
            ar.reg(a, "rhsar", (128, 8, 2 * C), BF16)
            ar.reg(a, "til", (128, 8, 4 * C), BF16)
            ar.reg(a, "vtok", (128, H * K), BF16)
            ar.reg(a, "stsb", (128, 8, 4 * C), BF16)
            ar.reg(a, "ntb", (128, 8, C), BF16)
            ar.reg(a, "pwst", (128, 8, 128), BF16)
            ar.reg(a, "pwts", (128, 8, 128), BF16)
            ar.reg(a, "ysbb", (128, 8, 128), BF16)
            ar.reg(a, "trsb", (128, 8, 128), BF16)
            a["psis"] = a["lhsbk"].rearrange("p a c -> p (a c)").rearrange(
                "p (a c) -> p a c", c=C)[:, 0:8, :]
            a["psb"] = a["rhsar"].rearrange("p a c -> p (a c)").rearrange(
                "p (a c) -> p a c", c=K)[:, 0:8, :]
            a["ostg"] = a["rhsar"].rearrange("p a c -> p (a c)").rearrange(
                "p (a c) -> p a c", c=K)[:, 8:16, :]
            ar.reg(a, "slst", (128, 8, K))
            ar.reg(a, "slbf", (128, 8, K), BF16)
            ar.reg(a, "accs", (128, 8, K))
            ar.reg(a, "accb", (128, 8, K), BF16)
            return a

        def seta2(ar):
            a = {}
            ar.reg(a, "gnwb", (128, 1024))
            ar.reg(a, "gnbb", (128, 1024))
            ar.reg(a, "bocol", (128, 8, 1))
            ar.reg(a, "wotb", (128, 8, 1024), BF16)
            ar.reg(a, "wtile", (128, 8, 128), BF16)
            ar.reg(a, "wtile2", (128, 8, 128), BF16)
            ar.reg(a, "wdtile", (128, 1024), BF16)
            ar.reg(a, "hstr", (128, 32, GRP), BF16)
            ar.reg(a, "xng", (128, 8, GRP), BF16)
            ar.reg(a, "psil", (128, 8, C), BF16)
            ar.reg(a, "pldl", (128, 8, K), BF16)
            ar.reg(a, "ztl", (128, 8, 128), BF16)
            ar.reg(a, "otr", (128, 8, 128))
            ar.reg(a, "gnsc", (128, 16, 8))
            return a

        def setb_shared(ar):
            a = {}
            ar.reg(a, "qts", (128, NH, 64 + TLOC), BF16)
            ar.reg(a, "vts", (128, 16, D), BF16)
            ar.reg(a, "vhalo", (128, D), BF16)
            ar.reg(a, "uvs", (128, NH * HD), BF16)
            ar.reg(a, "metr", (128, 8, 1))
            ar.reg(a, "acols", (128, 8, 17))
            ar.reg(a, "sqnc", (128, 8, 17))
            return a

        def setb0(ar):
            a = setb_shared(ar)
            ar.reg(a, "xng", (128, 8, GRP), BF16)
            ar.reg(a, "encvb", (128, 8, 1024), BF16)
            ar.reg(a, "encb", (128, 8, 128), BF16)
            ar.reg(a, "sigc", (128, 8, 4))
            ar.reg(a, "mlpc", (128, 8, 4))
            ar.reg(a, "bmc", (128, 8, 1))
            ar.reg(a, "xhalT", (128, 8, 64), BF16)
            return a

        def setb1(ar):
            a = setb_shared(ar)
            ar.reg(a, "dtl", (128, 192))
            ar.reg(a, "neg1", (128, 192))
            ar.reg(a, "neg2", (128, 192))
            ar.reg(a, "m8", (128, 8))
            ar.reg(a, "wse", (128, 192))
            ar.reg(a, "wsb", (128, 192), BF16)
            ar.reg(a, "expd", (128, 192))
            ar.reg(a, "wtb", (128, 384), BF16)
            ar.reg(a, "rsum", (128, 4))
            ar.reg(a, "qsq", (128, 128))
            ar.reg(a, "ospst", (128, 8, 128), BF16)
            ar.reg(a, "qmb", (128, 128), BF16)
            ar.reg(a, "qtkb", (128, 128), BF16)
            ar.reg(a, "ostok", (128, 128), BF16)
            ar.reg(a, "ctmp", (128, 128))
            return a

        def setb2(ar):
            a = {}
            ar.reg(a, "xng", (128, 8, GRP), BF16)
            ar.reg(a, "hstr", (128, 32, GRP), BF16)
            ar.reg(a, "wtile", (128, 8, 128), BF16)
            ar.reg(a, "wtile2", (128, 1024), BF16)
            ar.reg(a, "hq32", (128, GRP), I32)
            ar.reg(a, "pwl", (128, 8, GRP))
            ar.reg(a, "b1c", (128, 32, 1))
            ar.reg(a, "b2c", (128, 8, 1))
            ar.reg(a, "ospl", (128, 8, GRP), BF16)
            return a

        g["_mk"] = (Arena, seta, seta2, setb0, setb1, setb2, ARENA_COLS)

        b = Builder(nc, es, ["ld", "gld", "sld", "mm", "dv", "sc", "gp", "st", "cc", "fn"])
        G = type("G", (), g)

        emit(b, G, dict(
            xs=xs, xprev=xprev, flags=flags, wpk=wpk, wot=wot, mixv=mixv,
            gnwr=gnwr, gnbr=gnbr, bor=bor, wg_i=wg_i, wu_i=wu_i, wd_i=wd_i,
            encq=encq, encv=encv, mw1=mw1, mb1=mb1, mw2=mw2, mb2=mb2,
            basem=basem, hut=hut, hvv=hvv, decw=decw, f1w=f1w, f1b=f1b,
            f2w=f2w, f2b=f2b, identf=identf, identb=identb, mask4=mask4,
            masklo=masklo, idd2=idd2, bandm=bandm, bandb=bandb, ones2=ones2,
            sel2=sel2, onesr=onesr, onescol=onescol, out=out,
            sc_wproj=sc_wproj, sc_wg=sc_wg, sc_wu=sc_wu, sc_wd=sc_wd,
            sc_psi=sc_psi, sc_p=sc_p, sc_oloc=sc_oloc, sc_g=sc_g, sc_dbg=sc_dbg,
            sc_wf1=sc_wf1, sc_wf2=sc_wf2, sc_dec=sc_dec, sc_osp=sc_osp, sc_encq=sc_encq,
            cc1_in=cc1_in, cc1_out=cc1_out, cc2_in=cc2_in, cc2_out=cc2_out,
        ))

        block = es.enter_context(nc.Block())
        b.replay(block)

    return nc



def _merge(G, extra):
    base = {k: v for k, v in G.__dict__.items() if not k.startswith("__")}
    base.update(extra)
    return type("G", (), base)


PHASES = 99


def _finish_stub(b, G, d, dump_xres=False):
    b.fence()
    if dump_xres:
        ov = d["out"].rearrange("t e -> (t e)").rearrange("(p r) -> p r", p=128)
        dma(b, "sync", ov, G.xres[:, :, :].rearrange("p a c -> p (a c)"), sem="st")
    else:
        b.op("vector", "memset", G.tokst[:, :], 0.0, inc="dv")
        b.wait("sync", "dv")
        for i in range(16):
            dma(b, "sync", d["out"][i * 128:(i + 1) * 128, :], G.tokst[:, :], sem="st")
    b.wait("sync", "st")
    b.wait("gpsimd", "st")


def emit(b, G, d):
    Arena, seta, seta2, setb0, setb1, setb2, AC = G._mk
    import kernel as _k
    ph = getattr(_k, "PHASES", 99)
    GA = _merge(G, seta(Arena(G.arena, AC)))
    emit_p0(b, GA, d)
    if ph <= 1:
        return _finish_stub(b, GA, d)
    emit_wconv(b, GA, d)
    if ph <= 2:
        return _finish_stub(b, GA, d)
    b.op("vector", "memset", GA.slst[:, :, :], 0.0, inc="dv")
    for grp in range(min(NGRP, ph - 2)):
        emit_group(b, GA, d, grp)
    if ph <= 10:
        return _finish_stub(b, GA, d)
    emit_relay(b, GA, d)
    if ph <= 11:
        return _finish_stub(b, GA, d)
    GA2 = _merge(G, seta2(Arena(G.arena, AC)))
    emit_p3(b, GA2, d)
    if ph <= 12:
        return _finish_stub(b, GA2, d, dump_xres=True)
    GB0 = _merge(G, setb0(Arena(G.arena, AC)))
    emit_p4(b, GB0, d)
    if ph <= 13:
        return _finish_stub(b, GB0, d, dump_xres=True)
    emit_p5(b, GB0, d)
    if ph <= 14:
        return _finish_stub(b, GB0, d, dump_xres=True)
    GB1 = _merge(G, setb1(Arena(G.arena, AC)))
    emit_p6a(b, GB1, d)
    if ph <= 15:
        return _finish_stub(b, GB1, d, dump_xres=True)
    GB2 = _merge(G, setb2(Arena(G.arena, AC)))
    emit_p6b(b, GB2, d)




def v8(t, pages, cols):
    return t[:, 0:pages, :].rearrange("p a b -> p (a b)").rearrange("p (a c) -> p a c", c=cols)

def tt(b, eng, outp, a0, a1, op, inc="dv"):
    return b.op(eng, "tensor_tensor", outp, a0, a1, op, inc=inc)


def ts(b, eng, outp, a0, s1, s2, op0, op1=None, inc="dv"):
    if op1 is None:
        return b.op(eng, "tensor_scalar", outp, a0, s1, None, op0, inc=inc)
    return b.op(eng, "tensor_scalar", outp, a0, s1, s2, op0, op1, inc=inc)


def stt(b, eng, outp, in0, scalar, in1, op0, op1, inc="dv"):
    return b.op(eng, "scalar_tensor_tensor", outp, in0, scalar, in1, op0, op1, inc=inc)


def act(b, outp, inp, func, inc="sc", **kw):
    return b.op("scalar", "activation", outp, inp, func, inc=inc, **kw)


def mm(b, outp, lhsT, rhs, start=True, stop=True, tp=None, inc=None, sgc=False):
    kw = {}
    if tp is not None:
        kw["tile_position"] = tp
    if sgc:
        kw["skip_group_check"] = True
    return b.op("tensor", "matmul", outp, lhsT, rhs, inc=inc, start=start, stop=stop, **kw)


def tr(b, outp, inp, ident, tp=None, inc=None):
    kw = {}
    if tp is not None:
        kw["tile_position"] = tp
    return b.op("tensor", "transpose", outp, inp, ident, inc=inc, **kw)


def DPSUM(G, eb):
    if eb == 0:
        return G.psA[:, 0:256]
    if eb == 1:
        return G.psB[:, 0:256]
    if eb < 6:
        return G.psST[:, (eb - 2) * 2, 0:256]
    return G.psCH[:, (eb - 6) * 4:(eb - 6) * 4 + 2, :].rearrange("p a b -> p (a b)")


def emit_p0(b, G, d):
    for outp, inp in [
        (G.idf[:, :], d["identf"]), (G.idb[:, :], d["identb"]),
        (G.msk4[:, :], d["mask4"]), (G.msklo[:, :], d["masklo"]),
        (G.idd2[:, :], d["idd2"]),
        (G.bndm[:, :], d["bandm"]), (G.bndb[:, :], d["bandb"]),
        (G.one2[:, :], d["ones2"]), (G.sel2[:, :], d["sel2"]),
        (G.oner[:, :], d["onesr"]), (G.onec[:, :], d["onescol"]),
        (G.tokst2[0:1, :], d["xprev"]),
    ]:
        dma(b, "sync", outp, inp)
    dma(b, "sync", G.mixs[:, :, :], d["mixv"])
    dma(b, "sync", G.scr[0:1, 0, 0:32], d["flags"])
    ldv = b.semv["ld"]

    b.wait("tensor", "ld")
    mm(b, G.psA[:, 0:32], G.oner[:, :], G.scr[0:1, 0, 0:32], inc="mm")
    b.wait("vector", "mm")
    b.op("vector", "tensor_copy", G.flgb[:, :], G.psA[:, 0:32], inc="dv")


    b.wait("vector", "ld", ldv)
    b.op("vector", "reduce_sum", G.colsc[0:1, 0:1], G.tokst2[0:1, :], axis=AX)
    tt(b, "vector", G.scr[0:1, 0, 0:GRP], G.tokst2[0:1, 0:GRP], G.tokst2[0:1, 0:GRP], AT.mult)
    tt(b, "vector", G.scr[0:1, 1, 0:GRP], G.tokst2[0:1, GRP:2 * GRP], G.tokst2[0:1, GRP:2 * GRP], AT.mult)
    tt(b, "vector", G.scr[0:1, 2, 0:GRP], G.tokst2[0:1, 2 * GRP:3 * GRP], G.tokst2[0:1, 2 * GRP:3 * GRP], AT.mult)
    tt(b, "vector", G.scr[0:1, 3, 0:GRP], G.tokst2[0:1, 3 * GRP:4 * GRP], G.tokst2[0:1, 3 * GRP:4 * GRP], AT.mult)
    tt(b, "vector", G.scr[0:1, 0, 0:GRP], G.scr[0:1, 0, 0:GRP], G.scr[0:1, 1, 0:GRP], AT.add)
    tt(b, "vector", G.scr[0:1, 2, 0:GRP], G.scr[0:1, 2, 0:GRP], G.scr[0:1, 3, 0:GRP], AT.add)
    tt(b, "vector", G.scr[0:1, 0, 0:GRP], G.scr[0:1, 0, 0:GRP], G.scr[0:1, 2, 0:GRP], AT.add)
    b.op("vector", "reduce_sum", G.colsc[0:1, 1:2], G.scr[0:1, 0, 0:GRP], axis=AX)
    ts(b, "vector", G.colsc[0:1, 2:3], G.colsc[0:1, 0:1], 1.0 / D, None, AT.mult)
    ts(b, "vector", G.colsc[0:1, 3:4], G.colsc[0:1, 1:2], 1.0 / D, None, AT.mult)
    tt(b, "vector", G.colsc[0:1, 4:5], G.colsc[0:1, 2:3], G.colsc[0:1, 2:3], AT.mult)
    tt(b, "vector", G.colsc[0:1, 5:6], G.colsc[0:1, 3:4], G.colsc[0:1, 4:5], AT.subtract)
    ts(b, "vector", G.colsc[0:1, 5:6], G.colsc[0:1, 5:6], 1e-5, None, AT.add, inc="dv")
    b.wait("scalar", "dv")
    b.op("scalar", "sqrt", G.colsc[0:1, 6:7], G.colsc[0:1, 5:6], inc="sc")
    b.wait("vector", "sc")
    b.op("vector", "reciprocal", G.colsc[0:1, 7:8], G.colsc[0:1, 6:7])
    ts(b, "vector", G.tokst2[0:1, :], G.tokst2[0:1, :], G.colsc[0:1, 2:3], G.colsc[0:1, 7:8],
       AT.subtract, AT.mult, inc="dv")
    b.wait("tensor", "dv"); b.wait("tensor", "sc")
    for db in range(8):
        mm(b, G.psB[:, db:db + 1], G.tokst2[0:1, db * 128:(db + 1) * 128], G.onec[0:1, 0:1])
    b.wait("vector", "mm")
    b.op("vector", "tensor_copy", G.xnprevc[:, :, 0], G.psB[:, 0:8], inc="dv")

    for i in range(16):
        buf = G.tokst if i % 2 == 0 else G.tokst2
        b.wait("sync", "mm")
        dma(b, "sync", buf[:, :], d["xs"][i * 128:(i + 1) * 128, :])
        b.wait("vector", "ld")
        b.op("vector", "reduce_sum", G.colsc[:, 8:9], buf[:, :], axis=AX)
        b.wait("scalar", "ld")
        act(b, G.scr[:, 0:4, :].rearrange("p a c -> p (a c)"), buf[:, :], AF.Square)
        b.wait("vector", "sc")
        b.op("vector", "reduce_sum", G.colsc[:, 9:10], G.scr[:, 0:4, :].rearrange("p a c -> p (a c)"), axis=AX)
        ts(b, "vector", G.colsc[:, 10:11], G.colsc[:, 8:9], 1.0 / D, None, AT.mult)
        ts(b, "vector", G.colsc[:, 11:12], G.colsc[:, 9:10], 1.0 / D, None, AT.mult)
        tt(b, "vector", G.colsc[:, 12:13], G.colsc[:, 10:11], G.colsc[:, 10:11], AT.mult)
        tt(b, "vector", G.colsc[:, 13:14], G.colsc[:, 11:12], G.colsc[:, 12:13], AT.subtract)
        ts(b, "vector", G.colsc[:, 13:14], G.colsc[:, 13:14], 1e-5, None, AT.add, inc="dv")
        b.wait("scalar", "dv")
        b.op("scalar", "sqrt", G.colsc[:, 14:15], G.colsc[:, 13:14], inc="sc")
        b.wait("vector", "sc")
        b.op("vector", "reciprocal", G.colsc[:, 15:16], G.colsc[:, 14:15])
        b.op("vector", "tensor_copy", G.stats[:, i, 0:1], G.colsc[:, 10:11])
        b.op("vector", "tensor_copy", G.stats[:, i, 1:2], G.colsc[:, 15:16], inc="dv")
        b.wait("tensor", "ld")
        for db in range(8):
            pst = G.psA if db % 2 == 0 else G.psB
            b.wait("tensor", "dv"); b.wait("tensor", "sc")
            tr(b, pst[:, 256:384] if pst is G.psB else pst[:, 0:128],
               buf[:, db * 128:(db + 1) * 128], G.idf[:, :], inc="mm")
            b.wait("vector", "mm")
            psrc = pst[:, 256:384] if pst is G.psB else pst[:, 0:128]
            b.op("vector", "tensor_copy", G.xres[:, db, i * 128:(i + 1) * 128], psrc, inc="dv")

    b.fence()




def ln_bcast(b, G, statt, ti):
    b.wait("tensor", "dv"); b.wait("tensor", "sc")
    tr(b, G.psCH[0:1, 0, 0:128], statt[:, ti, 0:1], G.idf[:, :])
    tr(b, G.psCH[0:1, 0, 128:256] if False else G.psCH[0:1, 1, 0:128],
       statt[:, ti, 1:2], G.idf[:, :], inc="mm")
    b.wait("vector", "mm")
    b.op("vector", "tensor_copy", G.rowscr[0:1, 0:128], G.psCH[0:1, 0, 0:128])
    b.op("vector", "tensor_copy", G.rowscr[0:1, 128:256], G.psCH[0:1, 1, 0:128], inc="dv")
    b.wait("tensor", "dv"); b.wait("tensor", "sc")
    mm(b, G.psA[:, 0:128], G.oner[:, :], G.rowscr[0:1, 0:128])
    mm(b, G.psB[:, 0:128], G.oner[:, :], G.rowscr[0:1, 128:256], inc="mm")
    b.wait("vector", "mm")

def emit_wconv(b, G, d):
    for pi, p in enumerate("rwkvabg"):
        src = d["wpk"][p]
        for half in range(2):
            sl = slice(half * 512, (half + 1) * 512)
            b.wait("gpsimd", "gld")
            dma(b, "gpsimd", G.wbuf[:, 0:4, :],
                src[sl, :].rearrange("(hb p) e -> p hb e", p=128))
            b.wait("gpsimd", "gld")
            dma(b, "gpsimd", d["sc_wproj"][pi, sl, :].rearrange("(hb p) e -> p hb e", p=128),
                G.wbuf[:, 0:4, :])
    for srci, dst, rows, cols in ((d["wg_i"], d["sc_wg"], D, FF0),
                                  (d["wu_i"], d["sc_wu"], D, FF0),
                                  (d["wd_i"], d["sc_wd"], FF0, D)):
        step = 8192 // cols * 128
        for r0 in range(0, rows, step):
            sl = slice(r0, r0 + step)
            wv = G.wbuf_f[:, 0:(step // 128) * cols].rearrange("p (a c) -> p a c", c=cols)
            b.wait("gpsimd", "gld")
            dma(b, "gpsimd", wv, srci[sl, :].rearrange("(hb p) e -> p hb e", p=128))
            b.wait("gpsimd", "gld")
            dma(b, "gpsimd", dst[sl, :].rearrange("(hb p) e -> p hb e", p=128), wv)
    for h in range(NH):
        b.wait("gpsimd", "gld")
        dma(b, "gpsimd", G.wbuf[:, :, 0:128],
            d["encq"][h].rearrange("(db p) e -> p db e", p=128))
        b.wait("gpsimd", "gld")
        dma(b, "gpsimd", d["sc_encq"][h].rearrange("(db p) e -> p db e", p=128),
            G.wbuf[:, :, 0:128])
    for srci, dst, rows, cols in ((d["f1w"], d["sc_wf1"], D, FF1),
                                  (d["f2w"], d["sc_wf2"], FF1, D),
                                  (d["decw"], d["sc_dec"], D, D)):
        step = 8192 // cols * 128
        for r0 in range(0, rows, step):
            sl = slice(r0, r0 + step)
            wv = G.wbuf_f[:, 0:(step // 128) * cols].rearrange("p (a c) -> p a c", c=cols)
            b.wait("gpsimd", "gld")
            dma(b, "gpsimd", wv, srci[sl, :].rearrange("(hb p) e -> p hb e", p=128))
            b.wait("gpsimd", "gld")
            dma(b, "gpsimd", dst[sl, :].rearrange("(hb p) e -> p hb e", p=128), wv)
    b.fence()


def emit_group(b, G, d, grp):
    gs = grp * GRP
    t128 = gs // 128

    for half in range(2):
        ti = t128 + half
        ln_bcast(b, G, G.stats, ti)
        hs = slice(half * 128, (half + 1) * 128)
        for db in range(8):
            tt(b, "vector", G.scr[:, 0, 0:128], G.xres[:, db, gs + half * 128:gs + (half + 1) * 128],
               G.psA[:, 0:128], AT.subtract, inc=None)
            tt(b, "vector", G.xng[:, db, hs], G.scr[:, 0, 0:128], G.psB[:, 0:128], AT.mult, inc=None)
    tt(b, "vector", G.xxg[:, :, 1:GRP], G.xng[:, :, 0:GRP - 1], G.xng[:, :, 1:GRP], AT.subtract)
    tt(b, "vector", G.xxg[:, :, 0:1], G.xnprevc[:, :, :], G.xng[:, :, 0:1], AT.subtract)
    b.op("vector", "tensor_copy", G.xnprevc[:, :, :], G.xng[:, :, GRP - 1:GRP], inc="dv")

    def xmix_into(tgt, vi):
        b.wait("vector", "mm")
        b.wait("vector", "gld")
        vw = v8(G.scrb, 2, GRP)
        tt(b, "vector", vw, G.xxg[:, :, :],
           G.mixs[:, :, vi:vi + 1].broadcast_to([128, 8, GRP]), AT.mult, inc=None)
        tt(b, "vector", tgt[:, :, :], vw, G.xng[:, :, :], AT.add, inc="dv")

    b.wait("sync", "mm")
    dma(b, "sync", G.wbuf[:, :, :],
        d["sc_wproj"][6].rearrange("(db p) e -> p db e", p=128))
    xmix_into(G.xmx, 6)
    b.wait("tensor", "ld")
    b.wait("tensor", "dv"); b.wait("tensor", "sc")
    for tc in range(2):
        b.wait("tensor", "sc")
        for eh in range(2):
            pst = G.psA if eh == 0 else G.psB
            for db in range(8):
                mm(b, pst[:, 0:512], G.xmx[:, db, tc * 128:(tc + 1) * 128],
                   G.wbuf[:, db, eh * 512:(eh + 1) * 512], start=(db == 0), stop=(db == 7))
        b.wait("scalar", "mm")
        b.wait("scalar", "gld")
        act(b, G.scrb[:, 0, 0:512], G.psA[:, 0:512], AF.Sigmoid)
        act(b, G.scrb[:, 0, 512:1024], G.psB[:, 0:512], AF.Sigmoid, inc="sc")
        b.wait("gpsimd", "sc")
        dma(b, "gpsimd", d["sc_g"][gs + tc * 128:gs + (tc + 1) * 128, :],
            G.scrb[:, 0, :])
    xmix_into(G.xmv, 3)

    for p, vi, stream in (("r", 0, G.s_r), ("w", 1, None), ("k", 2, G.s_k),
                          ("a", 4, G.s_a), ("b", 5, G.s_b)):
        wi = "rwkvabg".index(p)
        b.wait("sync", "mm")
        b.wait("sync", "sc")
        dma(b, "sync", G.wbuf[:, :, :],
            d["sc_wproj"][wi].rearrange("(db p) e -> p db e", p=128))
        xmix_into(G.xmx, vi)
        b.wait("tensor", "ld")
        b.wait("tensor", "dv"); b.wait("tensor", "sc")
        for eb in range(8):
            pst = G.psA if eb % 2 == 0 else G.psB
            b.wait("tensor", "dv"); b.wait("tensor", "sc")
            for db in range(8):
                mm(b, pst[:, 0:GRP], G.wbuf[:, db, eb * 128:(eb + 1) * 128],
                   G.xmx[:, db, :], start=(db == 0), stop=(db == 7))
            if p == "r":
                b.wait("vector", "mm")
                b.op("vector", "tensor_copy", stream[:, eb, :], pst[:, 0:GRP], inc="dv")
            elif p == "k":
                b.wait("scalar", "mm")
                act(b, stream[:, eb, :], pst[:, 0:GRP], AF.Tanh, inc="dv")
            elif p == "w":
                b.wait("scalar", "mm")
                act(b, G.scr[:, 0, :], pst[:, 0:GRP], AF.Sigmoid, scale=-1.0, inc="sc")
                b.wait("vector", "sc")
                b.op("vector", "tensor_scalar_max", G.scr[:, 1, :], G.scr[:, 0, :], 1e-38, inc="dv")
                b.wait("scalar", "dv")
                act(b, G.wcum[:, eb, :], G.scr[:, 1, :], AF.Ln, inc="dv")
            else:
                b.wait("scalar", "mm")
                act(b, G.scr[:, 0, :], pst[:, 0:GRP], AF.Square, inc="sc")
                b.wait("tensor", "sc")
                mm(b, G.psST[0:2, 0, 0:GRP], G.one2[:, :], G.scr[:, 0, :], inc="mm")
                b.wait("vector", "mm")
                b.op("vector", "tensor_copy", G.scr[0:2, 1, :], G.psST[0:2, 0, 0:GRP], inc="dv")
                b.wait("scalar", "dv")
                b.op("scalar", "sqrt", G.scr[0:2, 1, :], G.scr[0:2, 1, :], inc="sc")
                b.wait("vector", "sc")
                b.op("vector", "tensor_scalar_max", G.scr[0:2, 1, :], G.scr[0:2, 1, :], 1e-6)
                b.op("vector", "reciprocal", G.scr[0:2, 1, :], G.scr[0:2, 1, :], inc="dv")
                b.wait("tensor", "dv"); b.wait("tensor", "sc")
                mm(b, G.psST[:, 1, 0:GRP], G.sel2[:, :], G.scr[0:2, 1, :], inc="mm")
                b.wait("vector", "mm")
                b.op("vector", "tensor_copy", G.scr[:, 2, :], G.psST[:, 1, 0:GRP])
                tt(b, "vector", stream[:, eb, :], pst[:, 0:GRP], G.scr[:, 2, :], AT.mult, inc="dv")

    b.wait("sync", "mm")
    b.wait("sync", "sc")
    b.wait("sync", "dv")
    dma(b, "sync", G.wbuf[:, :, :],
        d["sc_wproj"][3].rearrange("(db p) e -> p db e", p=128))
    b.wait("tensor", "ld")
    b.op("vector", "memset", G.scr[:, 0, 0:GRP], 1.0)
    for hp in range(8):
        b.op("vector", "tensor_tensor_scan", G.wcum[:, hp, :],
             G.scr[:, 0, 0:GRP], G.wcum[:, hp, :], 0.0, AT.mult, AT.add)

    for ci in range(CPG):
        emit_chunk(b, G, d, grp, grp * CPG + ci, ci * C)


def emit_chunk(b, G, d, grp, ch, cs):
    sl = slice(cs, cs + C)

    b.wait("tensor", "dv"); b.wait("tensor", "sc")
    for par in range(2):
        if par:
            b.wait("tensor", "dv")
        tp = (0, 64) if par else None
        po = 64 * par
        for eh in range(2):
            pst = G.psA if eh == 0 else G.psB
            for db in range(8):
                mm(b, pst[po:po + 64, 0:512], G.xmv[:, db, sl],
                   G.wbuf[:, db, eh * 512:(eh + 1) * 512], start=(db == 0), stop=(db == 7), tp=tp)
        b.wait("vector", "mm")
        b.op("vector", "tensor_copy", G.vtok[po:po + 64, 0:512], G.psA[po:po + 64, 0:512])
        b.op("vector", "tensor_copy", G.vtok[po:po + 64, 512:1024], G.psB[po:po + 64, 0:512], inc="dv")

    if cs == 0:
        b.op("vector", "tensor_copy", G.wcc[:, :, 0:C], G.wcum[:, :, sl])
    else:
        tt(b, "vector", G.wcc[:, :, 0:C], G.wcum[:, :, sl],
           G.wcum[:, :, cs - 1:cs].broadcast_to([128, 8, C]), AT.subtract)
    b.op("vector", "memset", G.wcc[:, :, C:C + 1], 0.0)
    b.op("vector", "tensor_copy", G.wcc[:, :, C + 1:2 * C], G.wcc[:, :, 0:C - 1])
    rf = G.wcc[:, :, C // 2 - 1:C // 2]
    tt(b, "vector", G.ftmp[:, :, 0:C], G.wcc[:, :, 0:C], rf.broadcast_to([128, 8, C]), AT.subtract)
    tt(b, "vector", G.ftmp[:, :, C:2 * C], G.wcc[:, :, C:2 * C], rf.broadcast_to([128, 8, C]),
       AT.subtract, inc="dv")
    b.wait("scalar", "dv")
    act(b, G.ftmp[:, :, 0:C], G.ftmp[:, :, 0:C], AF.Exp)
    act(b, G.ftmp[:, :, C:2 * C], G.ftmp[:, :, C:2 * C], AF.Exp)
    act(b, G.fcol[:, :, 0:1], rf, AF.Exp)
    act(b, G.fcol[:, :, 2:3], G.wcc[:, :, C - 1:C], AF.Exp, inc="sc")
    b.wait("vector", "sc")
    b.wait("vector", "gld")
    b.op("vector", "reciprocal", G.ftmp[:, :, 2 * C:3 * C], G.ftmp[:, :, 0:C])
    tt(b, "vector", G.rhsar[:, :, 0:C], G.s_a[:, :, sl], G.ftmp[:, :, C:2 * C], AT.mult)
    tt(b, "vector", G.rhsar[:, :, C:2 * C], G.s_r[:, :, sl], G.ftmp[:, :, 0:C], AT.mult)
    tt(b, "vector", G.lhsbk[:, :, 0:C], G.s_b[:, :, sl], G.ftmp[:, :, 2 * C:3 * C], AT.mult)
    tt(b, "vector", G.lhsbk[:, :, C:2 * C], G.s_k[:, :, sl], G.ftmp[:, :, 2 * C:3 * C], AT.mult)
    tt(b, "vector", G.til[:, :, 0:C], G.rhsar[:, :, 0:C],
       G.fcol[:, :, 0:1].broadcast_to([128, 8, C]), AT.mult)
    tt(b, "vector", G.til[:, :, C:2 * C], G.rhsar[:, :, C:2 * C],
       G.fcol[:, :, 0:1].broadcast_to([128, 8, C]), AT.mult)
    f6 = v8(G.scr, 2, C)
    tt(b, "vector", f6, G.ftmp[:, :, 2 * C:3 * C],
       G.ftmp[:, :, C - 1:C].broadcast_to([128, 8, C]), AT.mult)
    tt(b, "vector", G.til[:, :, 2 * C:3 * C], G.s_b[:, :, sl], f6, AT.mult)
    tt(b, "vector", G.til[:, :, 3 * C:4 * C], G.s_k[:, :, sl], f6, AT.mult, inc="dv")

    b.wait("tensor", "dv"); b.wait("tensor", "sc")
    for h in range(H):
        q = 64 * (h & 1)
        hp = h >> 1
        tp = (q, q) if q else None
        mm(b, G.psST[q:q + 64, hp, 0:128], G.lhsbk[q:q + 64, hp, 0:C],
           G.rhsar[q:q + 64, hp, :], tp=tp)
        mm(b, G.psST[q:q + 64, hp, 128:256], G.lhsbk[q:q + 64, hp, C:2 * C],
           G.rhsar[q:q + 64, hp, :], tp=tp)
        mm(b, G.psCH[q:q + 64, hp, 0:64], G.rhsar[q:q + 64, hp, 0:C],
           G.lhsbk[q:q + 64, hp, 0:C], tp=tp)
    b.wait("vector", "mm")
    tt(b, "vector", G.stsb[:, :, :], G.psST[:, :, :],
       G.msk4[:, :].unsqueeze(1).broadcast_to([128, 8, 256]), AT.mult)
    tt(b, "vector", G.ntb[:, :, :], G.psCH[:, :, 0:64],
       G.msklo[:, :].unsqueeze(1).broadcast_to([128, 8, 64]), AT.mult, inc="dv")

    b.wait("tensor", "dv"); b.wait("tensor", "sc")
    for h in range(H):
        q = 64 * (h & 1)
        hp = h >> 1
        tp = (q, q) if q else None
        mm(b, G.psST[q:q + 64, hp, 0:64], G.til[q:q + 64, hp, 2 * C:3 * C],
           G.idb[q:q + 64, q:q + 64], tp=tp)
        mm(b, G.psST[q:q + 64, hp, 64:128], G.til[q:q + 64, hp, 3 * C:4 * C],
           G.idb[q:q + 64, q:q + 64], tp=tp)
    b.wait("vector", "mm")
    b.op("vector", "tensor_copy", G.trsb[:, :, 0:128], G.psST[:, :, 0:128])
    b.op("vector", "tensor_copy", G.slbf[:, :, :], G.slst[:, :, :], inc="dv")

    b.wait("tensor", "dv"); b.wait("tensor", "sc")
    for h in range(H):
        q = 64 * (h & 1)
        hp = h >> 1
        tp = (q, q) if q else None
        mm(b, G.psCH[q:q + 64, hp, 64:128], G.til[q:q + 64, hp, 0:C],
           G.idb[q:q + 64, q:q + 64], tp=tp)
        mm(b, G.psCH[q:q + 64, hp, 0:64], G.til[q:q + 64, hp, 0:C],
           G.slbf[q:q + 64, hp, :], tp=tp, stop=False)
        mm(b, G.psCH[q:q + 64, hp, 0:64], G.stsb[q:q + 64, hp, 2 * C:3 * C],
           G.vtok[q:q + 64, h * K:(h + 1) * K], tp=tp, start=False)
    b.wait("vector", "mm")
    b.op("vector", "tensor_copy", G.ysbb[:, :, :], G.psCH[:, :, :], inc="dv")

    b.op("vector", "memset", G.pwst[0:64, :, 64:128], 0.0)
    b.op("vector", "memset", G.pwst[64:128, :, 0:64], 0.0)
    b.op("vector", "memset", G.pwts[0:64, :, 64:128], 0.0)
    b.op("vector", "memset", G.pwts[64:128, :, 0:64], 0.0)
    b.op("vector", "tensor_copy", G.pwst[0:64, :, 0:64], G.stsb[0:64, :, 0:C])
    b.op("vector", "tensor_copy", G.pwst[64:128, :, 64:128], G.stsb[64:128, :, 0:C])
    b.op("vector", "tensor_copy", G.pwts[0:64, :, 0:64], G.ntb[0:64, :, :])
    b.op("vector", "tensor_copy", G.pwts[64:128, :, 64:128], G.ntb[64:128, :, :], inc="dv")

    for lev in range(6):
        b.wait("tensor", "dv"); b.wait("tensor", "sc")
        for hp in range(8):
            mm(b, G.psCH[:, hp, 0:128], G.idb[:, :], G.ysbb[:, hp, :],
               start=True, stop=False)
            mm(b, G.psCH[:, hp, 0:128], G.pwst[:, hp, :], G.ysbb[:, hp, :],
               start=False, stop=True)
        b.wait("vector", "mm")
        b.op("vector", "tensor_copy", G.ysbb[:, :, :], G.psCH[:, :, :], inc="dv")
        if lev < 5:
            b.wait("tensor", "dv"); b.wait("tensor", "sc")
            for hp in range(8):
                mm(b, G.psST[:, hp, 0:128], G.pwts[:, hp, :], G.pwst[:, hp, :])
            if lev < 4:
                for hp in range(8):
                    mm(b, G.psST[:, hp, 128:256], G.pwst[:, hp, :], G.pwts[:, hp, :])
            b.wait("vector", "mm")
            b.op("vector", "tensor_copy", G.pwst[:, :, :], G.psST[:, :, 0:128])
            if lev < 4:
                b.op("vector", "tensor_copy", G.pwts[:, :, :], G.psST[:, :, 128:256])


    b.wait("tensor", "dv"); b.wait("tensor", "sc")
    for h in range(H):
        q = 64 * (h & 1)
        hp = h >> 1
        tp = (q, q) if q else None
        mm(b, G.psCH[q:q + 64, hp, 0:64], G.til[q:q + 64, hp, C:2 * C],
           G.slbf[q:q + 64, hp, :], tp=tp, stop=False)
        mm(b, G.psCH[q:q + 64, hp, 0:64], G.stsb[q:q + 64, hp, C:2 * C],
           G.ysbb[q:q + 64, hp, 0:64], tp=tp, start=False, stop=False)
        mm(b, G.psCH[q:q + 64, hp, 0:64], G.stsb[q:q + 64, hp, 3 * C:4 * C],
           G.vtok[q:q + 64, h * K:(h + 1) * K], tp=tp, start=False)
    b.wait("vector", "mm")
    b.op("vector", "tensor_copy", G.ostg[:, :, :], G.psCH[:, :, 0:64], inc="dv")
    olr = d["sc_oloc"].rearrange("tt (hp par e) -> tt par hp e", par=2, e=K)
    for par in range(2):
        b.wait("gpsimd", "dv")
        dma(b, "gpsimd", olr[ch * C:(ch + 1) * C, par], G.ostg[par * 64:(par + 1) * 64, :, :])

    b.wait("tensor", "dv"); b.wait("tensor", "sc")
    for h in range(H):
        q = 64 * (h & 1)
        hp = h >> 1
        tp = (q, q) if q else None
        mm(b, G.psST[q:q + 64, hp, 0:64], G.trsb[q:q + 64, hp, 0:64],
           G.ysbb[q:q + 64, hp, 0:64], tp=tp, stop=False)
        mm(b, G.psST[q:q + 64, hp, 0:64], G.trsb[q:q + 64, hp, 64:128],
           G.vtok[q:q + 64, h * K:(h + 1) * K], tp=tp, start=False)
    b.wait("vector", "mm")
    tt(b, "vector", G.slst[:, :, :], G.slst[:, :, :],
       G.fcol[:, :, 2:3].broadcast_to([128, 8, K]), AT.mult)
    tt(b, "vector", G.slst[:, :, :], G.slst[:, :, :], G.psST[:, :, 0:64], AT.add, inc="dv")

    b.wait("tensor", "dv"); b.wait("tensor", "sc")
    for h in range(H):
        q = 64 * (h & 1)
        hp = h >> 1
        tp = (q, q) if q else None
        mm(b, G.psST[q:q + 64, hp, 64:128], G.ysbb[q:q + 64, hp, 64:128],
           G.trsb[q:q + 64, hp, 0:64], tp=tp)
    b.wait("vector", "mm")
    dsc = v8(G.scr, 2, K)
    tt(b, "vector", dsc, G.idd2[:, :].unsqueeze(1).broadcast_to([128, 8, 64]),
       G.fcol[:, :, 2:3].broadcast_to([128, 8, K]), AT.mult)
    tt(b, "vector", G.psb[:, :, :], G.psST[:, :, 64:128], dsc, AT.add, inc="dv")
    pr = d["sc_p"].rearrange("c (hp par) k k2 -> c par k hp k2", par=2)
    for par in range(2):
        b.wait("gpsimd", "dv")
        dma(b, "gpsimd", pr[ch, par], G.psb[par * 64:(par + 1) * 64, :, :])
    if ch == 0:
        b.op("vector", "tensor_copy", G.accs[:, :, :], G.psb[:, :, :])
        b.op("vector", "tensor_copy", G.accb[:, :, :], G.psb[:, :, :], inc="dv")
    else:
        b.wait("tensor", "dv"); b.wait("tensor", "sc")
        for h in range(H):
            q = 64 * (h & 1)
            hp = h >> 1
            tp = (q, q) if q else None
            mm(b, G.psCH[q:q + 64, hp, 64:128], G.psb[q:q + 64, hp, :], G.accb[q:q + 64, hp, :], tp=tp)
        b.wait("vector", "mm")
        b.op("vector", "tensor_copy", G.accs[:, :, :], G.psCH[:, :, 64:128])
        b.op("vector", "tensor_copy", G.accb[:, :, :], G.psCH[:, :, 64:128], inc="dv")

    b.wait("tensor", "dv"); b.wait("tensor", "sc")
    for h in range(H):
        q = 64 * (h & 1)
        hp = h >> 1
        tp = (q, q) if q else None
        mm(b, G.psCH[q:q + 64, hp, 0:64], G.stsb[q:q + 64, hp, C:2 * C],
           G.ysbb[q:q + 64, hp, 64:128], tp=tp)
    b.wait("vector", "mm")
    psig = G.scrb[:, 0:1, 0:512].rearrange("p a b -> p (a b)").rearrange("p (a c) -> p a c", c=64)
    b.op("vector", "tensor_copy", psig, G.psCH[:, :, 0:64], inc="dv")
    b.wait("tensor", "dv"); b.wait("tensor", "sc")
    for h in range(H):
        q = 64 * (h & 1)
        hp = h >> 1
        tp = (q, q) if q else None
        mm(b, G.psCH[q:q + 64, hp, 64:128], psig[q:q + 64, hp, :],
           G.idb[q:q + 64, q:q + 64], tp=tp)
    b.wait("vector", "mm")
    tt(b, "vector", G.psis[:, :, :], G.psCH[:, :, 64:128],
       G.til[:, :, C:2 * C], AT.add, inc="dv")
    for par in range(2):
        b.wait("gpsimd", "dv")
        dma(b, "gpsimd",
            d["sc_psi"].rearrange("c (hp par) k t -> c par k hp t", par=2)[ch, par],
            G.psis[par * 64:(par + 1) * 64, :, :])


def emit_relay(b, G, d):
    b.fence()
    b.wait("tensor", "dv"); b.wait("tensor", "sc")
    for h in range(H):
        q = 64 * (h & 1)
        hp = h >> 1
        tp = (q, q) if q else None
        mm(b, G.psCH[q:q + 64, hp, 0:64], G.accs[q:q + 64, hp, :],
           G.idf[q:q + 64, q:q + 64], tp=tp)
    b.wait("vector", "mm")
    pct = v8(G.scr2, 2, 64)
    b.op("vector", "tensor_copy", pct, G.psCH[:, :, 0:64], inc="dv")
    b.wait("gpsimd", "dv")
    for h in range(H):
        q = 64 * (h & 1)
        hp = h >> 1
        dma(b, "gpsimd", d["cc1_in"][h, 0:64, :], pct[q:q + 64, hp, :])
        dma(b, "gpsimd", d["cc1_in"][h, 64:128, :], G.slst[q:q + 64, hp, :])
    b.wait("gpsimd", "gld")
    b.op("gpsimd", "collective_compute", "AllGather", AT.bypass,
         replica_groups=[list(range(8))],
         ins=[d["cc1_in"].opt()], outs=[d["cc1_out"].opt()], inc="cc")
    b.wait("gpsimd", "cc")
    for e in ("vector", "tensor", "sync"):
        b.wait(e, "cc")

    b.op("vector", "memset", G.dsb[:, :, :], 0.0)
    b.op("vector", "memset", G.dbf[:, :, :], 0.0, inc="dv")
    for j in range(7):
        b.wait("sync", "dv")
        b.wait("gpsimd", "mm")
        ccr = d["cc1_out"].rearrange("(j hp par) r k -> j par r hp k", j=8, par=2)
        for par in range(2):
            dma(b, "gpsimd", G.accb[par * 64:(par + 1) * 64, :, :], ccr[j, par, 0:64])
            dma(b, "sync", G.accs[par * 64:(par + 1) * 64, :, :], ccr[j, par, 64:128])
        b.wait("tensor", "ld")
        b.wait("tensor", "gld")
        b.wait("tensor", "dv"); b.wait("tensor", "sc")
        for h in range(H):
            q = 64 * (h & 1)
            hp = h >> 1
            tp = (q, q) if q else None
            mm(b, G.psCH[q:q + 64, hp, 0:64], G.accb[q:q + 64, hp, :], G.dbf[q:q + 64, hp, :], tp=tp)
        b.wait("vector", "mm")
        b.wait("vector", "ld")
        b.wait("vector", "gld")
        tmp = v8(G.scr, 2, 64)
        tt(b, "vector", tmp, G.psCH[:, :, 0:64], G.accs[:, :, :], AT.add)
        ts(b, "vector", tmp, tmp, G.flgb[:, j:j + 1], None, AT.mult)
        ts(b, "vector", G.dsb[:, :, :], G.dsb[:, :, :], G.flgb[:, 16 + j:17 + j], None, AT.mult)
        tt(b, "vector", G.dsb[:, :, :], G.dsb[:, :, :], tmp, AT.add)
        b.op("vector", "tensor_copy", G.dbf[:, :, :], G.dsb[:, :, :], inc="dv")
    if DEBUG:
        b.wait("gpsimd", "dv")
        dma(b, "gpsimd", d["sc_dbg"], G.dsb[:, :, :])
    b.fence()


def emit_p3(b, G, d):
    for grp in range(NGRP):
        gs = grp * GRP
        for half in range(2):
            ti = gs // 128 + half
            ln_bcast(b, G, G.stats, ti)
            hs = slice(half * 128, (half + 1) * 128)
            for db in range(8):
                tt(b, "vector", G.scr[:, 0, 0:128], G.xres[:, db, gs + half * 128:gs + (half + 1) * 128],
                   G.psA[:, 0:128], AT.subtract, inc=None)
                tt(b, "vector", G.xng[:, db, hs], G.scr[:, 0, 0:128], G.psB[:, 0:128], AT.mult, inc=None)
        for hb in range(FF0 // 128):
            b.wait("sync", "mm")
            dma(b, "sync", G.wtile[:, :, :],
                d["sc_wg"][:, hb * 128:(hb + 1) * 128].rearrange("(db p) e -> p db e", p=128))
            dma(b, "sync", G.wtile2[:, :, :],
                d["sc_wu"][:, hb * 128:(hb + 1) * 128].rearrange("(db p) e -> p db e", p=128))
            b.wait("tensor", "ld")
            b.wait("tensor", "dv"); b.wait("tensor", "sc")
            b.wait("tensor", "sc")
            for db in range(8):
                mm(b, G.psA[:, 0:GRP], G.wtile[:, db, :], G.xng[:, db, :],
                   start=(db == 0), stop=(db == 7))
            for db in range(8):
                mm(b, G.psB[:, 0:GRP], G.wtile2[:, db, :], G.xng[:, db, :],
                   start=(db == 0), stop=(db == 7))
            b.wait("scalar", "mm")
            act(b, G.scr[:, 0, :], G.psA[:, 0:GRP], AF.Silu, scale=1.0 / 127.0, inc="sc")
            b.wait("vector", "sc")
            stt(b, "vector", G.hstr[:, hb, :], G.psB[:, 0:GRP], 1.0 / 127.0, G.scr[:, 0, :],
                AT.mult, AT.mult, inc="dv")
        def dpsum(eb):
            return DPSUM(G, eb)
        for hb in range(FF0 // 128):
            b.wait("sync", "mm")
            dma(b, "sync", G.wdtile[:, :], d["sc_wd"][hb * 128:(hb + 1) * 128, :])
            b.wait("tensor", "ld")
            if hb == 0:
                b.wait("tensor", "dv"); b.wait("tensor", "sc")
                b.wait("tensor", "sc")
            for eb in range(8):
                mm(b, dpsum(eb), G.wdtile[:, eb * 128:(eb + 1) * 128], G.hstr[:, hb, :],
                   start=(hb == 0), stop=(hb == FF0 // 128 - 1))
        b.wait("vector", "mm")
        for eb in range(8):
            stt(b, "vector", G.xres[:, eb, gs:gs + GRP], dpsum(eb), 1.0 / 127.0,
                G.xres[:, eb, gs:gs + GRP], AT.mult, AT.add)
    b.fence()

    for i, srcrow, tgt in ((0, d["gnwr"], G.gnwb), (1, d["gnbr"], G.gnbb)):
        b.wait("sync", "mm")
        b.wait("sync", "dv")
        dma(b, "sync", G.rowscr[0:1, 0:512], srcrow[0:1, 0:512])
        b.wait("tensor", "ld")
        mm(b, G.psA[:, 0:512], G.oner[:, :], G.rowscr[0:1, 0:512], inc="mm")
        b.wait("sync", "mm")
        dma(b, "sync", G.rowscr[0:1, 0:512], srcrow[0:1, 512:1024])
        b.wait("vector", "mm")
        b.op("vector", "tensor_copy", tgt[:, 0:512], G.psA[:, 0:512], inc="dv")
        b.wait("tensor", "ld")
        b.wait("tensor", "dv"); b.wait("tensor", "sc")
        mm(b, G.psA[:, 0:512], G.oner[:, :], G.rowscr[0:1, 0:512], inc="mm")
        b.wait("vector", "mm")
        b.op("vector", "tensor_copy", tgt[:, 512:1024], G.psA[:, 0:512], inc="dv")
    dma(b, "sync", G.bocol[:, :, :], d["bor"].unsqueeze(2), slow=True)
    dma(b, "gpsimd", G.wotb[:, :, :],
        d["wot"].rearrange("(db p) e -> p db e", p=128))
    b.fence()

    for ti in range(16):
        b.wait("sync", "dv")
        dma(b, "sync", G.scrb[:, 0, :], d["sc_oloc"][ti * 128:(ti + 1) * 128, :])
        dma(b, "sync", G.scrb[:, 1, :], d["sc_g"][ti * 128:(ti + 1) * 128, :])
        for cpar in range(2):
            ch = 2 * ti + cpar
            b.wait("sync", "mm")
            psir = d["sc_psi"].rearrange("c (hp par) k t -> c par k hp t", par=2)
            pldr = d["sc_p"].rearrange("c (hp par) k k2 -> c par k hp k2", par=2)
            for par in range(2):
                dma(b, "sync", G.psil[par * 64:(par + 1) * 64, :, :], psir[ch, par])
                dma(b, "sync", G.pldl[par * 64:(par + 1) * 64, :, :], pldr[ch, par])
            b.wait("tensor", "ld")
            b.wait("tensor", "dv"); b.wait("tensor", "sc")
            for h in range(H):
                q = 64 * (h & 1)
                hp = h >> 1
                mm(b, G.psST[cpar * 64:(cpar + 1) * 64, h >> 2, (h % 4) * 64:(h % 4) * 64 + 64],
                   G.psil[q:q + 64, hp, :], G.dbf[q:q + 64, hp, :],
                   tp=(q, cpar * 64) if (q or cpar) else None)
            for h in range(H):
                q = 64 * (h & 1)
                hp = h >> 1
                mm(b, G.psCH[q:q + 64, hp, 0:64], G.pldl[q:q + 64, hp, :],
                   G.dbf[q:q + 64, hp, :], tp=(q, q) if q else None)
            b.wait("vector", "mm")
            b.op("vector", "tensor_copy", G.dsb[:, :, :], G.psCH[:, :, 0:64])
            b.op("vector", "tensor_copy", G.dbf[:, :, :], G.dsb[:, :, :], inc="dv")
        b.wait("vector", "ld")
        tt(b, "vector", G.otr_f[:, :], G.scrb[:, 0, :], G.psST[:, 0:4, :].rearrange("p a c -> p (a c)"), AT.add)
        gm = G.otr_f[:, :].rearrange("p (h k) -> p h k", h=16)
        b.op("vector", "reduce_sum", G.gnsc[:, :, 0:1], gm, axis=AX, inc="dv")
        b.wait("scalar", "dv")
        act(b, G.scr[:, 0:4, :].rearrange("p a c -> p (a c)"), G.otr_f[:, :], AF.Square, inc="sc")
        b.wait("vector", "sc")
        b.op("vector", "reduce_sum", G.gnsc[:, :, 1:2],
             G.scr[:, 0:4, :].rearrange("p a c -> p (a c)").rearrange("p (h k) -> p h k", h=16), axis=AX)
        ts(b, "vector", G.gnsc[:, :, 2:3], G.gnsc[:, :, 0:1], 1.0 / 64, None, AT.mult)
        ts(b, "vector", G.gnsc[:, :, 3:4], G.gnsc[:, :, 1:2], 1.0 / 64, None, AT.mult)
        tt(b, "vector", G.gnsc[:, :, 4:5], G.gnsc[:, :, 2:3], G.gnsc[:, :, 2:3], AT.mult)
        tt(b, "vector", G.gnsc[:, :, 5:6], G.gnsc[:, :, 3:4], G.gnsc[:, :, 4:5], AT.subtract)
        ts(b, "vector", G.gnsc[:, :, 5:6], G.gnsc[:, :, 5:6], GN_EPS, None, AT.add, inc="dv")
        b.wait("scalar", "dv")
        b.op("scalar", "sqrt", G.gnsc[:, :, 6:7], G.gnsc[:, :, 5:6], inc="sc")
        b.wait("vector", "sc")
        b.op("vector", "reciprocal", G.gnsc[:, :, 7:8], G.gnsc[:, :, 6:7])
        tt(b, "vector", gm, gm, G.gnsc[:, :, 2:3].broadcast_to([128, 16, 64]), AT.subtract)
        tt(b, "vector", gm, gm, G.gnsc[:, :, 7:8].broadcast_to([128, 16, 64]), AT.mult)
        flat = G.otr_f[:, :]
        tt(b, "vector", flat, flat, G.gnwb[:, :], AT.mult)
        tt(b, "vector", flat, flat, G.gnbb[:, :], AT.add)
        tt(b, "vector", flat, flat, G.scrb[:, 1, :], AT.mult, inc="dv")
        b.wait("tensor", "dv"); b.wait("tensor", "sc")
        for db in range(8):
            pst = G.psA if db % 2 == 0 else G.psB
            b.wait("tensor", "dv"); b.wait("tensor", "sc")
            tr(b, pst[:, 128:256], flat[:, db * 128:(db + 1) * 128], G.idf[:, :], inc="mm")
            b.wait("vector", "mm")
            b.op("vector", "tensor_copy", G.ztl[:, db, :], pst[:, 128:256], inc="dv")
        b.wait("tensor", "dv"); b.wait("tensor", "sc")
        for eb in range(8):
            pst = G.psA if eb % 2 == 0 else G.psB
            for db in range(8):
                mm(b, pst[:, 0:128], G.wotb[:, db, eb * 128:(eb + 1) * 128],
                   G.ztl[:, db, :], start=(db == 0), stop=(db == 7))
            b.wait("vector", "mm")
            ts(b, "vector", G.scr2[:, 0, 0:128], pst[:, 0:128], G.bocol[:, eb, 0:1], None, AT.add)
            tt(b, "vector", G.xres[:, eb, ti * 128:(ti + 1) * 128],
               G.xres[:, eb, ti * 128:(ti + 1) * 128], G.scr2[:, 0, 0:128], AT.add, inc="dv")
            b.wait("tensor", "dv"); b.wait("tensor", "sc")
    b.wait("tensor", "dv"); b.wait("tensor", "sc")
    for db in range(8):
        pst = G.psA if db % 2 == 0 else G.psB
        b.wait("tensor", "dv"); b.wait("tensor", "sc")
        tr(b, pst[0:64, 256:384], G.xres[:, db, TLOC - 64:TLOC], G.idf[:, :], inc="mm")
        b.wait("vector", "mm")
        b.op("vector", "tensor_copy", G.tokst[0:64, db * 128:(db + 1) * 128], pst[0:64, 256:384], inc="dv")
    b.wait("gpsimd", "dv")
    dma(b, "gpsimd", d["cc2_in"][0:64, :], G.tokst[0:64, :])
    b.fence()


def emit_p4(b, G, d):
    for db in range(8):
        dma(b, "gpsimd", G.encvb[:, db, :].rearrange("p (h c) -> p h c", h=8),
            d["encv"][:, db * 128:(db + 1) * 128, :].rearrange("h p e -> p h e"))
    for ti in range(16):
        tsl = slice(ti * 128, (ti + 1) * 128)
        b.wait("scalar", "dv")
        act(b, G.scr[:, 0:4, :].rearrange("p a c -> p (a c)").rearrange("p (a c) -> p a c", a=8)[:, :, 0:128],
            G.xres[:, :, tsl], AF.Square, inc="sc")
        b.wait("tensor", "sc")
        for db in range(8):
            mm(b, G.psCH[0:1, 0, 0:128], G.onec[:, :], G.xres[:, db, tsl],
               start=(db == 0), stop=(db == 7))
        for db in range(8):
            mm(b, G.psCH[0:1, 1, 0:128], G.onec[:, :],
               G.scr[:, 0:4, :].rearrange("p a c -> p (a c)").rearrange("p (a c) -> p a c", a=8)[:, db, 0:128],
               start=(db == 0), stop=(db == 7))
        b.wait("vector", "mm")
        ts(b, "vector", G.rowscr[0:1, 0:128], G.psCH[0:1, 0, 0:128], 1.0 / D, None, AT.mult)
        ts(b, "vector", G.rowscr[0:1, 128:256], G.psCH[0:1, 1, 0:128], 1.0 / D, None, AT.mult)
        tt(b, "vector", G.rowscr[0:1, 256:384], G.rowscr[0:1, 0:128], G.rowscr[0:1, 0:128], AT.mult)
        tt(b, "vector", G.rowscr[0:1, 128:256], G.rowscr[0:1, 128:256], G.rowscr[0:1, 256:384], AT.subtract)
        ts(b, "vector", G.rowscr[0:1, 128:256], G.rowscr[0:1, 128:256], 1e-5, None, AT.add, inc="dv")
        b.wait("scalar", "dv")
        b.op("scalar", "sqrt", G.rowscr[0:1, 128:256], G.rowscr[0:1, 128:256], inc="sc")
        b.wait("vector", "sc")
        b.op("vector", "reciprocal", G.rowscr[0:1, 128:256], G.rowscr[0:1, 128:256], inc="dv")
        b.wait("tensor", "dv"); b.wait("tensor", "sc")
        tr(b, G.psCH[:, 2, 0:1], G.rowscr[0:1, 0:128], G.idf[0:1, 0:1]) if False else None
        mm(b, G.psCH[:, 2, 0:1], G.rowscr[0:1, 0:128], G.onec[0:1, 0:1])
        mm(b, G.psCH[:, 3, 0:1], G.rowscr[0:1, 128:256], G.onec[0:1, 0:1], inc="mm")
        b.wait("vector", "mm")
        b.op("vector", "tensor_copy", G.stats2[:, ti, 0:1], G.psCH[:, 2, 0:1])
        b.op("vector", "tensor_copy", G.stats2[:, ti, 1:2], G.psCH[:, 3, 0:1], inc="dv")
    b.wait("gpsimd", "gld")
    for e in ("tensor", "vector", "scalar"):
        b.wait(e, "gld")
    for grp in range(NGRP):
        gs = grp * GRP
        for half in range(2):
            ti = gs // 128 + half
            ln_bcast(b, G, G.stats2, ti)
            hs = slice(half * 128, (half + 1) * 128)
            for db in range(8):
                tt(b, "vector", G.scr[:, 0, 0:128], G.xres[:, db, gs + half * 128:gs + (half + 1) * 128],
                   G.psA[:, 0:128], AT.subtract, inc=None)
                tt(b, "vector", G.xng[:, db, hs], G.scr[:, 0, 0:128], G.psB[:, 0:128], AT.mult, inc=None)
        b.wait("tensor", "dv"); b.wait("tensor", "sc")
        for h in range(NH):
            pst = G.psA if h % 2 == 0 else G.psB
            b.wait("sync", "mm")
            dma(b, "sync", G.encb[:, :, :],
                d["sc_encq"][h].rearrange("(db p) e -> p db e", p=128))
            b.wait("tensor", "ld")
            for db in range(8):
                mm(b, pst[:, 0:GRP], G.encb[:, db, :],
                   G.xng[:, db, :], start=(db == 0), stop=(db == 7))
            b.wait("scalar", "mm")
            act(b, G.qts[:, h, 64 + gs:64 + gs + GRP], pst[:, 0:GRP], AF.Relu, inc="dv")
            b.wait("tensor", "dv"); b.wait("tensor", "sc")
        for half in range(2):
            tc = gs // 128 + half
            pst = G.psA if half == 0 else G.psB
            b.wait("tensor", "dv"); b.wait("tensor", "sc")
            for db in range(8):
                mm(b, pst[:, 0:512], G.xng[:, db, half * 128:(half + 1) * 128],
                   G.encvb[:, db, 0:512], start=(db == 0), stop=(db == 7))
            b.wait("scalar", "mm")
            act(b, G.vts[:, tc, 0:512], pst[:, 0:512], AF.Relu, inc="dv")
            b.wait("tensor", "dv"); b.wait("tensor", "sc")
            for db in range(8):
                mm(b, pst[:, 0:512], G.xng[:, db, half * 128:(half + 1) * 128],
                   G.encvb[:, db, 512:1024], start=(db == 0), stop=(db == 7))
            b.wait("scalar", "mm")
            act(b, G.vts[:, tc, 512:1024], pst[:, 0:512], AF.Relu, inc="dv")
    b.fence()
    for h in range(NH):
        b.op("vector", "reduce_sum", G.sigc[:, h, 0:1], G.qts[:, h, 64:64 + TLOC], axis=AX)
        b.op("vector", "reduce_sum", G.sigc[:, h, 1:2], G.qts[:, h, 64:64 + TLOC:8], axis=AX)
    b.wait("gpsimd", "dv")
    dma(b, "gpsimd", d["cc2_in"][64:72, 0:128].rearrange("h p -> p h"), G.sigc[:, :, 0], slow=True)
    dma(b, "gpsimd", d["cc2_in"][72:80, 0:128].rearrange("h p -> p h"), G.sigc[:, :, 1], slow=True)
    b.op("vector", "memset", G.tokst2[0:16, 0:896], 0.0, inc="dv")
    b.wait("gpsimd", "dv")
    dma(b, "gpsimd", d["cc2_in"][64:80, 128:1024], G.tokst2[0:16, 0:896])
    b.wait("gpsimd", "gld")
    b.op("gpsimd", "collective_compute", "AllGather", AT.bypass,
         replica_groups=[list(range(8))],
         ins=[d["cc2_in"].opt()], outs=[d["cc2_out"].opt()], inc="cc")
    b.wait("gpsimd", "cc")
    for e in ("tensor", "vector", "scalar", "sync"):
        b.wait(e, "cc")


def emit_p5(b, G, d):
    b.op("vector", "memset", G.tokst[:, :], 0.0, inc="dv")
    for j in range(8):
        b.wait("sync", "dv")
        dma(b, "sync", G.tokst2[0:64, :], d["cc2_out"][j * 80:j * 80 + 64, :])
        b.wait("vector", "ld")
        ts(b, "vector", G.tokst2[0:64, :], G.tokst2[0:64, :], G.flgb[0:64, 24 + j:25 + j], None, AT.mult)
        tt(b, "vector", G.tokst[0:64, :], G.tokst[0:64, :], G.tokst2[0:64, :], AT.add, inc="dv")
    b.op("vector", "reduce_sum", G.colsc[0:64, 0:1], G.tokst[0:64, :], axis=AX, inc="dv")
    b.wait("scalar", "dv")
    act(b, G.scr[0:64, 0:4, :].rearrange("p a c -> p (a c)"), G.tokst[0:64, :], AF.Square, inc="sc")
    b.wait("vector", "sc")
    b.op("vector", "reduce_sum", G.colsc[0:64, 1:2], G.scr[0:64, 0:4, :].rearrange("p a c -> p (a c)"), axis=AX)
    ts(b, "vector", G.colsc[0:64, 2:3], G.colsc[0:64, 0:1], 1.0 / D, None, AT.mult)
    ts(b, "vector", G.colsc[0:64, 3:4], G.colsc[0:64, 1:2], 1.0 / D, None, AT.mult)
    tt(b, "vector", G.colsc[0:64, 4:5], G.colsc[0:64, 2:3], G.colsc[0:64, 2:3], AT.mult)
    tt(b, "vector", G.colsc[0:64, 5:6], G.colsc[0:64, 3:4], G.colsc[0:64, 4:5], AT.subtract)
    ts(b, "vector", G.colsc[0:64, 5:6], G.colsc[0:64, 5:6], 1e-5, None, AT.add, inc="dv")
    b.wait("scalar", "dv")
    b.op("scalar", "sqrt", G.colsc[0:64, 6:7], G.colsc[0:64, 5:6], inc="sc")
    b.wait("vector", "sc")
    b.op("vector", "reciprocal", G.colsc[0:64, 7:8], G.colsc[0:64, 6:7])
    ts(b, "vector", G.tokst[0:64, :], G.tokst[0:64, :], G.colsc[0:64, 2:3], G.colsc[0:64, 7:8],
       AT.subtract, AT.mult, inc="dv")
    b.wait("tensor", "dv"); b.wait("tensor", "sc")
    for db in range(8):
        pst = G.psA if db % 2 == 0 else G.psB
        b.wait("tensor", "dv"); b.wait("tensor", "sc")
        tr(b, pst[:, 0:64], G.tokst[0:64, db * 128:(db + 1) * 128], G.idf[0:64, 0:64], inc="mm")
        b.wait("vector", "mm")
        b.op("vector", "tensor_copy", G.xhalT[:, db, :], pst[:, 0:64], inc="dv")
    b.wait("tensor", "dv"); b.wait("tensor", "sc")
    for h in range(NH):
        pst = G.psA if h % 2 == 0 else G.psB
        b.wait("sync", "mm")
        dma(b, "sync", G.encb[:, :, :],
            d["sc_encq"][h].rearrange("(db p) e -> p db e", p=128))
        b.wait("tensor", "ld")
        for db in range(8):
            mm(b, pst[:, 64:128], G.encb[:, db, :],
               G.xhalT[:, db, :], start=(db == 0), stop=(db == 7))
        b.wait("scalar", "mm")
        act(b, G.qts[:, h, 0:64], pst[:, 64:128], AF.Relu, inc="dv")
        b.wait("tensor", "dv"); b.wait("tensor", "sc")
    for eh in range(2):
        pst = G.psA if eh == 0 else G.psB
        b.wait("tensor", "dv"); b.wait("tensor", "sc")
        for db in range(8):
            mm(b, pst[64:128, 0:512], G.xhalT[:, db, :], G.encvb[:, db, eh * 512:(eh + 1) * 512],
               start=(db == 0), stop=(db == 7), tp=(0, 64))
        b.wait("scalar", "mm")
        act(b, G.vhalo[64:128, eh * 512:(eh + 1) * 512], pst[64:128, 0:512], AF.Relu, inc="dv")
    b.op("vector", "memset", G.sigc[:, :, 2:4], 0.0, inc="dv")
    for j in range(8):
        b.wait("sync", "dv")
        dma(b, "sync", G.mlpc[:, :, 0], d["cc2_out"][j * 80 + 64:j * 80 + 72, 0:128].rearrange(
            "h p -> p h"), slow=True)
        dma(b, "sync", G.mlpc[:, :, 1], d["cc2_out"][j * 80 + 72:j * 80 + 80, 0:128].rearrange(
            "h p -> p h"), slow=True)
        b.wait("vector", "ld")
        ts(b, "vector", G.mlpc[:, :, 0:2], G.mlpc[:, :, 0:2], G.flgb[:, 8 + j:9 + j], None, AT.mult)
        tt(b, "vector", G.sigc[:, :, 2:4], G.sigc[:, :, 2:4], G.mlpc[:, :, 0:2], AT.add, inc="dv")
    ts(b, "vector", G.sigc[:, :, 2:3], G.sigc[:, :, 2:3], 1.0 / 8192.0, None, AT.mult)
    ts(b, "vector", G.sigc[:, :, 3:4], G.sigc[:, :, 3:4], 0.5 / 1024.0, None, AT.mult)
    tt(b, "vector", G.sigc[:, :, 2:3], G.sigc[:, :, 2:3], G.sigc[:, :, 3:4], AT.add)
    b.op("vector", "tensor_copy", G.scrb[:, 0, 0:8], G.sigc[:, :, 2], inc="dv")
    sigb = G.scrb[:, 0, 0:8].rearrange("p (a c) -> p a c", a=8)
    b.wait("gpsimd", "dv")
    dma(b, "gpsimd", G.encvb_f[:, 0:4096].rearrange("p (a c) -> p a c", a=8),
        d["mw1"].rearrange("(db p) e -> p db e", p=128))
    b.wait("sync", "dv")
    dma(b, "sync", G.mlpc[:, 0:4, 0:1], d["mb1"].unsqueeze(2), slow=True)
    dma(b, "sync", G.mlpc[:, :, 2:3], d["mb2"].unsqueeze(2), slow=True)
    dma(b, "sync", G.bmc[:, :, 0:1], d["basem"].unsqueeze(2), slow=True)
    b.wait("tensor", "gld")
    b.wait("tensor", "ld")
    for e4 in range(4):
        for db in range(8):
            mm(b, G.psCH[:, 4, e4:e4 + 1], G.encvb[:, db, e4 * 128:(e4 + 1) * 128],
               sigb[:, db, 0:1], start=(db == 0), stop=(db == 7))
    b.wait("vector", "mm")
    b.wait("vector", "ld")
    ts(b, "vector", G.mlpc[:, 0:4, 1:2].flatten().unsqueeze(1) if False else G.scr[:, 0, 0:4],
       G.psCH[:, 4, 0:4], 1.0, None, AT.mult)
    tt(b, "vector", G.scr[:, 0, 0:4], G.scr[:, 0, 0:4], G.mlpc[:, 0:4, 0], AT.add, inc="dv")
    b.wait("scalar", "dv")
    act(b, G.scr[:, 1, 0:4], G.scr[:, 0, 0:4], AF.Gelu, inc="sc")
    b.wait("vector", "sc")
    b.op("vector", "tensor_copy", G.scrb[:, 0, 16:20], G.scr[:, 1, 0:4], inc="dv")
    h1b = G.scrb[:, 0, 16:20].rearrange("p (a c) -> p a c", a=4)
    b.wait("gpsimd", "dv")
    dma(b, "gpsimd", G.encvb_f[:, 0:4096].rearrange("p (a c) -> p a c", a=4),
        d["mw2"].rearrange("(db p) e -> p db e", p=128))
    b.wait("tensor", "gld")
    for eb in range(8):
        for db in range(4):
            mm(b, G.psCH[:, 5, eb:eb + 1], G.encvb[:, db, eb * 128:(eb + 1) * 128],
               h1b[:, db, 0:1], start=(db == 0), stop=(db == 3))
    b.wait("vector", "mm")
    tt(b, "vector", G.scr[:, 0, 0:8], G.psCH[:, 5, 0:8], G.mlpc[:, :, 2], AT.add)
    ts(b, "vector", G.scr[:, 0, 0:8], G.scr[:, 0, 0:8], 0.1, None, AT.mult)
    tt(b, "vector", G.scr[:, 0, 0:8], G.scr[:, 0, 0:8], G.bmc[:, :, 0], AT.add, inc="dv")
    b.wait("scalar", "dv")
    act(b, G.scr[:, 1, 0:8], G.scr[:, 0, 0:8], AF.Sigmoid, scale=-1.0, inc="sc")
    b.wait("vector", "sc")
    b.op("vector", "tensor_scalar_max", G.scr[:, 1, 0:8], G.scr[:, 1, 0:8], 1e-38, inc="dv")
    b.wait("scalar", "dv")
    act(b, G.scr[:, 2, 0:8], G.scr[:, 1, 0:8], AF.Ln, inc="sc")
    b.wait("vector", "sc")
    ts(b, "vector", G.metr[:, :, 0:1], G.scr[:, 2, 0:8].unsqueeze(2) if False else
       G.scr[:, 2, 0:8].rearrange("p (a c) -> p a c", a=8), -1.0, None, AT.mult, inc="dv")
    for h in range(NH):
        b.wait("gpsimd", "dv")
        b.wait("gpsimd", "mm")
        dma(b, "gpsimd", G.scrb[0:32, 1, 0:128], d["hut"][h])
        dma(b, "gpsimd", G.scrb[0:32, 1, 128:256], d["hvv"][h])
        b.wait("tensor", "gld")
        mm(b, G.psCH[:, 6, 0:128], G.scrb[0:32, 1, 0:128], G.scrb[0:32, 1, 128:256], inc="mm")
        b.wait("vector", "mm")
        b.op("vector", "tensor_copy", G.uvs[:, h * 128:(h + 1) * 128], G.psCH[:, 6, 0:128], inc="dv")
    b.fence()


def emit_p6a(b, G, d):
    for h in range(NH):
        for c in range(17):
            if c == 16:
                qsl = G.qts[:, h, 0:64]
                rows = 64
            else:
                qsl = G.qts[:, h, 64 + c * 128:64 + (c + 1) * 128]
                rows = 128
            b.wait("vector", "mm")
            ts(b, "vector", G.qmb[:, 0:rows], qsl, G.metr[:, h, 0:1], None, AT.mult)
            tt(b, "vector", G.qsq[:, 0:rows], G.qmb[:, 0:rows], qsl, AT.mult)
            tt(b, "vector", G.scr[:, 0, 0:rows], qsl, qsl, AT.mult, inc="dv")
            b.wait("tensor", "dv"); b.wait("tensor", "sc")
            tgt = G.acols[0:rows, h, 16:17] if c == 16 else G.acols[:, h, c:c + 1]
            tg2 = G.sqnc[0:rows, h, 16:17] if c == 16 else G.sqnc[:, h, c:c + 1]
            mm(b, G.psCH[0:rows, 7, 0:1], G.qsq[:, 0:rows], G.onec[:, 0:1])
            mm(b, G.psCH[0:rows, 7, 1:2], G.scr[:, 0, 0:rows], G.onec[:, 0:1], inc="mm")
            b.wait("vector", "mm")
            b.op("vector", "tensor_copy", tgt, G.psCH[0:rows, 7, 0:1])
            b.op("vector", "tensor_copy", tg2, G.psCH[0:rows, 7, 1:2], inc="dv")

    for c in range(16):
        for h in range(NH):
            qsl = G.qts[:, h, 64 + c * 128:64 + (c + 1) * 128]
            b.wait("tensor", "dv"); b.wait("tensor", "sc")
            if c == 0:
                mm(b, G.psCH[0:1, 6, 0:64], G.acols[0:64, h, 16:17], G.idf[0:64, 0:64])
            else:
                mm(b, G.psCH[0:1, 6, 0:64], G.acols[64:128, h, c - 1:c],
                   G.idf[64:128, 64:128], tp=(64, 0))
            mm(b, G.psCH[0:1, 7, 0:128], G.acols[:, h, c:c + 1], G.idf[:, :], inc="mm")
            b.wait("vector", "mm")
            b.op("vector", "tensor_copy", G.rowscr[0:1, 256:320], G.psCH[0:1, 6, 0:64])
            b.op("vector", "tensor_copy", G.rowscr[0:1, 320:448], G.psCH[0:1, 7, 0:128], inc="dv")
            b.wait("tensor", "dv"); b.wait("tensor", "sc")
            mm(b, G.psB[:, 0:192], G.oner[:, :], G.rowscr[0:1, 256:448])
            ts(b, "vector", G.qmb[:, :], qsl, G.metr[:, h, 0:1], None, AT.mult, inc="dv")
            b.wait("tensor", "dv"); b.wait("tensor", "sc")
            mm(b, G.psA[:, 0:192], G.qmb[:, :], G.qts[:, h, c * 128:c * 128 + 192], inc="mm")
            b.wait("vector", "mm")
            ts(b, "vector", G.colsc[:, 16:17], G.acols[:, h, c:c + 1], 1e-8, None, AT.add)
            ts(b, "vector", G.dtl[:, :], G.psA[:, 0:192], -2.0, G.colsc[:, 16:17], AT.mult, AT.add)
            tt(b, "vector", G.dtl[:, :], G.dtl[:, :], G.psB[:, 0:192], AT.add)
            b.op("vector", "tensor_scalar_max", G.dtl[:, :], G.dtl[:, :], 1e-8, inc="dv")
            b.wait("scalar", "dv")
            b.op("scalar", "sqrt", G.dtl[:, :], G.dtl[:, :], inc="sc")
            b.wait("vector", "sc")
            if c == 0:
                tt(b, "vector", G.colsc[:, 17:18], G.dtl[:, 64:65], G.flgb[:, 7:8], AT.mult)
                ts(b, "vector", G.dtl[:, 0:64], G.dtl[:, 0:64], G.flgb[:, 23:24], None, AT.mult)
                tt(b, "vector", G.dtl[:, 0:64], G.dtl[:, 0:64],
                   G.colsc[:, 17:18].broadcast_to([128, 64]), AT.add)
            stt(b, "vector", G.neg1[:, :], G.dtl[:, :], -1.0, G.bndb[:, :], AT.mult, AT.subtract, inc="dv")
            b.wait("scalar", "dv")
            act(b, G.expd[:, :], G.neg1[:, :], AF.Exp, inc="sc")
            b.wait("vector", "sc")
            cur, nxt = G.neg1, G.neg2
            for it in range(4):
                b.op("vector", "max", G.m8[:, :], cur[:, :])
                b.op("vector", "match_replace", nxt[:, :], G.m8[:, :], cur[:, :], MARK)
                cur, nxt = nxt, cur
            ts(b, "vector", G.wse[:, :], cur[:, :], -0.5e20, None, AT.is_le)
            tt(b, "vector", G.wse[:, :], G.wse[:, :], G.expd[:, :], AT.mult, inc="dv")
            if c == 0:
                b.op("vector", "reduce_sum", G.rsum[:, 0:1], G.wse[:, 0:64], axis=AX)
                tt(b, "vector", G.rsum[:, 1:2], G.rsum[:, 0:1], G.flgb[:, 7:8], AT.mult)
                ts(b, "vector", G.wse[:, 0:64], G.wse[:, 0:64], G.flgb[:, 23:24], None, AT.mult)
                tt(b, "vector", G.wse[:, 64:65], G.wse[:, 64:65], G.rsum[:, 1:2], AT.add, inc="dv")
            b.op("vector", "reduce_sum", G.rsum[:, 2:3], G.wse[:, :], axis=AX)
            ts(b, "vector", G.rsum[:, 2:3], G.rsum[:, 2:3], 1e-8, None, AT.add)
            b.op("vector", "reciprocal", G.rsum[:, 3:4], G.rsum[:, 2:3])
            ts(b, "vector", G.wse[:, :], G.wse[:, :], G.rsum[:, 3:4], None, AT.mult)
            b.op("vector", "tensor_copy", G.wsb[:, :], G.wse[:, :], inc="dv")
            b.wait("tensor", "dv"); b.wait("tensor", "sc")
            mm(b, G.psCH[64:128, 0, 0:128], G.wsb[:, 0:64], G.idb[:, :], tp=(0, 64))
            mm(b, G.psCH[0:64, 1, 0:128], G.wsb[:, 64:128], G.idb[:, :])
            mm(b, G.psCH[64:128, 2, 0:128], G.wsb[:, 128:192], G.idb[:, :],
               tp=(0, 64), inc="mm")
            b.wait("vector", "mm")
            b.op("vector", "tensor_copy", G.wtb[64:128, 0:128], G.psCH[64:128, 0, 0:128])
            b.op("vector", "tensor_copy", G.wtb[0:64, 128:256], G.psCH[0:64, 1, 0:128])
            b.op("vector", "tensor_copy", G.wtb[64:128, 256:384],
                 G.psCH[64:128, 2, 0:128], inc="dv")
            b.wait("tensor", "dv"); b.wait("tensor", "sc")
            vprev = G.vhalo[64:128, h * 128:(h + 1) * 128] if c == 0 else                 G.vts[64:128, c - 1, h * 128:(h + 1) * 128]
            mm(b, G.psCH[:, 3, 0:128], G.wtb[64:128, 0:128], vprev, tp=(64, 0), stop=False)
            mm(b, G.psCH[:, 3, 0:128], G.wtb[0:64, 128:256],
               G.vts[0:64, c, h * 128:(h + 1) * 128], start=False, stop=False)
            mm(b, G.psCH[:, 3, 0:128], G.wtb[64:128, 256:384],
               G.vts[64:128, c, h * 128:(h + 1) * 128], tp=(64, 0), start=False)
            mm(b, G.psCH[:, 4, 0:128], qsl, G.uvs[:, h * 128:(h + 1) * 128])
            mm(b, G.psCH[:, 5, 0:128], qsl, G.idb[:, :], inc="mm")
            b.wait("vector", "mm")
            ts(b, "vector", G.colsc[:, 18:19], G.sqnc[:, h, c:c + 1], 0.04, None, AT.is_gt, inc="dv")
            b.wait("scalar", "dv")
            b.op("scalar", "sqrt", G.colsc[:, 19:20], G.sqnc[:, h, c:c + 1], inc="sc")
            b.wait("vector", "sc")
            b.op("vector", "tensor_scalar_max", G.colsc[:, 19:20], G.colsc[:, 19:20], 1e-12)
            b.op("vector", "reciprocal", G.colsc[:, 20:21], G.colsc[:, 19:20])
            tt(b, "vector", G.colsc[:, 21:22], G.colsc[:, 18:19], G.colsc[:, 20:21], AT.mult)
            ts(b, "vector", G.colsc[:, 21:22], G.colsc[:, 21:22], 0.1, None, AT.mult)
            b.op("vector", "tensor_copy", G.qsq[:, :], G.psCH[:, 4, 0:128])
            stt(b, "vector", G.ctmp[:, :], G.qsq[:, :], G.colsc[:, 21:22],
                G.psCH[:, 3, 0:128], AT.mult, AT.add)
            tt(b, "vector", G.ostok[:, :], G.ctmp[:, :], G.psCH[:, 5, 0:128],
               AT.mult, inc="dv")
            b.wait("tensor", "dv"); b.wait("tensor", "sc")
            mm(b, G.psCH[:, 6, 0:128], G.ostok[:, :], G.idb[:, :], inc="mm")
            b.wait("vector", "mm")
            b.wait("vector", "gld")
            b.op("vector", "tensor_copy", G.ospst[:, h, :], G.psCH[:, 6, 0:128], inc="dv")
        b.wait("gpsimd", "dv")
        dma(b, "gpsimd", d["sc_osp"].rearrange("h p t -> p h t")[:, :, c * 128:(c + 1) * 128],
            G.ospst[:, :, :])
    b.fence()


def emit_p6b(b, G, d):
    MAGIC = 12582912.0
    b.op("vector", "memset", G.colsc[:, 22:23], 0.25, inc="dv")
    dma(b, "sync", G.b1c[:, :, 0:1], d["f1b"].unsqueeze(2), slow=True)
    dma(b, "sync", G.b2c[:, :, 0:1], d["f2b"].unsqueeze(2), slow=True)
    for grp in range(NGRP):
        gs = grp * GRP
        for half in range(2):
            ti = gs // 128 + half
            ln_bcast(b, G, G.stats2, ti)
            hs = slice(half * 128, (half + 1) * 128)
            for db in range(8):
                tt(b, "vector", G.scr[:, 0, 0:128], G.xres[:, db, gs + half * 128:gs + (half + 1) * 128],
                   G.psA[:, 0:128], AT.subtract, inc=None)
                tt(b, "vector", G.xng[:, db, hs], G.scr[:, 0, 0:128], G.psB[:, 0:128], AT.mult, inc=None)
        for fb in range(FF1 // 128):
            b.wait("sync", "mm")
            dma(b, "sync", G.wtile[:, :, :],
                d["sc_wf1"][:, fb * 128:(fb + 1) * 128].rearrange("(db p) e -> p db e", p=128))
            b.wait("tensor", "ld")
            b.wait("tensor", "dv"); b.wait("tensor", "sc")
            b.wait("tensor", "sc")
            for db in range(8):
                mm(b, G.psA[:, 0:GRP], G.wtile[:, db, :], G.xng[:, db, :],
                   start=(db == 0), stop=(db == 7))
            b.wait("vector", "mm")
            ts(b, "vector", G.pwl[:, 0, :], G.psA[:, 0:GRP], G.b1c[:, fb, 0:1], 256.0, AT.add, AT.mult)
            b.op("vector", "tensor_copy", G.hq32[:, :], G.pwl[:, 0, :])
            b.op("vector", "tensor_copy", G.pwl[:, 1, :], G.hq32[:, :], inc="dv")
            hq = G.pwl[:, 1, :]
            ts(b, "vector", G.pwl[:, 2, :], hq, 1.0 / 64.0, 80.5078125, AT.mult, AT.add)
            b.op("vector", "tensor_copy", G.hq32[:, :], G.pwl[:, 2, :])
            b.op("vector", "tensor_copy", G.pwl[:, 2, :], G.hq32[:, :])
            ts(b, "vector", G.pwl[:, 2, :], G.pwl[:, 2, :], 81.0, 0.25, AT.subtract, AT.mult)
            ts(b, "vector", G.pwl[:, 2, :], G.pwl[:, 2, :], -4.0, 4.0, AT.max, AT.min, inc="dv")
            XL = G.pwl[:, 2, :]
            b.wait("scalar", "dv")
            act(b, G.pwl[:, 3, :], XL, AF.Gelu)
            act(b, G.pwl[:, 4, :], XL, AF.Gelu, bias=G.colsc[:, 22:23], inc="sc")
            b.wait("vector", "sc")
            tt(b, "vector", G.pwl[:, 5, :], G.pwl[:, 4, :], G.pwl[:, 3, :], AT.subtract)
            stt(b, "vector", G.pwl[:, 6, :], hq, 1.0 / 256.0, XL, AT.mult, AT.subtract)
            stt(b, "vector", G.pwl[:, 6, :], G.pwl[:, 6, :], 4.0, G.pwl[:, 5, :], AT.mult, AT.mult)
            tt(b, "vector", G.pwl[:, 6, :], G.pwl[:, 6, :], G.pwl[:, 3, :], AT.add)
            ts(b, "vector", G.pwl[:, 7, :], hq, 1024.0, None, AT.is_gt)
            ts(b, "vector", G.pwl[:, 5, :], hq, 1.0 / 256.0, None, AT.mult)
            tt(b, "vector", G.pwl[:, 5, :], G.pwl[:, 5, :], G.pwl[:, 6, :], AT.subtract)
            tt(b, "vector", G.pwl[:, 5, :], G.pwl[:, 5, :], G.pwl[:, 7, :], AT.mult)
            tt(b, "vector", G.pwl[:, 6, :], G.pwl[:, 6, :], G.pwl[:, 5, :], AT.add)
            ts(b, "vector", G.pwl[:, 7, :], hq, -1024.0, None, AT.is_ge)
            tt(b, "vector", G.pwl[:, 6, :], G.pwl[:, 6, :], G.pwl[:, 7, :], AT.mult)
            ts(b, "vector", G.pwl[:, 0, :], G.pwl[:, 6, :], 256.0, None, AT.mult)
            b.op("vector", "tensor_copy", G.hq32[:, :], G.pwl[:, 0, :])
            b.op("vector", "tensor_copy", G.pwl[:, 0, :], G.hq32[:, :])
            ts(b, "vector", G.hstr[:, fb, :], G.pwl[:, 0, :], 1.0 / 256.0, None, AT.mult, inc="dv")
        def dpsum(eb):
            return DPSUM(G, eb)
        for fb in range(FF1 // 128):
            b.wait("sync", "mm")
            dma(b, "sync", G.wtile2[:, :], d["sc_wf2"][fb * 128:(fb + 1) * 128, :])
            b.wait("tensor", "ld")
            if fb == 0:
                b.wait("tensor", "dv"); b.wait("tensor", "sc")
                b.wait("tensor", "sc")
            for eb in range(8):
                mm(b, dpsum(eb), G.wtile2[:, eb * 128:(eb + 1) * 128],
                   G.hstr[:, fb, :], start=(fb == 0), stop=(fb == FF1 // 128 - 1))
        b.wait("vector", "mm")
        for eb in range(8):
            ts(b, "vector", G.pwl[:, 0, :], dpsum(eb), G.b2c[:, eb, 0:1], 256.0, AT.add, AT.mult)
            b.op("vector", "tensor_copy", G.hq32[:, :], G.pwl[:, 0, :])
            b.op("vector", "tensor_copy", G.pwl[:, 0, :], G.hq32[:, :])
            stt(b, "vector", G.xres[:, eb, gs:gs + GRP], G.pwl[:, 0, :], 1.0 / 256.0,
                G.xres[:, eb, gs:gs + GRP], AT.mult, AT.add)
        for hb in range(8):
            b.wait("sync", "mm")
            dma(b, "sync", G.wtile2[:, :], d["sc_dec"][hb * 128:(hb + 1) * 128, :])
            dma(b, "sync", G.ospl[:, hb, :], d["sc_osp"][hb, :, gs:gs + GRP])
            b.wait("tensor", "ld")
            if hb == 0:
                b.wait("tensor", "dv"); b.wait("tensor", "sc")
            for eb in range(8):
                mm(b, dpsum(eb), G.wtile2[:, eb * 128:(eb + 1) * 128],
                   G.ospl[:, hb, :], start=(hb == 0), stop=(hb == 7))
        b.wait("vector", "mm")
        for eb in range(8):
            tt(b, "vector", G.xres[:, eb, gs:gs + GRP], G.xres[:, eb, gs:gs + GRP],
               dpsum(eb), AT.add)
        for half in range(2):
            ti = gs // 128 + half
            b.wait("tensor", "dv"); b.wait("tensor", "sc")
            b.wait("vector", "st")
            for db in range(8):
                pst = G.psA if db % 2 == 0 else G.psB
                if db >= 2:
                    b.wait("tensor", "dv"); b.wait("tensor", "sc")
                tr(b, pst[:, 256:384], G.xres[:, db, ti * 128:(ti + 1) * 128], G.idf[:, :], inc="mm")
                b.wait("vector", "mm")
                b.op("vector", "tensor_copy", G.tokst[:, db * 128:(db + 1) * 128], pst[:, 256:384], inc="dv")
            b.wait("sync", "dv")
            dma(b, "sync", d["out"][ti * 128:(ti + 1) * 128, :], G.tokst[:, :], sem="st")
    b.wait("sync", "st")
    b.wait("gpsimd", "st")


def kernel(**inputs):
    ref_dt = np.float32
    x = np.ascontiguousarray(inputs["L0_attn_x"], dtype=np.float32)
    B, T, _ = x.shape
    nc = build_program()

    wT = {p: np.ascontiguousarray(inputs[f"L0_attn_W_{p}"].T) for p in "rwkvabg"}
    wot = np.ascontiguousarray(inputs["L0_attn_W_o"].T)
    mixv8 = np.stack([inputs[f"L0_attn_x_{p}"] for p in "rwkvabg"] + [np.zeros(D, np.float32)])
    mixv = np.ascontiguousarray(mixv8.reshape(8, 8, 128).transpose(2, 1, 0)).astype(np.float32)
    def dimmaj(vec):
        n = vec.shape[0] // 128
        return np.ascontiguousarray(np.asarray(vec, np.float32).reshape(n, 128).T)
    wg_i = np.ascontiguousarray(inputs["L0_ffn_w_gate"].T)
    wu_i = np.ascontiguousarray(inputs["L0_ffn_w_up"].T)
    wd_i = np.ascontiguousarray(inputs["L0_ffn_w_down"].T)
    hut = np.ascontiguousarray(np.transpose(inputs["L1_attn_hebb_U"], (0, 2, 1)))

    ident = np.eye(128, dtype=np.float32)
    up_s = np.triu(np.ones((64, 64), np.float32), 1)
    up_i = np.triu(np.ones((64, 64), np.float32), 0)
    m4 = np.concatenate([up_s, up_i, up_s, up_i], axis=1)
    mask4 = np.concatenate([m4, m4], axis=0)
    lo_s = np.tril(np.ones((64, 64), np.float32), -1)
    masklo = np.concatenate([lo_s, lo_s], axis=0)
    idd2 = np.concatenate([np.eye(64, dtype=np.float32)] * 2, axis=0)
    ttt = np.arange(128)[:, None]
    jjj = np.arange(192)[None, :]
    bm = ((jjj >= ttt + 1) & (jjj <= ttt + 64)).astype(np.float32)
    bandb = (1.0 - bm) * 1.0e30
    ones2 = np.zeros((128, 2), np.float32)
    ones2[0:64, 0] = 1.0
    ones2[64:128, 1] = 1.0
    sel2 = np.zeros((2, 128), np.float32)
    sel2[0, 0:64] = 1.0
    sel2[1, 64:128] = 1.0

    in_maps = []
    for c in range(8):
        bb, gp = c // 4, c % 4
        t0 = gp * TLOC
        xsl = np.ascontiguousarray(x[bb, t0:t0 + TLOC])
        if gp == 0:
            xp = np.zeros((1, D), np.float32)
        else:
            xp = np.ascontiguousarray(x[bb, t0 - 1:t0])
        fl = np.zeros((1, 32), np.float32)
        for j in range(7):
            if (j // 4 == bb) and (j < c):
                fl[0, j] = 1.0
                fl[0, 16 + j] = 0.0
            else:
                fl[0, j] = 0.0
                fl[0, 16 + j] = 1.0
        for j in range(8):
            fl[0, 24 + j] = 1.0 if (gp > 0 and j == c - 1) else 0.0
        fl[0, 7] = 1.0 if gp == 0 else 0.0
        fl[0, 23] = 0.0 if gp == 0 else 1.0
        for j in range(8):
            fl[0, 8 + j] = 1.0 if (j // 4 == bb) else 0.0
        im = dict(
            xs=xsl, xprev=xp, flags=fl,
            mixv=mixv, gnwr=inputs["L0_attn_gn_w"][None], gnbr=inputs["L0_attn_gn_b"][None],
            bor=dimmaj(inputs["L0_attn_b_o"]),
            wot=wot, wg_i=wg_i, wu_i=wu_i, wd_i=wd_i,
            encq=np.ascontiguousarray(inputs["L1_attn_enc_q"]),
            encv=np.ascontiguousarray(inputs["L1_attn_enc_v"]),
            mw1=np.ascontiguousarray(inputs["L1_attn_metric_w1"]),
            mb1=dimmaj(inputs["L1_attn_metric_b1"]),
            mw2=np.ascontiguousarray(inputs["L1_attn_metric_w2"]),
            mb2=dimmaj(inputs["L1_attn_metric_b2"]),
            basem=dimmaj(inputs["L1_attn_base_metric"].reshape(-1)),
            hut=hut, hvv=np.ascontiguousarray(inputs["L1_attn_hebb_V"]),
            decw=np.ascontiguousarray(inputs["L1_attn_decoder"]),
            f1w=np.ascontiguousarray(inputs["L1_ffn_w1"]), f1b=dimmaj(inputs["L1_ffn_b1"]),
            f2w=np.ascontiguousarray(inputs["L1_ffn_w2"]), f2b=dimmaj(inputs["L1_ffn_b2"]),
            identf=ident, identb=ident.astype(np.float32),
            mask4=mask4, masklo=masklo, idd2=idd2, bandm=bm, bandb=bandb,
            ones2=ones2, sel2=sel2, onesr=np.ones((1, 128), np.float32),
            onescol=np.ones((128, 1), np.float32),
        )
        for p in "rwkvabg":
            im[f"w{p}t"] = wT[p]
        in_maps.append(im)

    import ml_dtypes
    in_maps2 = []
    for im in in_maps:
        im2 = dict(im)
        im2["identb"] = ident.astype(ml_dtypes.bfloat16)
        in_maps2.append(im2)

    res = run_bass_kernel_spmd(nc, in_maps2, core_ids=list(range(8)))
    global LAST_RES
    LAST_RES = res
    outs = np.zeros((B, T, D), dtype=ref_dt)
    for c in range(8):
        bb, gp = c // 4, c % 4
        outs[bb, gp * TLOC:(gp + 1) * TLOC] = res.results[c]["out"]
    return outs


# version 03
# speedup vs baseline: 1.1796x; 1.1796x over previous
import numpy as np
from contextlib import ExitStack
import concourse.bass as bass
import concourse.mybir as mybir
from concourse.bass_utils import run_bass_kernel_spmd

F32 = mybir.dt.float32
BF16 = mybir.dt.bfloat16
I32 = mybir.dt.int32
AT = mybir.AluOpType
AF = mybir.ActivationFunctionType
AX = mybir.AxisListType.X

TLOC, D, C, H, K = 2048, 1024, 64, 16, 64
NCH = TLOC // C
NH, HD = 8, 128
GRP = 256
NGRP = TLOC // GRP
CPG = GRP // C
FF0, FF1 = 3072, 4096
MARK = -1.0e20
DEBUG = False
LAST_RES = None
GN_EPS = 0.00064
ENGS = ("sync", "tensor", "vector", "scalar", "gpsimd")


class Builder:

    def __init__(self, nc, es, semnames):
        self.nc = nc
        self.q = {e: [] for e in ENGS}
        self.semh = {n: es.enter_context(nc.semaphore(n)) for n in semnames}
        self.semv = {n: 0 for n in semnames}
        self.accessed = {}

    def _rec(self, eng, fn):
        self.q[eng].append(fn)

    @staticmethod
    def _fp(x):
        t = getattr(x, "tensor", None)
        if t is None:
            return None
        span = 1
        try:
            for st, n in x.ap:
                span += abs(int(st)) * (int(n) - 1)
        except Exception:
            span = 1 << 30
        off = int(x.offset)
        return (t.name, off, off + span)

    @staticmethod
    def _ov(a, bb):
        return a[0] == bb[0] and a[1] < bb[2] and bb[1] < a[2]

    def op(self, eng, meth, *a, inc=None, incn=1, **k):
        semh = self.semh
        import traceback as _tb
        site = _tb.extract_stack(limit=6)

        ENGSEM = {"vector": "dv", "scalar": "sc", "gpsimd": "gp", "tensor": "mm"}
        if (inc is None and eng in ENGSEM and meth not in
                ("sem_inc", "drain", "dma_start", "wait_ge", "collective_compute")):
            inc = ENGSEM[eng]
        if meth == "dma_start" and inc is not None and getattr(self, "_pubd", None):
            v = self._pubd.pop(inc, None)
            if v is not None:
                def swt(e, inc=inc, v=v):
                    e.wait_ge(semh[inc], v)
                self._rec(eng, swt)
        if getattr(self, "_pub", None) and self._pub.get(eng) and meth not in (
                "sem_inc", "drain", "wait_ge"):
            def drp(e):
                e.drain()
            self._rec(eng, drp)
            acc0 = self.accessed.setdefault(eng, {"w": [], "r": []})
            acc0["w"] = []
            acc0["r"] = []
            self._pub[eng] = False
        if eng in ("vector", "scalar", "gpsimd", "tensor") and meth not in (
                "sem_inc", "drain", "dma_start"):
            fps = [f for f in (self._fp(x) for x in list(a) + list(k.values()))
                   if f is not None]
            wr = [f for f in ([self._fp(a[0])] if a and self._fp(a[0]) else [])]
            acc = self.accessed.setdefault(eng, {"w": [], "r": []})
            need = any(self._ov(f, w) for f in fps for w in acc["w"]) or                    any(self._ov(w, r) for w in wr for r in acc["r"])
            if need:
                def dr(e):
                    e.drain()
                self._rec(eng, dr)
                acc["w"] = []
                acc["r"] = []
            acc["w"].extend(wr)
            acc["r"].extend(fps[1:] if len(fps) > 1 else [])

        if inc in ("dv", "sc", "gp", "mm") and meth not in (
                "sem_inc", "drain", "wait_ge"):
            if not hasattr(self, "_lastinc"):
                self._lastinc = {}
            if self._lastinc.get(inc, eng) != eng:
                self.wait(eng, inc)
            self._lastinc[inc] = eng

        def run(e, meth=meth, a=a, k=k, inc=inc, incn=incn, site=site):
            try:
                ins = getattr(e, meth)(*a, **k)
            except Exception:
                print("FAILED OP:", meth)
                for f in site:
                    if "kernel.py" in f.filename and f.name not in ("op", "run"):
                        print("   at", f.lineno, f.name, "|", f.line)
                raise
            if inc is not None:
                ins.then_inc(semh[inc], incn)
        self._rec(eng, run)
        if inc is not None:
            self.semv[inc] += incn
            return self.semv[inc]
        return None

    def wait(self, eng, sem, val=None):
        v = self.semv[sem] if val is None else val
        semh = self.semh
        SEM2ENG = {"dv": "vector", "sc": "scalar", "gp": "gpsimd", "mm": "tensor"}
        if sem in SEM2ENG and SEM2ENG[sem] != eng:
            if not hasattr(self, "_pub"):
                self._pub = {}
            self._pub[SEM2ENG[sem]] = True
        if sem in ("ld", "gld", "st", "sld"):
            if not hasattr(self, "_pubd"):
                self._pubd = {}
            self._pubd[sem] = max(v, self._pubd.get(sem, 0))

        def run(e, sem=sem, v=v):
            e.wait_ge(semh[sem], v)
        self._rec(eng, run)

    def sig(self, eng, sem, n=1):
        semh = self.semh
        self.semv[sem] += n
        v = self.semv[sem]

        def run(e, sem=sem, n=n):
            e.sem_inc(semh[sem], n)
        self._rec(eng, run)
        return v

    def fence(self, engines=("tensor", "vector", "scalar", "gpsimd", "sync")):
        for e in engines:
            for sem in ("mm", "dv", "sc", "gp", "ld", "gld", "sld", "cc"):
                if self.semv[sem]:
                    self.wait(e, sem)

    def replay(self, block):
        for ename in ENGS:
            dec = getattr(block, ename)
            ops = self.q[ename]

            def body(e, ops=ops):
                for fn in ops:
                    fn(e)
            dec(body)



def dma(b, qeng, out, in_, sem=None, slow=False, **kw):
    sem = sem or ("gld" if qeng == "gpsimd" else "ld")
    if slow:
        kw["allow_slow_non_contiguous"] = True
    return b.op(qeng, "dma_start", inc=sem, incn=16, out=out, in_=in_, **kw)


def build_program():
    nc = bass.Bass("TRN2", target_bir_lowering=False, debug=False)

    din = {}

    def IN(name, shape, dt=F32):
        din[name] = nc.dram_tensor(name, list(shape), dt, kind="ExternalInput").ap()
        return din[name]

    xs = IN("xs", (TLOC, D))
    xprev = IN("xprev", (1, D))
    flags = IN("flags", (1, 32))
    wpk = {p: IN(f"w{p}t", (D, D)) for p in "rwkvabg"}
    wot = IN("wot", (D, D))
    mixv = IN("mixv", (128, 8, 8))
    gnwr = IN("gnwr", (1, D))
    gnbr = IN("gnbr", (1, D))
    bor = IN("bor", (128, 8))
    wg_i = IN("wg_i", (D, FF0), I32)
    wu_i = IN("wu_i", (D, FF0), I32)
    wd_i = IN("wd_i", (FF0, D), I32)
    encq = IN("encq", (NH, D, HD))
    encv = IN("encv", (NH, D, HD))
    mw1 = IN("mw1", (D, 512))
    mb1 = IN("mb1", (128, 4))
    mw2 = IN("mw2", (512, D))
    mb2 = IN("mb2", (128, 8))
    basem = IN("basem", (128, 8))
    hut = IN("hut", (NH, 32, HD))
    hvv = IN("hvv", (NH, 32, HD))
    decw = IN("decw", (D, D))
    f1w = IN("f1w", (D, FF1))
    f1b = IN("f1b", (128, 32))
    f2w = IN("f2w", (FF1, D))
    f2b = IN("f2b", (128, 8))
    identf = IN("identf", (128, 128))
    identb = IN("identb", (128, 128), BF16)
    mask4 = IN("mask4", (128, 256))
    masklo = IN("masklo", (128, 64))
    idd2 = IN("idd2", (128, 64))
    bandm = IN("bandm", (128, 192))
    bandb = IN("bandb", (128, 192))
    ones2 = IN("ones2", (128, 2))
    sel2 = IN("sel2", (2, 128))
    onesr = IN("onesr", (1, 128))
    onescol = IN("onescol", (128, 1))

    out_h = nc.dram_tensor("out", [TLOC, D], F32, kind="ExternalOutput")
    out = out_h.ap()

    def DT(name, shape, dt=F32):
        skip = ("cc1_in", "cc2_in", "cc1_out", "cc2_out")
        if DEBUG and name not in skip:
            return nc.dram_tensor(name, list(shape), dt, kind="ExternalOutput").ap()
        return nc.dram_tensor(name, list(shape), dt).ap()

    sc_wproj = DT("sc_wproj", (7, D, D), BF16)
    sc_wg = DT("sc_wg", (D, FF0), BF16)
    sc_wu = DT("sc_wu", (D, FF0), BF16)
    sc_wd = DT("sc_wd", (FF0, D), BF16)
    sc_psi = DT("sc_psi", (NCH, H, K, C), BF16)
    sc_p = DT("sc_p", (NCH, H, K, K), BF16)
    sc_oloc = DT("sc_oloc", (TLOC, D), BF16)
    sc_g = DT("sc_g", (TLOC, D), BF16)
    sc_dbg = DT("sc_dbg", (128, 8, K), F32)
    cc1_in = DT("cc1_in", (H, 128, K))
    cc1_out = DT("cc1_out", (8 * H, 128, K))
    cc2_in = DT("cc2_in", (80, D))
    cc2_out = DT("cc2_out", (8 * 80, D))
    sc_wf1 = DT("sc_wf1", (D, FF1), BF16)
    sc_wf2 = DT("sc_wf2", (FF1, D), BF16)
    sc_dec = DT("sc_dec", (D, D), BF16)
    sc_osp = DT("sc_osp", (NH, 128, TLOC), BF16)
    sc_encq = DT("sc_encq", (NH, D, HD), BF16)

    es = ExitStack()

    def SB(name, shape, dt=F32):
        return es.enter_context(nc.sbuf_tensor(name, list(shape), dt))

    with es:
        g = {}
        g["xres"] = SB("xres", (128, 8, TLOC))
        g["stats"] = SB("stats", (128, 16, 2))
        g["stats2"] = SB("stats2", (128, 16, 2))
        g["mixs"] = SB("mixs", (128, 8, 8))
        g["xnprevc"] = SB("xnprevc", (128, 8, 1))
        g["idf"] = SB("idf", (128, 128))
        g["idb"] = SB("idb", (128, 128), BF16)
        g["msk4"] = SB("msk4", (128, 256))
        g["msklo"] = SB("msklo", (128, 64))
        g["idd2"] = SB("idd2s", (128, 64))
        g["bndm"] = SB("bndm", (128, 192))
        g["bndb"] = SB("bndb", (128, 192))
        g["one2"] = SB("one2", (128, 2))
        g["sel2"] = SB("sel2s", (2, 128))
        g["oner"] = SB("oner", (1, 128))
        g["onec"] = SB("onec", (128, 1))
        g["flgb"] = SB("flgb", (128, 32))
        g["rowscr"] = SB("rowscr", (1, 512))
        g["colsc"] = SB("colsc", (128, 24))
        g["dsb"] = SB("dsb", (128, 8, K))
        g["dbf"] = SB("dbf", (128, 8, K), BF16)
        g["scr"] = SB("scr", (128, 4, GRP))
        g["scr2"] = SB("scr2", (128, 4, GRP))
        g["scrb"] = SB("scrb", (128, 2, 1024), BF16)
        g["tokst"] = SB("tokst", (128, 1024))
        g["tokst2"] = SB("tokst2", (128, 1024))
        ARENA_COLS = 49792
        g["arena"] = SB("arena", (128, ARENA_COLS), BF16)

        g["psA"] = es.enter_context(nc.psum_tensor("psA", [128, 512], F32))
        g["psB"] = es.enter_context(nc.psum_tensor("psB", [128, 512], F32))
        g["psST"] = es.enter_context(nc.psum_tensor("psST", [128, 8, 256], F32))
        g["psCH"] = es.enter_context(nc.psum_tensor("psCH", [128, 8, 128], F32))

        class Arena:
            def __init__(self, t, cols):
                self.t = t
                self.cols = cols
                self.off = 0

            def take(self, shape, dt=F32):
                inner = 1
                for s in shape[1:]:
                    inner *= s
                n = inner * (2 if dt in (F32, I32) else 1)
                n = (n + 63) // 64 * 64
                assert self.off + n <= self.cols, f"arena overflow {self.off}+{n}"
                sl = self.t[:, self.off:self.off + n]
                self.off += n
                flat = (sl if dt == BF16 else sl.bitcast(dt))[:, 0:inner]
                v = flat
                if len(shape) == 3:
                    v = flat.rearrange("p (a c) -> p a c", a=shape[1])
                return v

            def reg(self, a, name, shape, dt=F32):
                inner = 1
                for s in shape[1:]:
                    inner *= s
                n = inner * (2 if dt in (F32, I32) else 1)
                n = (n + 63) // 64 * 64
                assert self.off + n <= self.cols, f"arena overflow {self.off}+{n}"
                sl = self.t[:, self.off:self.off + n]
                self.off += n
                flat = (sl if dt == BF16 else sl.bitcast(dt))[:, 0:inner]
                v = flat
                if len(shape) == 3:
                    v = flat.rearrange("p (a c) -> p a c", a=shape[1])
                a[name] = v
                a[name + "_f"] = flat
                return v

        def seta(ar):
            a = {}
            ar.reg(a, "wbuf", (128, 8, 1024), BF16)
            ar.reg(a, "s_r", (128, 8, GRP), BF16)
            ar.reg(a, "s_k", (128, 8, GRP), BF16)
            ar.reg(a, "s_a", (128, 8, GRP), BF16)
            ar.reg(a, "s_b", (128, 8, GRP), BF16)
            ar.reg(a, "wcum", (128, 8, GRP))
            ar.reg(a, "xng", (128, 8, GRP), BF16)
            ar.reg(a, "xxg", (128, 8, GRP), BF16)
            ar.reg(a, "xmx", (128, 8, GRP), BF16)
            ar.reg(a, "xmv", (128, 8, GRP), BF16)
            ar.reg(a, "wcc", (128, 8, 2 * C))
            ar.reg(a, "ftmp", (128, 8, 3 * C))
            ar.reg(a, "fcol", (128, 8, 4))
            ar.reg(a, "lhsbk", (128, 8, 2 * C), BF16)
            ar.reg(a, "rhsar", (128, 8, 2 * C), BF16)
            ar.reg(a, "til", (128, 8, 4 * C), BF16)
            ar.reg(a, "vtok", (128, H * K), BF16)
            ar.reg(a, "stsb", (128, 8, 4 * C), BF16)
            ar.reg(a, "ntb", (128, 8, C), BF16)
            ar.reg(a, "pwst", (128, 8, 128), BF16)
            ar.reg(a, "pwts", (128, 8, 128), BF16)
            ar.reg(a, "ysbb", (128, 8, 128), BF16)
            ar.reg(a, "trsb", (128, 8, 128), BF16)
            a["psis"] = a["lhsbk"].rearrange("p a c -> p (a c)").rearrange(
                "p (a c) -> p a c", c=C)[:, 0:8, :]
            a["psb"] = a["rhsar"].rearrange("p a c -> p (a c)").rearrange(
                "p (a c) -> p a c", c=K)[:, 0:8, :]
            a["ostg"] = a["rhsar"].rearrange("p a c -> p (a c)").rearrange(
                "p (a c) -> p a c", c=K)[:, 8:16, :]
            ar.reg(a, "slst", (128, 8, K))
            ar.reg(a, "slbf", (128, 8, K), BF16)
            ar.reg(a, "accs", (128, 8, K))
            ar.reg(a, "accb", (128, 8, K), BF16)
            return a

        def seta2(ar):
            a = {}
            ar.reg(a, "gnwb", (128, 1024))
            ar.reg(a, "gnbb", (128, 1024))
            ar.reg(a, "bocol", (128, 8, 1))
            ar.reg(a, "wotb", (128, 8, 1024), BF16)
            ar.reg(a, "wtile", (128, 8, 128), BF16)
            ar.reg(a, "wtile2", (128, 8, 128), BF16)
            ar.reg(a, "wtileB", (128, 8, 128), BF16)
            ar.reg(a, "wtile2B", (128, 8, 128), BF16)
            ar.reg(a, "wdtile", (128, 1024), BF16)
            ar.reg(a, "wdtileB", (128, 1024), BF16)
            ar.reg(a, "hstr", (128, 32, GRP), BF16)
            ar.reg(a, "xng", (128, 8, GRP), BF16)
            ar.reg(a, "psil", (128, 8, C), BF16)
            ar.reg(a, "pldl", (128, 8, K), BF16)
            ar.reg(a, "ztl", (128, 8, 128), BF16)
            ar.reg(a, "otr", (128, 8, 128))
            ar.reg(a, "gnsc", (128, 16, 8))
            return a

        def setb_shared(ar):
            a = {}
            ar.reg(a, "qts", (128, NH, 64 + TLOC), BF16)
            ar.reg(a, "vts", (128, 16, D), BF16)
            ar.reg(a, "vhalo", (128, D), BF16)
            ar.reg(a, "uvs", (128, NH * HD), BF16)
            ar.reg(a, "metr", (128, 8, 1))
            ar.reg(a, "acols", (128, 8, 17))
            ar.reg(a, "sqnc", (128, 8, 17))
            return a

        def setb0(ar):
            a = setb_shared(ar)
            ar.reg(a, "xng", (128, 8, GRP), BF16)
            ar.reg(a, "encvb", (128, 8, 1024), BF16)
            ar.reg(a, "encb", (128, 8, 128), BF16)
            ar.reg(a, "sigc", (128, 8, 4))
            ar.reg(a, "mlpc", (128, 8, 4))
            ar.reg(a, "bmc", (128, 8, 1))
            ar.reg(a, "xhalT", (128, 8, 64), BF16)
            return a

        def setb1(ar):
            a = setb_shared(ar)
            ar.reg(a, "dtl", (128, 192))
            ar.reg(a, "neg1", (128, 192))
            ar.reg(a, "neg2", (128, 192))
            ar.reg(a, "m8", (128, 8))
            ar.reg(a, "wse", (128, 192))
            ar.reg(a, "wsb", (128, 192), BF16)
            ar.reg(a, "expd", (128, 192))
            ar.reg(a, "wtb", (128, 384), BF16)
            ar.reg(a, "rsum", (128, 4))
            ar.reg(a, "qsq", (128, 128))
            ar.reg(a, "ospst", (128, 8, 128), BF16)
            ar.reg(a, "qmb", (128, 128), BF16)
            ar.reg(a, "qtkb", (128, 128), BF16)
            ar.reg(a, "ostok", (128, 128), BF16)
            ar.reg(a, "ctmp", (128, 128))
            return a

        def setb2(ar):
            a = {}
            ar.reg(a, "xng", (128, 8, GRP), BF16)
            ar.reg(a, "hstr", (128, 32, GRP), BF16)
            ar.reg(a, "wtile", (128, 8, 128), BF16)
            ar.reg(a, "wtileB", (128, 8, 128), BF16)
            ar.reg(a, "wtile2", (128, 1024), BF16)
            ar.reg(a, "wtile2B", (128, 1024), BF16)
            ar.reg(a, "hq32", (128, 2 * GRP), I32)
            ar.reg(a, "pwl", (128, 8, 2 * GRP))
            ar.reg(a, "b1c", (128, 32, 1))
            ar.reg(a, "b2c", (128, 8, 1))
            ar.reg(a, "ospl", (128, 8, GRP), BF16)
            return a

        g["_mk"] = (Arena, seta, seta2, setb0, setb1, setb2, ARENA_COLS)

        b = Builder(nc, es, ["ld", "gld", "sld", "mm", "dv", "sc", "gp", "st", "cc", "fn"])
        G = type("G", (), g)

        emit(b, G, dict(
            xs=xs, xprev=xprev, flags=flags, wpk=wpk, wot=wot, mixv=mixv,
            gnwr=gnwr, gnbr=gnbr, bor=bor, wg_i=wg_i, wu_i=wu_i, wd_i=wd_i,
            encq=encq, encv=encv, mw1=mw1, mb1=mb1, mw2=mw2, mb2=mb2,
            basem=basem, hut=hut, hvv=hvv, decw=decw, f1w=f1w, f1b=f1b,
            f2w=f2w, f2b=f2b, identf=identf, identb=identb, mask4=mask4,
            masklo=masklo, idd2=idd2, bandm=bandm, bandb=bandb, ones2=ones2,
            sel2=sel2, onesr=onesr, onescol=onescol, out=out,
            sc_wproj=sc_wproj, sc_wg=sc_wg, sc_wu=sc_wu, sc_wd=sc_wd,
            sc_psi=sc_psi, sc_p=sc_p, sc_oloc=sc_oloc, sc_g=sc_g, sc_dbg=sc_dbg,
            sc_wf1=sc_wf1, sc_wf2=sc_wf2, sc_dec=sc_dec, sc_osp=sc_osp, sc_encq=sc_encq,
            cc1_in=cc1_in, cc1_out=cc1_out, cc2_in=cc2_in, cc2_out=cc2_out,
        ))

        block = es.enter_context(nc.Block())
        b.replay(block)

    return nc



def _merge(G, extra):
    base = {k: v for k, v in G.__dict__.items() if not k.startswith("__")}
    base.update(extra)
    return type("G", (), base)


PHASES = 99


def _finish_stub(b, G, d, dump_xres=False):
    b.fence()
    if dump_xres:
        ov = d["out"].rearrange("t e -> (t e)").rearrange("(p r) -> p r", p=128)
        dma(b, "sync", ov, G.xres[:, :, :].rearrange("p a c -> p (a c)"), sem="st")
    else:
        b.op("vector", "memset", G.tokst[:, :], 0.0, inc="dv")
        b.wait("sync", "dv")
        for i in range(16):
            dma(b, "sync", d["out"][i * 128:(i + 1) * 128, :], G.tokst[:, :], sem="st")
    b.wait("sync", "st")
    b.wait("gpsimd", "st")


def emit(b, G, d):
    Arena, seta, seta2, setb0, setb1, setb2, AC = G._mk
    import kernel as _k
    ph = getattr(_k, "PHASES", 99)
    GA = _merge(G, seta(Arena(G.arena, AC)))
    emit_p0(b, GA, d)
    if ph <= 1:
        return _finish_stub(b, GA, d)
    emit_wconv(b, GA, d)
    if ph <= 2:
        return _finish_stub(b, GA, d)
    b.op("vector", "memset", GA.slst[:, :, :], 0.0, inc="dv")
    for grp in range(min(NGRP, ph - 2)):
        emit_group(b, GA, d, grp)
    if ph <= 10:
        return _finish_stub(b, GA, d)
    emit_relay(b, GA, d)
    if ph <= 11:
        return _finish_stub(b, GA, d)
    GA2 = _merge(G, seta2(Arena(G.arena, AC)))
    emit_p3(b, GA2, d)
    if ph <= 12:
        return _finish_stub(b, GA2, d, dump_xres=True)
    GB0 = _merge(G, setb0(Arena(G.arena, AC)))
    emit_p4(b, GB0, d)
    if ph <= 13:
        return _finish_stub(b, GB0, d, dump_xres=True)
    emit_p5(b, GB0, d)
    if ph <= 14:
        return _finish_stub(b, GB0, d, dump_xres=True)
    GB1 = _merge(G, setb1(Arena(G.arena, AC)))
    emit_p6a(b, GB1, d)
    if ph <= 15:
        return _finish_stub(b, GB1, d, dump_xres=True)
    GB2 = _merge(G, setb2(Arena(G.arena, AC)))
    emit_p6b(b, GB2, d)




def v8(t, pages, cols):
    return t[:, 0:pages, :].rearrange("p a b -> p (a b)").rearrange("p (a c) -> p a c", c=cols)

def tt(b, eng, outp, a0, a1, op, inc="dv"):
    return b.op(eng, "tensor_tensor", outp, a0, a1, op, inc=inc)


def ts(b, eng, outp, a0, s1, s2, op0, op1=None, inc="dv"):
    if op1 is None:
        return b.op(eng, "tensor_scalar", outp, a0, s1, None, op0, inc=inc)
    return b.op(eng, "tensor_scalar", outp, a0, s1, s2, op0, op1, inc=inc)


def stt(b, eng, outp, in0, scalar, in1, op0, op1, inc="dv"):
    return b.op(eng, "scalar_tensor_tensor", outp, in0, scalar, in1, op0, op1, inc=inc)


def act(b, outp, inp, func, inc="sc", **kw):
    return b.op("scalar", "activation", outp, inp, func, inc=inc, **kw)


def mm(b, outp, lhsT, rhs, start=True, stop=True, tp=None, inc=None, sgc=False):
    kw = {}
    if tp is not None:
        kw["tile_position"] = tp
    if sgc:
        kw["skip_group_check"] = True
    return b.op("tensor", "matmul", outp, lhsT, rhs, inc=inc, start=start, stop=stop, **kw)


def tr(b, outp, inp, ident, tp=None, inc=None):
    kw = {}
    if tp is not None:
        kw["tile_position"] = tp
    return b.op("tensor", "transpose", outp, inp, ident, inc=inc, **kw)


def DPSUM(G, eb):
    if eb == 0:
        return G.psA[:, 0:256]
    if eb == 1:
        return G.psB[:, 0:256]
    if eb < 6:
        return G.psST[:, (eb - 2) * 2, 0:256]
    return G.psCH[:, (eb - 6) * 4:(eb - 6) * 4 + 2, :].rearrange("p a b -> p (a b)")


def emit_p0(b, G, d):
    for outp, inp in [
        (G.idf[:, :], d["identf"]), (G.idb[:, :], d["identb"]),
        (G.msk4[:, :], d["mask4"]), (G.msklo[:, :], d["masklo"]),
        (G.idd2[:, :], d["idd2"]),
        (G.bndm[:, :], d["bandm"]), (G.bndb[:, :], d["bandb"]),
        (G.one2[:, :], d["ones2"]), (G.sel2[:, :], d["sel2"]),
        (G.oner[:, :], d["onesr"]), (G.onec[:, :], d["onescol"]),
        (G.tokst2[0:1, :], d["xprev"]),
    ]:
        dma(b, "sync", outp, inp)
    dma(b, "sync", G.mixs[:, :, :], d["mixv"])
    dma(b, "sync", G.scr[0:1, 0, 0:32], d["flags"])
    ldv = b.semv["ld"]

    b.wait("tensor", "ld")
    mm(b, G.psA[:, 0:32], G.oner[:, :], G.scr[0:1, 0, 0:32], inc="mm")
    b.wait("vector", "mm")
    b.op("vector", "tensor_copy", G.flgb[:, :], G.psA[:, 0:32], inc="dv")


    b.wait("vector", "ld", ldv)
    b.op("vector", "reduce_sum", G.colsc[0:1, 0:1], G.tokst2[0:1, :], axis=AX)
    tt(b, "vector", G.scr[0:1, 0, 0:GRP], G.tokst2[0:1, 0:GRP], G.tokst2[0:1, 0:GRP], AT.mult)
    tt(b, "vector", G.scr[0:1, 1, 0:GRP], G.tokst2[0:1, GRP:2 * GRP], G.tokst2[0:1, GRP:2 * GRP], AT.mult)
    tt(b, "vector", G.scr[0:1, 2, 0:GRP], G.tokst2[0:1, 2 * GRP:3 * GRP], G.tokst2[0:1, 2 * GRP:3 * GRP], AT.mult)
    tt(b, "vector", G.scr[0:1, 3, 0:GRP], G.tokst2[0:1, 3 * GRP:4 * GRP], G.tokst2[0:1, 3 * GRP:4 * GRP], AT.mult)
    tt(b, "vector", G.scr[0:1, 0, 0:GRP], G.scr[0:1, 0, 0:GRP], G.scr[0:1, 1, 0:GRP], AT.add)
    tt(b, "vector", G.scr[0:1, 2, 0:GRP], G.scr[0:1, 2, 0:GRP], G.scr[0:1, 3, 0:GRP], AT.add)
    tt(b, "vector", G.scr[0:1, 0, 0:GRP], G.scr[0:1, 0, 0:GRP], G.scr[0:1, 2, 0:GRP], AT.add)
    b.op("vector", "reduce_sum", G.colsc[0:1, 1:2], G.scr[0:1, 0, 0:GRP], axis=AX)
    ts(b, "vector", G.colsc[0:1, 2:3], G.colsc[0:1, 0:1], 1.0 / D, None, AT.mult)
    ts(b, "vector", G.colsc[0:1, 3:4], G.colsc[0:1, 1:2], 1.0 / D, None, AT.mult)
    tt(b, "vector", G.colsc[0:1, 4:5], G.colsc[0:1, 2:3], G.colsc[0:1, 2:3], AT.mult)
    tt(b, "vector", G.colsc[0:1, 5:6], G.colsc[0:1, 3:4], G.colsc[0:1, 4:5], AT.subtract)
    ts(b, "vector", G.colsc[0:1, 5:6], G.colsc[0:1, 5:6], 1e-5, None, AT.add, inc="dv")
    b.wait("scalar", "dv")
    b.op("scalar", "sqrt", G.colsc[0:1, 6:7], G.colsc[0:1, 5:6], inc="sc")
    b.wait("vector", "sc")
    b.op("vector", "reciprocal", G.colsc[0:1, 7:8], G.colsc[0:1, 6:7])
    ts(b, "vector", G.tokst2[0:1, :], G.tokst2[0:1, :], G.colsc[0:1, 2:3], G.colsc[0:1, 7:8],
       AT.subtract, AT.mult, inc="dv")
    b.wait("tensor", "dv"); b.wait("tensor", "sc")
    for db in range(8):
        mm(b, G.psB[:, db:db + 1], G.tokst2[0:1, db * 128:(db + 1) * 128], G.onec[0:1, 0:1])
    b.wait("vector", "mm")
    b.op("vector", "tensor_copy", G.xnprevc[:, :, 0], G.psB[:, 0:8], inc="dv")

    for i in range(16):
        buf = G.tokst if i % 2 == 0 else G.tokst2
        b.wait("sync", "mm")
        dma(b, "sync", buf[:, :], d["xs"][i * 128:(i + 1) * 128, :])
        b.wait("vector", "ld")
        b.op("vector", "reduce_sum", G.colsc[:, 8:9], buf[:, :], axis=AX)
        b.wait("scalar", "ld")
        act(b, G.scr[:, 0:4, :].rearrange("p a c -> p (a c)"), buf[:, :], AF.Square)
        b.wait("vector", "sc")
        b.op("vector", "reduce_sum", G.colsc[:, 9:10], G.scr[:, 0:4, :].rearrange("p a c -> p (a c)"), axis=AX)
        ts(b, "vector", G.colsc[:, 10:11], G.colsc[:, 8:9], 1.0 / D, None, AT.mult)
        ts(b, "vector", G.colsc[:, 11:12], G.colsc[:, 9:10], 1.0 / D, None, AT.mult)
        tt(b, "vector", G.colsc[:, 12:13], G.colsc[:, 10:11], G.colsc[:, 10:11], AT.mult)
        tt(b, "vector", G.colsc[:, 13:14], G.colsc[:, 11:12], G.colsc[:, 12:13], AT.subtract)
        ts(b, "vector", G.colsc[:, 13:14], G.colsc[:, 13:14], 1e-5, None, AT.add, inc="dv")
        b.wait("scalar", "dv")
        b.op("scalar", "sqrt", G.colsc[:, 14:15], G.colsc[:, 13:14], inc="sc")
        b.wait("vector", "sc")
        b.op("vector", "reciprocal", G.colsc[:, 15:16], G.colsc[:, 14:15])
        b.op("vector", "tensor_copy", G.stats[:, i, 0:1], G.colsc[:, 10:11])
        b.op("vector", "tensor_copy", G.stats[:, i, 1:2], G.colsc[:, 15:16], inc="dv")
        b.wait("tensor", "ld")
        for db in range(8):
            pst = G.psA if db % 2 == 0 else G.psB
            b.wait("tensor", "dv"); b.wait("tensor", "sc")
            tr(b, pst[:, 256:384] if pst is G.psB else pst[:, 0:128],
               buf[:, db * 128:(db + 1) * 128], G.idf[:, :], inc="mm")
            b.wait("vector", "mm")
            psrc = pst[:, 256:384] if pst is G.psB else pst[:, 0:128]
            b.op("vector", "tensor_copy", G.xres[:, db, i * 128:(i + 1) * 128], psrc, inc="dv")

    b.fence()




def ln_bcast(b, G, statt, ti):
    b.wait("tensor", "dv"); b.wait("tensor", "sc")
    tr(b, G.psCH[0:1, 0, 0:128], statt[:, ti, 0:1], G.idf[:, :])
    tr(b, G.psCH[0:1, 0, 128:256] if False else G.psCH[0:1, 1, 0:128],
       statt[:, ti, 1:2], G.idf[:, :], inc="mm")
    b.wait("vector", "mm")
    b.op("vector", "tensor_copy", G.rowscr[0:1, 0:128], G.psCH[0:1, 0, 0:128])
    b.op("vector", "tensor_copy", G.rowscr[0:1, 128:256], G.psCH[0:1, 1, 0:128], inc="dv")
    b.wait("tensor", "dv"); b.wait("tensor", "sc")
    mm(b, G.psA[:, 0:128], G.oner[:, :], G.rowscr[0:1, 0:128])
    mm(b, G.psB[:, 0:128], G.oner[:, :], G.rowscr[0:1, 128:256], inc="mm")
    b.wait("vector", "mm")

def emit_wconv(b, G, d):
    for pi, p in enumerate("rwkvabg"):
        src = d["wpk"][p]
        for half in range(2):
            sl = slice(half * 512, (half + 1) * 512)
            b.wait("gpsimd", "gld")
            dma(b, "gpsimd", G.wbuf[:, 0:4, :],
                src[sl, :].rearrange("(hb p) e -> p hb e", p=128))
            b.wait("gpsimd", "gld")
            dma(b, "gpsimd", d["sc_wproj"][pi, sl, :].rearrange("(hb p) e -> p hb e", p=128),
                G.wbuf[:, 0:4, :])
    for srci, dst, rows, cols in ((d["wg_i"], d["sc_wg"], D, FF0),
                                  (d["wu_i"], d["sc_wu"], D, FF0),
                                  (d["wd_i"], d["sc_wd"], FF0, D)):
        step = 8192 // cols * 128
        for r0 in range(0, rows, step):
            sl = slice(r0, r0 + step)
            wv = G.wbuf_f[:, 0:(step // 128) * cols].rearrange("p (a c) -> p a c", c=cols)
            b.wait("gpsimd", "gld")
            dma(b, "gpsimd", wv, srci[sl, :].rearrange("(hb p) e -> p hb e", p=128))
            b.wait("gpsimd", "gld")
            dma(b, "gpsimd", dst[sl, :].rearrange("(hb p) e -> p hb e", p=128), wv)
    for h in range(NH):
        b.wait("gpsimd", "gld")
        dma(b, "gpsimd", G.wbuf[:, :, 0:128],
            d["encq"][h].rearrange("(db p) e -> p db e", p=128))
        b.wait("gpsimd", "gld")
        dma(b, "gpsimd", d["sc_encq"][h].rearrange("(db p) e -> p db e", p=128),
            G.wbuf[:, :, 0:128])
    for srci, dst, rows, cols in ((d["f1w"], d["sc_wf1"], D, FF1),
                                  (d["f2w"], d["sc_wf2"], FF1, D),
                                  (d["decw"], d["sc_dec"], D, D)):
        step = 8192 // cols * 128
        for r0 in range(0, rows, step):
            sl = slice(r0, r0 + step)
            wv = G.wbuf_f[:, 0:(step // 128) * cols].rearrange("p (a c) -> p a c", c=cols)
            b.wait("gpsimd", "gld")
            dma(b, "gpsimd", wv, srci[sl, :].rearrange("(hb p) e -> p hb e", p=128))
            b.wait("gpsimd", "gld")
            dma(b, "gpsimd", dst[sl, :].rearrange("(hb p) e -> p hb e", p=128), wv)
    b.fence()


def emit_group(b, G, d, grp):
    gs = grp * GRP
    t128 = gs // 128

    for half in range(2):
        ti = t128 + half
        ln_bcast(b, G, G.stats, ti)
        hs = slice(half * 128, (half + 1) * 128)
        for db in range(8):
            tt(b, "vector", G.scr[:, 0, 0:128], G.xres[:, db, gs + half * 128:gs + (half + 1) * 128],
               G.psA[:, 0:128], AT.subtract, inc=None)
            tt(b, "vector", G.xng[:, db, hs], G.scr[:, 0, 0:128], G.psB[:, 0:128], AT.mult, inc=None)
    tt(b, "vector", G.xxg[:, :, 1:GRP], G.xng[:, :, 0:GRP - 1], G.xng[:, :, 1:GRP], AT.subtract)
    tt(b, "vector", G.xxg[:, :, 0:1], G.xnprevc[:, :, :], G.xng[:, :, 0:1], AT.subtract)
    b.op("vector", "tensor_copy", G.xnprevc[:, :, :], G.xng[:, :, GRP - 1:GRP], inc="dv")

    def xmix_into(tgt, vi):
        b.wait("vector", "mm")
        b.wait("vector", "gld")
        vw = v8(G.scrb, 2, GRP)
        tt(b, "vector", vw, G.xxg[:, :, :],
           G.mixs[:, :, vi:vi + 1].broadcast_to([128, 8, GRP]), AT.mult, inc=None)
        tt(b, "vector", tgt[:, :, :], vw, G.xng[:, :, :], AT.add, inc="dv")

    b.wait("sync", "mm")
    dma(b, "sync", G.wbuf[:, :, :],
        d["sc_wproj"][6].rearrange("(db p) e -> p db e", p=128))
    xmix_into(G.xmx, 6)
    b.wait("tensor", "ld")
    b.wait("tensor", "dv"); b.wait("tensor", "sc")
    for tc in range(2):
        b.wait("tensor", "sc")
        for eh in range(2):
            pst = G.psA if eh == 0 else G.psB
            for db in range(8):
                mm(b, pst[:, 0:512], G.xmx[:, db, tc * 128:(tc + 1) * 128],
                   G.wbuf[:, db, eh * 512:(eh + 1) * 512], start=(db == 0), stop=(db == 7))
        b.wait("scalar", "mm")
        b.wait("scalar", "gld")
        act(b, G.scrb[:, 0, 0:512], G.psA[:, 0:512], AF.Sigmoid)
        act(b, G.scrb[:, 0, 512:1024], G.psB[:, 0:512], AF.Sigmoid, inc="sc")
        b.wait("gpsimd", "sc")
        dma(b, "gpsimd", d["sc_g"][gs + tc * 128:gs + (tc + 1) * 128, :],
            G.scrb[:, 0, :])
    xmix_into(G.xmv, 3)

    for p, vi, stream in (("r", 0, G.s_r), ("w", 1, None), ("k", 2, G.s_k),
                          ("a", 4, G.s_a), ("b", 5, G.s_b)):
        wi = "rwkvabg".index(p)
        b.wait("sync", "mm")
        b.wait("sync", "sc")
        dma(b, "sync", G.wbuf[:, :, :],
            d["sc_wproj"][wi].rearrange("(db p) e -> p db e", p=128))
        xmix_into(G.xmx, vi)
        b.wait("tensor", "ld")
        b.wait("tensor", "dv"); b.wait("tensor", "sc")
        for eb in range(8):
            pst = G.psA if eb % 2 == 0 else G.psB
            b.wait("tensor", "dv"); b.wait("tensor", "sc")
            for db in range(8):
                mm(b, pst[:, 0:GRP], G.wbuf[:, db, eb * 128:(eb + 1) * 128],
                   G.xmx[:, db, :], start=(db == 0), stop=(db == 7))
            if p == "r":
                b.wait("vector", "mm")
                b.op("vector", "tensor_copy", stream[:, eb, :], pst[:, 0:GRP], inc="dv")
            elif p == "k":
                b.wait("scalar", "mm")
                act(b, stream[:, eb, :], pst[:, 0:GRP], AF.Tanh, inc="dv")
            elif p == "w":
                b.wait("scalar", "mm")
                act(b, G.scr[:, 0, :], pst[:, 0:GRP], AF.Sigmoid, scale=-1.0, inc="sc")
                b.wait("vector", "sc")
                b.op("vector", "tensor_scalar_max", G.scr[:, 1, :], G.scr[:, 0, :], 1e-38, inc="dv")
                b.wait("scalar", "dv")
                act(b, G.wcum[:, eb, :], G.scr[:, 1, :], AF.Ln, inc="dv")
            else:
                b.wait("scalar", "mm")
                act(b, G.scr[:, 0, :], pst[:, 0:GRP], AF.Square, inc="sc")
                b.wait("tensor", "sc")
                mm(b, G.psST[0:2, 0, 0:GRP], G.one2[:, :], G.scr[:, 0, :], inc="mm")
                b.wait("vector", "mm")
                b.op("vector", "tensor_copy", G.scr[0:2, 1, :], G.psST[0:2, 0, 0:GRP], inc="dv")
                b.wait("scalar", "dv")
                b.op("scalar", "sqrt", G.scr[0:2, 1, :], G.scr[0:2, 1, :], inc="sc")
                b.wait("vector", "sc")
                b.op("vector", "tensor_scalar_max", G.scr[0:2, 1, :], G.scr[0:2, 1, :], 1e-6)
                b.op("vector", "reciprocal", G.scr[0:2, 1, :], G.scr[0:2, 1, :], inc="dv")
                b.wait("tensor", "dv"); b.wait("tensor", "sc")
                mm(b, G.psST[:, 1, 0:GRP], G.sel2[:, :], G.scr[0:2, 1, :], inc="mm")
                b.wait("vector", "mm")
                b.op("vector", "tensor_copy", G.scr[:, 2, :], G.psST[:, 1, 0:GRP])
                tt(b, "vector", stream[:, eb, :], pst[:, 0:GRP], G.scr[:, 2, :], AT.mult, inc="dv")

    b.wait("sync", "mm")
    b.wait("sync", "sc")
    b.wait("sync", "dv")
    dma(b, "sync", G.wbuf[:, :, :],
        d["sc_wproj"][3].rearrange("(db p) e -> p db e", p=128))
    b.wait("tensor", "ld")
    b.op("vector", "memset", G.scr[:, 0, 0:GRP], 1.0)
    for hp in range(8):
        b.op("vector", "tensor_tensor_scan", G.wcum[:, hp, :],
             G.scr[:, 0, 0:GRP], G.wcum[:, hp, :], 0.0, AT.mult, AT.add)

    for ci in range(CPG):
        emit_chunk(b, G, d, grp, grp * CPG + ci, ci * C)


def emit_chunk(b, G, d, grp, ch, cs):
    sl = slice(cs, cs + C)

    b.wait("tensor", "dv"); b.wait("tensor", "sc")
    for par in range(2):
        if par:
            b.wait("tensor", "dv")
        tp = (0, 64) if par else None
        po = 64 * par
        for eh in range(2):
            pst = G.psA if eh == 0 else G.psB
            for db in range(8):
                mm(b, pst[po:po + 64, 0:512], G.xmv[:, db, sl],
                   G.wbuf[:, db, eh * 512:(eh + 1) * 512], start=(db == 0), stop=(db == 7), tp=tp)
        b.wait("vector", "mm")
        b.op("vector", "tensor_copy", G.vtok[po:po + 64, 0:512], G.psA[po:po + 64, 0:512])
        b.op("vector", "tensor_copy", G.vtok[po:po + 64, 512:1024], G.psB[po:po + 64, 0:512], inc="dv")

    if cs == 0:
        b.op("vector", "tensor_copy", G.wcc[:, :, 0:C], G.wcum[:, :, sl])
    else:
        tt(b, "vector", G.wcc[:, :, 0:C], G.wcum[:, :, sl],
           G.wcum[:, :, cs - 1:cs].broadcast_to([128, 8, C]), AT.subtract)
    b.op("vector", "memset", G.wcc[:, :, C:C + 1], 0.0)
    b.op("vector", "tensor_copy", G.wcc[:, :, C + 1:2 * C], G.wcc[:, :, 0:C - 1])
    rf = G.wcc[:, :, C // 2 - 1:C // 2]
    tt(b, "vector", G.ftmp[:, :, 0:C], G.wcc[:, :, 0:C], rf.broadcast_to([128, 8, C]), AT.subtract)
    tt(b, "vector", G.ftmp[:, :, C:2 * C], G.wcc[:, :, C:2 * C], rf.broadcast_to([128, 8, C]),
       AT.subtract, inc="dv")
    b.wait("scalar", "dv")
    act(b, G.ftmp[:, :, 0:C], G.ftmp[:, :, 0:C], AF.Exp)
    act(b, G.ftmp[:, :, C:2 * C], G.ftmp[:, :, C:2 * C], AF.Exp)
    act(b, G.fcol[:, :, 0:1], rf, AF.Exp)
    act(b, G.fcol[:, :, 2:3], G.wcc[:, :, C - 1:C], AF.Exp, inc="sc")
    b.wait("vector", "sc")
    b.wait("vector", "gld")
    b.op("vector", "reciprocal", G.ftmp[:, :, 2 * C:3 * C], G.ftmp[:, :, 0:C])
    tt(b, "vector", G.rhsar[:, :, 0:C], G.s_a[:, :, sl], G.ftmp[:, :, C:2 * C], AT.mult)
    tt(b, "vector", G.rhsar[:, :, C:2 * C], G.s_r[:, :, sl], G.ftmp[:, :, 0:C], AT.mult)
    tt(b, "vector", G.lhsbk[:, :, 0:C], G.s_b[:, :, sl], G.ftmp[:, :, 2 * C:3 * C], AT.mult)
    tt(b, "vector", G.lhsbk[:, :, C:2 * C], G.s_k[:, :, sl], G.ftmp[:, :, 2 * C:3 * C], AT.mult)
    tt(b, "vector", G.til[:, :, 0:C], G.rhsar[:, :, 0:C],
       G.fcol[:, :, 0:1].broadcast_to([128, 8, C]), AT.mult)
    tt(b, "vector", G.til[:, :, C:2 * C], G.rhsar[:, :, C:2 * C],
       G.fcol[:, :, 0:1].broadcast_to([128, 8, C]), AT.mult)
    f6 = v8(G.scr, 2, C)
    tt(b, "vector", f6, G.ftmp[:, :, 2 * C:3 * C],
       G.ftmp[:, :, C - 1:C].broadcast_to([128, 8, C]), AT.mult)
    tt(b, "vector", G.til[:, :, 2 * C:3 * C], G.s_b[:, :, sl], f6, AT.mult)
    tt(b, "vector", G.til[:, :, 3 * C:4 * C], G.s_k[:, :, sl], f6, AT.mult, inc="dv")

    b.wait("tensor", "dv"); b.wait("tensor", "sc")
    for h in range(H):
        q = 64 * (h & 1)
        hp = h >> 1
        tp = (q, q) if q else None
        mm(b, G.psST[q:q + 64, hp, 0:128], G.lhsbk[q:q + 64, hp, 0:C],
           G.rhsar[q:q + 64, hp, :], tp=tp)
        mm(b, G.psST[q:q + 64, hp, 128:256], G.lhsbk[q:q + 64, hp, C:2 * C],
           G.rhsar[q:q + 64, hp, :], tp=tp)
        mm(b, G.psCH[q:q + 64, hp, 0:64], G.rhsar[q:q + 64, hp, 0:C],
           G.lhsbk[q:q + 64, hp, 0:C], tp=tp)
    b.wait("vector", "mm")
    tt(b, "vector", G.stsb[:, :, :], G.psST[:, :, :],
       G.msk4[:, :].unsqueeze(1).broadcast_to([128, 8, 256]), AT.mult)
    tt(b, "vector", G.ntb[:, :, :], G.psCH[:, :, 0:64],
       G.msklo[:, :].unsqueeze(1).broadcast_to([128, 8, 64]), AT.mult, inc="dv")

    b.wait("tensor", "dv"); b.wait("tensor", "sc")
    for h in range(H):
        q = 64 * (h & 1)
        hp = h >> 1
        tp = (q, q) if q else None
        mm(b, G.psST[q:q + 64, hp, 0:64], G.til[q:q + 64, hp, 2 * C:3 * C],
           G.idb[q:q + 64, q:q + 64], tp=tp)
        mm(b, G.psST[q:q + 64, hp, 64:128], G.til[q:q + 64, hp, 3 * C:4 * C],
           G.idb[q:q + 64, q:q + 64], tp=tp)
    b.wait("vector", "mm")
    b.op("vector", "tensor_copy", G.trsb[:, :, 0:128], G.psST[:, :, 0:128])
    b.op("vector", "tensor_copy", G.slbf[:, :, :], G.slst[:, :, :], inc="dv")

    b.wait("tensor", "dv"); b.wait("tensor", "sc")
    for h in range(H):
        q = 64 * (h & 1)
        hp = h >> 1
        tp = (q, q) if q else None
        mm(b, G.psCH[q:q + 64, hp, 64:128], G.til[q:q + 64, hp, 0:C],
           G.idb[q:q + 64, q:q + 64], tp=tp)
        mm(b, G.psCH[q:q + 64, hp, 0:64], G.til[q:q + 64, hp, 0:C],
           G.slbf[q:q + 64, hp, :], tp=tp, stop=False)
        mm(b, G.psCH[q:q + 64, hp, 0:64], G.stsb[q:q + 64, hp, 2 * C:3 * C],
           G.vtok[q:q + 64, h * K:(h + 1) * K], tp=tp, start=False)
    b.wait("vector", "mm")
    b.op("vector", "tensor_copy", G.ysbb[:, :, :], G.psCH[:, :, :], inc="dv")

    b.op("vector", "memset", G.pwst[0:64, :, 64:128], 0.0)
    b.op("vector", "memset", G.pwst[64:128, :, 0:64], 0.0)
    b.op("vector", "memset", G.pwts[0:64, :, 64:128], 0.0)
    b.op("vector", "memset", G.pwts[64:128, :, 0:64], 0.0)
    b.op("vector", "tensor_copy", G.pwst[0:64, :, 0:64], G.stsb[0:64, :, 0:C])
    b.op("vector", "tensor_copy", G.pwst[64:128, :, 64:128], G.stsb[64:128, :, 0:C])
    b.op("vector", "tensor_copy", G.pwts[0:64, :, 0:64], G.ntb[0:64, :, :])
    b.op("vector", "tensor_copy", G.pwts[64:128, :, 64:128], G.ntb[64:128, :, :], inc="dv")

    for lev in range(6):
        b.wait("tensor", "dv"); b.wait("tensor", "sc")
        for hp in range(8):
            mm(b, G.psCH[:, hp, 0:128], G.idb[:, :], G.ysbb[:, hp, :],
               start=True, stop=False)
            mm(b, G.psCH[:, hp, 0:128], G.pwst[:, hp, :], G.ysbb[:, hp, :],
               start=False, stop=True)
        b.wait("vector", "mm")
        b.op("vector", "tensor_copy", G.ysbb[:, :, :], G.psCH[:, :, :], inc="dv")
        if lev < 5:
            b.wait("tensor", "dv"); b.wait("tensor", "sc")
            for hp in range(8):
                mm(b, G.psST[:, hp, 0:128], G.pwts[:, hp, :], G.pwst[:, hp, :])
            if lev < 4:
                for hp in range(8):
                    mm(b, G.psST[:, hp, 128:256], G.pwst[:, hp, :], G.pwts[:, hp, :])
            b.wait("vector", "mm")
            b.op("vector", "tensor_copy", G.pwst[:, :, :], G.psST[:, :, 0:128])
            if lev < 4:
                b.op("vector", "tensor_copy", G.pwts[:, :, :], G.psST[:, :, 128:256])


    b.wait("tensor", "dv"); b.wait("tensor", "sc")
    for h in range(H):
        q = 64 * (h & 1)
        hp = h >> 1
        tp = (q, q) if q else None
        mm(b, G.psCH[q:q + 64, hp, 0:64], G.til[q:q + 64, hp, C:2 * C],
           G.slbf[q:q + 64, hp, :], tp=tp, stop=False)
        mm(b, G.psCH[q:q + 64, hp, 0:64], G.stsb[q:q + 64, hp, C:2 * C],
           G.ysbb[q:q + 64, hp, 0:64], tp=tp, start=False, stop=False)
        mm(b, G.psCH[q:q + 64, hp, 0:64], G.stsb[q:q + 64, hp, 3 * C:4 * C],
           G.vtok[q:q + 64, h * K:(h + 1) * K], tp=tp, start=False)
    b.wait("vector", "mm")
    b.op("vector", "tensor_copy", G.ostg[:, :, :], G.psCH[:, :, 0:64], inc="dv")
    olr = d["sc_oloc"].rearrange("tt (hp par e) -> tt par hp e", par=2, e=K)
    for par in range(2):
        b.wait("gpsimd", "dv")
        dma(b, "gpsimd", olr[ch * C:(ch + 1) * C, par], G.ostg[par * 64:(par + 1) * 64, :, :])

    b.wait("tensor", "dv"); b.wait("tensor", "sc")
    for h in range(H):
        q = 64 * (h & 1)
        hp = h >> 1
        tp = (q, q) if q else None
        mm(b, G.psST[q:q + 64, hp, 0:64], G.trsb[q:q + 64, hp, 0:64],
           G.ysbb[q:q + 64, hp, 0:64], tp=tp, stop=False)
        mm(b, G.psST[q:q + 64, hp, 0:64], G.trsb[q:q + 64, hp, 64:128],
           G.vtok[q:q + 64, h * K:(h + 1) * K], tp=tp, start=False)
    b.wait("vector", "mm")
    tt(b, "vector", G.slst[:, :, :], G.slst[:, :, :],
       G.fcol[:, :, 2:3].broadcast_to([128, 8, K]), AT.mult)
    tt(b, "vector", G.slst[:, :, :], G.slst[:, :, :], G.psST[:, :, 0:64], AT.add, inc="dv")

    b.wait("tensor", "dv"); b.wait("tensor", "sc")
    for h in range(H):
        q = 64 * (h & 1)
        hp = h >> 1
        tp = (q, q) if q else None
        mm(b, G.psST[q:q + 64, hp, 64:128], G.ysbb[q:q + 64, hp, 64:128],
           G.trsb[q:q + 64, hp, 0:64], tp=tp)
    b.wait("vector", "mm")
    dsc = v8(G.scr, 2, K)
    tt(b, "vector", dsc, G.idd2[:, :].unsqueeze(1).broadcast_to([128, 8, 64]),
       G.fcol[:, :, 2:3].broadcast_to([128, 8, K]), AT.mult)
    tt(b, "vector", G.psb[:, :, :], G.psST[:, :, 64:128], dsc, AT.add, inc="dv")
    pr = d["sc_p"].rearrange("c (hp par) k k2 -> c par k hp k2", par=2)
    for par in range(2):
        b.wait("gpsimd", "dv")
        dma(b, "gpsimd", pr[ch, par], G.psb[par * 64:(par + 1) * 64, :, :])
    if ch == 0:
        b.op("vector", "tensor_copy", G.accs[:, :, :], G.psb[:, :, :])
        b.op("vector", "tensor_copy", G.accb[:, :, :], G.psb[:, :, :], inc="dv")
    else:
        b.wait("tensor", "dv"); b.wait("tensor", "sc")
        for h in range(H):
            q = 64 * (h & 1)
            hp = h >> 1
            tp = (q, q) if q else None
            mm(b, G.psCH[q:q + 64, hp, 64:128], G.psb[q:q + 64, hp, :], G.accb[q:q + 64, hp, :], tp=tp)
        b.wait("vector", "mm")
        b.op("vector", "tensor_copy", G.accs[:, :, :], G.psCH[:, :, 64:128])
        b.op("vector", "tensor_copy", G.accb[:, :, :], G.psCH[:, :, 64:128], inc="dv")

    b.wait("tensor", "dv"); b.wait("tensor", "sc")
    for h in range(H):
        q = 64 * (h & 1)
        hp = h >> 1
        tp = (q, q) if q else None
        mm(b, G.psCH[q:q + 64, hp, 0:64], G.stsb[q:q + 64, hp, C:2 * C],
           G.ysbb[q:q + 64, hp, 64:128], tp=tp)
    b.wait("vector", "mm")
    psig = G.scrb[:, 0:1, 0:512].rearrange("p a b -> p (a b)").rearrange("p (a c) -> p a c", c=64)
    b.op("vector", "tensor_copy", psig, G.psCH[:, :, 0:64], inc="dv")
    b.wait("tensor", "dv"); b.wait("tensor", "sc")
    for h in range(H):
        q = 64 * (h & 1)
        hp = h >> 1
        tp = (q, q) if q else None
        mm(b, G.psCH[q:q + 64, hp, 64:128], psig[q:q + 64, hp, :],
           G.idb[q:q + 64, q:q + 64], tp=tp)
    b.wait("vector", "mm")
    tt(b, "vector", G.psis[:, :, :], G.psCH[:, :, 64:128],
       G.til[:, :, C:2 * C], AT.add, inc="dv")
    for par in range(2):
        b.wait("gpsimd", "dv")
        dma(b, "gpsimd",
            d["sc_psi"].rearrange("c (hp par) k t -> c par k hp t", par=2)[ch, par],
            G.psis[par * 64:(par + 1) * 64, :, :])


def emit_relay(b, G, d):
    b.fence()
    b.wait("tensor", "dv"); b.wait("tensor", "sc")
    for h in range(H):
        q = 64 * (h & 1)
        hp = h >> 1
        tp = (q, q) if q else None
        mm(b, G.psCH[q:q + 64, hp, 0:64], G.accs[q:q + 64, hp, :],
           G.idf[q:q + 64, q:q + 64], tp=tp)
    b.wait("vector", "mm")
    pct = v8(G.scr2, 2, 64)
    b.op("vector", "tensor_copy", pct, G.psCH[:, :, 0:64], inc="dv")
    b.wait("gpsimd", "dv")
    for h in range(H):
        q = 64 * (h & 1)
        hp = h >> 1
        dma(b, "gpsimd", d["cc1_in"][h, 0:64, :], pct[q:q + 64, hp, :])
        dma(b, "gpsimd", d["cc1_in"][h, 64:128, :], G.slst[q:q + 64, hp, :])
    b.wait("gpsimd", "gld")
    b.op("gpsimd", "collective_compute", "AllGather", AT.bypass,
         replica_groups=[list(range(8))],
         ins=[d["cc1_in"].opt()], outs=[d["cc1_out"].opt()], inc="cc")
    b.wait("gpsimd", "cc")
    for e in ("vector", "tensor", "sync"):
        b.wait(e, "cc")

    b.op("vector", "memset", G.dsb[:, :, :], 0.0)
    b.op("vector", "memset", G.dbf[:, :, :], 0.0, inc="dv")
    for j in range(7):
        b.wait("sync", "dv")
        b.wait("gpsimd", "mm")
        ccr = d["cc1_out"].rearrange("(j hp par) r k -> j par r hp k", j=8, par=2)
        for par in range(2):
            dma(b, "gpsimd", G.accb[par * 64:(par + 1) * 64, :, :], ccr[j, par, 0:64])
            dma(b, "sync", G.accs[par * 64:(par + 1) * 64, :, :], ccr[j, par, 64:128])
        b.wait("tensor", "ld")
        b.wait("tensor", "gld")
        b.wait("tensor", "dv"); b.wait("tensor", "sc")
        for h in range(H):
            q = 64 * (h & 1)
            hp = h >> 1
            tp = (q, q) if q else None
            mm(b, G.psCH[q:q + 64, hp, 0:64], G.accb[q:q + 64, hp, :], G.dbf[q:q + 64, hp, :], tp=tp)
        b.wait("vector", "mm")
        b.wait("vector", "ld")
        b.wait("vector", "gld")
        tmp = v8(G.scr, 2, 64)
        tt(b, "vector", tmp, G.psCH[:, :, 0:64], G.accs[:, :, :], AT.add)
        ts(b, "vector", tmp, tmp, G.flgb[:, j:j + 1], None, AT.mult)
        ts(b, "vector", G.dsb[:, :, :], G.dsb[:, :, :], G.flgb[:, 16 + j:17 + j], None, AT.mult)
        tt(b, "vector", G.dsb[:, :, :], G.dsb[:, :, :], tmp, AT.add)
        b.op("vector", "tensor_copy", G.dbf[:, :, :], G.dsb[:, :, :], inc="dv")
    if DEBUG:
        b.wait("gpsimd", "dv")
        dma(b, "gpsimd", d["sc_dbg"], G.dsb[:, :, :])
    b.fence()


def emit_p3(b, G, d):
    for grp in range(NGRP):
        gs = grp * GRP
        for half in range(2):
            ti = gs // 128 + half
            ln_bcast(b, G, G.stats, ti)
            hs = slice(half * 128, (half + 1) * 128)
            for db in range(8):
                tt(b, "vector", G.scr[:, 0, 0:128], G.xres[:, db, gs + half * 128:gs + (half + 1) * 128],
                   G.psA[:, 0:128], AT.subtract, inc=None)
                tt(b, "vector", G.xng[:, db, hs], G.scr[:, 0, 0:128], G.psB[:, 0:128], AT.mult, inc=None)
        guhist = {}
        guev = {}
        b.wait("tensor", "dv"); b.wait("tensor", "sc")
        for hb in range(FF0 // 128):
            even = hb % 2 == 0
            wt, wt2 = (G.wtile, G.wtile2) if even else (G.wtileB, G.wtile2B)
            pg = G.psA[:, 0:GRP] if even else G.psST[:, 0, 0:GRP]
            pu = G.psB[:, 0:GRP] if even else G.psST[:, 2, 0:GRP]
            scp = G.scr[:, 0, :] if even else G.scr[:, 1, :]
            lag = guhist.get(hb - 2, G._p3gu.get(grp - 1, 0) if hasattr(G, "_p3gu") else 0)
            if lag:
                b.wait("sync", "mm", val=lag)
            dma(b, "sync", wt[:, :, :],
                d["sc_wg"][:, hb * 128:(hb + 1) * 128].rearrange("(db p) e -> p db e", p=128))
            dma(b, "sync", wt2[:, :, :],
                d["sc_wu"][:, hb * 128:(hb + 1) * 128].rearrange("(db p) e -> p db e", p=128))
            b.wait("tensor", "ld")
            if hb >= 2:
                b.wait("tensor", "sc", val=guev[hb - 2][0])
                b.wait("tensor", "dv", val=guev[hb - 2][1])
            for db in range(8):
                mm(b, pg, wt[:, db, :], G.xng[:, db, :], start=(db == 0), stop=(db == 7))
            for db in range(8):
                mm(b, pu, wt2[:, db, :], G.xng[:, db, :], start=(db == 0), stop=(db == 7))
            guhist[hb] = b.semv["mm"]
            b.wait("scalar", "mm")
            if hb >= 2:
                b.wait("scalar", "dv", val=guev[hb - 2][1])
            act(b, scp, pg, AF.Silu, scale=1.0 / 127.0, inc="sc")
            b.wait("vector", "sc")
            stt(b, "vector", G.hstr[:, hb, :], pu, 1.0 / 127.0, scp,
                AT.mult, AT.mult, inc="dv")
            guev[hb] = (b.semv["sc"], b.semv["dv"])
        if not hasattr(G, "_p3gu"):
            G._p3gu = {}
        G._p3gu[grp] = guhist[FF0 // 128 - 1]
        def dpsum(eb):
            return DPSUM(G, eb)
        dnhist = {}
        b.wait("tensor", "dv"); b.wait("tensor", "sc")
        for hb in range(FF0 // 128):
            wdt = G.wdtile if hb % 2 == 0 else G.wdtileB
            lag = dnhist.get(hb - 2, G._p3dn.get(grp - 1, 0) if hasattr(G, "_p3dn") else 0)
            if lag:
                b.wait("sync", "mm", val=lag)
            dma(b, "sync", wdt[:, :], d["sc_wd"][hb * 128:(hb + 1) * 128, :])
            b.wait("tensor", "ld")
            for eb in range(8):
                mm(b, dpsum(eb), wdt[:, eb * 128:(eb + 1) * 128], G.hstr[:, hb, :],
                   start=(hb == 0), stop=(hb == FF0 // 128 - 1))
            dnhist[hb] = b.semv["mm"]
        if not hasattr(G, "_p3dn"):
            G._p3dn = {}
        G._p3dn[grp] = dnhist[FF0 // 128 - 1]
        b.wait("vector", "mm")
        for eb in range(8):
            stt(b, "vector", G.xres[:, eb, gs:gs + GRP], dpsum(eb), 1.0 / 127.0,
                G.xres[:, eb, gs:gs + GRP], AT.mult, AT.add)
    b.fence()

    for i, srcrow, tgt in ((0, d["gnwr"], G.gnwb), (1, d["gnbr"], G.gnbb)):
        b.wait("sync", "mm")
        b.wait("sync", "dv")
        dma(b, "sync", G.rowscr[0:1, 0:512], srcrow[0:1, 0:512])
        b.wait("tensor", "ld")
        mm(b, G.psA[:, 0:512], G.oner[:, :], G.rowscr[0:1, 0:512], inc="mm")
        b.wait("sync", "mm")
        dma(b, "sync", G.rowscr[0:1, 0:512], srcrow[0:1, 512:1024])
        b.wait("vector", "mm")
        b.op("vector", "tensor_copy", tgt[:, 0:512], G.psA[:, 0:512], inc="dv")
        b.wait("tensor", "ld")
        b.wait("tensor", "dv"); b.wait("tensor", "sc")
        mm(b, G.psA[:, 0:512], G.oner[:, :], G.rowscr[0:1, 0:512], inc="mm")
        b.wait("vector", "mm")
        b.op("vector", "tensor_copy", tgt[:, 512:1024], G.psA[:, 0:512], inc="dv")
    dma(b, "sync", G.bocol[:, :, :], d["bor"].unsqueeze(2), slow=True)
    dma(b, "gpsimd", G.wotb[:, :, :],
        d["wot"].rearrange("(db p) e -> p db e", p=128))
    b.fence()

    for ti in range(16):
        b.wait("sync", "dv")
        dma(b, "sync", G.scrb[:, 0, :], d["sc_oloc"][ti * 128:(ti + 1) * 128, :])
        dma(b, "sync", G.scrb[:, 1, :], d["sc_g"][ti * 128:(ti + 1) * 128, :])
        for cpar in range(2):
            ch = 2 * ti + cpar
            b.wait("sync", "mm")
            psir = d["sc_psi"].rearrange("c (hp par) k t -> c par k hp t", par=2)
            pldr = d["sc_p"].rearrange("c (hp par) k k2 -> c par k hp k2", par=2)
            for par in range(2):
                dma(b, "sync", G.psil[par * 64:(par + 1) * 64, :, :], psir[ch, par])
                dma(b, "sync", G.pldl[par * 64:(par + 1) * 64, :, :], pldr[ch, par])
            b.wait("tensor", "ld")
            b.wait("tensor", "dv"); b.wait("tensor", "sc")
            for h in range(H):
                q = 64 * (h & 1)
                hp = h >> 1
                mm(b, G.psST[cpar * 64:(cpar + 1) * 64, h >> 2, (h % 4) * 64:(h % 4) * 64 + 64],
                   G.psil[q:q + 64, hp, :], G.dbf[q:q + 64, hp, :],
                   tp=(q, cpar * 64) if (q or cpar) else None)
            for h in range(H):
                q = 64 * (h & 1)
                hp = h >> 1
                mm(b, G.psCH[q:q + 64, hp, 0:64], G.pldl[q:q + 64, hp, :],
                   G.dbf[q:q + 64, hp, :], tp=(q, q) if q else None)
            b.wait("vector", "mm")
            b.op("vector", "tensor_copy", G.dsb[:, :, :], G.psCH[:, :, 0:64])
            b.op("vector", "tensor_copy", G.dbf[:, :, :], G.dsb[:, :, :], inc="dv")
        b.wait("vector", "ld")
        tt(b, "vector", G.otr_f[:, :], G.scrb[:, 0, :], G.psST[:, 0:4, :].rearrange("p a c -> p (a c)"), AT.add)
        gm = G.otr_f[:, :].rearrange("p (h k) -> p h k", h=16)
        b.op("vector", "reduce_sum", G.gnsc[:, :, 0:1], gm, axis=AX, inc="dv")
        b.wait("scalar", "dv")
        act(b, G.scr[:, 0:4, :].rearrange("p a c -> p (a c)"), G.otr_f[:, :], AF.Square, inc="sc")
        b.wait("vector", "sc")
        b.op("vector", "reduce_sum", G.gnsc[:, :, 1:2],
             G.scr[:, 0:4, :].rearrange("p a c -> p (a c)").rearrange("p (h k) -> p h k", h=16), axis=AX)
        ts(b, "vector", G.gnsc[:, :, 2:3], G.gnsc[:, :, 0:1], 1.0 / 64, None, AT.mult)
        ts(b, "vector", G.gnsc[:, :, 3:4], G.gnsc[:, :, 1:2], 1.0 / 64, None, AT.mult)
        tt(b, "vector", G.gnsc[:, :, 4:5], G.gnsc[:, :, 2:3], G.gnsc[:, :, 2:3], AT.mult)
        tt(b, "vector", G.gnsc[:, :, 5:6], G.gnsc[:, :, 3:4], G.gnsc[:, :, 4:5], AT.subtract)
        ts(b, "vector", G.gnsc[:, :, 5:6], G.gnsc[:, :, 5:6], GN_EPS, None, AT.add, inc="dv")
        b.wait("scalar", "dv")
        b.op("scalar", "sqrt", G.gnsc[:, :, 6:7], G.gnsc[:, :, 5:6], inc="sc")
        b.wait("vector", "sc")
        b.op("vector", "reciprocal", G.gnsc[:, :, 7:8], G.gnsc[:, :, 6:7])
        tt(b, "vector", gm, gm, G.gnsc[:, :, 2:3].broadcast_to([128, 16, 64]), AT.subtract)
        tt(b, "vector", gm, gm, G.gnsc[:, :, 7:8].broadcast_to([128, 16, 64]), AT.mult)
        flat = G.otr_f[:, :]
        tt(b, "vector", flat, flat, G.gnwb[:, :], AT.mult)
        tt(b, "vector", flat, flat, G.gnbb[:, :], AT.add)
        tt(b, "vector", flat, flat, G.scrb[:, 1, :], AT.mult, inc="dv")
        b.wait("tensor", "dv"); b.wait("tensor", "sc")
        for db in range(8):
            pst = G.psA if db % 2 == 0 else G.psB
            b.wait("tensor", "dv"); b.wait("tensor", "sc")
            tr(b, pst[:, 128:256], flat[:, db * 128:(db + 1) * 128], G.idf[:, :], inc="mm")
            b.wait("vector", "mm")
            b.op("vector", "tensor_copy", G.ztl[:, db, :], pst[:, 128:256], inc="dv")
        b.wait("tensor", "dv"); b.wait("tensor", "sc")
        for eb in range(8):
            pst = G.psA if eb % 2 == 0 else G.psB
            for db in range(8):
                mm(b, pst[:, 0:128], G.wotb[:, db, eb * 128:(eb + 1) * 128],
                   G.ztl[:, db, :], start=(db == 0), stop=(db == 7))
            b.wait("vector", "mm")
            ts(b, "vector", G.scr2[:, 0, 0:128], pst[:, 0:128], G.bocol[:, eb, 0:1], None, AT.add)
            tt(b, "vector", G.xres[:, eb, ti * 128:(ti + 1) * 128],
               G.xres[:, eb, ti * 128:(ti + 1) * 128], G.scr2[:, 0, 0:128], AT.add, inc="dv")
            b.wait("tensor", "dv"); b.wait("tensor", "sc")
    b.wait("tensor", "dv"); b.wait("tensor", "sc")
    for db in range(8):
        pst = G.psA if db % 2 == 0 else G.psB
        b.wait("tensor", "dv"); b.wait("tensor", "sc")
        tr(b, pst[0:64, 256:384], G.xres[:, db, TLOC - 64:TLOC], G.idf[:, :], inc="mm")
        b.wait("vector", "mm")
        b.op("vector", "tensor_copy", G.tokst[0:64, db * 128:(db + 1) * 128], pst[0:64, 256:384], inc="dv")
    b.wait("gpsimd", "dv")
    dma(b, "gpsimd", d["cc2_in"][0:64, :], G.tokst[0:64, :])
    b.fence()


def emit_p4(b, G, d):
    for db in range(8):
        dma(b, "gpsimd", G.encvb[:, db, :].rearrange("p (h c) -> p h c", h=8),
            d["encv"][:, db * 128:(db + 1) * 128, :].rearrange("h p e -> p h e"))
    for ti in range(16):
        tsl = slice(ti * 128, (ti + 1) * 128)
        b.wait("scalar", "dv")
        act(b, G.scr[:, 0:4, :].rearrange("p a c -> p (a c)").rearrange("p (a c) -> p a c", a=8)[:, :, 0:128],
            G.xres[:, :, tsl], AF.Square, inc="sc")
        b.wait("tensor", "sc")
        for db in range(8):
            mm(b, G.psCH[0:1, 0, 0:128], G.onec[:, :], G.xres[:, db, tsl],
               start=(db == 0), stop=(db == 7))
        for db in range(8):
            mm(b, G.psCH[0:1, 1, 0:128], G.onec[:, :],
               G.scr[:, 0:4, :].rearrange("p a c -> p (a c)").rearrange("p (a c) -> p a c", a=8)[:, db, 0:128],
               start=(db == 0), stop=(db == 7))
        b.wait("vector", "mm")
        ts(b, "vector", G.rowscr[0:1, 0:128], G.psCH[0:1, 0, 0:128], 1.0 / D, None, AT.mult)
        ts(b, "vector", G.rowscr[0:1, 128:256], G.psCH[0:1, 1, 0:128], 1.0 / D, None, AT.mult)
        tt(b, "vector", G.rowscr[0:1, 256:384], G.rowscr[0:1, 0:128], G.rowscr[0:1, 0:128], AT.mult)
        tt(b, "vector", G.rowscr[0:1, 128:256], G.rowscr[0:1, 128:256], G.rowscr[0:1, 256:384], AT.subtract)
        ts(b, "vector", G.rowscr[0:1, 128:256], G.rowscr[0:1, 128:256], 1e-5, None, AT.add, inc="dv")
        b.wait("scalar", "dv")
        b.op("scalar", "sqrt", G.rowscr[0:1, 128:256], G.rowscr[0:1, 128:256], inc="sc")
        b.wait("vector", "sc")
        b.op("vector", "reciprocal", G.rowscr[0:1, 128:256], G.rowscr[0:1, 128:256], inc="dv")
        b.wait("tensor", "dv"); b.wait("tensor", "sc")
        tr(b, G.psCH[:, 2, 0:1], G.rowscr[0:1, 0:128], G.idf[0:1, 0:1]) if False else None
        mm(b, G.psCH[:, 2, 0:1], G.rowscr[0:1, 0:128], G.onec[0:1, 0:1])
        mm(b, G.psCH[:, 3, 0:1], G.rowscr[0:1, 128:256], G.onec[0:1, 0:1], inc="mm")
        b.wait("vector", "mm")
        b.op("vector", "tensor_copy", G.stats2[:, ti, 0:1], G.psCH[:, 2, 0:1])
        b.op("vector", "tensor_copy", G.stats2[:, ti, 1:2], G.psCH[:, 3, 0:1], inc="dv")
    b.wait("gpsimd", "gld")
    for e in ("tensor", "vector", "scalar"):
        b.wait(e, "gld")
    for grp in range(NGRP):
        gs = grp * GRP
        for half in range(2):
            ti = gs // 128 + half
            ln_bcast(b, G, G.stats2, ti)
            hs = slice(half * 128, (half + 1) * 128)
            for db in range(8):
                tt(b, "vector", G.scr[:, 0, 0:128], G.xres[:, db, gs + half * 128:gs + (half + 1) * 128],
                   G.psA[:, 0:128], AT.subtract, inc=None)
                tt(b, "vector", G.xng[:, db, hs], G.scr[:, 0, 0:128], G.psB[:, 0:128], AT.mult, inc=None)
        b.wait("tensor", "dv"); b.wait("tensor", "sc")
        for h in range(NH):
            pst = G.psA if h % 2 == 0 else G.psB
            b.wait("sync", "mm")
            dma(b, "sync", G.encb[:, :, :],
                d["sc_encq"][h].rearrange("(db p) e -> p db e", p=128))
            b.wait("tensor", "ld")
            for db in range(8):
                mm(b, pst[:, 0:GRP], G.encb[:, db, :],
                   G.xng[:, db, :], start=(db == 0), stop=(db == 7))
            b.wait("scalar", "mm")
            act(b, G.qts[:, h, 64 + gs:64 + gs + GRP], pst[:, 0:GRP], AF.Relu, inc="dv")
            b.wait("tensor", "dv"); b.wait("tensor", "sc")
        for half in range(2):
            tc = gs // 128 + half
            pst = G.psA if half == 0 else G.psB
            b.wait("tensor", "dv"); b.wait("tensor", "sc")
            for db in range(8):
                mm(b, pst[:, 0:512], G.xng[:, db, half * 128:(half + 1) * 128],
                   G.encvb[:, db, 0:512], start=(db == 0), stop=(db == 7))
            b.wait("scalar", "mm")
            act(b, G.vts[:, tc, 0:512], pst[:, 0:512], AF.Relu, inc="dv")
            b.wait("tensor", "dv"); b.wait("tensor", "sc")
            for db in range(8):
                mm(b, pst[:, 0:512], G.xng[:, db, half * 128:(half + 1) * 128],
                   G.encvb[:, db, 512:1024], start=(db == 0), stop=(db == 7))
            b.wait("scalar", "mm")
            act(b, G.vts[:, tc, 512:1024], pst[:, 0:512], AF.Relu, inc="dv")
    b.fence()
    for h in range(NH):
        b.op("vector", "reduce_sum", G.sigc[:, h, 0:1], G.qts[:, h, 64:64 + TLOC], axis=AX)
        b.op("vector", "reduce_sum", G.sigc[:, h, 1:2], G.qts[:, h, 64:64 + TLOC:8], axis=AX)
    b.wait("gpsimd", "dv")
    dma(b, "gpsimd", d["cc2_in"][64:72, 0:128].rearrange("h p -> p h"), G.sigc[:, :, 0], slow=True)
    dma(b, "gpsimd", d["cc2_in"][72:80, 0:128].rearrange("h p -> p h"), G.sigc[:, :, 1], slow=True)
    b.op("vector", "memset", G.tokst2[0:16, 0:896], 0.0, inc="dv")
    b.wait("gpsimd", "dv")
    dma(b, "gpsimd", d["cc2_in"][64:80, 128:1024], G.tokst2[0:16, 0:896])
    b.wait("gpsimd", "gld")
    b.op("gpsimd", "collective_compute", "AllGather", AT.bypass,
         replica_groups=[list(range(8))],
         ins=[d["cc2_in"].opt()], outs=[d["cc2_out"].opt()], inc="cc")
    b.wait("gpsimd", "cc")
    for e in ("tensor", "vector", "scalar", "sync"):
        b.wait(e, "cc")


def emit_p5(b, G, d):
    b.op("vector", "memset", G.tokst[:, :], 0.0, inc="dv")
    for j in range(8):
        b.wait("sync", "dv")
        dma(b, "sync", G.tokst2[0:64, :], d["cc2_out"][j * 80:j * 80 + 64, :])
        b.wait("vector", "ld")
        ts(b, "vector", G.tokst2[0:64, :], G.tokst2[0:64, :], G.flgb[0:64, 24 + j:25 + j], None, AT.mult)
        tt(b, "vector", G.tokst[0:64, :], G.tokst[0:64, :], G.tokst2[0:64, :], AT.add, inc="dv")
    b.op("vector", "reduce_sum", G.colsc[0:64, 0:1], G.tokst[0:64, :], axis=AX, inc="dv")
    b.wait("scalar", "dv")
    act(b, G.scr[0:64, 0:4, :].rearrange("p a c -> p (a c)"), G.tokst[0:64, :], AF.Square, inc="sc")
    b.wait("vector", "sc")
    b.op("vector", "reduce_sum", G.colsc[0:64, 1:2], G.scr[0:64, 0:4, :].rearrange("p a c -> p (a c)"), axis=AX)
    ts(b, "vector", G.colsc[0:64, 2:3], G.colsc[0:64, 0:1], 1.0 / D, None, AT.mult)
    ts(b, "vector", G.colsc[0:64, 3:4], G.colsc[0:64, 1:2], 1.0 / D, None, AT.mult)
    tt(b, "vector", G.colsc[0:64, 4:5], G.colsc[0:64, 2:3], G.colsc[0:64, 2:3], AT.mult)
    tt(b, "vector", G.colsc[0:64, 5:6], G.colsc[0:64, 3:4], G.colsc[0:64, 4:5], AT.subtract)
    ts(b, "vector", G.colsc[0:64, 5:6], G.colsc[0:64, 5:6], 1e-5, None, AT.add, inc="dv")
    b.wait("scalar", "dv")
    b.op("scalar", "sqrt", G.colsc[0:64, 6:7], G.colsc[0:64, 5:6], inc="sc")
    b.wait("vector", "sc")
    b.op("vector", "reciprocal", G.colsc[0:64, 7:8], G.colsc[0:64, 6:7])
    ts(b, "vector", G.tokst[0:64, :], G.tokst[0:64, :], G.colsc[0:64, 2:3], G.colsc[0:64, 7:8],
       AT.subtract, AT.mult, inc="dv")
    b.wait("tensor", "dv"); b.wait("tensor", "sc")
    for db in range(8):
        pst = G.psA if db % 2 == 0 else G.psB
        b.wait("tensor", "dv"); b.wait("tensor", "sc")
        tr(b, pst[:, 0:64], G.tokst[0:64, db * 128:(db + 1) * 128], G.idf[0:64, 0:64], inc="mm")
        b.wait("vector", "mm")
        b.op("vector", "tensor_copy", G.xhalT[:, db, :], pst[:, 0:64], inc="dv")
    b.wait("tensor", "dv"); b.wait("tensor", "sc")
    for h in range(NH):
        pst = G.psA if h % 2 == 0 else G.psB
        b.wait("sync", "mm")
        dma(b, "sync", G.encb[:, :, :],
            d["sc_encq"][h].rearrange("(db p) e -> p db e", p=128))
        b.wait("tensor", "ld")
        for db in range(8):
            mm(b, pst[:, 64:128], G.encb[:, db, :],
               G.xhalT[:, db, :], start=(db == 0), stop=(db == 7))
        b.wait("scalar", "mm")
        act(b, G.qts[:, h, 0:64], pst[:, 64:128], AF.Relu, inc="dv")
        b.wait("tensor", "dv"); b.wait("tensor", "sc")
    for eh in range(2):
        pst = G.psA if eh == 0 else G.psB
        b.wait("tensor", "dv"); b.wait("tensor", "sc")
        for db in range(8):
            mm(b, pst[64:128, 0:512], G.xhalT[:, db, :], G.encvb[:, db, eh * 512:(eh + 1) * 512],
               start=(db == 0), stop=(db == 7), tp=(0, 64))
        b.wait("scalar", "mm")
        act(b, G.vhalo[64:128, eh * 512:(eh + 1) * 512], pst[64:128, 0:512], AF.Relu, inc="dv")
    b.op("vector", "memset", G.sigc[:, :, 2:4], 0.0, inc="dv")
    for j in range(8):
        b.wait("sync", "dv")
        dma(b, "sync", G.mlpc[:, :, 0], d["cc2_out"][j * 80 + 64:j * 80 + 72, 0:128].rearrange(
            "h p -> p h"), slow=True)
        dma(b, "sync", G.mlpc[:, :, 1], d["cc2_out"][j * 80 + 72:j * 80 + 80, 0:128].rearrange(
            "h p -> p h"), slow=True)
        b.wait("vector", "ld")
        ts(b, "vector", G.mlpc[:, :, 0:2], G.mlpc[:, :, 0:2], G.flgb[:, 8 + j:9 + j], None, AT.mult)
        tt(b, "vector", G.sigc[:, :, 2:4], G.sigc[:, :, 2:4], G.mlpc[:, :, 0:2], AT.add, inc="dv")
    ts(b, "vector", G.sigc[:, :, 2:3], G.sigc[:, :, 2:3], 1.0 / 8192.0, None, AT.mult)
    ts(b, "vector", G.sigc[:, :, 3:4], G.sigc[:, :, 3:4], 0.5 / 1024.0, None, AT.mult)
    tt(b, "vector", G.sigc[:, :, 2:3], G.sigc[:, :, 2:3], G.sigc[:, :, 3:4], AT.add)
    b.op("vector", "tensor_copy", G.scrb[:, 0, 0:8], G.sigc[:, :, 2], inc="dv")
    sigb = G.scrb[:, 0, 0:8].rearrange("p (a c) -> p a c", a=8)
    b.wait("gpsimd", "dv")
    dma(b, "gpsimd", G.encvb_f[:, 0:4096].rearrange("p (a c) -> p a c", a=8),
        d["mw1"].rearrange("(db p) e -> p db e", p=128))
    b.wait("sync", "dv")
    dma(b, "sync", G.mlpc[:, 0:4, 0:1], d["mb1"].unsqueeze(2), slow=True)
    dma(b, "sync", G.mlpc[:, :, 2:3], d["mb2"].unsqueeze(2), slow=True)
    dma(b, "sync", G.bmc[:, :, 0:1], d["basem"].unsqueeze(2), slow=True)
    b.wait("tensor", "gld")
    b.wait("tensor", "ld")
    for e4 in range(4):
        for db in range(8):
            mm(b, G.psCH[:, 4, e4:e4 + 1], G.encvb[:, db, e4 * 128:(e4 + 1) * 128],
               sigb[:, db, 0:1], start=(db == 0), stop=(db == 7))
    b.wait("vector", "mm")
    b.wait("vector", "ld")
    ts(b, "vector", G.mlpc[:, 0:4, 1:2].flatten().unsqueeze(1) if False else G.scr[:, 0, 0:4],
       G.psCH[:, 4, 0:4], 1.0, None, AT.mult)
    tt(b, "vector", G.scr[:, 0, 0:4], G.scr[:, 0, 0:4], G.mlpc[:, 0:4, 0], AT.add, inc="dv")
    b.wait("scalar", "dv")
    act(b, G.scr[:, 1, 0:4], G.scr[:, 0, 0:4], AF.Gelu, inc="sc")
    b.wait("vector", "sc")
    b.op("vector", "tensor_copy", G.scrb[:, 0, 16:20], G.scr[:, 1, 0:4], inc="dv")
    h1b = G.scrb[:, 0, 16:20].rearrange("p (a c) -> p a c", a=4)
    b.wait("gpsimd", "dv")
    dma(b, "gpsimd", G.encvb_f[:, 0:4096].rearrange("p (a c) -> p a c", a=4),
        d["mw2"].rearrange("(db p) e -> p db e", p=128))
    b.wait("tensor", "gld")
    for eb in range(8):
        for db in range(4):
            mm(b, G.psCH[:, 5, eb:eb + 1], G.encvb[:, db, eb * 128:(eb + 1) * 128],
               h1b[:, db, 0:1], start=(db == 0), stop=(db == 3))
    b.wait("vector", "mm")
    tt(b, "vector", G.scr[:, 0, 0:8], G.psCH[:, 5, 0:8], G.mlpc[:, :, 2], AT.add)
    ts(b, "vector", G.scr[:, 0, 0:8], G.scr[:, 0, 0:8], 0.1, None, AT.mult)
    tt(b, "vector", G.scr[:, 0, 0:8], G.scr[:, 0, 0:8], G.bmc[:, :, 0], AT.add, inc="dv")
    b.wait("scalar", "dv")
    act(b, G.scr[:, 1, 0:8], G.scr[:, 0, 0:8], AF.Sigmoid, scale=-1.0, inc="sc")
    b.wait("vector", "sc")
    b.op("vector", "tensor_scalar_max", G.scr[:, 1, 0:8], G.scr[:, 1, 0:8], 1e-38, inc="dv")
    b.wait("scalar", "dv")
    act(b, G.scr[:, 2, 0:8], G.scr[:, 1, 0:8], AF.Ln, inc="sc")
    b.wait("vector", "sc")
    ts(b, "vector", G.metr[:, :, 0:1], G.scr[:, 2, 0:8].unsqueeze(2) if False else
       G.scr[:, 2, 0:8].rearrange("p (a c) -> p a c", a=8), -1.0, None, AT.mult, inc="dv")
    for h in range(NH):
        b.wait("gpsimd", "dv")
        b.wait("gpsimd", "mm")
        dma(b, "gpsimd", G.scrb[0:32, 1, 0:128], d["hut"][h])
        dma(b, "gpsimd", G.scrb[0:32, 1, 128:256], d["hvv"][h])
        b.wait("tensor", "gld")
        mm(b, G.psCH[:, 6, 0:128], G.scrb[0:32, 1, 0:128], G.scrb[0:32, 1, 128:256], inc="mm")
        b.wait("vector", "mm")
        b.op("vector", "tensor_copy", G.uvs[:, h * 128:(h + 1) * 128], G.psCH[:, 6, 0:128], inc="dv")
    b.fence()


def emit_p6a(b, G, d):
    for h in range(NH):
        for c in range(17):
            if c == 16:
                qsl = G.qts[:, h, 0:64]
                rows = 64
            else:
                qsl = G.qts[:, h, 64 + c * 128:64 + (c + 1) * 128]
                rows = 128
            b.wait("vector", "mm")
            ts(b, "vector", G.qmb[:, 0:rows], qsl, G.metr[:, h, 0:1], None, AT.mult)
            tt(b, "vector", G.qsq[:, 0:rows], G.qmb[:, 0:rows], qsl, AT.mult)
            tt(b, "vector", G.scr[:, 0, 0:rows], qsl, qsl, AT.mult, inc="dv")
            b.wait("tensor", "dv"); b.wait("tensor", "sc")
            tgt = G.acols[0:rows, h, 16:17] if c == 16 else G.acols[:, h, c:c + 1]
            tg2 = G.sqnc[0:rows, h, 16:17] if c == 16 else G.sqnc[:, h, c:c + 1]
            mm(b, G.psCH[0:rows, 7, 0:1], G.qsq[:, 0:rows], G.onec[:, 0:1])
            mm(b, G.psCH[0:rows, 7, 1:2], G.scr[:, 0, 0:rows], G.onec[:, 0:1], inc="mm")
            b.wait("vector", "mm")
            b.op("vector", "tensor_copy", tgt, G.psCH[0:rows, 7, 0:1])
            b.op("vector", "tensor_copy", tg2, G.psCH[0:rows, 7, 1:2], inc="dv")

    for c in range(16):
        for h in range(NH):
            qsl = G.qts[:, h, 64 + c * 128:64 + (c + 1) * 128]
            b.wait("tensor", "dv"); b.wait("tensor", "sc")
            if c == 0:
                mm(b, G.psCH[0:1, 6, 0:64], G.acols[0:64, h, 16:17], G.idf[0:64, 0:64])
            else:
                mm(b, G.psCH[0:1, 6, 0:64], G.acols[64:128, h, c - 1:c],
                   G.idf[64:128, 64:128], tp=(64, 0))
            mm(b, G.psCH[0:1, 7, 0:128], G.acols[:, h, c:c + 1], G.idf[:, :], inc="mm")
            b.wait("vector", "mm")
            b.op("vector", "tensor_copy", G.rowscr[0:1, 256:320], G.psCH[0:1, 6, 0:64])
            b.op("vector", "tensor_copy", G.rowscr[0:1, 320:448], G.psCH[0:1, 7, 0:128], inc="dv")
            b.wait("tensor", "dv"); b.wait("tensor", "sc")
            mm(b, G.psB[:, 0:192], G.oner[:, :], G.rowscr[0:1, 256:448])
            ts(b, "vector", G.qmb[:, :], qsl, G.metr[:, h, 0:1], None, AT.mult, inc="dv")
            b.wait("tensor", "dv"); b.wait("tensor", "sc")
            mm(b, G.psA[:, 0:192], G.qmb[:, :], G.qts[:, h, c * 128:c * 128 + 192], inc="mm")
            b.wait("vector", "mm")
            ts(b, "vector", G.colsc[:, 16:17], G.acols[:, h, c:c + 1], 1e-8, None, AT.add)
            ts(b, "vector", G.dtl[:, :], G.psA[:, 0:192], -2.0, G.colsc[:, 16:17], AT.mult, AT.add)
            tt(b, "vector", G.dtl[:, :], G.dtl[:, :], G.psB[:, 0:192], AT.add)
            b.op("vector", "tensor_scalar_max", G.dtl[:, :], G.dtl[:, :], 1e-8, inc="dv")
            b.wait("scalar", "dv")
            b.op("scalar", "sqrt", G.dtl[:, :], G.dtl[:, :], inc="sc")
            b.wait("vector", "sc")
            if c == 0:
                tt(b, "vector", G.colsc[:, 17:18], G.dtl[:, 64:65], G.flgb[:, 7:8], AT.mult)
                ts(b, "vector", G.dtl[:, 0:64], G.dtl[:, 0:64], G.flgb[:, 23:24], None, AT.mult)
                tt(b, "vector", G.dtl[:, 0:64], G.dtl[:, 0:64],
                   G.colsc[:, 17:18].broadcast_to([128, 64]), AT.add)
            stt(b, "vector", G.neg1[:, :], G.dtl[:, :], -1.0, G.bndb[:, :], AT.mult, AT.subtract, inc="dv")
            b.wait("scalar", "dv")
            act(b, G.expd[:, :], G.neg1[:, :], AF.Exp, inc="sc")
            b.wait("vector", "sc")
            cur, nxt = G.neg1, G.neg2
            for it in range(4):
                b.op("vector", "max", G.m8[:, :], cur[:, :])
                b.op("vector", "match_replace", nxt[:, :], G.m8[:, :], cur[:, :], MARK)
                cur, nxt = nxt, cur
            ts(b, "vector", G.wse[:, :], cur[:, :], -0.5e20, None, AT.is_le)
            tt(b, "vector", G.wse[:, :], G.wse[:, :], G.expd[:, :], AT.mult, inc="dv")
            if c == 0:
                b.op("vector", "reduce_sum", G.rsum[:, 0:1], G.wse[:, 0:64], axis=AX)
                tt(b, "vector", G.rsum[:, 1:2], G.rsum[:, 0:1], G.flgb[:, 7:8], AT.mult)
                ts(b, "vector", G.wse[:, 0:64], G.wse[:, 0:64], G.flgb[:, 23:24], None, AT.mult)
                tt(b, "vector", G.wse[:, 64:65], G.wse[:, 64:65], G.rsum[:, 1:2], AT.add, inc="dv")
            b.op("vector", "reduce_sum", G.rsum[:, 2:3], G.wse[:, :], axis=AX)
            ts(b, "vector", G.rsum[:, 2:3], G.rsum[:, 2:3], 1e-8, None, AT.add)
            b.op("vector", "reciprocal", G.rsum[:, 3:4], G.rsum[:, 2:3])
            ts(b, "vector", G.wse[:, :], G.wse[:, :], G.rsum[:, 3:4], None, AT.mult)
            b.op("vector", "tensor_copy", G.wsb[:, :], G.wse[:, :], inc="dv")
            b.wait("tensor", "dv"); b.wait("tensor", "sc")
            mm(b, G.psCH[64:128, 0, 0:128], G.wsb[:, 0:64], G.idb[:, :], tp=(0, 64))
            mm(b, G.psCH[0:64, 1, 0:128], G.wsb[:, 64:128], G.idb[:, :])
            mm(b, G.psCH[64:128, 2, 0:128], G.wsb[:, 128:192], G.idb[:, :],
               tp=(0, 64), inc="mm")
            b.wait("vector", "mm")
            b.op("vector", "tensor_copy", G.wtb[64:128, 0:128], G.psCH[64:128, 0, 0:128])
            b.op("vector", "tensor_copy", G.wtb[0:64, 128:256], G.psCH[0:64, 1, 0:128])
            b.op("vector", "tensor_copy", G.wtb[64:128, 256:384],
                 G.psCH[64:128, 2, 0:128], inc="dv")
            b.wait("tensor", "dv"); b.wait("tensor", "sc")
            vprev = G.vhalo[64:128, h * 128:(h + 1) * 128] if c == 0 else                 G.vts[64:128, c - 1, h * 128:(h + 1) * 128]
            mm(b, G.psCH[:, 3, 0:128], G.wtb[64:128, 0:128], vprev, tp=(64, 0), stop=False)
            mm(b, G.psCH[:, 3, 0:128], G.wtb[0:64, 128:256],
               G.vts[0:64, c, h * 128:(h + 1) * 128], start=False, stop=False)
            mm(b, G.psCH[:, 3, 0:128], G.wtb[64:128, 256:384],
               G.vts[64:128, c, h * 128:(h + 1) * 128], tp=(64, 0), start=False)
            mm(b, G.psCH[:, 4, 0:128], qsl, G.uvs[:, h * 128:(h + 1) * 128])
            mm(b, G.psCH[:, 5, 0:128], qsl, G.idb[:, :], inc="mm")
            b.wait("vector", "mm")
            ts(b, "vector", G.colsc[:, 18:19], G.sqnc[:, h, c:c + 1], 0.04, None, AT.is_gt, inc="dv")
            b.wait("scalar", "dv")
            b.op("scalar", "sqrt", G.colsc[:, 19:20], G.sqnc[:, h, c:c + 1], inc="sc")
            b.wait("vector", "sc")
            b.op("vector", "tensor_scalar_max", G.colsc[:, 19:20], G.colsc[:, 19:20], 1e-12)
            b.op("vector", "reciprocal", G.colsc[:, 20:21], G.colsc[:, 19:20])
            tt(b, "vector", G.colsc[:, 21:22], G.colsc[:, 18:19], G.colsc[:, 20:21], AT.mult)
            ts(b, "vector", G.colsc[:, 21:22], G.colsc[:, 21:22], 0.1, None, AT.mult)
            b.op("vector", "tensor_copy", G.qsq[:, :], G.psCH[:, 4, 0:128])
            stt(b, "vector", G.ctmp[:, :], G.qsq[:, :], G.colsc[:, 21:22],
                G.psCH[:, 3, 0:128], AT.mult, AT.add)
            tt(b, "vector", G.ostok[:, :], G.ctmp[:, :], G.psCH[:, 5, 0:128],
               AT.mult, inc="dv")
            b.wait("tensor", "dv"); b.wait("tensor", "sc")
            mm(b, G.psCH[:, 6, 0:128], G.ostok[:, :], G.idb[:, :], inc="mm")
            b.wait("vector", "mm")
            b.wait("vector", "gld")
            b.op("vector", "tensor_copy", G.ospst[:, h, :], G.psCH[:, 6, 0:128], inc="dv")
        b.wait("gpsimd", "dv")
        dma(b, "gpsimd", d["sc_osp"].rearrange("h p t -> p h t")[:, :, c * 128:(c + 1) * 128],
            G.ospst[:, :, :])
    b.fence()


def emit_p6b(b, G, d):
    MAGIC = 12582912.0
    b.op("vector", "memset", G.colsc[:, 22:23], 0.25, inc="dv")
    dma(b, "sync", G.b1c[:, :, 0:1], d["f1b"].unsqueeze(2), slow=True)
    dma(b, "sync", G.b2c[:, :, 0:1], d["f2b"].unsqueeze(2), slow=True)
    lastw1 = {0: 0, 1: 0}
    lastw2dec = 0
    for grp in range(NGRP):
        gs = grp * GRP
        for half in range(2):
            ti = gs // 128 + half
            ln_bcast(b, G, G.stats2, ti)
            hs = slice(half * 128, (half + 1) * 128)
            for db in range(8):
                tt(b, "vector", G.scr[:, 0, 0:128], G.xres[:, db, gs + half * 128:gs + (half + 1) * 128],
                   G.psA[:, 0:128], AT.subtract, inc=None)
                tt(b, "vector", G.xng[:, db, hs], G.scr[:, 0, 0:128], G.psB[:, 0:128], AT.mult, inc=None)
        w1hist = {}
        b.wait("tensor", "dv"); b.wait("tensor", "sc")
        for it in range(FF1 // 256):
            for half in range(2):
                fb = 2 * it + half
                wt = G.wtile if half == 0 else G.wtileB
                pst = G.psA if half == 0 else G.psB
                lag = w1hist[(it - 1, half)] if it >= 1 else lastw1[half]
                if lag:
                    b.wait("sync", "mm", val=lag)
                dma(b, "sync", wt[:, :, :],
                    d["sc_wf1"][:, fb * 128:(fb + 1) * 128].rearrange("(db p) e -> p db e", p=128))
                b.wait("tensor", "ld")
                for db in range(8):
                    mm(b, pst[:, 0:GRP], wt[:, db, :], G.xng[:, db, :],
                       start=(db == 0), stop=(db == 7))
                w1hist[(it, half)] = b.semv["mm"]
            b.wait("vector", "mm")
            ts(b, "vector", G.pwl[:, 0, 0:GRP], G.psA[:, 0:GRP], G.b1c[:, 2 * it, 0:1],
               256.0, AT.add, AT.mult)
            ts(b, "vector", G.pwl[:, 0, GRP:2 * GRP], G.psB[:, 0:GRP], G.b1c[:, 2 * it + 1, 0:1],
               256.0, AT.add, AT.mult, inc="dv")
            b.wait("tensor", "dv")
            b.op("vector", "tensor_copy", G.hq32[:, :], G.pwl[:, 0, :])
            b.op("vector", "tensor_copy", G.pwl[:, 1, :], G.hq32[:, :], inc="dv")
            hq = G.pwl[:, 1, :]
            ts(b, "vector", G.pwl[:, 2, :], hq, 1.0 / 64.0, 80.5078125, AT.mult, AT.add)
            b.op("vector", "tensor_copy", G.hq32[:, :], G.pwl[:, 2, :])
            b.op("vector", "tensor_copy", G.pwl[:, 2, :], G.hq32[:, :])
            ts(b, "vector", G.pwl[:, 2, :], G.pwl[:, 2, :], 81.0, 0.25, AT.subtract, AT.mult)
            ts(b, "vector", G.pwl[:, 2, :], G.pwl[:, 2, :], -4.0, 4.0, AT.max, AT.min, inc="dv")
            XL = G.pwl[:, 2, :]
            b.wait("scalar", "dv")
            act(b, G.pwl[:, 3, :], XL, AF.Gelu)
            act(b, G.pwl[:, 4, :], XL, AF.Gelu, bias=G.colsc[:, 22:23], inc="sc")
            b.wait("vector", "sc")
            tt(b, "vector", G.pwl[:, 5, :], G.pwl[:, 4, :], G.pwl[:, 3, :], AT.subtract)
            stt(b, "vector", G.pwl[:, 6, :], hq, 1.0 / 256.0, XL, AT.mult, AT.subtract)
            stt(b, "vector", G.pwl[:, 6, :], G.pwl[:, 6, :], 4.0, G.pwl[:, 5, :], AT.mult, AT.mult)
            tt(b, "vector", G.pwl[:, 6, :], G.pwl[:, 6, :], G.pwl[:, 3, :], AT.add)
            ts(b, "vector", G.pwl[:, 7, :], hq, 1024.0, None, AT.is_gt)
            ts(b, "vector", G.pwl[:, 5, :], hq, 1.0 / 256.0, None, AT.mult)
            tt(b, "vector", G.pwl[:, 5, :], G.pwl[:, 5, :], G.pwl[:, 6, :], AT.subtract)
            tt(b, "vector", G.pwl[:, 5, :], G.pwl[:, 5, :], G.pwl[:, 7, :], AT.mult)
            tt(b, "vector", G.pwl[:, 6, :], G.pwl[:, 6, :], G.pwl[:, 5, :], AT.add)
            ts(b, "vector", G.pwl[:, 7, :], hq, -1024.0, None, AT.is_ge)
            tt(b, "vector", G.pwl[:, 6, :], G.pwl[:, 6, :], G.pwl[:, 7, :], AT.mult)
            ts(b, "vector", G.pwl[:, 0, :], G.pwl[:, 6, :], 256.0, None, AT.mult)
            b.op("vector", "tensor_copy", G.hq32[:, :], G.pwl[:, 0, :])
            b.op("vector", "tensor_copy", G.pwl[:, 0, :], G.hq32[:, :])
            ts(b, "vector", G.hstr[:, 2 * it:2 * it + 2, :].rearrange("p a c -> p (a c)"),
               G.pwl[:, 0, :], 1.0 / 256.0, None, AT.mult, inc="dv")
        def dpsum(eb):
            return DPSUM(G, eb)
        w2hist = {}
        b.wait("tensor", "dv"); b.wait("tensor", "sc")
        for fb in range(FF1 // 128):
            wt2 = G.wtile2 if fb % 2 == 0 else G.wtile2B
            lag = w2hist[fb - 2] if fb >= 2 else lastw2dec
            if lag:
                b.wait("sync", "mm", val=lag)
            dma(b, "sync", wt2[:, :], d["sc_wf2"][fb * 128:(fb + 1) * 128, :])
            b.wait("tensor", "ld")
            for eb in range(8):
                mm(b, dpsum(eb), wt2[:, eb * 128:(eb + 1) * 128],
                   G.hstr[:, fb, :], start=(fb == 0), stop=(fb == FF1 // 128 - 1))
            w2hist[fb] = b.semv["mm"]
        b.wait("vector", "mm")
        for eb in range(8):
            ts(b, "vector", G.pwl[:, 0, 0:GRP], dpsum(eb), G.b2c[:, eb, 0:1], 256.0, AT.add, AT.mult)
            b.op("vector", "tensor_copy", G.hq32[:, 0:GRP], G.pwl[:, 0, 0:GRP])
            b.op("vector", "tensor_copy", G.pwl[:, 0, 0:GRP], G.hq32[:, 0:GRP])
            stt(b, "vector", G.xres[:, eb, gs:gs + GRP], G.pwl[:, 0, 0:GRP], 1.0 / 256.0,
                G.xres[:, eb, gs:gs + GRP], AT.mult, AT.add)
        b.wait("tensor", "dv"); b.wait("tensor", "sc")
        dchist = {}
        for hb in range(8):
            wt2 = G.wtile2 if hb % 2 == 0 else G.wtile2B
            b.wait("sync", "mm", val=dchist[hb - 2] if hb >= 2 else w2hist[30 + hb])
            dma(b, "sync", wt2[:, :], d["sc_dec"][hb * 128:(hb + 1) * 128, :])
            dma(b, "sync", G.ospl[:, hb, :], d["sc_osp"][hb, :, gs:gs + GRP])
            b.wait("tensor", "ld")
            for eb in range(8):
                mm(b, dpsum(eb), wt2[:, eb * 128:(eb + 1) * 128],
                   G.ospl[:, hb, :], start=(hb == 0), stop=(hb == 7))
            dchist[hb] = b.semv["mm"]
        lastw1 = {0: w1hist[(FF1 // 256 - 1, 0)], 1: w1hist[(FF1 // 256 - 1, 1)]}
        lastw2dec = dchist[7]
        b.wait("vector", "mm")
        for eb in range(8):
            tt(b, "vector", G.xres[:, eb, gs:gs + GRP], G.xres[:, eb, gs:gs + GRP],
               dpsum(eb), AT.add)
        for half in range(2):
            ti = gs // 128 + half
            b.wait("tensor", "dv"); b.wait("tensor", "sc")
            b.wait("vector", "st")
            for db in range(8):
                pst = G.psA if db % 2 == 0 else G.psB
                if db >= 2:
                    b.wait("tensor", "dv"); b.wait("tensor", "sc")
                tr(b, pst[:, 256:384], G.xres[:, db, ti * 128:(ti + 1) * 128], G.idf[:, :], inc="mm")
                b.wait("vector", "mm")
                b.op("vector", "tensor_copy", G.tokst[:, db * 128:(db + 1) * 128], pst[:, 256:384], inc="dv")
            b.wait("sync", "dv")
            dma(b, "sync", d["out"][ti * 128:(ti + 1) * 128, :], G.tokst[:, :], sem="st")
    b.wait("sync", "st")
    b.wait("gpsimd", "st")


def kernel(**inputs):
    ref_dt = np.float32
    x = np.ascontiguousarray(inputs["L0_attn_x"], dtype=np.float32)
    B, T, _ = x.shape
    nc = build_program()

    wT = {p: np.ascontiguousarray(inputs[f"L0_attn_W_{p}"].T) for p in "rwkvabg"}
    wot = np.ascontiguousarray(inputs["L0_attn_W_o"].T)
    mixv8 = np.stack([inputs[f"L0_attn_x_{p}"] for p in "rwkvabg"] + [np.zeros(D, np.float32)])
    mixv = np.ascontiguousarray(mixv8.reshape(8, 8, 128).transpose(2, 1, 0)).astype(np.float32)
    def dimmaj(vec):
        n = vec.shape[0] // 128
        return np.ascontiguousarray(np.asarray(vec, np.float32).reshape(n, 128).T)
    wg_i = np.ascontiguousarray(inputs["L0_ffn_w_gate"].T)
    wu_i = np.ascontiguousarray(inputs["L0_ffn_w_up"].T)
    wd_i = np.ascontiguousarray(inputs["L0_ffn_w_down"].T)
    hut = np.ascontiguousarray(np.transpose(inputs["L1_attn_hebb_U"], (0, 2, 1)))

    ident = np.eye(128, dtype=np.float32)
    up_s = np.triu(np.ones((64, 64), np.float32), 1)
    up_i = np.triu(np.ones((64, 64), np.float32), 0)
    m4 = np.concatenate([up_s, up_i, up_s, up_i], axis=1)
    mask4 = np.concatenate([m4, m4], axis=0)
    lo_s = np.tril(np.ones((64, 64), np.float32), -1)
    masklo = np.concatenate([lo_s, lo_s], axis=0)
    idd2 = np.concatenate([np.eye(64, dtype=np.float32)] * 2, axis=0)
    ttt = np.arange(128)[:, None]
    jjj = np.arange(192)[None, :]
    bm = ((jjj >= ttt + 1) & (jjj <= ttt + 64)).astype(np.float32)
    bandb = (1.0 - bm) * 1.0e30
    ones2 = np.zeros((128, 2), np.float32)
    ones2[0:64, 0] = 1.0
    ones2[64:128, 1] = 1.0
    sel2 = np.zeros((2, 128), np.float32)
    sel2[0, 0:64] = 1.0
    sel2[1, 64:128] = 1.0

    in_maps = []
    for c in range(8):
        bb, gp = c // 4, c % 4
        t0 = gp * TLOC
        xsl = np.ascontiguousarray(x[bb, t0:t0 + TLOC])
        if gp == 0:
            xp = np.zeros((1, D), np.float32)
        else:
            xp = np.ascontiguousarray(x[bb, t0 - 1:t0])
        fl = np.zeros((1, 32), np.float32)
        for j in range(7):
            if (j // 4 == bb) and (j < c):
                fl[0, j] = 1.0
                fl[0, 16 + j] = 0.0
            else:
                fl[0, j] = 0.0
                fl[0, 16 + j] = 1.0
        for j in range(8):
            fl[0, 24 + j] = 1.0 if (gp > 0 and j == c - 1) else 0.0
        fl[0, 7] = 1.0 if gp == 0 else 0.0
        fl[0, 23] = 0.0 if gp == 0 else 1.0
        for j in range(8):
            fl[0, 8 + j] = 1.0 if (j // 4 == bb) else 0.0
        im = dict(
            xs=xsl, xprev=xp, flags=fl,
            mixv=mixv, gnwr=inputs["L0_attn_gn_w"][None], gnbr=inputs["L0_attn_gn_b"][None],
            bor=dimmaj(inputs["L0_attn_b_o"]),
            wot=wot, wg_i=wg_i, wu_i=wu_i, wd_i=wd_i,
            encq=np.ascontiguousarray(inputs["L1_attn_enc_q"]),
            encv=np.ascontiguousarray(inputs["L1_attn_enc_v"]),
            mw1=np.ascontiguousarray(inputs["L1_attn_metric_w1"]),
            mb1=dimmaj(inputs["L1_attn_metric_b1"]),
            mw2=np.ascontiguousarray(inputs["L1_attn_metric_w2"]),
            mb2=dimmaj(inputs["L1_attn_metric_b2"]),
            basem=dimmaj(inputs["L1_attn_base_metric"].reshape(-1)),
            hut=hut, hvv=np.ascontiguousarray(inputs["L1_attn_hebb_V"]),
            decw=np.ascontiguousarray(inputs["L1_attn_decoder"]),
            f1w=np.ascontiguousarray(inputs["L1_ffn_w1"]), f1b=dimmaj(inputs["L1_ffn_b1"]),
            f2w=np.ascontiguousarray(inputs["L1_ffn_w2"]), f2b=dimmaj(inputs["L1_ffn_b2"]),
            identf=ident, identb=ident.astype(np.float32),
            mask4=mask4, masklo=masklo, idd2=idd2, bandm=bm, bandb=bandb,
            ones2=ones2, sel2=sel2, onesr=np.ones((1, 128), np.float32),
            onescol=np.ones((128, 1), np.float32),
        )
        for p in "rwkvabg":
            im[f"w{p}t"] = wT[p]
        in_maps.append(im)

    import ml_dtypes
    in_maps2 = []
    for im in in_maps:
        im2 = dict(im)
        im2["identb"] = ident.astype(ml_dtypes.bfloat16)
        in_maps2.append(im2)

    res = run_bass_kernel_spmd(nc, in_maps2, core_ids=list(range(8)))
    global LAST_RES
    LAST_RES = res
    outs = np.zeros((B, T, D), dtype=ref_dt)
    for c in range(8):
        bb, gp = c // 4, c % 4
        outs[bb, gp * TLOC:(gp + 1) * TLOC] = res.results[c]["out"]
    return outs


# version 04
# speedup vs baseline: 1.2127x; 1.0281x over previous
import numpy as np
from contextlib import ExitStack
import concourse.bass as bass
import concourse.mybir as mybir
from concourse.bass_utils import run_bass_kernel_spmd

F32 = mybir.dt.float32
BF16 = mybir.dt.bfloat16
I32 = mybir.dt.int32
AT = mybir.AluOpType
AF = mybir.ActivationFunctionType
AX = mybir.AxisListType.X

TLOC, D, C, H, K = 2048, 1024, 64, 16, 64
NCH = TLOC // C
NH, HD = 8, 128
GRP = 256
NGRP = TLOC // GRP
CPG = GRP // C
FF0, FF1 = 3072, 4096
MARK = -1.0e20
DEBUG = False
LAST_RES = None
GN_EPS = 0.00064
ENGS = ("sync", "tensor", "vector", "scalar", "gpsimd")


class Builder:

    def __init__(self, nc, es, semnames):
        self.nc = nc
        self.q = {e: [] for e in ENGS}
        self.semh = {n: es.enter_context(nc.semaphore(n)) for n in semnames}
        self.semv = {n: 0 for n in semnames}
        self.accessed = {}

    def _rec(self, eng, fn):
        self.q[eng].append(fn)

    @staticmethod
    def _fp(x):
        t = getattr(x, "tensor", None)
        if t is None:
            return None
        span = 1
        try:
            for st, n in x.ap:
                span += abs(int(st)) * (int(n) - 1)
        except Exception:
            span = 1 << 30
        off = int(x.offset)
        return (t.name, off, off + span)

    @staticmethod
    def _ov(a, bb):
        return a[0] == bb[0] and a[1] < bb[2] and bb[1] < a[2]

    def op(self, eng, meth, *a, inc=None, incn=1, **k):
        semh = self.semh
        import traceback as _tb
        site = _tb.extract_stack(limit=6)

        ENGSEM = {"vector": "dv", "scalar": "sc", "gpsimd": "gp", "tensor": "mm"}
        if (inc is None and eng in ENGSEM and meth not in
                ("sem_inc", "drain", "dma_start", "wait_ge", "collective_compute")):
            inc = ENGSEM[eng]
        if meth == "dma_start" and inc is not None and getattr(self, "_pubd", None):
            v = self._pubd.pop(inc, None)
            if v is not None:
                def swt(e, inc=inc, v=v):
                    e.wait_ge(semh[inc], v)
                self._rec(eng, swt)
        if getattr(self, "_pub", None) and self._pub.get(eng) and meth not in (
                "sem_inc", "drain", "wait_ge"):
            def drp(e):
                e.drain()
            self._rec(eng, drp)
            acc0 = self.accessed.setdefault(eng, {"w": [], "r": []})
            acc0["w"] = []
            acc0["r"] = []
            self._pub[eng] = False
        if eng in ("vector", "scalar", "gpsimd", "tensor") and meth not in (
                "sem_inc", "drain", "dma_start"):
            fps = [f for f in (self._fp(x) for x in list(a) + list(k.values()))
                   if f is not None]
            wr = [f for f in ([self._fp(a[0])] if a and self._fp(a[0]) else [])]
            acc = self.accessed.setdefault(eng, {"w": [], "r": []})
            need = any(self._ov(f, w) for f in fps for w in acc["w"]) or                    any(self._ov(w, r) for w in wr for r in acc["r"])
            if need:
                def dr(e):
                    e.drain()
                self._rec(eng, dr)
                acc["w"] = []
                acc["r"] = []
            acc["w"].extend(wr)
            acc["r"].extend(fps[1:] if len(fps) > 1 else [])

        if inc in ("dv", "sc", "gp", "mm") and meth not in (
                "sem_inc", "drain", "wait_ge"):
            if not hasattr(self, "_lastinc"):
                self._lastinc = {}
            if self._lastinc.get(inc, eng) != eng:
                self.wait(eng, inc)
            self._lastinc[inc] = eng

        def run(e, meth=meth, a=a, k=k, inc=inc, incn=incn, site=site):
            try:
                ins = getattr(e, meth)(*a, **k)
            except Exception:
                print("FAILED OP:", meth)
                for f in site:
                    if "kernel.py" in f.filename and f.name not in ("op", "run"):
                        print("   at", f.lineno, f.name, "|", f.line)
                raise
            if inc is not None:
                ins.then_inc(semh[inc], incn)
        self._rec(eng, run)
        if inc is not None:
            self.semv[inc] += incn
            return self.semv[inc]
        return None

    def wait(self, eng, sem, val=None):
        v = self.semv[sem] if val is None else val
        semh = self.semh
        SEM2ENG = {"dv": "vector", "sc": "scalar", "gp": "gpsimd", "mm": "tensor"}
        if sem in SEM2ENG and SEM2ENG[sem] != eng:
            if not hasattr(self, "_pub"):
                self._pub = {}
            self._pub[SEM2ENG[sem]] = True
        if sem in ("ld", "gld", "st", "sld"):
            if not hasattr(self, "_pubd"):
                self._pubd = {}
            self._pubd[sem] = max(v, self._pubd.get(sem, 0))

        def run(e, sem=sem, v=v):
            e.wait_ge(semh[sem], v)
        self._rec(eng, run)

    def sig(self, eng, sem, n=1):
        semh = self.semh
        self.semv[sem] += n
        v = self.semv[sem]

        def run(e, sem=sem, n=n):
            e.sem_inc(semh[sem], n)
        self._rec(eng, run)
        return v

    def fence(self, engines=("tensor", "vector", "scalar", "gpsimd", "sync")):
        for e in engines:
            for sem in ("mm", "dv", "sc", "gp", "ld", "gld", "sld", "cc"):
                if self.semv[sem]:
                    self.wait(e, sem)

    def replay(self, block):
        for ename in ENGS:
            dec = getattr(block, ename)
            ops = self.q[ename]

            def body(e, ops=ops):
                for fn in ops:
                    fn(e)
            dec(body)



def dma(b, qeng, out, in_, sem=None, slow=False, **kw):
    sem = sem or ("gld" if qeng == "gpsimd" else "ld")
    if slow:
        kw["allow_slow_non_contiguous"] = True
    return b.op(qeng, "dma_start", inc=sem, incn=16, out=out, in_=in_, **kw)


def build_program():
    nc = bass.Bass("TRN2", target_bir_lowering=False, debug=False)

    din = {}

    def IN(name, shape, dt=F32):
        din[name] = nc.dram_tensor(name, list(shape), dt, kind="ExternalInput").ap()
        return din[name]

    xs = IN("xs", (TLOC, D))
    xprev = IN("xprev", (1, D))
    flags = IN("flags", (1, 32))
    wpk = {p: IN(f"w{p}t", (D, D)) for p in "rwkvabg"}
    wot = IN("wot", (D, D))
    mixv = IN("mixv", (128, 8, 8))
    gnwr = IN("gnwr", (1, D))
    gnbr = IN("gnbr", (1, D))
    bor = IN("bor", (128, 8))
    wg_i = IN("wg_i", (D, FF0), I32)
    wu_i = IN("wu_i", (D, FF0), I32)
    wd_i = IN("wd_i", (FF0, D), I32)
    encq = IN("encq", (NH, D, HD))
    encv = IN("encv", (NH, D, HD))
    mw1 = IN("mw1", (D, 512))
    mb1 = IN("mb1", (128, 4))
    mw2 = IN("mw2", (512, D))
    mb2 = IN("mb2", (128, 8))
    basem = IN("basem", (128, 8))
    hut = IN("hut", (NH, 32, HD))
    hvv = IN("hvv", (NH, 32, HD))
    decw = IN("decw", (D, D))
    f1w = IN("f1w", (D, FF1))
    f1b = IN("f1b", (128, 32))
    f2w = IN("f2w", (FF1, D))
    f2b = IN("f2b", (128, 8))
    identf = IN("identf", (128, 128))
    identb = IN("identb", (128, 128), BF16)
    mask4 = IN("mask4", (128, 256))
    masklo = IN("masklo", (128, 64))
    idd2 = IN("idd2", (128, 64))
    bandm = IN("bandm", (128, 192))
    bandb = IN("bandb", (128, 192))
    ones2 = IN("ones2", (128, 2))
    sel2 = IN("sel2", (2, 128))
    onesr = IN("onesr", (1, 128))
    onescol = IN("onescol", (128, 1))

    out_h = nc.dram_tensor("out", [TLOC, D], F32, kind="ExternalOutput")
    out = out_h.ap()

    def DT(name, shape, dt=F32):
        skip = ("cc1_in", "cc2_in", "cc1_out", "cc2_out")
        if DEBUG and name not in skip:
            return nc.dram_tensor(name, list(shape), dt, kind="ExternalOutput").ap()
        return nc.dram_tensor(name, list(shape), dt).ap()

    sc_wproj = DT("sc_wproj", (7, D, D), BF16)
    sc_wg = DT("sc_wg", (D, FF0), BF16)
    sc_wu = DT("sc_wu", (D, FF0), BF16)
    sc_wd = DT("sc_wd", (FF0, D), BF16)
    sc_psi = DT("sc_psi", (NCH, H, K, C), BF16)
    sc_p = DT("sc_p", (NCH, H, K, K), BF16)
    sc_oloc = DT("sc_oloc", (TLOC, D), BF16)
    sc_g = DT("sc_g", (TLOC, D), BF16)
    sc_dbg = DT("sc_dbg", (128, 8, K), F32)
    cc1_in = DT("cc1_in", (H, 128, K))
    cc1_out = DT("cc1_out", (8 * H, 128, K))
    cc2_in = DT("cc2_in", (80, D))
    cc2_out = DT("cc2_out", (8 * 80, D))
    sc_wf1 = DT("sc_wf1", (D, FF1), BF16)
    sc_wf2 = DT("sc_wf2", (FF1, D), BF16)
    sc_dec = DT("sc_dec", (D, D), BF16)
    sc_osp = DT("sc_osp", (NH, 128, TLOC), BF16)
    sc_encq = DT("sc_encq", (NH, D, HD), BF16)

    es = ExitStack()

    def SB(name, shape, dt=F32):
        return es.enter_context(nc.sbuf_tensor(name, list(shape), dt))

    with es:
        g = {}
        g["xres"] = SB("xres", (128, 8, TLOC))
        g["stats"] = SB("stats", (128, 16, 2))
        g["stats2"] = SB("stats2", (128, 16, 2))
        g["mixs"] = SB("mixs", (128, 8, 8))
        g["xnprevc"] = SB("xnprevc", (128, 8, 1))
        g["idf"] = SB("idf", (128, 128))
        g["idb"] = SB("idb", (128, 128), BF16)
        g["msk4"] = SB("msk4", (128, 256))
        g["msklo"] = SB("msklo", (128, 64))
        g["idd2"] = SB("idd2s", (128, 64))
        g["bndm"] = SB("bndm", (128, 192))
        g["bndb"] = SB("bndb", (128, 192))
        g["one2"] = SB("one2", (128, 2))
        g["sel2"] = SB("sel2s", (2, 128))
        g["oner"] = SB("oner", (1, 128))
        g["onec"] = SB("onec", (128, 1))
        g["flgb"] = SB("flgb", (128, 32))
        g["rowscr"] = SB("rowscr", (1, 512))
        g["colsc"] = SB("colsc", (128, 24))
        g["dsb"] = SB("dsb", (128, 8, K))
        g["dbf"] = SB("dbf", (128, 8, K), BF16)
        g["scr"] = SB("scr", (128, 4, GRP))
        g["scr2"] = SB("scr2", (128, 4, GRP))
        g["scrb"] = SB("scrb", (128, 2, 1024), BF16)
        g["tokst"] = SB("tokst", (128, 1024))
        g["tokst2"] = SB("tokst2", (128, 1024))
        ARENA_COLS = 49792
        g["arena"] = SB("arena", (128, ARENA_COLS), BF16)

        g["psA"] = es.enter_context(nc.psum_tensor("psA", [128, 512], F32))
        g["psB"] = es.enter_context(nc.psum_tensor("psB", [128, 512], F32))
        g["psST"] = es.enter_context(nc.psum_tensor("psST", [128, 8, 256], F32))
        g["psCH"] = es.enter_context(nc.psum_tensor("psCH", [128, 8, 128], F32))

        class Arena:
            def __init__(self, t, cols):
                self.t = t
                self.cols = cols
                self.off = 0

            def take(self, shape, dt=F32):
                inner = 1
                for s in shape[1:]:
                    inner *= s
                n = inner * (2 if dt in (F32, I32) else 1)
                n = (n + 63) // 64 * 64
                assert self.off + n <= self.cols, f"arena overflow {self.off}+{n}"
                sl = self.t[:, self.off:self.off + n]
                self.off += n
                flat = (sl if dt == BF16 else sl.bitcast(dt))[:, 0:inner]
                v = flat
                if len(shape) == 3:
                    v = flat.rearrange("p (a c) -> p a c", a=shape[1])
                return v

            def reg(self, a, name, shape, dt=F32):
                inner = 1
                for s in shape[1:]:
                    inner *= s
                n = inner * (2 if dt in (F32, I32) else 1)
                n = (n + 63) // 64 * 64
                assert self.off + n <= self.cols, f"arena overflow {self.off}+{n}"
                sl = self.t[:, self.off:self.off + n]
                self.off += n
                flat = (sl if dt == BF16 else sl.bitcast(dt))[:, 0:inner]
                v = flat
                if len(shape) == 3:
                    v = flat.rearrange("p (a c) -> p a c", a=shape[1])
                a[name] = v
                a[name + "_f"] = flat
                return v

        def seta(ar):
            a = {}
            ar.reg(a, "wbuf", (128, 8, 1024), BF16)
            ar.reg(a, "s_r", (128, 8, GRP), BF16)
            ar.reg(a, "s_k", (128, 8, GRP), BF16)
            ar.reg(a, "s_a", (128, 8, GRP), BF16)
            ar.reg(a, "s_b", (128, 8, GRP), BF16)
            ar.reg(a, "wcum", (128, 8, GRP))
            ar.reg(a, "xng", (128, 8, GRP), BF16)
            ar.reg(a, "xxg", (128, 8, GRP), BF16)
            ar.reg(a, "xmx", (128, 8, GRP), BF16)
            ar.reg(a, "xmv", (128, 8, GRP), BF16)
            ar.reg(a, "wcc", (128, 8, 2 * C))
            ar.reg(a, "ftmp", (128, 8, 3 * C))
            ar.reg(a, "fcol", (128, 8, 4))
            ar.reg(a, "lhsbk", (128, 8, 2 * C), BF16)
            ar.reg(a, "rhsar", (128, 8, 2 * C), BF16)
            ar.reg(a, "til", (128, 8, 4 * C), BF16)
            ar.reg(a, "vtok", (128, H * K), BF16)
            ar.reg(a, "stsb", (128, 8, 4 * C), BF16)
            ar.reg(a, "ntb", (128, 8, C), BF16)
            ar.reg(a, "pwst", (128, 8, 128), BF16)
            ar.reg(a, "pwts", (128, 8, 128), BF16)
            ar.reg(a, "ysbb", (128, 8, 128), BF16)
            ar.reg(a, "trsb", (128, 8, 128), BF16)
            a["psis"] = a["lhsbk"].rearrange("p a c -> p (a c)").rearrange(
                "p (a c) -> p a c", c=C)[:, 0:8, :]
            a["psb"] = a["rhsar"].rearrange("p a c -> p (a c)").rearrange(
                "p (a c) -> p a c", c=K)[:, 0:8, :]
            a["ostg"] = a["rhsar"].rearrange("p a c -> p (a c)").rearrange(
                "p (a c) -> p a c", c=K)[:, 8:16, :]
            ar.reg(a, "slst", (128, 8, K))
            ar.reg(a, "slbf", (128, 8, K), BF16)
            ar.reg(a, "accs", (128, 8, K))
            ar.reg(a, "accb", (128, 8, K), BF16)
            return a

        def seta2(ar):
            a = {}
            ar.reg(a, "gnwb", (128, 1024))
            ar.reg(a, "gnbb", (128, 1024))
            ar.reg(a, "bocol", (128, 8, 1))
            ar.reg(a, "wotb", (128, 8, 1024), BF16)
            ar.reg(a, "wtile", (128, 8, 128), BF16)
            ar.reg(a, "wtile2", (128, 8, 128), BF16)
            ar.reg(a, "wtileB", (128, 8, 128), BF16)
            ar.reg(a, "wtile2B", (128, 8, 128), BF16)
            ar.reg(a, "wdtile", (128, 1024), BF16)
            ar.reg(a, "wdtileB", (128, 1024), BF16)
            ar.reg(a, "hstr", (128, 32, GRP), BF16)
            ar.reg(a, "xng", (128, 8, GRP), BF16)
            ar.reg(a, "psil", (128, 8, C), BF16)
            ar.reg(a, "pldl", (128, 8, K), BF16)
            ar.reg(a, "ztl", (128, 8, 128), BF16)
            ar.reg(a, "otr", (128, 8, 128))
            ar.reg(a, "gnsc", (128, 16, 8))
            return a

        def setb_shared(ar):
            a = {}
            ar.reg(a, "qts", (128, NH, 64 + TLOC), BF16)
            ar.reg(a, "vts", (128, 16, D), BF16)
            ar.reg(a, "vhalo", (128, D), BF16)
            ar.reg(a, "uvs", (128, NH * HD), BF16)
            ar.reg(a, "metr", (128, 8, 1))
            ar.reg(a, "acols", (128, 8, 17))
            ar.reg(a, "sqnc", (128, 8, 17))
            return a

        def setb0(ar):
            a = setb_shared(ar)
            ar.reg(a, "xng", (128, 8, GRP), BF16)
            ar.reg(a, "encvb", (128, 8, 1024), BF16)
            ar.reg(a, "encb", (128, 8, 128), BF16)
            ar.reg(a, "sigc", (128, 8, 4))
            ar.reg(a, "mlpc", (128, 8, 4))
            ar.reg(a, "bmc", (128, 8, 1))
            ar.reg(a, "xhalT", (128, 8, 64), BF16)
            return a

        def setb1(ar):
            a = setb_shared(ar)
            ar.reg(a, "dtl", (128, 192))
            ar.reg(a, "neg1", (128, 192))
            ar.reg(a, "neg2", (128, 192))
            ar.reg(a, "m8", (128, 8))
            ar.reg(a, "wse", (128, 192))
            ar.reg(a, "wsb", (128, 192), BF16)
            ar.reg(a, "expd", (128, 192))
            ar.reg(a, "wtb", (128, 384), BF16)
            ar.reg(a, "rsum", (128, 4))
            ar.reg(a, "qsq", (128, 128))
            ar.reg(a, "ospst", (128, 8, 128), BF16)
            ar.reg(a, "qmb", (128, 128), BF16)
            ar.reg(a, "qtkb", (128, 128), BF16)
            ar.reg(a, "ostok", (128, 128), BF16)
            ar.reg(a, "ctmp", (128, 128))
            return a

        def setb2(ar):
            a = {}
            ar.reg(a, "xng", (128, 8, GRP), BF16)
            ar.reg(a, "hstr", (128, 32, GRP), BF16)
            ar.reg(a, "wtile", (128, 8, 128), BF16)
            ar.reg(a, "wtileB", (128, 8, 128), BF16)
            ar.reg(a, "wtile2", (128, 1024), BF16)
            ar.reg(a, "wtile2B", (128, 1024), BF16)
            ar.reg(a, "hq32", (128, 2 * GRP), I32)
            ar.reg(a, "pwl", (128, 8, 2 * GRP))
            ar.reg(a, "b1c", (128, 32, 1))
            ar.reg(a, "b2c", (128, 8, 1))
            ar.reg(a, "ospl", (128, 8, GRP), BF16)
            return a

        g["_mk"] = (Arena, seta, seta2, setb0, setb1, setb2, ARENA_COLS)

        b = Builder(nc, es, ["ld", "gld", "sld", "mm", "dv", "sc", "gp", "st", "cc", "fn"])
        G = type("G", (), g)

        emit(b, G, dict(
            xs=xs, xprev=xprev, flags=flags, wpk=wpk, wot=wot, mixv=mixv,
            gnwr=gnwr, gnbr=gnbr, bor=bor, wg_i=wg_i, wu_i=wu_i, wd_i=wd_i,
            encq=encq, encv=encv, mw1=mw1, mb1=mb1, mw2=mw2, mb2=mb2,
            basem=basem, hut=hut, hvv=hvv, decw=decw, f1w=f1w, f1b=f1b,
            f2w=f2w, f2b=f2b, identf=identf, identb=identb, mask4=mask4,
            masklo=masklo, idd2=idd2, bandm=bandm, bandb=bandb, ones2=ones2,
            sel2=sel2, onesr=onesr, onescol=onescol, out=out,
            sc_wproj=sc_wproj, sc_wg=sc_wg, sc_wu=sc_wu, sc_wd=sc_wd,
            sc_psi=sc_psi, sc_p=sc_p, sc_oloc=sc_oloc, sc_g=sc_g, sc_dbg=sc_dbg,
            sc_wf1=sc_wf1, sc_wf2=sc_wf2, sc_dec=sc_dec, sc_osp=sc_osp, sc_encq=sc_encq,
            cc1_in=cc1_in, cc1_out=cc1_out, cc2_in=cc2_in, cc2_out=cc2_out,
        ))

        block = es.enter_context(nc.Block())
        b.replay(block)

    return nc



def _merge(G, extra):
    base = {k: v for k, v in G.__dict__.items() if not k.startswith("__")}
    base.update(extra)
    return type("G", (), base)


PHASES = 99


def _finish_stub(b, G, d, dump_xres=False):
    b.fence()
    if dump_xres:
        ov = d["out"].rearrange("t e -> (t e)").rearrange("(p r) -> p r", p=128)
        dma(b, "sync", ov, G.xres[:, :, :].rearrange("p a c -> p (a c)"), sem="st")
    else:
        b.op("vector", "memset", G.tokst[:, :], 0.0, inc="dv")
        b.wait("sync", "dv")
        for i in range(16):
            dma(b, "sync", d["out"][i * 128:(i + 1) * 128, :], G.tokst[:, :], sem="st")
    b.wait("sync", "st")
    b.wait("gpsimd", "st")


def emit(b, G, d):
    Arena, seta, seta2, setb0, setb1, setb2, AC = G._mk
    import kernel as _k
    ph = getattr(_k, "PHASES", 99)
    GA = _merge(G, seta(Arena(G.arena, AC)))
    emit_p0(b, GA, d)
    if ph <= 1:
        return _finish_stub(b, GA, d)
    emit_wconv(b, GA, d)
    if ph <= 2:
        return _finish_stub(b, GA, d)
    b.op("vector", "memset", GA.slst[:, :, :], 0.0, inc="dv")
    for grp in range(min(NGRP, ph - 2)):
        emit_group(b, GA, d, grp)
    if ph <= 10:
        return _finish_stub(b, GA, d)
    emit_relay(b, GA, d)
    if ph <= 11:
        return _finish_stub(b, GA, d)
    GA2 = _merge(G, seta2(Arena(G.arena, AC)))
    emit_p3(b, GA2, d)
    if ph <= 12:
        return _finish_stub(b, GA2, d, dump_xres=True)
    GB0 = _merge(G, setb0(Arena(G.arena, AC)))
    emit_p4(b, GB0, d)
    if ph <= 13:
        return _finish_stub(b, GB0, d, dump_xres=True)
    emit_p5(b, GB0, d)
    if ph <= 14:
        return _finish_stub(b, GB0, d, dump_xres=True)
    GB1 = _merge(G, setb1(Arena(G.arena, AC)))
    emit_p6a(b, GB1, d)
    if ph <= 15:
        return _finish_stub(b, GB1, d, dump_xres=True)
    GB2 = _merge(G, setb2(Arena(G.arena, AC)))
    emit_p6b(b, GB2, d)




def v8(t, pages, cols):
    return t[:, 0:pages, :].rearrange("p a b -> p (a b)").rearrange("p (a c) -> p a c", c=cols)

def tt(b, eng, outp, a0, a1, op, inc="dv"):
    return b.op(eng, "tensor_tensor", outp, a0, a1, op, inc=inc)


def ts(b, eng, outp, a0, s1, s2, op0, op1=None, inc="dv"):
    if op1 is None:
        return b.op(eng, "tensor_scalar", outp, a0, s1, None, op0, inc=inc)
    return b.op(eng, "tensor_scalar", outp, a0, s1, s2, op0, op1, inc=inc)


def stt(b, eng, outp, in0, scalar, in1, op0, op1, inc="dv"):
    return b.op(eng, "scalar_tensor_tensor", outp, in0, scalar, in1, op0, op1, inc=inc)


def act(b, outp, inp, func, inc="sc", **kw):
    return b.op("scalar", "activation", outp, inp, func, inc=inc, **kw)


def mm(b, outp, lhsT, rhs, start=True, stop=True, tp=None, inc=None, sgc=False):
    kw = {}
    if tp is not None:
        kw["tile_position"] = tp
    if sgc:
        kw["skip_group_check"] = True
    return b.op("tensor", "matmul", outp, lhsT, rhs, inc=inc, start=start, stop=stop, **kw)


def tr(b, outp, inp, ident, tp=None, inc=None):
    kw = {}
    if tp is not None:
        kw["tile_position"] = tp
    return b.op("tensor", "transpose", outp, inp, ident, inc=inc, **kw)


def DPSUM(G, eb):
    if eb == 0:
        return G.psA[:, 0:256]
    if eb == 1:
        return G.psB[:, 0:256]
    if eb < 6:
        return G.psST[:, (eb - 2) * 2, 0:256]
    return G.psCH[:, (eb - 6) * 4:(eb - 6) * 4 + 2, :].rearrange("p a b -> p (a b)")


def emit_p0(b, G, d):
    for outp, inp in [
        (G.idf[:, :], d["identf"]), (G.idb[:, :], d["identb"]),
        (G.msk4[:, :], d["mask4"]), (G.msklo[:, :], d["masklo"]),
        (G.idd2[:, :], d["idd2"]),
        (G.bndm[:, :], d["bandm"]), (G.bndb[:, :], d["bandb"]),
        (G.one2[:, :], d["ones2"]), (G.sel2[:, :], d["sel2"]),
        (G.oner[:, :], d["onesr"]), (G.onec[:, :], d["onescol"]),
        (G.tokst2[0:1, :], d["xprev"]),
    ]:
        dma(b, "sync", outp, inp)
    dma(b, "sync", G.mixs[:, :, :], d["mixv"])
    dma(b, "sync", G.scr[0:1, 0, 0:32], d["flags"])
    ldv = b.semv["ld"]

    b.wait("tensor", "ld")
    mm(b, G.psA[:, 0:32], G.oner[:, :], G.scr[0:1, 0, 0:32], inc="mm")
    b.wait("vector", "mm")
    b.op("vector", "tensor_copy", G.flgb[:, :], G.psA[:, 0:32], inc="dv")


    b.wait("vector", "ld", ldv)
    b.op("vector", "reduce_sum", G.colsc[0:1, 0:1], G.tokst2[0:1, :], axis=AX)
    tt(b, "vector", G.scr[0:1, 0, 0:GRP], G.tokst2[0:1, 0:GRP], G.tokst2[0:1, 0:GRP], AT.mult)
    tt(b, "vector", G.scr[0:1, 1, 0:GRP], G.tokst2[0:1, GRP:2 * GRP], G.tokst2[0:1, GRP:2 * GRP], AT.mult)
    tt(b, "vector", G.scr[0:1, 2, 0:GRP], G.tokst2[0:1, 2 * GRP:3 * GRP], G.tokst2[0:1, 2 * GRP:3 * GRP], AT.mult)
    tt(b, "vector", G.scr[0:1, 3, 0:GRP], G.tokst2[0:1, 3 * GRP:4 * GRP], G.tokst2[0:1, 3 * GRP:4 * GRP], AT.mult)
    tt(b, "vector", G.scr[0:1, 0, 0:GRP], G.scr[0:1, 0, 0:GRP], G.scr[0:1, 1, 0:GRP], AT.add)
    tt(b, "vector", G.scr[0:1, 2, 0:GRP], G.scr[0:1, 2, 0:GRP], G.scr[0:1, 3, 0:GRP], AT.add)
    tt(b, "vector", G.scr[0:1, 0, 0:GRP], G.scr[0:1, 0, 0:GRP], G.scr[0:1, 2, 0:GRP], AT.add)
    b.op("vector", "reduce_sum", G.colsc[0:1, 1:2], G.scr[0:1, 0, 0:GRP], axis=AX)
    ts(b, "vector", G.colsc[0:1, 2:3], G.colsc[0:1, 0:1], 1.0 / D, None, AT.mult)
    ts(b, "vector", G.colsc[0:1, 3:4], G.colsc[0:1, 1:2], 1.0 / D, None, AT.mult)
    tt(b, "vector", G.colsc[0:1, 4:5], G.colsc[0:1, 2:3], G.colsc[0:1, 2:3], AT.mult)
    tt(b, "vector", G.colsc[0:1, 5:6], G.colsc[0:1, 3:4], G.colsc[0:1, 4:5], AT.subtract)
    ts(b, "vector", G.colsc[0:1, 5:6], G.colsc[0:1, 5:6], 1e-5, None, AT.add, inc="dv")
    b.wait("scalar", "dv")
    b.op("scalar", "sqrt", G.colsc[0:1, 6:7], G.colsc[0:1, 5:6], inc="sc")
    b.wait("vector", "sc")
    b.op("vector", "reciprocal", G.colsc[0:1, 7:8], G.colsc[0:1, 6:7])
    ts(b, "vector", G.tokst2[0:1, :], G.tokst2[0:1, :], G.colsc[0:1, 2:3], G.colsc[0:1, 7:8],
       AT.subtract, AT.mult, inc="dv")
    b.wait("tensor", "dv"); b.wait("tensor", "sc")
    for db in range(8):
        mm(b, G.psB[:, db:db + 1], G.tokst2[0:1, db * 128:(db + 1) * 128], G.onec[0:1, 0:1])
    b.wait("vector", "mm")
    b.op("vector", "tensor_copy", G.xnprevc[:, :, 0], G.psB[:, 0:8], inc="dv")

    for i in range(16):
        buf = G.tokst if i % 2 == 0 else G.tokst2
        b.wait("sync", "mm")
        dma(b, "sync", buf[:, :], d["xs"][i * 128:(i + 1) * 128, :])
        b.wait("vector", "ld")
        b.op("vector", "reduce_sum", G.colsc[:, 8:9], buf[:, :], axis=AX)
        b.wait("scalar", "ld")
        act(b, G.scr[:, 0:4, :].rearrange("p a c -> p (a c)"), buf[:, :], AF.Square)
        b.wait("vector", "sc")
        b.op("vector", "reduce_sum", G.colsc[:, 9:10], G.scr[:, 0:4, :].rearrange("p a c -> p (a c)"), axis=AX)
        ts(b, "vector", G.colsc[:, 10:11], G.colsc[:, 8:9], 1.0 / D, None, AT.mult)
        ts(b, "vector", G.colsc[:, 11:12], G.colsc[:, 9:10], 1.0 / D, None, AT.mult)
        tt(b, "vector", G.colsc[:, 12:13], G.colsc[:, 10:11], G.colsc[:, 10:11], AT.mult)
        tt(b, "vector", G.colsc[:, 13:14], G.colsc[:, 11:12], G.colsc[:, 12:13], AT.subtract)
        ts(b, "vector", G.colsc[:, 13:14], G.colsc[:, 13:14], 1e-5, None, AT.add, inc="dv")
        b.wait("scalar", "dv")
        b.op("scalar", "sqrt", G.colsc[:, 14:15], G.colsc[:, 13:14], inc="sc")
        b.wait("vector", "sc")
        b.op("vector", "reciprocal", G.colsc[:, 15:16], G.colsc[:, 14:15])
        b.op("vector", "tensor_copy", G.stats[:, i, 0:1], G.colsc[:, 10:11])
        b.op("vector", "tensor_copy", G.stats[:, i, 1:2], G.colsc[:, 15:16], inc="dv")
        b.wait("tensor", "ld")
        for db in range(8):
            pst = G.psA if db % 2 == 0 else G.psB
            b.wait("tensor", "dv"); b.wait("tensor", "sc")
            tr(b, pst[:, 256:384] if pst is G.psB else pst[:, 0:128],
               buf[:, db * 128:(db + 1) * 128], G.idf[:, :], inc="mm")
            b.wait("vector", "mm")
            psrc = pst[:, 256:384] if pst is G.psB else pst[:, 0:128]
            b.op("vector", "tensor_copy", G.xres[:, db, i * 128:(i + 1) * 128], psrc, inc="dv")

    b.fence()




def ln_bcast(b, G, statt, ti):
    b.wait("tensor", "dv"); b.wait("tensor", "sc")
    tr(b, G.psCH[0:1, 0, 0:128], statt[:, ti, 0:1], G.idf[:, :])
    tr(b, G.psCH[0:1, 0, 128:256] if False else G.psCH[0:1, 1, 0:128],
       statt[:, ti, 1:2], G.idf[:, :], inc="mm")
    b.wait("vector", "mm")
    b.op("vector", "tensor_copy", G.rowscr[0:1, 0:128], G.psCH[0:1, 0, 0:128])
    b.op("vector", "tensor_copy", G.rowscr[0:1, 128:256], G.psCH[0:1, 1, 0:128], inc="dv")
    b.wait("tensor", "dv"); b.wait("tensor", "sc")
    mm(b, G.psA[:, 0:128], G.oner[:, :], G.rowscr[0:1, 0:128])
    mm(b, G.psB[:, 0:128], G.oner[:, :], G.rowscr[0:1, 128:256], inc="mm")
    b.wait("vector", "mm")

def emit_wconv(b, G, d):
    for pi, p in enumerate("rwkvabg"):
        src = d["wpk"][p]
        for half in range(2):
            sl = slice(half * 512, (half + 1) * 512)
            b.wait("gpsimd", "gld")
            dma(b, "gpsimd", G.wbuf[:, 0:4, :],
                src[sl, :].rearrange("(hb p) e -> p hb e", p=128))
            b.wait("gpsimd", "gld")
            dma(b, "gpsimd", d["sc_wproj"][pi, sl, :].rearrange("(hb p) e -> p hb e", p=128),
                G.wbuf[:, 0:4, :])
    for srci, dst, rows, cols in ((d["wg_i"], d["sc_wg"], D, FF0),
                                  (d["wu_i"], d["sc_wu"], D, FF0),
                                  (d["wd_i"], d["sc_wd"], FF0, D)):
        step = 8192 // cols * 128
        for r0 in range(0, rows, step):
            sl = slice(r0, r0 + step)
            wv = G.wbuf_f[:, 0:(step // 128) * cols].rearrange("p (a c) -> p a c", c=cols)
            b.wait("gpsimd", "gld")
            dma(b, "gpsimd", wv, srci[sl, :].rearrange("(hb p) e -> p hb e", p=128))
            b.wait("gpsimd", "gld")
            dma(b, "gpsimd", dst[sl, :].rearrange("(hb p) e -> p hb e", p=128), wv)
    for h in range(NH):
        b.wait("gpsimd", "gld")
        dma(b, "gpsimd", G.wbuf[:, :, 0:128],
            d["encq"][h].rearrange("(db p) e -> p db e", p=128))
        b.wait("gpsimd", "gld")
        dma(b, "gpsimd", d["sc_encq"][h].rearrange("(db p) e -> p db e", p=128),
            G.wbuf[:, :, 0:128])
    for srci, dst, rows, cols in ((d["f1w"], d["sc_wf1"], D, FF1),
                                  (d["f2w"], d["sc_wf2"], FF1, D),
                                  (d["decw"], d["sc_dec"], D, D)):
        step = 8192 // cols * 128
        for r0 in range(0, rows, step):
            sl = slice(r0, r0 + step)
            wv = G.wbuf_f[:, 0:(step // 128) * cols].rearrange("p (a c) -> p a c", c=cols)
            b.wait("gpsimd", "gld")
            dma(b, "gpsimd", wv, srci[sl, :].rearrange("(hb p) e -> p hb e", p=128))
            b.wait("gpsimd", "gld")
            dma(b, "gpsimd", dst[sl, :].rearrange("(hb p) e -> p hb e", p=128), wv)
    b.fence()


def emit_group(b, G, d, grp):
    gs = grp * GRP
    t128 = gs // 128

    for half in range(2):
        ti = t128 + half
        ln_bcast(b, G, G.stats, ti)
        hs = slice(half * 128, (half + 1) * 128)
        for db in range(8):
            tt(b, "vector", G.scr[:, 0, 0:128], G.xres[:, db, gs + half * 128:gs + (half + 1) * 128],
               G.psA[:, 0:128], AT.subtract, inc=None)
            tt(b, "vector", G.xng[:, db, hs], G.scr[:, 0, 0:128], G.psB[:, 0:128], AT.mult, inc=None)
    tt(b, "vector", G.xxg[:, :, 1:GRP], G.xng[:, :, 0:GRP - 1], G.xng[:, :, 1:GRP], AT.subtract)
    tt(b, "vector", G.xxg[:, :, 0:1], G.xnprevc[:, :, :], G.xng[:, :, 0:1], AT.subtract)
    b.op("vector", "tensor_copy", G.xnprevc[:, :, :], G.xng[:, :, GRP - 1:GRP], inc="dv")

    def xmix_into(tgt, vi):
        b.wait("vector", "mm")
        b.wait("vector", "gld")
        vw = v8(G.scrb, 2, GRP)
        tt(b, "vector", vw, G.xxg[:, :, :],
           G.mixs[:, :, vi:vi + 1].broadcast_to([128, 8, GRP]), AT.mult, inc=None)
        tt(b, "vector", tgt[:, :, :], vw, G.xng[:, :, :], AT.add, inc="dv")

    b.wait("sync", "mm")
    dma(b, "sync", G.wbuf[:, :, :],
        d["sc_wproj"][6].rearrange("(db p) e -> p db e", p=128))
    xmix_into(G.xmx, 6)
    b.wait("tensor", "ld")
    b.wait("tensor", "dv"); b.wait("tensor", "sc")
    for tc in range(2):
        b.wait("tensor", "sc")
        for eh in range(2):
            pst = G.psA if eh == 0 else G.psB
            for db in range(8):
                mm(b, pst[:, 0:512], G.xmx[:, db, tc * 128:(tc + 1) * 128],
                   G.wbuf[:, db, eh * 512:(eh + 1) * 512], start=(db == 0), stop=(db == 7))
        b.wait("scalar", "mm")
        b.wait("scalar", "gld")
        act(b, G.scrb[:, 0, 0:512], G.psA[:, 0:512], AF.Sigmoid)
        act(b, G.scrb[:, 0, 512:1024], G.psB[:, 0:512], AF.Sigmoid, inc="sc")
        b.wait("gpsimd", "sc")
        dma(b, "gpsimd", d["sc_g"][gs + tc * 128:gs + (tc + 1) * 128, :],
            G.scrb[:, 0, :])
    xmix_into(G.xmv, 3)

    for p, vi, stream in (("r", 0, G.s_r), ("w", 1, None), ("k", 2, G.s_k),
                          ("a", 4, G.s_a), ("b", 5, G.s_b)):
        wi = "rwkvabg".index(p)
        b.wait("sync", "mm")
        b.wait("sync", "sc")
        wsrc = d["sc_wproj"][wi].rearrange("(db p) e -> p db e", p=128)
        ldv = {}
        for eb in range(8):
            dma(b, "sync", G.wbuf[:, :, eb * 128:(eb + 1) * 128],
                wsrc[:, :, eb * 128:(eb + 1) * 128])
            ldv[eb] = b.semv["ld"]
            b.wait("sync", "ld", val=ldv[eb])
        xmix_into(G.xmx, vi)
        pev = {}
        ent = (b.semv["dv"], b.semv["sc"])
        b.wait("tensor", "dv"); b.wait("tensor", "sc")
        for eb in range(8):
            pst = G.psA if eb % 2 == 0 else G.psB
            dvv, scv = pev.get(eb - 2, ent)
            b.wait("tensor", "dv", val=dvv)
            b.wait("tensor", "sc", val=scv)
            b.wait("tensor", "ld", val=ldv[eb])
            for db in range(8):
                mm(b, pst[:, 0:GRP], G.wbuf[:, db, eb * 128:(eb + 1) * 128],
                   G.xmx[:, db, :], start=(db == 0), stop=(db == 7))
            if p == "r":
                b.wait("vector", "mm")
                b.op("vector", "tensor_copy", stream[:, eb, :], pst[:, 0:GRP], inc="dv")
            elif p == "k":
                b.wait("scalar", "mm")
                act(b, stream[:, eb, :], pst[:, 0:GRP], AF.Tanh, inc="dv")
            elif p == "w":
                b.wait("scalar", "mm")
                act(b, G.scr[:, 0, :], pst[:, 0:GRP], AF.Sigmoid, scale=-1.0, inc="sc")
                b.wait("vector", "sc")
                b.op("vector", "tensor_scalar_max", G.scr[:, 1, :], G.scr[:, 0, :], 1e-38, inc="dv")
                b.wait("scalar", "dv")
                act(b, G.wcum[:, eb, :], G.scr[:, 1, :], AF.Ln, inc="dv")
            else:
                b.wait("scalar", "mm")
                act(b, G.scr[:, 0, :], pst[:, 0:GRP], AF.Square, inc="sc")
                b.wait("tensor", "sc")
                mm(b, G.psST[0:2, 0, 0:GRP], G.one2[:, :], G.scr[:, 0, :], inc="mm")
                b.wait("vector", "mm")
                b.op("vector", "tensor_copy", G.scr[0:2, 1, :], G.psST[0:2, 0, 0:GRP], inc="dv")
                b.wait("scalar", "dv")
                b.op("scalar", "sqrt", G.scr[0:2, 1, :], G.scr[0:2, 1, :], inc="sc")
                b.wait("vector", "sc")
                b.op("vector", "tensor_scalar_max", G.scr[0:2, 1, :], G.scr[0:2, 1, :], 1e-6)
                b.op("vector", "reciprocal", G.scr[0:2, 1, :], G.scr[0:2, 1, :], inc="dv")
                b.wait("tensor", "dv"); b.wait("tensor", "sc")
                mm(b, G.psST[:, 1, 0:GRP], G.sel2[:, :], G.scr[0:2, 1, :], inc="mm")
                b.wait("vector", "mm")
                b.op("vector", "tensor_copy", G.scr[:, 2, :], G.psST[:, 1, 0:GRP])
                tt(b, "vector", stream[:, eb, :], pst[:, 0:GRP], G.scr[:, 2, :], AT.mult, inc="dv")
            pev[eb] = (b.semv["dv"], b.semv["sc"])

    b.wait("sync", "mm")
    b.wait("sync", "sc")
    b.wait("sync", "dv")
    dma(b, "sync", G.wbuf[:, :, :],
        d["sc_wproj"][3].rearrange("(db p) e -> p db e", p=128))
    b.wait("tensor", "ld")
    b.op("vector", "memset", G.scr[:, 0, 0:GRP], 1.0)
    for hp in range(8):
        b.op("vector", "tensor_tensor_scan", G.wcum[:, hp, :],
             G.scr[:, 0, 0:GRP], G.wcum[:, hp, :], 0.0, AT.mult, AT.add)

    for ci in range(CPG):
        emit_chunk(b, G, d, grp, grp * CPG + ci, ci * C)


def emit_chunk(b, G, d, grp, ch, cs):
    sl = slice(cs, cs + C)

    b.wait("tensor", "dv"); b.wait("tensor", "sc")
    for par in range(2):
        if par:
            b.wait("tensor", "dv")
        tp = (0, 64) if par else None
        po = 64 * par
        for eh in range(2):
            pst = G.psA if eh == 0 else G.psB
            for db in range(8):
                mm(b, pst[po:po + 64, 0:512], G.xmv[:, db, sl],
                   G.wbuf[:, db, eh * 512:(eh + 1) * 512], start=(db == 0), stop=(db == 7), tp=tp)
        b.wait("vector", "mm")
        b.op("vector", "tensor_copy", G.vtok[po:po + 64, 0:512], G.psA[po:po + 64, 0:512])
        b.op("vector", "tensor_copy", G.vtok[po:po + 64, 512:1024], G.psB[po:po + 64, 0:512], inc="dv")

    if cs == 0:
        b.op("vector", "tensor_copy", G.wcc[:, :, 0:C], G.wcum[:, :, sl])
    else:
        tt(b, "vector", G.wcc[:, :, 0:C], G.wcum[:, :, sl],
           G.wcum[:, :, cs - 1:cs].broadcast_to([128, 8, C]), AT.subtract)
    b.op("vector", "memset", G.wcc[:, :, C:C + 1], 0.0)
    b.op("vector", "tensor_copy", G.wcc[:, :, C + 1:2 * C], G.wcc[:, :, 0:C - 1])
    rf = G.wcc[:, :, C // 2 - 1:C // 2]
    tt(b, "vector", G.ftmp[:, :, 0:C], G.wcc[:, :, 0:C], rf.broadcast_to([128, 8, C]), AT.subtract)
    tt(b, "vector", G.ftmp[:, :, C:2 * C], G.wcc[:, :, C:2 * C], rf.broadcast_to([128, 8, C]),
       AT.subtract, inc="dv")
    b.wait("scalar", "dv")
    act(b, G.ftmp[:, :, 0:C], G.ftmp[:, :, 0:C], AF.Exp)
    act(b, G.ftmp[:, :, C:2 * C], G.ftmp[:, :, C:2 * C], AF.Exp)
    act(b, G.fcol[:, :, 0:1], rf, AF.Exp)
    act(b, G.fcol[:, :, 2:3], G.wcc[:, :, C - 1:C], AF.Exp, inc="sc")
    b.wait("vector", "sc")
    b.wait("vector", "gld")
    b.op("vector", "reciprocal", G.ftmp[:, :, 2 * C:3 * C], G.ftmp[:, :, 0:C])
    tt(b, "vector", G.rhsar[:, :, 0:C], G.s_a[:, :, sl], G.ftmp[:, :, C:2 * C], AT.mult)
    tt(b, "vector", G.rhsar[:, :, C:2 * C], G.s_r[:, :, sl], G.ftmp[:, :, 0:C], AT.mult)
    tt(b, "vector", G.lhsbk[:, :, 0:C], G.s_b[:, :, sl], G.ftmp[:, :, 2 * C:3 * C], AT.mult)
    tt(b, "vector", G.lhsbk[:, :, C:2 * C], G.s_k[:, :, sl], G.ftmp[:, :, 2 * C:3 * C], AT.mult)
    tt(b, "vector", G.til[:, :, 0:C], G.rhsar[:, :, 0:C],
       G.fcol[:, :, 0:1].broadcast_to([128, 8, C]), AT.mult)
    tt(b, "vector", G.til[:, :, C:2 * C], G.rhsar[:, :, C:2 * C],
       G.fcol[:, :, 0:1].broadcast_to([128, 8, C]), AT.mult)
    f6 = v8(G.scr, 2, C)
    tt(b, "vector", f6, G.ftmp[:, :, 2 * C:3 * C],
       G.ftmp[:, :, C - 1:C].broadcast_to([128, 8, C]), AT.mult)
    tt(b, "vector", G.til[:, :, 2 * C:3 * C], G.s_b[:, :, sl], f6, AT.mult)
    tt(b, "vector", G.til[:, :, 3 * C:4 * C], G.s_k[:, :, sl], f6, AT.mult, inc="dv")

    b.wait("tensor", "dv"); b.wait("tensor", "sc")
    for h in range(H):
        q = 64 * (h & 1)
        hp = h >> 1
        tp = (q, q) if q else None
        mm(b, G.psST[q:q + 64, hp, 0:128], G.lhsbk[q:q + 64, hp, 0:C],
           G.rhsar[q:q + 64, hp, :], tp=tp)
        mm(b, G.psST[q:q + 64, hp, 128:256], G.lhsbk[q:q + 64, hp, C:2 * C],
           G.rhsar[q:q + 64, hp, :], tp=tp)
        mm(b, G.psCH[q:q + 64, hp, 0:64], G.rhsar[q:q + 64, hp, 0:C],
           G.lhsbk[q:q + 64, hp, 0:C], tp=tp)
    b.wait("vector", "mm")
    tt(b, "vector", G.stsb[:, :, :], G.psST[:, :, :],
       G.msk4[:, :].unsqueeze(1).broadcast_to([128, 8, 256]), AT.mult)
    tt(b, "vector", G.ntb[:, :, :], G.psCH[:, :, 0:64],
       G.msklo[:, :].unsqueeze(1).broadcast_to([128, 8, 64]), AT.mult, inc="dv")

    b.wait("tensor", "dv"); b.wait("tensor", "sc")
    for h in range(H):
        q = 64 * (h & 1)
        hp = h >> 1
        tp = (q, q) if q else None
        mm(b, G.psST[q:q + 64, hp, 0:64], G.til[q:q + 64, hp, 2 * C:3 * C],
           G.idb[q:q + 64, q:q + 64], tp=tp)
        mm(b, G.psST[q:q + 64, hp, 64:128], G.til[q:q + 64, hp, 3 * C:4 * C],
           G.idb[q:q + 64, q:q + 64], tp=tp)
    b.wait("vector", "mm")
    b.op("vector", "tensor_copy", G.trsb[:, :, 0:128], G.psST[:, :, 0:128])
    b.op("vector", "tensor_copy", G.slbf[:, :, :], G.slst[:, :, :], inc="dv")

    b.wait("tensor", "dv"); b.wait("tensor", "sc")
    for h in range(H):
        q = 64 * (h & 1)
        hp = h >> 1
        tp = (q, q) if q else None
        mm(b, G.psCH[q:q + 64, hp, 64:128], G.til[q:q + 64, hp, 0:C],
           G.idb[q:q + 64, q:q + 64], tp=tp)
        mm(b, G.psCH[q:q + 64, hp, 0:64], G.til[q:q + 64, hp, 0:C],
           G.slbf[q:q + 64, hp, :], tp=tp, stop=False)
        mm(b, G.psCH[q:q + 64, hp, 0:64], G.stsb[q:q + 64, hp, 2 * C:3 * C],
           G.vtok[q:q + 64, h * K:(h + 1) * K], tp=tp, start=False)
    b.wait("vector", "mm")
    b.op("vector", "tensor_copy", G.ysbb[:, :, :], G.psCH[:, :, :], inc="dv")

    b.op("vector", "memset", G.pwst[0:64, :, 64:128], 0.0)
    b.op("vector", "memset", G.pwst[64:128, :, 0:64], 0.0)
    b.op("vector", "memset", G.pwts[0:64, :, 64:128], 0.0)
    b.op("vector", "memset", G.pwts[64:128, :, 0:64], 0.0)
    b.op("vector", "tensor_copy", G.pwst[0:64, :, 0:64], G.stsb[0:64, :, 0:C])
    b.op("vector", "tensor_copy", G.pwst[64:128, :, 64:128], G.stsb[64:128, :, 0:C])
    b.op("vector", "tensor_copy", G.pwts[0:64, :, 0:64], G.ntb[0:64, :, :])
    b.op("vector", "tensor_copy", G.pwts[64:128, :, 64:128], G.ntb[64:128, :, :], inc="dv")

    for lev in range(6):
        b.wait("tensor", "dv"); b.wait("tensor", "sc")
        for hp in range(8):
            mm(b, G.psCH[:, hp, 0:128], G.idb[:, :], G.ysbb[:, hp, :],
               start=True, stop=False)
            mm(b, G.psCH[:, hp, 0:128], G.pwst[:, hp, :], G.ysbb[:, hp, :],
               start=False, stop=True)
        b.wait("vector", "mm")
        b.op("vector", "tensor_copy", G.ysbb[:, :, :], G.psCH[:, :, :], inc="dv")
        if lev < 5:
            b.wait("tensor", "dv"); b.wait("tensor", "sc")
            for hp in range(8):
                mm(b, G.psST[:, hp, 0:128], G.pwts[:, hp, :], G.pwst[:, hp, :])
            if lev < 4:
                for hp in range(8):
                    mm(b, G.psST[:, hp, 128:256], G.pwst[:, hp, :], G.pwts[:, hp, :])
            b.wait("vector", "mm")
            b.op("vector", "tensor_copy", G.pwst[:, :, :], G.psST[:, :, 0:128])
            if lev < 4:
                b.op("vector", "tensor_copy", G.pwts[:, :, :], G.psST[:, :, 128:256])


    b.wait("tensor", "dv"); b.wait("tensor", "sc")
    for h in range(H):
        q = 64 * (h & 1)
        hp = h >> 1
        tp = (q, q) if q else None
        mm(b, G.psCH[q:q + 64, hp, 0:64], G.til[q:q + 64, hp, C:2 * C],
           G.slbf[q:q + 64, hp, :], tp=tp, stop=False)
        mm(b, G.psCH[q:q + 64, hp, 0:64], G.stsb[q:q + 64, hp, C:2 * C],
           G.ysbb[q:q + 64, hp, 0:64], tp=tp, start=False, stop=False)
        mm(b, G.psCH[q:q + 64, hp, 0:64], G.stsb[q:q + 64, hp, 3 * C:4 * C],
           G.vtok[q:q + 64, h * K:(h + 1) * K], tp=tp, start=False)
    b.wait("vector", "mm")
    b.op("vector", "tensor_copy", G.ostg[:, :, :], G.psCH[:, :, 0:64], inc="dv")
    olr = d["sc_oloc"].rearrange("tt (hp par e) -> tt par hp e", par=2, e=K)
    for par in range(2):
        b.wait("gpsimd", "dv")
        dma(b, "gpsimd", olr[ch * C:(ch + 1) * C, par], G.ostg[par * 64:(par + 1) * 64, :, :])

    b.wait("tensor", "dv"); b.wait("tensor", "sc")
    for h in range(H):
        q = 64 * (h & 1)
        hp = h >> 1
        tp = (q, q) if q else None
        mm(b, G.psST[q:q + 64, hp, 0:64], G.trsb[q:q + 64, hp, 0:64],
           G.ysbb[q:q + 64, hp, 0:64], tp=tp, stop=False)
        mm(b, G.psST[q:q + 64, hp, 0:64], G.trsb[q:q + 64, hp, 64:128],
           G.vtok[q:q + 64, h * K:(h + 1) * K], tp=tp, start=False)
    b.wait("vector", "mm")
    tt(b, "vector", G.slst[:, :, :], G.slst[:, :, :],
       G.fcol[:, :, 2:3].broadcast_to([128, 8, K]), AT.mult)
    tt(b, "vector", G.slst[:, :, :], G.slst[:, :, :], G.psST[:, :, 0:64], AT.add, inc="dv")

    b.wait("tensor", "dv"); b.wait("tensor", "sc")
    for h in range(H):
        q = 64 * (h & 1)
        hp = h >> 1
        tp = (q, q) if q else None
        mm(b, G.psST[q:q + 64, hp, 64:128], G.ysbb[q:q + 64, hp, 64:128],
           G.trsb[q:q + 64, hp, 0:64], tp=tp)
    b.wait("vector", "mm")
    dsc = v8(G.scr, 2, K)
    tt(b, "vector", dsc, G.idd2[:, :].unsqueeze(1).broadcast_to([128, 8, 64]),
       G.fcol[:, :, 2:3].broadcast_to([128, 8, K]), AT.mult)
    tt(b, "vector", G.psb[:, :, :], G.psST[:, :, 64:128], dsc, AT.add, inc="dv")
    pr = d["sc_p"].rearrange("c (hp par) k k2 -> c par k hp k2", par=2)
    for par in range(2):
        b.wait("gpsimd", "dv")
        dma(b, "gpsimd", pr[ch, par], G.psb[par * 64:(par + 1) * 64, :, :])
    if ch == 0:
        b.op("vector", "tensor_copy", G.accs[:, :, :], G.psb[:, :, :])
        b.op("vector", "tensor_copy", G.accb[:, :, :], G.psb[:, :, :], inc="dv")
    else:
        b.wait("tensor", "dv"); b.wait("tensor", "sc")
        for h in range(H):
            q = 64 * (h & 1)
            hp = h >> 1
            tp = (q, q) if q else None
            mm(b, G.psCH[q:q + 64, hp, 64:128], G.psb[q:q + 64, hp, :], G.accb[q:q + 64, hp, :], tp=tp)
        b.wait("vector", "mm")
        b.op("vector", "tensor_copy", G.accs[:, :, :], G.psCH[:, :, 64:128])
        b.op("vector", "tensor_copy", G.accb[:, :, :], G.psCH[:, :, 64:128], inc="dv")

    b.wait("tensor", "dv"); b.wait("tensor", "sc")
    for h in range(H):
        q = 64 * (h & 1)
        hp = h >> 1
        tp = (q, q) if q else None
        mm(b, G.psCH[q:q + 64, hp, 0:64], G.stsb[q:q + 64, hp, C:2 * C],
           G.ysbb[q:q + 64, hp, 64:128], tp=tp)
    b.wait("vector", "mm")
    psig = G.scrb[:, 0:1, 0:512].rearrange("p a b -> p (a b)").rearrange("p (a c) -> p a c", c=64)
    b.op("vector", "tensor_copy", psig, G.psCH[:, :, 0:64], inc="dv")
    b.wait("tensor", "dv"); b.wait("tensor", "sc")
    for h in range(H):
        q = 64 * (h & 1)
        hp = h >> 1
        tp = (q, q) if q else None
        mm(b, G.psCH[q:q + 64, hp, 64:128], psig[q:q + 64, hp, :],
           G.idb[q:q + 64, q:q + 64], tp=tp)
    b.wait("vector", "mm")
    tt(b, "vector", G.psis[:, :, :], G.psCH[:, :, 64:128],
       G.til[:, :, C:2 * C], AT.add, inc="dv")
    for par in range(2):
        b.wait("gpsimd", "dv")
        dma(b, "gpsimd",
            d["sc_psi"].rearrange("c (hp par) k t -> c par k hp t", par=2)[ch, par],
            G.psis[par * 64:(par + 1) * 64, :, :])


def emit_relay(b, G, d):
    b.fence()
    b.wait("tensor", "dv"); b.wait("tensor", "sc")
    for h in range(H):
        q = 64 * (h & 1)
        hp = h >> 1
        tp = (q, q) if q else None
        mm(b, G.psCH[q:q + 64, hp, 0:64], G.accs[q:q + 64, hp, :],
           G.idf[q:q + 64, q:q + 64], tp=tp)
    b.wait("vector", "mm")
    pct = v8(G.scr2, 2, 64)
    b.op("vector", "tensor_copy", pct, G.psCH[:, :, 0:64], inc="dv")
    b.wait("gpsimd", "dv")
    for h in range(H):
        q = 64 * (h & 1)
        hp = h >> 1
        dma(b, "gpsimd", d["cc1_in"][h, 0:64, :], pct[q:q + 64, hp, :])
        dma(b, "gpsimd", d["cc1_in"][h, 64:128, :], G.slst[q:q + 64, hp, :])
    b.wait("gpsimd", "gld")
    b.op("gpsimd", "collective_compute", "AllGather", AT.bypass,
         replica_groups=[list(range(8))],
         ins=[d["cc1_in"].opt()], outs=[d["cc1_out"].opt()], inc="cc")
    b.wait("gpsimd", "cc")
    for e in ("vector", "tensor", "sync"):
        b.wait(e, "cc")

    b.op("vector", "memset", G.dsb[:, :, :], 0.0)
    b.op("vector", "memset", G.dbf[:, :, :], 0.0, inc="dv")
    for j in range(7):
        b.wait("sync", "dv")
        b.wait("gpsimd", "mm")
        ccr = d["cc1_out"].rearrange("(j hp par) r k -> j par r hp k", j=8, par=2)
        for par in range(2):
            dma(b, "gpsimd", G.accb[par * 64:(par + 1) * 64, :, :], ccr[j, par, 0:64])
            dma(b, "sync", G.accs[par * 64:(par + 1) * 64, :, :], ccr[j, par, 64:128])
        b.wait("tensor", "ld")
        b.wait("tensor", "gld")
        b.wait("tensor", "dv"); b.wait("tensor", "sc")
        for h in range(H):
            q = 64 * (h & 1)
            hp = h >> 1
            tp = (q, q) if q else None
            mm(b, G.psCH[q:q + 64, hp, 0:64], G.accb[q:q + 64, hp, :], G.dbf[q:q + 64, hp, :], tp=tp)
        b.wait("vector", "mm")
        b.wait("vector", "ld")
        b.wait("vector", "gld")
        tmp = v8(G.scr, 2, 64)
        tt(b, "vector", tmp, G.psCH[:, :, 0:64], G.accs[:, :, :], AT.add)
        ts(b, "vector", tmp, tmp, G.flgb[:, j:j + 1], None, AT.mult)
        ts(b, "vector", G.dsb[:, :, :], G.dsb[:, :, :], G.flgb[:, 16 + j:17 + j], None, AT.mult)
        tt(b, "vector", G.dsb[:, :, :], G.dsb[:, :, :], tmp, AT.add)
        b.op("vector", "tensor_copy", G.dbf[:, :, :], G.dsb[:, :, :], inc="dv")
    if DEBUG:
        b.wait("gpsimd", "dv")
        dma(b, "gpsimd", d["sc_dbg"], G.dsb[:, :, :])
    b.fence()


def emit_p3(b, G, d):
    for grp in range(NGRP):
        gs = grp * GRP
        for half in range(2):
            ti = gs // 128 + half
            ln_bcast(b, G, G.stats, ti)
            hs = slice(half * 128, (half + 1) * 128)
            for db in range(8):
                tt(b, "vector", G.scr[:, 0, 0:128], G.xres[:, db, gs + half * 128:gs + (half + 1) * 128],
                   G.psA[:, 0:128], AT.subtract, inc=None)
                tt(b, "vector", G.xng[:, db, hs], G.scr[:, 0, 0:128], G.psB[:, 0:128], AT.mult, inc=None)
        guhist = {}
        guev = {}
        b.wait("tensor", "dv"); b.wait("tensor", "sc")
        for hb in range(FF0 // 128):
            even = hb % 2 == 0
            wt, wt2 = (G.wtile, G.wtile2) if even else (G.wtileB, G.wtile2B)
            pg = G.psA[:, 0:GRP] if even else G.psST[:, 0, 0:GRP]
            pu = G.psB[:, 0:GRP] if even else G.psST[:, 2, 0:GRP]
            scp = G.scr[:, 0, :] if even else G.scr[:, 1, :]
            lag = guhist.get(hb - 2, G._p3gu.get(grp - 1, 0) if hasattr(G, "_p3gu") else 0)
            if lag:
                b.wait("sync", "mm", val=lag)
            dma(b, "sync", wt[:, :, :],
                d["sc_wg"][:, hb * 128:(hb + 1) * 128].rearrange("(db p) e -> p db e", p=128))
            dma(b, "sync", wt2[:, :, :],
                d["sc_wu"][:, hb * 128:(hb + 1) * 128].rearrange("(db p) e -> p db e", p=128))
            b.wait("tensor", "ld")
            if hb >= 2:
                b.wait("tensor", "sc", val=guev[hb - 2][0])
                b.wait("tensor", "dv", val=guev[hb - 2][1])
            for db in range(8):
                mm(b, pg, wt[:, db, :], G.xng[:, db, :], start=(db == 0), stop=(db == 7))
            for db in range(8):
                mm(b, pu, wt2[:, db, :], G.xng[:, db, :], start=(db == 0), stop=(db == 7))
            guhist[hb] = b.semv["mm"]
            b.wait("scalar", "mm")
            if hb >= 2:
                b.wait("scalar", "dv", val=guev[hb - 2][1])
            act(b, scp, pg, AF.Silu, scale=1.0 / 127.0, inc="sc")
            b.wait("vector", "sc")
            stt(b, "vector", G.hstr[:, hb, :], pu, 1.0 / 127.0, scp,
                AT.mult, AT.mult, inc="dv")
            guev[hb] = (b.semv["sc"], b.semv["dv"])
        if not hasattr(G, "_p3gu"):
            G._p3gu = {}
        G._p3gu[grp] = guhist[FF0 // 128 - 1]
        def dpsum(eb):
            return DPSUM(G, eb)
        dnhist = {}
        b.wait("tensor", "dv"); b.wait("tensor", "sc")
        for hb in range(FF0 // 128):
            wdt = G.wdtile if hb % 2 == 0 else G.wdtileB
            lag = dnhist.get(hb - 2, G._p3dn.get(grp - 1, 0) if hasattr(G, "_p3dn") else 0)
            if lag:
                b.wait("sync", "mm", val=lag)
            dma(b, "sync", wdt[:, :], d["sc_wd"][hb * 128:(hb + 1) * 128, :])
            b.wait("tensor", "ld")
            for eb in range(8):
                mm(b, dpsum(eb), wdt[:, eb * 128:(eb + 1) * 128], G.hstr[:, hb, :],
                   start=(hb == 0), stop=(hb == FF0 // 128 - 1))
            dnhist[hb] = b.semv["mm"]
        if not hasattr(G, "_p3dn"):
            G._p3dn = {}
        G._p3dn[grp] = dnhist[FF0 // 128 - 1]
        b.wait("vector", "mm")
        for eb in range(8):
            stt(b, "vector", G.xres[:, eb, gs:gs + GRP], dpsum(eb), 1.0 / 127.0,
                G.xres[:, eb, gs:gs + GRP], AT.mult, AT.add)
    b.fence()

    for i, srcrow, tgt in ((0, d["gnwr"], G.gnwb), (1, d["gnbr"], G.gnbb)):
        b.wait("sync", "mm")
        b.wait("sync", "dv")
        dma(b, "sync", G.rowscr[0:1, 0:512], srcrow[0:1, 0:512])
        b.wait("tensor", "ld")
        mm(b, G.psA[:, 0:512], G.oner[:, :], G.rowscr[0:1, 0:512], inc="mm")
        b.wait("sync", "mm")
        dma(b, "sync", G.rowscr[0:1, 0:512], srcrow[0:1, 512:1024])
        b.wait("vector", "mm")
        b.op("vector", "tensor_copy", tgt[:, 0:512], G.psA[:, 0:512], inc="dv")
        b.wait("tensor", "ld")
        b.wait("tensor", "dv"); b.wait("tensor", "sc")
        mm(b, G.psA[:, 0:512], G.oner[:, :], G.rowscr[0:1, 0:512], inc="mm")
        b.wait("vector", "mm")
        b.op("vector", "tensor_copy", tgt[:, 512:1024], G.psA[:, 0:512], inc="dv")
    dma(b, "sync", G.bocol[:, :, :], d["bor"].unsqueeze(2), slow=True)
    dma(b, "gpsimd", G.wotb[:, :, :],
        d["wot"].rearrange("(db p) e -> p db e", p=128))
    b.fence()

    for ti in range(16):
        b.wait("sync", "dv")
        dma(b, "sync", G.scrb[:, 0, :], d["sc_oloc"][ti * 128:(ti + 1) * 128, :])
        dma(b, "sync", G.scrb[:, 1, :], d["sc_g"][ti * 128:(ti + 1) * 128, :])
        for cpar in range(2):
            ch = 2 * ti + cpar
            b.wait("sync", "mm")
            psir = d["sc_psi"].rearrange("c (hp par) k t -> c par k hp t", par=2)
            pldr = d["sc_p"].rearrange("c (hp par) k k2 -> c par k hp k2", par=2)
            for par in range(2):
                dma(b, "sync", G.psil[par * 64:(par + 1) * 64, :, :], psir[ch, par])
                dma(b, "sync", G.pldl[par * 64:(par + 1) * 64, :, :], pldr[ch, par])
            b.wait("tensor", "ld")
            b.wait("tensor", "dv"); b.wait("tensor", "sc")
            for h in range(H):
                q = 64 * (h & 1)
                hp = h >> 1
                mm(b, G.psST[cpar * 64:(cpar + 1) * 64, h >> 2, (h % 4) * 64:(h % 4) * 64 + 64],
                   G.psil[q:q + 64, hp, :], G.dbf[q:q + 64, hp, :],
                   tp=(q, cpar * 64) if (q or cpar) else None)
            for h in range(H):
                q = 64 * (h & 1)
                hp = h >> 1
                mm(b, G.psCH[q:q + 64, hp, 0:64], G.pldl[q:q + 64, hp, :],
                   G.dbf[q:q + 64, hp, :], tp=(q, q) if q else None)
            b.wait("vector", "mm")
            b.op("vector", "tensor_copy", G.dsb[:, :, :], G.psCH[:, :, 0:64])
            b.op("vector", "tensor_copy", G.dbf[:, :, :], G.dsb[:, :, :], inc="dv")
        b.wait("vector", "ld")
        tt(b, "vector", G.otr_f[:, :], G.scrb[:, 0, :], G.psST[:, 0:4, :].rearrange("p a c -> p (a c)"), AT.add)
        gm = G.otr_f[:, :].rearrange("p (h k) -> p h k", h=16)
        b.op("vector", "reduce_sum", G.gnsc[:, :, 0:1], gm, axis=AX, inc="dv")
        b.wait("scalar", "dv")
        act(b, G.scr[:, 0:4, :].rearrange("p a c -> p (a c)"), G.otr_f[:, :], AF.Square, inc="sc")
        b.wait("vector", "sc")
        b.op("vector", "reduce_sum", G.gnsc[:, :, 1:2],
             G.scr[:, 0:4, :].rearrange("p a c -> p (a c)").rearrange("p (h k) -> p h k", h=16), axis=AX)
        ts(b, "vector", G.gnsc[:, :, 2:3], G.gnsc[:, :, 0:1], 1.0 / 64, None, AT.mult)
        ts(b, "vector", G.gnsc[:, :, 3:4], G.gnsc[:, :, 1:2], 1.0 / 64, None, AT.mult)
        tt(b, "vector", G.gnsc[:, :, 4:5], G.gnsc[:, :, 2:3], G.gnsc[:, :, 2:3], AT.mult)
        tt(b, "vector", G.gnsc[:, :, 5:6], G.gnsc[:, :, 3:4], G.gnsc[:, :, 4:5], AT.subtract)
        ts(b, "vector", G.gnsc[:, :, 5:6], G.gnsc[:, :, 5:6], GN_EPS, None, AT.add, inc="dv")
        b.wait("scalar", "dv")
        b.op("scalar", "sqrt", G.gnsc[:, :, 6:7], G.gnsc[:, :, 5:6], inc="sc")
        b.wait("vector", "sc")
        b.op("vector", "reciprocal", G.gnsc[:, :, 7:8], G.gnsc[:, :, 6:7])
        tt(b, "vector", gm, gm, G.gnsc[:, :, 2:3].broadcast_to([128, 16, 64]), AT.subtract)
        tt(b, "vector", gm, gm, G.gnsc[:, :, 7:8].broadcast_to([128, 16, 64]), AT.mult)
        flat = G.otr_f[:, :]
        tt(b, "vector", flat, flat, G.gnwb[:, :], AT.mult)
        tt(b, "vector", flat, flat, G.gnbb[:, :], AT.add)
        tt(b, "vector", flat, flat, G.scrb[:, 1, :], AT.mult, inc="dv")
        b.wait("tensor", "dv"); b.wait("tensor", "sc")
        for db in range(8):
            pst = G.psA if db % 2 == 0 else G.psB
            b.wait("tensor", "dv"); b.wait("tensor", "sc")
            tr(b, pst[:, 128:256], flat[:, db * 128:(db + 1) * 128], G.idf[:, :], inc="mm")
            b.wait("vector", "mm")
            b.op("vector", "tensor_copy", G.ztl[:, db, :], pst[:, 128:256], inc="dv")
        b.wait("tensor", "dv"); b.wait("tensor", "sc")
        for eb in range(8):
            pst = G.psA if eb % 2 == 0 else G.psB
            for db in range(8):
                mm(b, pst[:, 0:128], G.wotb[:, db, eb * 128:(eb + 1) * 128],
                   G.ztl[:, db, :], start=(db == 0), stop=(db == 7))
            b.wait("vector", "mm")
            ts(b, "vector", G.scr2[:, 0, 0:128], pst[:, 0:128], G.bocol[:, eb, 0:1], None, AT.add)
            tt(b, "vector", G.xres[:, eb, ti * 128:(ti + 1) * 128],
               G.xres[:, eb, ti * 128:(ti + 1) * 128], G.scr2[:, 0, 0:128], AT.add, inc="dv")
            b.wait("tensor", "dv"); b.wait("tensor", "sc")
    b.wait("tensor", "dv"); b.wait("tensor", "sc")
    for db in range(8):
        pst = G.psA if db % 2 == 0 else G.psB
        b.wait("tensor", "dv"); b.wait("tensor", "sc")
        tr(b, pst[0:64, 256:384], G.xres[:, db, TLOC - 64:TLOC], G.idf[:, :], inc="mm")
        b.wait("vector", "mm")
        b.op("vector", "tensor_copy", G.tokst[0:64, db * 128:(db + 1) * 128], pst[0:64, 256:384], inc="dv")
    b.wait("gpsimd", "dv")
    dma(b, "gpsimd", d["cc2_in"][0:64, :], G.tokst[0:64, :])
    b.fence()


def emit_p4(b, G, d):
    for db in range(8):
        dma(b, "gpsimd", G.encvb[:, db, :].rearrange("p (h c) -> p h c", h=8),
            d["encv"][:, db * 128:(db + 1) * 128, :].rearrange("h p e -> p h e"))
    for ti in range(16):
        tsl = slice(ti * 128, (ti + 1) * 128)
        b.wait("scalar", "dv")
        act(b, G.scr[:, 0:4, :].rearrange("p a c -> p (a c)").rearrange("p (a c) -> p a c", a=8)[:, :, 0:128],
            G.xres[:, :, tsl], AF.Square, inc="sc")
        b.wait("tensor", "sc")
        for db in range(8):
            mm(b, G.psCH[0:1, 0, 0:128], G.onec[:, :], G.xres[:, db, tsl],
               start=(db == 0), stop=(db == 7))
        for db in range(8):
            mm(b, G.psCH[0:1, 1, 0:128], G.onec[:, :],
               G.scr[:, 0:4, :].rearrange("p a c -> p (a c)").rearrange("p (a c) -> p a c", a=8)[:, db, 0:128],
               start=(db == 0), stop=(db == 7))
        b.wait("vector", "mm")
        ts(b, "vector", G.rowscr[0:1, 0:128], G.psCH[0:1, 0, 0:128], 1.0 / D, None, AT.mult)
        ts(b, "vector", G.rowscr[0:1, 128:256], G.psCH[0:1, 1, 0:128], 1.0 / D, None, AT.mult)
        tt(b, "vector", G.rowscr[0:1, 256:384], G.rowscr[0:1, 0:128], G.rowscr[0:1, 0:128], AT.mult)
        tt(b, "vector", G.rowscr[0:1, 128:256], G.rowscr[0:1, 128:256], G.rowscr[0:1, 256:384], AT.subtract)
        ts(b, "vector", G.rowscr[0:1, 128:256], G.rowscr[0:1, 128:256], 1e-5, None, AT.add, inc="dv")
        b.wait("scalar", "dv")
        b.op("scalar", "sqrt", G.rowscr[0:1, 128:256], G.rowscr[0:1, 128:256], inc="sc")
        b.wait("vector", "sc")
        b.op("vector", "reciprocal", G.rowscr[0:1, 128:256], G.rowscr[0:1, 128:256], inc="dv")
        b.wait("tensor", "dv"); b.wait("tensor", "sc")
        tr(b, G.psCH[:, 2, 0:1], G.rowscr[0:1, 0:128], G.idf[0:1, 0:1]) if False else None
        mm(b, G.psCH[:, 2, 0:1], G.rowscr[0:1, 0:128], G.onec[0:1, 0:1])
        mm(b, G.psCH[:, 3, 0:1], G.rowscr[0:1, 128:256], G.onec[0:1, 0:1], inc="mm")
        b.wait("vector", "mm")
        b.op("vector", "tensor_copy", G.stats2[:, ti, 0:1], G.psCH[:, 2, 0:1])
        b.op("vector", "tensor_copy", G.stats2[:, ti, 1:2], G.psCH[:, 3, 0:1], inc="dv")
    b.wait("gpsimd", "gld")
    for e in ("tensor", "vector", "scalar"):
        b.wait(e, "gld")
    for grp in range(NGRP):
        gs = grp * GRP
        for half in range(2):
            ti = gs // 128 + half
            ln_bcast(b, G, G.stats2, ti)
            hs = slice(half * 128, (half + 1) * 128)
            for db in range(8):
                tt(b, "vector", G.scr[:, 0, 0:128], G.xres[:, db, gs + half * 128:gs + (half + 1) * 128],
                   G.psA[:, 0:128], AT.subtract, inc=None)
                tt(b, "vector", G.xng[:, db, hs], G.scr[:, 0, 0:128], G.psB[:, 0:128], AT.mult, inc=None)
        b.wait("tensor", "dv"); b.wait("tensor", "sc")
        for h in range(NH):
            pst = G.psA if h % 2 == 0 else G.psB
            b.wait("sync", "mm")
            dma(b, "sync", G.encb[:, :, :],
                d["sc_encq"][h].rearrange("(db p) e -> p db e", p=128))
            b.wait("tensor", "ld")
            for db in range(8):
                mm(b, pst[:, 0:GRP], G.encb[:, db, :],
                   G.xng[:, db, :], start=(db == 0), stop=(db == 7))
            b.wait("scalar", "mm")
            act(b, G.qts[:, h, 64 + gs:64 + gs + GRP], pst[:, 0:GRP], AF.Relu, inc="dv")
            b.wait("tensor", "dv"); b.wait("tensor", "sc")
        for half in range(2):
            tc = gs // 128 + half
            pst = G.psA if half == 0 else G.psB
            b.wait("tensor", "dv"); b.wait("tensor", "sc")
            for db in range(8):
                mm(b, pst[:, 0:512], G.xng[:, db, half * 128:(half + 1) * 128],
                   G.encvb[:, db, 0:512], start=(db == 0), stop=(db == 7))
            b.wait("scalar", "mm")
            act(b, G.vts[:, tc, 0:512], pst[:, 0:512], AF.Relu, inc="dv")
            b.wait("tensor", "dv"); b.wait("tensor", "sc")
            for db in range(8):
                mm(b, pst[:, 0:512], G.xng[:, db, half * 128:(half + 1) * 128],
                   G.encvb[:, db, 512:1024], start=(db == 0), stop=(db == 7))
            b.wait("scalar", "mm")
            act(b, G.vts[:, tc, 512:1024], pst[:, 0:512], AF.Relu, inc="dv")
    b.fence()
    for h in range(NH):
        b.op("vector", "reduce_sum", G.sigc[:, h, 0:1], G.qts[:, h, 64:64 + TLOC], axis=AX)
        b.op("vector", "reduce_sum", G.sigc[:, h, 1:2], G.qts[:, h, 64:64 + TLOC:8], axis=AX)
    b.wait("gpsimd", "dv")
    dma(b, "gpsimd", d["cc2_in"][64:72, 0:128].rearrange("h p -> p h"), G.sigc[:, :, 0], slow=True)
    dma(b, "gpsimd", d["cc2_in"][72:80, 0:128].rearrange("h p -> p h"), G.sigc[:, :, 1], slow=True)
    b.op("vector", "memset", G.tokst2[0:16, 0:896], 0.0, inc="dv")
    b.wait("gpsimd", "dv")
    dma(b, "gpsimd", d["cc2_in"][64:80, 128:1024], G.tokst2[0:16, 0:896])
    b.wait("gpsimd", "gld")
    b.op("gpsimd", "collective_compute", "AllGather", AT.bypass,
         replica_groups=[list(range(8))],
         ins=[d["cc2_in"].opt()], outs=[d["cc2_out"].opt()], inc="cc")
    b.wait("gpsimd", "cc")
    for e in ("tensor", "vector", "scalar", "sync"):
        b.wait(e, "cc")


def emit_p5(b, G, d):
    b.op("vector", "memset", G.tokst[:, :], 0.0, inc="dv")
    for j in range(8):
        b.wait("sync", "dv")
        dma(b, "sync", G.tokst2[0:64, :], d["cc2_out"][j * 80:j * 80 + 64, :])
        b.wait("vector", "ld")
        ts(b, "vector", G.tokst2[0:64, :], G.tokst2[0:64, :], G.flgb[0:64, 24 + j:25 + j], None, AT.mult)
        tt(b, "vector", G.tokst[0:64, :], G.tokst[0:64, :], G.tokst2[0:64, :], AT.add, inc="dv")
    b.op("vector", "reduce_sum", G.colsc[0:64, 0:1], G.tokst[0:64, :], axis=AX, inc="dv")
    b.wait("scalar", "dv")
    act(b, G.scr[0:64, 0:4, :].rearrange("p a c -> p (a c)"), G.tokst[0:64, :], AF.Square, inc="sc")
    b.wait("vector", "sc")
    b.op("vector", "reduce_sum", G.colsc[0:64, 1:2], G.scr[0:64, 0:4, :].rearrange("p a c -> p (a c)"), axis=AX)
    ts(b, "vector", G.colsc[0:64, 2:3], G.colsc[0:64, 0:1], 1.0 / D, None, AT.mult)
    ts(b, "vector", G.colsc[0:64, 3:4], G.colsc[0:64, 1:2], 1.0 / D, None, AT.mult)
    tt(b, "vector", G.colsc[0:64, 4:5], G.colsc[0:64, 2:3], G.colsc[0:64, 2:3], AT.mult)
    tt(b, "vector", G.colsc[0:64, 5:6], G.colsc[0:64, 3:4], G.colsc[0:64, 4:5], AT.subtract)
    ts(b, "vector", G.colsc[0:64, 5:6], G.colsc[0:64, 5:6], 1e-5, None, AT.add, inc="dv")
    b.wait("scalar", "dv")
    b.op("scalar", "sqrt", G.colsc[0:64, 6:7], G.colsc[0:64, 5:6], inc="sc")
    b.wait("vector", "sc")
    b.op("vector", "reciprocal", G.colsc[0:64, 7:8], G.colsc[0:64, 6:7])
    ts(b, "vector", G.tokst[0:64, :], G.tokst[0:64, :], G.colsc[0:64, 2:3], G.colsc[0:64, 7:8],
       AT.subtract, AT.mult, inc="dv")
    b.wait("tensor", "dv"); b.wait("tensor", "sc")
    for db in range(8):
        pst = G.psA if db % 2 == 0 else G.psB
        b.wait("tensor", "dv"); b.wait("tensor", "sc")
        tr(b, pst[:, 0:64], G.tokst[0:64, db * 128:(db + 1) * 128], G.idf[0:64, 0:64], inc="mm")
        b.wait("vector", "mm")
        b.op("vector", "tensor_copy", G.xhalT[:, db, :], pst[:, 0:64], inc="dv")
    b.wait("tensor", "dv"); b.wait("tensor", "sc")
    for h in range(NH):
        pst = G.psA if h % 2 == 0 else G.psB
        b.wait("sync", "mm")
        dma(b, "sync", G.encb[:, :, :],
            d["sc_encq"][h].rearrange("(db p) e -> p db e", p=128))
        b.wait("tensor", "ld")
        for db in range(8):
            mm(b, pst[:, 64:128], G.encb[:, db, :],
               G.xhalT[:, db, :], start=(db == 0), stop=(db == 7))
        b.wait("scalar", "mm")
        act(b, G.qts[:, h, 0:64], pst[:, 64:128], AF.Relu, inc="dv")
        b.wait("tensor", "dv"); b.wait("tensor", "sc")
    for eh in range(2):
        pst = G.psA if eh == 0 else G.psB
        b.wait("tensor", "dv"); b.wait("tensor", "sc")
        for db in range(8):
            mm(b, pst[64:128, 0:512], G.xhalT[:, db, :], G.encvb[:, db, eh * 512:(eh + 1) * 512],
               start=(db == 0), stop=(db == 7), tp=(0, 64))
        b.wait("scalar", "mm")
        act(b, G.vhalo[64:128, eh * 512:(eh + 1) * 512], pst[64:128, 0:512], AF.Relu, inc="dv")
    b.op("vector", "memset", G.sigc[:, :, 2:4], 0.0, inc="dv")
    for j in range(8):
        b.wait("sync", "dv")
        dma(b, "sync", G.mlpc[:, :, 0], d["cc2_out"][j * 80 + 64:j * 80 + 72, 0:128].rearrange(
            "h p -> p h"), slow=True)
        dma(b, "sync", G.mlpc[:, :, 1], d["cc2_out"][j * 80 + 72:j * 80 + 80, 0:128].rearrange(
            "h p -> p h"), slow=True)
        b.wait("vector", "ld")
        ts(b, "vector", G.mlpc[:, :, 0:2], G.mlpc[:, :, 0:2], G.flgb[:, 8 + j:9 + j], None, AT.mult)
        tt(b, "vector", G.sigc[:, :, 2:4], G.sigc[:, :, 2:4], G.mlpc[:, :, 0:2], AT.add, inc="dv")
    ts(b, "vector", G.sigc[:, :, 2:3], G.sigc[:, :, 2:3], 1.0 / 8192.0, None, AT.mult)
    ts(b, "vector", G.sigc[:, :, 3:4], G.sigc[:, :, 3:4], 0.5 / 1024.0, None, AT.mult)
    tt(b, "vector", G.sigc[:, :, 2:3], G.sigc[:, :, 2:3], G.sigc[:, :, 3:4], AT.add)
    b.op("vector", "tensor_copy", G.scrb[:, 0, 0:8], G.sigc[:, :, 2], inc="dv")
    sigb = G.scrb[:, 0, 0:8].rearrange("p (a c) -> p a c", a=8)
    b.wait("gpsimd", "dv")
    dma(b, "gpsimd", G.encvb_f[:, 0:4096].rearrange("p (a c) -> p a c", a=8),
        d["mw1"].rearrange("(db p) e -> p db e", p=128))
    b.wait("sync", "dv")
    dma(b, "sync", G.mlpc[:, 0:4, 0:1], d["mb1"].unsqueeze(2), slow=True)
    dma(b, "sync", G.mlpc[:, :, 2:3], d["mb2"].unsqueeze(2), slow=True)
    dma(b, "sync", G.bmc[:, :, 0:1], d["basem"].unsqueeze(2), slow=True)
    b.wait("tensor", "gld")
    b.wait("tensor", "ld")
    for e4 in range(4):
        for db in range(8):
            mm(b, G.psCH[:, 4, e4:e4 + 1], G.encvb[:, db, e4 * 128:(e4 + 1) * 128],
               sigb[:, db, 0:1], start=(db == 0), stop=(db == 7))
    b.wait("vector", "mm")
    b.wait("vector", "ld")
    ts(b, "vector", G.mlpc[:, 0:4, 1:2].flatten().unsqueeze(1) if False else G.scr[:, 0, 0:4],
       G.psCH[:, 4, 0:4], 1.0, None, AT.mult)
    tt(b, "vector", G.scr[:, 0, 0:4], G.scr[:, 0, 0:4], G.mlpc[:, 0:4, 0], AT.add, inc="dv")
    b.wait("scalar", "dv")
    act(b, G.scr[:, 1, 0:4], G.scr[:, 0, 0:4], AF.Gelu, inc="sc")
    b.wait("vector", "sc")
    b.op("vector", "tensor_copy", G.scrb[:, 0, 16:20], G.scr[:, 1, 0:4], inc="dv")
    h1b = G.scrb[:, 0, 16:20].rearrange("p (a c) -> p a c", a=4)
    b.wait("gpsimd", "dv")
    dma(b, "gpsimd", G.encvb_f[:, 0:4096].rearrange("p (a c) -> p a c", a=4),
        d["mw2"].rearrange("(db p) e -> p db e", p=128))
    b.wait("tensor", "gld")
    for eb in range(8):
        for db in range(4):
            mm(b, G.psCH[:, 5, eb:eb + 1], G.encvb[:, db, eb * 128:(eb + 1) * 128],
               h1b[:, db, 0:1], start=(db == 0), stop=(db == 3))
    b.wait("vector", "mm")
    tt(b, "vector", G.scr[:, 0, 0:8], G.psCH[:, 5, 0:8], G.mlpc[:, :, 2], AT.add)
    ts(b, "vector", G.scr[:, 0, 0:8], G.scr[:, 0, 0:8], 0.1, None, AT.mult)
    tt(b, "vector", G.scr[:, 0, 0:8], G.scr[:, 0, 0:8], G.bmc[:, :, 0], AT.add, inc="dv")
    b.wait("scalar", "dv")
    act(b, G.scr[:, 1, 0:8], G.scr[:, 0, 0:8], AF.Sigmoid, scale=-1.0, inc="sc")
    b.wait("vector", "sc")
    b.op("vector", "tensor_scalar_max", G.scr[:, 1, 0:8], G.scr[:, 1, 0:8], 1e-38, inc="dv")
    b.wait("scalar", "dv")
    act(b, G.scr[:, 2, 0:8], G.scr[:, 1, 0:8], AF.Ln, inc="sc")
    b.wait("vector", "sc")
    ts(b, "vector", G.metr[:, :, 0:1], G.scr[:, 2, 0:8].unsqueeze(2) if False else
       G.scr[:, 2, 0:8].rearrange("p (a c) -> p a c", a=8), -1.0, None, AT.mult, inc="dv")
    for h in range(NH):
        b.wait("gpsimd", "dv")
        b.wait("gpsimd", "mm")
        dma(b, "gpsimd", G.scrb[0:32, 1, 0:128], d["hut"][h])
        dma(b, "gpsimd", G.scrb[0:32, 1, 128:256], d["hvv"][h])
        b.wait("tensor", "gld")
        mm(b, G.psCH[:, 6, 0:128], G.scrb[0:32, 1, 0:128], G.scrb[0:32, 1, 128:256], inc="mm")
        b.wait("vector", "mm")
        b.op("vector", "tensor_copy", G.uvs[:, h * 128:(h + 1) * 128], G.psCH[:, 6, 0:128], inc="dv")
    b.fence()


def emit_p6a(b, G, d):
    for h in range(NH):
        for c in range(17):
            if c == 16:
                qsl = G.qts[:, h, 0:64]
                rows = 64
            else:
                qsl = G.qts[:, h, 64 + c * 128:64 + (c + 1) * 128]
                rows = 128
            b.wait("vector", "mm")
            ts(b, "vector", G.qmb[:, 0:rows], qsl, G.metr[:, h, 0:1], None, AT.mult)
            tt(b, "vector", G.qsq[:, 0:rows], G.qmb[:, 0:rows], qsl, AT.mult)
            tt(b, "vector", G.scr[:, 0, 0:rows], qsl, qsl, AT.mult, inc="dv")
            b.wait("tensor", "dv"); b.wait("tensor", "sc")
            tgt = G.acols[0:rows, h, 16:17] if c == 16 else G.acols[:, h, c:c + 1]
            tg2 = G.sqnc[0:rows, h, 16:17] if c == 16 else G.sqnc[:, h, c:c + 1]
            mm(b, G.psCH[0:rows, 7, 0:1], G.qsq[:, 0:rows], G.onec[:, 0:1])
            mm(b, G.psCH[0:rows, 7, 1:2], G.scr[:, 0, 0:rows], G.onec[:, 0:1], inc="mm")
            b.wait("vector", "mm")
            b.op("vector", "tensor_copy", tgt, G.psCH[0:rows, 7, 0:1])
            b.op("vector", "tensor_copy", tg2, G.psCH[0:rows, 7, 1:2], inc="dv")

    for c in range(16):
        for h in range(NH):
            qsl = G.qts[:, h, 64 + c * 128:64 + (c + 1) * 128]
            b.wait("tensor", "dv"); b.wait("tensor", "sc")
            if c == 0:
                mm(b, G.psCH[0:1, 6, 0:64], G.acols[0:64, h, 16:17], G.idf[0:64, 0:64])
            else:
                mm(b, G.psCH[0:1, 6, 0:64], G.acols[64:128, h, c - 1:c],
                   G.idf[64:128, 64:128], tp=(64, 0))
            mm(b, G.psCH[0:1, 7, 0:128], G.acols[:, h, c:c + 1], G.idf[:, :], inc="mm")
            b.wait("vector", "mm")
            b.op("vector", "tensor_copy", G.rowscr[0:1, 256:320], G.psCH[0:1, 6, 0:64])
            b.op("vector", "tensor_copy", G.rowscr[0:1, 320:448], G.psCH[0:1, 7, 0:128], inc="dv")
            b.wait("tensor", "dv"); b.wait("tensor", "sc")
            mm(b, G.psB[:, 0:192], G.oner[:, :], G.rowscr[0:1, 256:448])
            ts(b, "vector", G.qmb[:, :], qsl, G.metr[:, h, 0:1], None, AT.mult, inc="dv")
            b.wait("tensor", "dv"); b.wait("tensor", "sc")
            mm(b, G.psA[:, 0:192], G.qmb[:, :], G.qts[:, h, c * 128:c * 128 + 192], inc="mm")
            b.wait("vector", "mm")
            ts(b, "vector", G.colsc[:, 16:17], G.acols[:, h, c:c + 1], 1e-8, None, AT.add)
            ts(b, "vector", G.dtl[:, :], G.psA[:, 0:192], -2.0, G.colsc[:, 16:17], AT.mult, AT.add)
            tt(b, "vector", G.dtl[:, :], G.dtl[:, :], G.psB[:, 0:192], AT.add)
            b.op("vector", "tensor_scalar_max", G.dtl[:, :], G.dtl[:, :], 1e-8, inc="dv")
            b.wait("scalar", "dv")
            b.op("scalar", "sqrt", G.dtl[:, :], G.dtl[:, :], inc="sc")
            b.wait("vector", "sc")
            if c == 0:
                tt(b, "vector", G.colsc[:, 17:18], G.dtl[:, 64:65], G.flgb[:, 7:8], AT.mult)
                ts(b, "vector", G.dtl[:, 0:64], G.dtl[:, 0:64], G.flgb[:, 23:24], None, AT.mult)
                tt(b, "vector", G.dtl[:, 0:64], G.dtl[:, 0:64],
                   G.colsc[:, 17:18].broadcast_to([128, 64]), AT.add)
            stt(b, "vector", G.neg1[:, :], G.dtl[:, :], -1.0, G.bndb[:, :], AT.mult, AT.subtract, inc="dv")
            b.wait("scalar", "dv")
            act(b, G.expd[:, :], G.neg1[:, :], AF.Exp, inc="sc")
            b.wait("vector", "sc")
            cur, nxt = G.neg1, G.neg2
            for it in range(4):
                b.op("vector", "max", G.m8[:, :], cur[:, :])
                b.op("vector", "match_replace", nxt[:, :], G.m8[:, :], cur[:, :], MARK)
                cur, nxt = nxt, cur
            ts(b, "vector", G.wse[:, :], cur[:, :], -0.5e20, None, AT.is_le)
            tt(b, "vector", G.wse[:, :], G.wse[:, :], G.expd[:, :], AT.mult, inc="dv")
            if c == 0:
                b.op("vector", "reduce_sum", G.rsum[:, 0:1], G.wse[:, 0:64], axis=AX)
                tt(b, "vector", G.rsum[:, 1:2], G.rsum[:, 0:1], G.flgb[:, 7:8], AT.mult)
                ts(b, "vector", G.wse[:, 0:64], G.wse[:, 0:64], G.flgb[:, 23:24], None, AT.mult)
                tt(b, "vector", G.wse[:, 64:65], G.wse[:, 64:65], G.rsum[:, 1:2], AT.add, inc="dv")
            b.op("vector", "reduce_sum", G.rsum[:, 2:3], G.wse[:, :], axis=AX)
            ts(b, "vector", G.rsum[:, 2:3], G.rsum[:, 2:3], 1e-8, None, AT.add)
            b.op("vector", "reciprocal", G.rsum[:, 3:4], G.rsum[:, 2:3])
            ts(b, "vector", G.wse[:, :], G.wse[:, :], G.rsum[:, 3:4], None, AT.mult)
            b.op("vector", "tensor_copy", G.wsb[:, :], G.wse[:, :], inc="dv")
            b.wait("tensor", "dv"); b.wait("tensor", "sc")
            mm(b, G.psCH[64:128, 0, 0:128], G.wsb[:, 0:64], G.idb[:, :], tp=(0, 64))
            mm(b, G.psCH[0:64, 1, 0:128], G.wsb[:, 64:128], G.idb[:, :])
            mm(b, G.psCH[64:128, 2, 0:128], G.wsb[:, 128:192], G.idb[:, :],
               tp=(0, 64), inc="mm")
            b.wait("vector", "mm")
            b.op("vector", "tensor_copy", G.wtb[64:128, 0:128], G.psCH[64:128, 0, 0:128])
            b.op("vector", "tensor_copy", G.wtb[0:64, 128:256], G.psCH[0:64, 1, 0:128])
            b.op("vector", "tensor_copy", G.wtb[64:128, 256:384],
                 G.psCH[64:128, 2, 0:128], inc="dv")
            b.wait("tensor", "dv"); b.wait("tensor", "sc")
            vprev = G.vhalo[64:128, h * 128:(h + 1) * 128] if c == 0 else                 G.vts[64:128, c - 1, h * 128:(h + 1) * 128]
            mm(b, G.psCH[:, 3, 0:128], G.wtb[64:128, 0:128], vprev, tp=(64, 0), stop=False)
            mm(b, G.psCH[:, 3, 0:128], G.wtb[0:64, 128:256],
               G.vts[0:64, c, h * 128:(h + 1) * 128], start=False, stop=False)
            mm(b, G.psCH[:, 3, 0:128], G.wtb[64:128, 256:384],
               G.vts[64:128, c, h * 128:(h + 1) * 128], tp=(64, 0), start=False)
            mm(b, G.psCH[:, 4, 0:128], qsl, G.uvs[:, h * 128:(h + 1) * 128])
            mm(b, G.psCH[:, 5, 0:128], qsl, G.idb[:, :], inc="mm")
            b.wait("vector", "mm")
            ts(b, "vector", G.colsc[:, 18:19], G.sqnc[:, h, c:c + 1], 0.04, None, AT.is_gt, inc="dv")
            b.wait("scalar", "dv")
            b.op("scalar", "sqrt", G.colsc[:, 19:20], G.sqnc[:, h, c:c + 1], inc="sc")
            b.wait("vector", "sc")
            b.op("vector", "tensor_scalar_max", G.colsc[:, 19:20], G.colsc[:, 19:20], 1e-12)
            b.op("vector", "reciprocal", G.colsc[:, 20:21], G.colsc[:, 19:20])
            tt(b, "vector", G.colsc[:, 21:22], G.colsc[:, 18:19], G.colsc[:, 20:21], AT.mult)
            ts(b, "vector", G.colsc[:, 21:22], G.colsc[:, 21:22], 0.1, None, AT.mult)
            b.op("vector", "tensor_copy", G.qsq[:, :], G.psCH[:, 4, 0:128])
            stt(b, "vector", G.ctmp[:, :], G.qsq[:, :], G.colsc[:, 21:22],
                G.psCH[:, 3, 0:128], AT.mult, AT.add)
            tt(b, "vector", G.ostok[:, :], G.ctmp[:, :], G.psCH[:, 5, 0:128],
               AT.mult, inc="dv")
            b.wait("tensor", "dv"); b.wait("tensor", "sc")
            mm(b, G.psCH[:, 6, 0:128], G.ostok[:, :], G.idb[:, :], inc="mm")
            b.wait("vector", "mm")
            b.wait("vector", "gld")
            b.op("vector", "tensor_copy", G.ospst[:, h, :], G.psCH[:, 6, 0:128], inc="dv")
        b.wait("gpsimd", "dv")
        dma(b, "gpsimd", d["sc_osp"].rearrange("h p t -> p h t")[:, :, c * 128:(c + 1) * 128],
            G.ospst[:, :, :])
    b.fence()


def emit_p6b(b, G, d):
    MAGIC = 12582912.0
    b.op("vector", "memset", G.colsc[:, 22:23], 0.25, inc="dv")
    dma(b, "sync", G.b1c[:, :, 0:1], d["f1b"].unsqueeze(2), slow=True)
    dma(b, "sync", G.b2c[:, :, 0:1], d["f2b"].unsqueeze(2), slow=True)
    lastw1 = {0: 0, 1: 0}
    lastw2dec = 0
    for grp in range(NGRP):
        gs = grp * GRP
        for half in range(2):
            ti = gs // 128 + half
            ln_bcast(b, G, G.stats2, ti)
            hs = slice(half * 128, (half + 1) * 128)
            for db in range(8):
                tt(b, "vector", G.scr[:, 0, 0:128], G.xres[:, db, gs + half * 128:gs + (half + 1) * 128],
                   G.psA[:, 0:128], AT.subtract, inc=None)
                tt(b, "vector", G.xng[:, db, hs], G.scr[:, 0, 0:128], G.psB[:, 0:128], AT.mult, inc=None)
        w1hist = {}
        b.wait("tensor", "dv"); b.wait("tensor", "sc")
        for it in range(FF1 // 256):
            for half in range(2):
                fb = 2 * it + half
                wt = G.wtile if half == 0 else G.wtileB
                pst = G.psA if half == 0 else G.psB
                lag = w1hist[(it - 1, half)] if it >= 1 else lastw1[half]
                if lag:
                    b.wait("sync", "mm", val=lag)
                dma(b, "sync", wt[:, :, :],
                    d["sc_wf1"][:, fb * 128:(fb + 1) * 128].rearrange("(db p) e -> p db e", p=128))
                b.wait("tensor", "ld")
                for db in range(8):
                    mm(b, pst[:, 0:GRP], wt[:, db, :], G.xng[:, db, :],
                       start=(db == 0), stop=(db == 7))
                w1hist[(it, half)] = b.semv["mm"]
            b.wait("vector", "mm")
            ts(b, "vector", G.pwl[:, 0, 0:GRP], G.psA[:, 0:GRP], G.b1c[:, 2 * it, 0:1],
               256.0, AT.add, AT.mult)
            ts(b, "vector", G.pwl[:, 0, GRP:2 * GRP], G.psB[:, 0:GRP], G.b1c[:, 2 * it + 1, 0:1],
               256.0, AT.add, AT.mult, inc="dv")
            b.wait("tensor", "dv")
            b.op("vector", "tensor_copy", G.hq32[:, :], G.pwl[:, 0, :])
            b.op("vector", "tensor_copy", G.pwl[:, 1, :], G.hq32[:, :], inc="dv")
            hq = G.pwl[:, 1, :]
            ts(b, "vector", G.pwl[:, 2, :], hq, 1.0 / 64.0, 80.5078125, AT.mult, AT.add)
            b.op("vector", "tensor_copy", G.hq32[:, :], G.pwl[:, 2, :])
            b.op("vector", "tensor_copy", G.pwl[:, 2, :], G.hq32[:, :])
            ts(b, "vector", G.pwl[:, 2, :], G.pwl[:, 2, :], 81.0, 0.25, AT.subtract, AT.mult)
            ts(b, "vector", G.pwl[:, 2, :], G.pwl[:, 2, :], -4.0, 4.0, AT.max, AT.min, inc="dv")
            XL = G.pwl[:, 2, :]
            b.wait("scalar", "dv")
            act(b, G.pwl[:, 3, :], XL, AF.Gelu)
            act(b, G.pwl[:, 4, :], XL, AF.Gelu, bias=G.colsc[:, 22:23], inc="sc")
            b.wait("vector", "sc")
            tt(b, "vector", G.pwl[:, 5, :], G.pwl[:, 4, :], G.pwl[:, 3, :], AT.subtract)
            stt(b, "vector", G.pwl[:, 6, :], hq, 1.0 / 256.0, XL, AT.mult, AT.subtract)
            stt(b, "vector", G.pwl[:, 6, :], G.pwl[:, 6, :], 4.0, G.pwl[:, 5, :], AT.mult, AT.mult)
            tt(b, "vector", G.pwl[:, 6, :], G.pwl[:, 6, :], G.pwl[:, 3, :], AT.add)
            ts(b, "vector", G.pwl[:, 7, :], hq, 1024.0, None, AT.is_gt)
            ts(b, "vector", G.pwl[:, 5, :], hq, 1.0 / 256.0, None, AT.mult)
            tt(b, "vector", G.pwl[:, 5, :], G.pwl[:, 5, :], G.pwl[:, 6, :], AT.subtract)
            tt(b, "vector", G.pwl[:, 5, :], G.pwl[:, 5, :], G.pwl[:, 7, :], AT.mult)
            tt(b, "vector", G.pwl[:, 6, :], G.pwl[:, 6, :], G.pwl[:, 5, :], AT.add)
            ts(b, "vector", G.pwl[:, 7, :], hq, -1024.0, None, AT.is_ge)
            tt(b, "vector", G.pwl[:, 6, :], G.pwl[:, 6, :], G.pwl[:, 7, :], AT.mult)
            ts(b, "vector", G.pwl[:, 0, :], G.pwl[:, 6, :], 256.0, None, AT.mult)
            b.op("vector", "tensor_copy", G.hq32[:, :], G.pwl[:, 0, :])
            b.op("vector", "tensor_copy", G.pwl[:, 0, :], G.hq32[:, :])
            ts(b, "vector", G.hstr[:, 2 * it:2 * it + 2, :].rearrange("p a c -> p (a c)"),
               G.pwl[:, 0, :], 1.0 / 256.0, None, AT.mult, inc="dv")
        def dpsum(eb):
            return DPSUM(G, eb)
        w2hist = {}
        b.wait("tensor", "dv"); b.wait("tensor", "sc")
        for fb in range(FF1 // 128):
            wt2 = G.wtile2 if fb % 2 == 0 else G.wtile2B
            lag = w2hist[fb - 2] if fb >= 2 else lastw2dec
            if lag:
                b.wait("sync", "mm", val=lag)
            dma(b, "sync", wt2[:, :], d["sc_wf2"][fb * 128:(fb + 1) * 128, :])
            b.wait("tensor", "ld")
            for eb in range(8):
                mm(b, dpsum(eb), wt2[:, eb * 128:(eb + 1) * 128],
                   G.hstr[:, fb, :], start=(fb == 0), stop=(fb == FF1 // 128 - 1))
            w2hist[fb] = b.semv["mm"]
        b.wait("vector", "mm")
        for eb in range(8):
            ts(b, "vector", G.pwl[:, 0, 0:GRP], dpsum(eb), G.b2c[:, eb, 0:1], 256.0, AT.add, AT.mult)
            b.op("vector", "tensor_copy", G.hq32[:, 0:GRP], G.pwl[:, 0, 0:GRP])
            b.op("vector", "tensor_copy", G.pwl[:, 0, 0:GRP], G.hq32[:, 0:GRP])
            stt(b, "vector", G.xres[:, eb, gs:gs + GRP], G.pwl[:, 0, 0:GRP], 1.0 / 256.0,
                G.xres[:, eb, gs:gs + GRP], AT.mult, AT.add)
        b.wait("tensor", "dv"); b.wait("tensor", "sc")
        dchist = {}
        for hb in range(8):
            wt2 = G.wtile2 if hb % 2 == 0 else G.wtile2B
            b.wait("sync", "mm", val=dchist[hb - 2] if hb >= 2 else w2hist[30 + hb])
            dma(b, "sync", wt2[:, :], d["sc_dec"][hb * 128:(hb + 1) * 128, :])
            dma(b, "sync", G.ospl[:, hb, :], d["sc_osp"][hb, :, gs:gs + GRP])
            b.wait("tensor", "ld")
            for eb in range(8):
                mm(b, dpsum(eb), wt2[:, eb * 128:(eb + 1) * 128],
                   G.ospl[:, hb, :], start=(hb == 0), stop=(hb == 7))
            dchist[hb] = b.semv["mm"]
        lastw1 = {0: w1hist[(FF1 // 256 - 1, 0)], 1: w1hist[(FF1 // 256 - 1, 1)]}
        lastw2dec = dchist[7]
        b.wait("vector", "mm")
        for eb in range(8):
            tt(b, "vector", G.xres[:, eb, gs:gs + GRP], G.xres[:, eb, gs:gs + GRP],
               dpsum(eb), AT.add)
        for half in range(2):
            ti = gs // 128 + half
            b.wait("tensor", "dv"); b.wait("tensor", "sc")
            b.wait("vector", "st")
            for db in range(8):
                pst = G.psA if db % 2 == 0 else G.psB
                if db >= 2:
                    b.wait("tensor", "dv"); b.wait("tensor", "sc")
                tr(b, pst[:, 256:384], G.xres[:, db, ti * 128:(ti + 1) * 128], G.idf[:, :], inc="mm")
                b.wait("vector", "mm")
                b.op("vector", "tensor_copy", G.tokst[:, db * 128:(db + 1) * 128], pst[:, 256:384], inc="dv")
            b.wait("sync", "dv")
            dma(b, "sync", d["out"][ti * 128:(ti + 1) * 128, :], G.tokst[:, :], sem="st")
    b.wait("sync", "st")
    b.wait("gpsimd", "st")


def kernel(**inputs):
    ref_dt = np.float32
    x = np.ascontiguousarray(inputs["L0_attn_x"], dtype=np.float32)
    B, T, _ = x.shape
    nc = build_program()

    wT = {p: np.ascontiguousarray(inputs[f"L0_attn_W_{p}"].T) for p in "rwkvabg"}
    wot = np.ascontiguousarray(inputs["L0_attn_W_o"].T)
    mixv8 = np.stack([inputs[f"L0_attn_x_{p}"] for p in "rwkvabg"] + [np.zeros(D, np.float32)])
    mixv = np.ascontiguousarray(mixv8.reshape(8, 8, 128).transpose(2, 1, 0)).astype(np.float32)
    def dimmaj(vec):
        n = vec.shape[0] // 128
        return np.ascontiguousarray(np.asarray(vec, np.float32).reshape(n, 128).T)
    wg_i = np.ascontiguousarray(inputs["L0_ffn_w_gate"].T)
    wu_i = np.ascontiguousarray(inputs["L0_ffn_w_up"].T)
    wd_i = np.ascontiguousarray(inputs["L0_ffn_w_down"].T)
    hut = np.ascontiguousarray(np.transpose(inputs["L1_attn_hebb_U"], (0, 2, 1)))

    ident = np.eye(128, dtype=np.float32)
    up_s = np.triu(np.ones((64, 64), np.float32), 1)
    up_i = np.triu(np.ones((64, 64), np.float32), 0)
    m4 = np.concatenate([up_s, up_i, up_s, up_i], axis=1)
    mask4 = np.concatenate([m4, m4], axis=0)
    lo_s = np.tril(np.ones((64, 64), np.float32), -1)
    masklo = np.concatenate([lo_s, lo_s], axis=0)
    idd2 = np.concatenate([np.eye(64, dtype=np.float32)] * 2, axis=0)
    ttt = np.arange(128)[:, None]
    jjj = np.arange(192)[None, :]
    bm = ((jjj >= ttt + 1) & (jjj <= ttt + 64)).astype(np.float32)
    bandb = (1.0 - bm) * 1.0e30
    ones2 = np.zeros((128, 2), np.float32)
    ones2[0:64, 0] = 1.0
    ones2[64:128, 1] = 1.0
    sel2 = np.zeros((2, 128), np.float32)
    sel2[0, 0:64] = 1.0
    sel2[1, 64:128] = 1.0

    in_maps = []
    for c in range(8):
        bb, gp = c // 4, c % 4
        t0 = gp * TLOC
        xsl = np.ascontiguousarray(x[bb, t0:t0 + TLOC])
        if gp == 0:
            xp = np.zeros((1, D), np.float32)
        else:
            xp = np.ascontiguousarray(x[bb, t0 - 1:t0])
        fl = np.zeros((1, 32), np.float32)
        for j in range(7):
            if (j // 4 == bb) and (j < c):
                fl[0, j] = 1.0
                fl[0, 16 + j] = 0.0
            else:
                fl[0, j] = 0.0
                fl[0, 16 + j] = 1.0
        for j in range(8):
            fl[0, 24 + j] = 1.0 if (gp > 0 and j == c - 1) else 0.0
        fl[0, 7] = 1.0 if gp == 0 else 0.0
        fl[0, 23] = 0.0 if gp == 0 else 1.0
        for j in range(8):
            fl[0, 8 + j] = 1.0 if (j // 4 == bb) else 0.0
        im = dict(
            xs=xsl, xprev=xp, flags=fl,
            mixv=mixv, gnwr=inputs["L0_attn_gn_w"][None], gnbr=inputs["L0_attn_gn_b"][None],
            bor=dimmaj(inputs["L0_attn_b_o"]),
            wot=wot, wg_i=wg_i, wu_i=wu_i, wd_i=wd_i,
            encq=np.ascontiguousarray(inputs["L1_attn_enc_q"]),
            encv=np.ascontiguousarray(inputs["L1_attn_enc_v"]),
            mw1=np.ascontiguousarray(inputs["L1_attn_metric_w1"]),
            mb1=dimmaj(inputs["L1_attn_metric_b1"]),
            mw2=np.ascontiguousarray(inputs["L1_attn_metric_w2"]),
            mb2=dimmaj(inputs["L1_attn_metric_b2"]),
            basem=dimmaj(inputs["L1_attn_base_metric"].reshape(-1)),
            hut=hut, hvv=np.ascontiguousarray(inputs["L1_attn_hebb_V"]),
            decw=np.ascontiguousarray(inputs["L1_attn_decoder"]),
            f1w=np.ascontiguousarray(inputs["L1_ffn_w1"]), f1b=dimmaj(inputs["L1_ffn_b1"]),
            f2w=np.ascontiguousarray(inputs["L1_ffn_w2"]), f2b=dimmaj(inputs["L1_ffn_b2"]),
            identf=ident, identb=ident.astype(np.float32),
            mask4=mask4, masklo=masklo, idd2=idd2, bandm=bm, bandb=bandb,
            ones2=ones2, sel2=sel2, onesr=np.ones((1, 128), np.float32),
            onescol=np.ones((128, 1), np.float32),
        )
        for p in "rwkvabg":
            im[f"w{p}t"] = wT[p]
        in_maps.append(im)

    import ml_dtypes
    in_maps2 = []
    for im in in_maps:
        im2 = dict(im)
        im2["identb"] = ident.astype(ml_dtypes.bfloat16)
        in_maps2.append(im2)

    res = run_bass_kernel_spmd(nc, in_maps2, core_ids=list(range(8)))
    global LAST_RES
    LAST_RES = res
    outs = np.zeros((B, T, D), dtype=ref_dt)
    for c in range(8):
        bb, gp = c // 4, c % 4
        outs[bb, gp * TLOC:(gp + 1) * TLOC] = res.results[c]["out"]
    return outs


# version 05 (final)
# speedup vs baseline: 1.2196x; 1.0057x over previous
import numpy as np
from contextlib import ExitStack
import concourse.bass as bass
import concourse.mybir as mybir
from concourse.bass_utils import run_bass_kernel_spmd

F32 = mybir.dt.float32
BF16 = mybir.dt.bfloat16
I32 = mybir.dt.int32
AT = mybir.AluOpType
AF = mybir.ActivationFunctionType
AX = mybir.AxisListType.X

TLOC, D, C, H, K = 2048, 1024, 64, 16, 64
NCH = TLOC // C
NH, HD = 8, 128
GRP = 256
NGRP = TLOC // GRP
CPG = GRP // C
FF0, FF1 = 3072, 4096
MARK = -1.0e20
DEBUG = False
LAST_RES = None
GN_EPS = 0.00064
ENGS = ("sync", "tensor", "vector", "scalar", "gpsimd")


class Builder:

    def __init__(self, nc, es, semnames):
        self.nc = nc
        self.q = {e: [] for e in ENGS}
        self.semh = {n: es.enter_context(nc.semaphore(n)) for n in semnames}
        self.semv = {n: 0 for n in semnames}
        self.accessed = {}

    def _rec(self, eng, fn):
        self.q[eng].append(fn)

    @staticmethod
    def _fp(x):
        t = getattr(x, "tensor", None)
        if t is None:
            return None
        span = 1
        try:
            for st, n in x.ap:
                span += abs(int(st)) * (int(n) - 1)
        except Exception:
            span = 1 << 30
        off = int(x.offset)
        return (t.name, off, off + span)

    @staticmethod
    def _ov(a, bb):
        return a[0] == bb[0] and a[1] < bb[2] and bb[1] < a[2]

    def op(self, eng, meth, *a, inc=None, incn=1, **k):
        semh = self.semh
        import traceback as _tb
        site = _tb.extract_stack(limit=6)

        ENGSEM = {"vector": "dv", "scalar": "sc", "gpsimd": "gp", "tensor": "mm"}
        if (inc is None and eng in ENGSEM and meth not in
                ("sem_inc", "drain", "dma_start", "wait_ge", "collective_compute")):
            inc = ENGSEM[eng]
        if meth == "dma_start" and inc is not None and getattr(self, "_pubd", None):
            v = self._pubd.pop(inc, None)
            if v is not None:
                def swt(e, inc=inc, v=v):
                    e.wait_ge(semh[inc], v)
                self._rec(eng, swt)
        if getattr(self, "_pub", None) and self._pub.get(eng) and meth not in (
                "sem_inc", "drain", "wait_ge"):
            def drp(e):
                e.drain()
            self._rec(eng, drp)
            acc0 = self.accessed.setdefault(eng, {"w": [], "r": []})
            acc0["w"] = []
            acc0["r"] = []
            self._pub[eng] = False
        if eng in ("vector", "scalar", "gpsimd", "tensor") and meth not in (
                "sem_inc", "drain", "dma_start"):
            fps = [f for f in (self._fp(x) for x in list(a) + list(k.values()))
                   if f is not None]
            wr = [f for f in ([self._fp(a[0])] if a and self._fp(a[0]) else [])]
            acc = self.accessed.setdefault(eng, {"w": [], "r": []})
            need = any(self._ov(f, w) for f in fps for w in acc["w"]) or                    any(self._ov(w, r) for w in wr for r in acc["r"])
            if need:
                def dr(e):
                    e.drain()
                self._rec(eng, dr)
                acc["w"] = []
                acc["r"] = []
            acc["w"].extend(wr)
            acc["r"].extend(fps[1:] if len(fps) > 1 else [])

        if inc in ("dv", "sc", "gp", "mm") and meth not in (
                "sem_inc", "drain", "wait_ge"):
            if not hasattr(self, "_lastinc"):
                self._lastinc = {}
            if self._lastinc.get(inc, eng) != eng:
                self.wait(eng, inc)
            self._lastinc[inc] = eng

        def run(e, meth=meth, a=a, k=k, inc=inc, incn=incn, site=site):
            try:
                ins = getattr(e, meth)(*a, **k)
            except Exception:
                print("FAILED OP:", meth)
                for f in site:
                    if "kernel.py" in f.filename and f.name not in ("op", "run"):
                        print("   at", f.lineno, f.name, "|", f.line)
                raise
            if inc is not None:
                ins.then_inc(semh[inc], incn)
        self._rec(eng, run)
        if inc is not None:
            self.semv[inc] += incn
            return self.semv[inc]
        return None

    def wait(self, eng, sem, val=None):
        v = self.semv[sem] if val is None else val
        semh = self.semh
        SEM2ENG = {"dv": "vector", "sc": "scalar", "gp": "gpsimd", "mm": "tensor"}
        if sem in SEM2ENG and SEM2ENG[sem] != eng:
            if not hasattr(self, "_pub"):
                self._pub = {}
            self._pub[SEM2ENG[sem]] = True
        if sem in ("ld", "gld", "st", "sld"):
            if not hasattr(self, "_pubd"):
                self._pubd = {}
            self._pubd[sem] = max(v, self._pubd.get(sem, 0))

        def run(e, sem=sem, v=v):
            e.wait_ge(semh[sem], v)
        self._rec(eng, run)

    def sig(self, eng, sem, n=1):
        semh = self.semh
        self.semv[sem] += n
        v = self.semv[sem]

        def run(e, sem=sem, n=n):
            e.sem_inc(semh[sem], n)
        self._rec(eng, run)
        return v

    def fence(self, engines=("tensor", "vector", "scalar", "gpsimd", "sync")):
        for e in engines:
            for sem in ("mm", "dv", "sc", "gp", "ld", "gld", "sld", "cc"):
                if self.semv[sem]:
                    self.wait(e, sem)

    def replay(self, block):
        for ename in ENGS:
            dec = getattr(block, ename)
            ops = self.q[ename]

            def body(e, ops=ops):
                for fn in ops:
                    fn(e)
            dec(body)



def dma(b, qeng, out, in_, sem=None, slow=False, **kw):
    sem = sem or ("gld" if qeng == "gpsimd" else "ld")
    if slow:
        kw["allow_slow_non_contiguous"] = True
    return b.op(qeng, "dma_start", inc=sem, incn=16, out=out, in_=in_, **kw)


def build_program():
    nc = bass.Bass("TRN2", target_bir_lowering=False, debug=False)

    din = {}

    def IN(name, shape, dt=F32):
        din[name] = nc.dram_tensor(name, list(shape), dt, kind="ExternalInput").ap()
        return din[name]

    xs = IN("xs", (TLOC, D))
    xprev = IN("xprev", (1, D))
    flags = IN("flags", (1, 32))
    wpk = {p: IN(f"w{p}t", (D, D)) for p in "rwkvabg"}
    wot = IN("wot", (D, D))
    mixv = IN("mixv", (128, 8, 8))
    gnwr = IN("gnwr", (1, D))
    gnbr = IN("gnbr", (1, D))
    bor = IN("bor", (128, 8))
    wg_i = IN("wg_i", (D, FF0), I32)
    wu_i = IN("wu_i", (D, FF0), I32)
    wd_i = IN("wd_i", (FF0, D), I32)
    encq = IN("encq", (NH, D, HD))
    encv = IN("encv", (NH, D, HD))
    mw1 = IN("mw1", (D, 512))
    mb1 = IN("mb1", (128, 4))
    mw2 = IN("mw2", (512, D))
    mb2 = IN("mb2", (128, 8))
    basem = IN("basem", (128, 8))
    hut = IN("hut", (NH, 32, HD))
    hvv = IN("hvv", (NH, 32, HD))
    decw = IN("decw", (D, D))
    f1w = IN("f1w", (D, FF1))
    f1b = IN("f1b", (128, 32))
    f2w = IN("f2w", (FF1, D))
    f2b = IN("f2b", (128, 8))
    identf = IN("identf", (128, 128))
    identb = IN("identb", (128, 128), BF16)
    mask4 = IN("mask4", (128, 256))
    masklo = IN("masklo", (128, 64))
    idd2 = IN("idd2", (128, 64))
    bandm = IN("bandm", (128, 192))
    bandb = IN("bandb", (128, 192))
    ones2 = IN("ones2", (128, 2))
    sel2 = IN("sel2", (2, 128))
    onesr = IN("onesr", (1, 128))
    onescol = IN("onescol", (128, 1))

    out_h = nc.dram_tensor("out", [TLOC, D], F32, kind="ExternalOutput")
    out = out_h.ap()

    def DT(name, shape, dt=F32):
        skip = ("cc1_in", "cc2_in", "cc1_out", "cc2_out")
        if DEBUG and name not in skip:
            return nc.dram_tensor(name, list(shape), dt, kind="ExternalOutput").ap()
        return nc.dram_tensor(name, list(shape), dt).ap()

    sc_wproj = DT("sc_wproj", (7, D, D), BF16)
    sc_wg = DT("sc_wg", (D, FF0), BF16)
    sc_wu = DT("sc_wu", (D, FF0), BF16)
    sc_wd = DT("sc_wd", (FF0, D), BF16)
    sc_psi = DT("sc_psi", (NCH, H, K, C), BF16)
    sc_p = DT("sc_p", (NCH, H, K, K), BF16)
    sc_oloc = DT("sc_oloc", (TLOC, D), BF16)
    sc_g = DT("sc_g", (TLOC, D), BF16)
    sc_dbg = DT("sc_dbg", (128, 8, K), F32)
    cc1_in = DT("cc1_in", (H, 128, K))
    cc1_out = DT("cc1_out", (8 * H, 128, K))
    cc2_in = DT("cc2_in", (80, D))
    cc2_out = DT("cc2_out", (8 * 80, D))
    sc_wf1 = DT("sc_wf1", (D, FF1), BF16)
    sc_wf2 = DT("sc_wf2", (FF1, D), BF16)
    sc_dec = DT("sc_dec", (D, D), BF16)
    sc_osp = DT("sc_osp", (NH, 128, TLOC), BF16)
    sc_encq = DT("sc_encq", (NH, D, HD), BF16)

    es = ExitStack()

    def SB(name, shape, dt=F32):
        return es.enter_context(nc.sbuf_tensor(name, list(shape), dt))

    with es:
        g = {}
        g["xres"] = SB("xres", (128, 8, TLOC))
        g["stats"] = SB("stats", (128, 16, 2))
        g["stats2"] = SB("stats2", (128, 16, 2))
        g["mixs"] = SB("mixs", (128, 8, 8))
        g["xnprevc"] = SB("xnprevc", (128, 8, 1))
        g["idf"] = SB("idf", (128, 128))
        g["idb"] = SB("idb", (128, 128), BF16)
        g["msk4"] = SB("msk4", (128, 256))
        g["msklo"] = SB("msklo", (128, 64))
        g["idd2"] = SB("idd2s", (128, 64))
        g["bndm"] = SB("bndm", (128, 192))
        g["bndb"] = SB("bndb", (128, 192))
        g["one2"] = SB("one2", (128, 2))
        g["sel2"] = SB("sel2s", (2, 128))
        g["oner"] = SB("oner", (1, 128))
        g["onec"] = SB("onec", (128, 1))
        g["flgb"] = SB("flgb", (128, 32))
        g["rowscr"] = SB("rowscr", (1, 512))
        g["colsc"] = SB("colsc", (128, 24))
        g["dsb"] = SB("dsb", (128, 8, K))
        g["dbf"] = SB("dbf", (128, 8, K), BF16)
        g["scr"] = SB("scr", (128, 4, GRP))
        g["scr2"] = SB("scr2", (128, 4, GRP))
        g["scrb"] = SB("scrb", (128, 2, 1024), BF16)
        g["tokst"] = SB("tokst", (128, 1024))
        g["tokst2"] = SB("tokst2", (128, 1024))
        ARENA_COLS = 49792
        g["arena"] = SB("arena", (128, ARENA_COLS), BF16)

        g["psA"] = es.enter_context(nc.psum_tensor("psA", [128, 512], F32))
        g["psB"] = es.enter_context(nc.psum_tensor("psB", [128, 512], F32))
        g["psST"] = es.enter_context(nc.psum_tensor("psST", [128, 8, 256], F32))
        g["psCH"] = es.enter_context(nc.psum_tensor("psCH", [128, 8, 128], F32))

        class Arena:
            def __init__(self, t, cols):
                self.t = t
                self.cols = cols
                self.off = 0

            def take(self, shape, dt=F32):
                inner = 1
                for s in shape[1:]:
                    inner *= s
                n = inner * (2 if dt in (F32, I32) else 1)
                n = (n + 63) // 64 * 64
                assert self.off + n <= self.cols, f"arena overflow {self.off}+{n}"
                sl = self.t[:, self.off:self.off + n]
                self.off += n
                flat = (sl if dt == BF16 else sl.bitcast(dt))[:, 0:inner]
                v = flat
                if len(shape) == 3:
                    v = flat.rearrange("p (a c) -> p a c", a=shape[1])
                return v

            def reg(self, a, name, shape, dt=F32):
                inner = 1
                for s in shape[1:]:
                    inner *= s
                n = inner * (2 if dt in (F32, I32) else 1)
                n = (n + 63) // 64 * 64
                assert self.off + n <= self.cols, f"arena overflow {self.off}+{n}"
                sl = self.t[:, self.off:self.off + n]
                self.off += n
                flat = (sl if dt == BF16 else sl.bitcast(dt))[:, 0:inner]
                v = flat
                if len(shape) == 3:
                    v = flat.rearrange("p (a c) -> p a c", a=shape[1])
                a[name] = v
                a[name + "_f"] = flat
                return v

        def seta(ar):
            a = {}
            ar.reg(a, "wbuf", (128, 8, 1024), BF16)
            ar.reg(a, "s_r", (128, 8, GRP), BF16)
            ar.reg(a, "s_k", (128, 8, GRP), BF16)
            ar.reg(a, "s_a", (128, 8, GRP), BF16)
            ar.reg(a, "s_b", (128, 8, GRP), BF16)
            ar.reg(a, "wcum", (128, 8, GRP))
            ar.reg(a, "xng", (128, 8, GRP), BF16)
            ar.reg(a, "xxg", (128, 8, GRP), BF16)
            ar.reg(a, "xmx", (128, 8, GRP), BF16)
            ar.reg(a, "xmv", (128, 8, GRP), BF16)
            ar.reg(a, "wcc", (128, 8, 2 * C))
            ar.reg(a, "ftmp", (128, 8, 3 * C))
            ar.reg(a, "fcol", (128, 8, 4))
            ar.reg(a, "lhsbk", (128, 8, 2 * C), BF16)
            ar.reg(a, "rhsar", (128, 8, 2 * C), BF16)
            ar.reg(a, "til", (128, 8, 4 * C), BF16)
            ar.reg(a, "vtok", (128, H * K), BF16)
            ar.reg(a, "stsb", (128, 8, 4 * C), BF16)
            ar.reg(a, "ntb", (128, 8, C), BF16)
            ar.reg(a, "pwst", (128, 8, 128), BF16)
            ar.reg(a, "pwts", (128, 8, 128), BF16)
            ar.reg(a, "ysbb", (128, 8, 128), BF16)
            ar.reg(a, "trsb", (128, 8, 128), BF16)
            a["psis"] = a["lhsbk"].rearrange("p a c -> p (a c)").rearrange(
                "p (a c) -> p a c", c=C)[:, 0:8, :]
            a["psb"] = a["rhsar"].rearrange("p a c -> p (a c)").rearrange(
                "p (a c) -> p a c", c=K)[:, 0:8, :]
            a["ostg"] = a["rhsar"].rearrange("p a c -> p (a c)").rearrange(
                "p (a c) -> p a c", c=K)[:, 8:16, :]
            ar.reg(a, "slst", (128, 8, K))
            ar.reg(a, "slbf", (128, 8, K), BF16)
            ar.reg(a, "accs", (128, 8, K))
            ar.reg(a, "accb", (128, 8, K), BF16)
            return a

        def seta2(ar):
            a = {}
            ar.reg(a, "gnwb", (128, 1024))
            ar.reg(a, "gnbb", (128, 1024))
            ar.reg(a, "bocol", (128, 8, 1))
            ar.reg(a, "wotb", (128, 8, 1024), BF16)
            ar.reg(a, "wtile", (128, 8, 128), BF16)
            ar.reg(a, "wtile2", (128, 8, 128), BF16)
            ar.reg(a, "wtileB", (128, 8, 128), BF16)
            ar.reg(a, "wtile2B", (128, 8, 128), BF16)
            ar.reg(a, "wdtile", (128, 1024), BF16)
            ar.reg(a, "wdtileB", (128, 1024), BF16)
            ar.reg(a, "hstr", (128, 32, GRP), BF16)
            ar.reg(a, "xng", (128, 8, GRP), BF16)
            ar.reg(a, "psil", (128, 8, C), BF16)
            ar.reg(a, "pldl", (128, 8, K), BF16)
            ar.reg(a, "ztl", (128, 8, 128), BF16)
            ar.reg(a, "otr", (128, 8, 128))
            ar.reg(a, "gnsc", (128, 16, 8))
            return a

        def setb_shared(ar):
            a = {}
            ar.reg(a, "qts", (128, NH, 64 + TLOC), BF16)
            ar.reg(a, "vts", (128, 16, D), BF16)
            ar.reg(a, "vhalo", (128, D), BF16)
            ar.reg(a, "uvs", (128, NH * HD), BF16)
            ar.reg(a, "metr", (128, 8, 1))
            ar.reg(a, "acols", (128, 8, 17))
            ar.reg(a, "sqnc", (128, 8, 17))
            return a

        def setb0(ar):
            a = setb_shared(ar)
            ar.reg(a, "xng", (128, 8, GRP), BF16)
            ar.reg(a, "encvb", (128, 8, 1024), BF16)
            ar.reg(a, "encb", (128, 8, 128), BF16)
            ar.reg(a, "sigc", (128, 8, 4))
            ar.reg(a, "mlpc", (128, 8, 4))
            ar.reg(a, "bmc", (128, 8, 1))
            ar.reg(a, "xhalT", (128, 8, 64), BF16)
            return a

        def setb1(ar):
            a = setb_shared(ar)
            ar.reg(a, "dtl", (128, 192))
            ar.reg(a, "neg1", (128, 192))
            ar.reg(a, "neg2", (128, 192))
            ar.reg(a, "m8", (128, 8))
            ar.reg(a, "wse", (128, 192))
            ar.reg(a, "wsb", (128, 192), BF16)
            ar.reg(a, "expd", (128, 192))
            ar.reg(a, "wtb", (128, 384), BF16)
            ar.reg(a, "rsum", (128, 4))
            ar.reg(a, "qsq", (128, 128))
            ar.reg(a, "ospst", (128, 8, 128), BF16)
            ar.reg(a, "qmb", (128, 128), BF16)
            ar.reg(a, "qtkb", (128, 128), BF16)
            ar.reg(a, "ostok", (128, 128), BF16)
            ar.reg(a, "ctmp", (128, 128))
            return a

        def setb2(ar):
            a = {}
            ar.reg(a, "xng", (128, 8, GRP), BF16)
            ar.reg(a, "hstr", (128, 32, GRP), BF16)
            ar.reg(a, "wtile", (128, 8, 128), BF16)
            ar.reg(a, "wtileB", (128, 8, 128), BF16)
            ar.reg(a, "wtile2", (128, 1024), BF16)
            ar.reg(a, "wtile2B", (128, 1024), BF16)
            ar.reg(a, "hq32", (128, 2 * GRP), I32)
            ar.reg(a, "pwl", (128, 8, 2 * GRP))
            ar.reg(a, "b1c", (128, 32, 1))
            ar.reg(a, "b2c", (128, 8, 1))
            ar.reg(a, "ospl", (128, 8, GRP), BF16)
            return a

        g["_mk"] = (Arena, seta, seta2, setb0, setb1, setb2, ARENA_COLS)

        b = Builder(nc, es, ["ld", "gld", "sld", "mm", "dv", "sc", "gp", "st", "cc", "fn"])
        G = type("G", (), g)

        emit(b, G, dict(
            xs=xs, xprev=xprev, flags=flags, wpk=wpk, wot=wot, mixv=mixv,
            gnwr=gnwr, gnbr=gnbr, bor=bor, wg_i=wg_i, wu_i=wu_i, wd_i=wd_i,
            encq=encq, encv=encv, mw1=mw1, mb1=mb1, mw2=mw2, mb2=mb2,
            basem=basem, hut=hut, hvv=hvv, decw=decw, f1w=f1w, f1b=f1b,
            f2w=f2w, f2b=f2b, identf=identf, identb=identb, mask4=mask4,
            masklo=masklo, idd2=idd2, bandm=bandm, bandb=bandb, ones2=ones2,
            sel2=sel2, onesr=onesr, onescol=onescol, out=out,
            sc_wproj=sc_wproj, sc_wg=sc_wg, sc_wu=sc_wu, sc_wd=sc_wd,
            sc_psi=sc_psi, sc_p=sc_p, sc_oloc=sc_oloc, sc_g=sc_g, sc_dbg=sc_dbg,
            sc_wf1=sc_wf1, sc_wf2=sc_wf2, sc_dec=sc_dec, sc_osp=sc_osp, sc_encq=sc_encq,
            cc1_in=cc1_in, cc1_out=cc1_out, cc2_in=cc2_in, cc2_out=cc2_out,
        ))

        block = es.enter_context(nc.Block())
        b.replay(block)

    return nc



def _merge(G, extra):
    base = {k: v for k, v in G.__dict__.items() if not k.startswith("__")}
    base.update(extra)
    return type("G", (), base)


PHASES = 99


def _finish_stub(b, G, d, dump_xres=False):
    b.fence()
    if dump_xres:
        ov = d["out"].rearrange("t e -> (t e)").rearrange("(p r) -> p r", p=128)
        dma(b, "sync", ov, G.xres[:, :, :].rearrange("p a c -> p (a c)"), sem="st")
    else:
        b.op("vector", "memset", G.tokst[:, :], 0.0, inc="dv")
        b.wait("sync", "dv")
        for i in range(16):
            dma(b, "sync", d["out"][i * 128:(i + 1) * 128, :], G.tokst[:, :], sem="st")
    b.wait("sync", "st")
    b.wait("gpsimd", "st")


def emit(b, G, d):
    Arena, seta, seta2, setb0, setb1, setb2, AC = G._mk
    import kernel as _k
    ph = getattr(_k, "PHASES", 99)
    GA = _merge(G, seta(Arena(G.arena, AC)))
    emit_p0(b, GA, d)
    if ph <= 1:
        return _finish_stub(b, GA, d)
    emit_wconv(b, GA, d)
    if ph <= 2:
        return _finish_stub(b, GA, d)
    b.op("vector", "memset", GA.slst[:, :, :], 0.0, inc="dv")
    for grp in range(min(NGRP, ph - 2)):
        emit_group(b, GA, d, grp)
    if ph <= 10:
        return _finish_stub(b, GA, d)
    emit_relay(b, GA, d)
    if ph <= 11:
        return _finish_stub(b, GA, d)
    GA2 = _merge(G, seta2(Arena(G.arena, AC)))
    emit_p3(b, GA2, d)
    if ph <= 12:
        return _finish_stub(b, GA2, d, dump_xres=True)
    GB0 = _merge(G, setb0(Arena(G.arena, AC)))
    emit_p4(b, GB0, d)
    if ph <= 13:
        return _finish_stub(b, GB0, d, dump_xres=True)
    emit_p5(b, GB0, d)
    if ph <= 14:
        return _finish_stub(b, GB0, d, dump_xres=True)
    GB1 = _merge(G, setb1(Arena(G.arena, AC)))
    emit_p6a(b, GB1, d)
    if ph <= 15:
        return _finish_stub(b, GB1, d, dump_xres=True)
    GB2 = _merge(G, setb2(Arena(G.arena, AC)))
    emit_p6b(b, GB2, d)




def v8(t, pages, cols):
    return t[:, 0:pages, :].rearrange("p a b -> p (a b)").rearrange("p (a c) -> p a c", c=cols)

def tt(b, eng, outp, a0, a1, op, inc="dv"):
    return b.op(eng, "tensor_tensor", outp, a0, a1, op, inc=inc)


def ts(b, eng, outp, a0, s1, s2, op0, op1=None, inc="dv"):
    if op1 is None:
        return b.op(eng, "tensor_scalar", outp, a0, s1, None, op0, inc=inc)
    return b.op(eng, "tensor_scalar", outp, a0, s1, s2, op0, op1, inc=inc)


def stt(b, eng, outp, in0, scalar, in1, op0, op1, inc="dv"):
    return b.op(eng, "scalar_tensor_tensor", outp, in0, scalar, in1, op0, op1, inc=inc)


def act(b, outp, inp, func, inc="sc", **kw):
    return b.op("scalar", "activation", outp, inp, func, inc=inc, **kw)


def mm(b, outp, lhsT, rhs, start=True, stop=True, tp=None, inc=None, sgc=False):
    kw = {}
    if tp is not None:
        kw["tile_position"] = tp
    if sgc:
        kw["skip_group_check"] = True
    return b.op("tensor", "matmul", outp, lhsT, rhs, inc=inc, start=start, stop=stop, **kw)


def tr(b, outp, inp, ident, tp=None, inc=None):
    kw = {}
    if tp is not None:
        kw["tile_position"] = tp
    return b.op("tensor", "transpose", outp, inp, ident, inc=inc, **kw)


def DPSUM(G, eb):
    if eb == 0:
        return G.psA[:, 0:256]
    if eb == 1:
        return G.psB[:, 0:256]
    if eb < 6:
        return G.psST[:, (eb - 2) * 2, 0:256]
    return G.psCH[:, (eb - 6) * 4:(eb - 6) * 4 + 2, :].rearrange("p a b -> p (a b)")


def emit_p0(b, G, d):
    for outp, inp in [
        (G.idf[:, :], d["identf"]), (G.idb[:, :], d["identb"]),
        (G.msk4[:, :], d["mask4"]), (G.msklo[:, :], d["masklo"]),
        (G.idd2[:, :], d["idd2"]),
        (G.bndm[:, :], d["bandm"]), (G.bndb[:, :], d["bandb"]),
        (G.one2[:, :], d["ones2"]), (G.sel2[:, :], d["sel2"]),
        (G.oner[:, :], d["onesr"]), (G.onec[:, :], d["onescol"]),
        (G.tokst2[0:1, :], d["xprev"]),
    ]:
        dma(b, "sync", outp, inp)
    dma(b, "sync", G.mixs[:, :, :], d["mixv"])
    dma(b, "sync", G.scr[0:1, 0, 0:32], d["flags"])
    ldv = b.semv["ld"]

    b.wait("tensor", "ld")
    mm(b, G.psA[:, 0:32], G.oner[:, :], G.scr[0:1, 0, 0:32], inc="mm")
    b.wait("vector", "mm")
    b.op("vector", "tensor_copy", G.flgb[:, :], G.psA[:, 0:32], inc="dv")


    b.wait("vector", "ld", ldv)
    b.op("vector", "reduce_sum", G.colsc[0:1, 0:1], G.tokst2[0:1, :], axis=AX)
    tt(b, "vector", G.scr[0:1, 0, 0:GRP], G.tokst2[0:1, 0:GRP], G.tokst2[0:1, 0:GRP], AT.mult)
    tt(b, "vector", G.scr[0:1, 1, 0:GRP], G.tokst2[0:1, GRP:2 * GRP], G.tokst2[0:1, GRP:2 * GRP], AT.mult)
    tt(b, "vector", G.scr[0:1, 2, 0:GRP], G.tokst2[0:1, 2 * GRP:3 * GRP], G.tokst2[0:1, 2 * GRP:3 * GRP], AT.mult)
    tt(b, "vector", G.scr[0:1, 3, 0:GRP], G.tokst2[0:1, 3 * GRP:4 * GRP], G.tokst2[0:1, 3 * GRP:4 * GRP], AT.mult)
    tt(b, "vector", G.scr[0:1, 0, 0:GRP], G.scr[0:1, 0, 0:GRP], G.scr[0:1, 1, 0:GRP], AT.add)
    tt(b, "vector", G.scr[0:1, 2, 0:GRP], G.scr[0:1, 2, 0:GRP], G.scr[0:1, 3, 0:GRP], AT.add)
    tt(b, "vector", G.scr[0:1, 0, 0:GRP], G.scr[0:1, 0, 0:GRP], G.scr[0:1, 2, 0:GRP], AT.add)
    b.op("vector", "reduce_sum", G.colsc[0:1, 1:2], G.scr[0:1, 0, 0:GRP], axis=AX)
    ts(b, "vector", G.colsc[0:1, 2:3], G.colsc[0:1, 0:1], 1.0 / D, None, AT.mult)
    ts(b, "vector", G.colsc[0:1, 3:4], G.colsc[0:1, 1:2], 1.0 / D, None, AT.mult)
    tt(b, "vector", G.colsc[0:1, 4:5], G.colsc[0:1, 2:3], G.colsc[0:1, 2:3], AT.mult)
    tt(b, "vector", G.colsc[0:1, 5:6], G.colsc[0:1, 3:4], G.colsc[0:1, 4:5], AT.subtract)
    ts(b, "vector", G.colsc[0:1, 5:6], G.colsc[0:1, 5:6], 1e-5, None, AT.add, inc="dv")
    b.wait("scalar", "dv")
    b.op("scalar", "sqrt", G.colsc[0:1, 6:7], G.colsc[0:1, 5:6], inc="sc")
    b.wait("vector", "sc")
    b.op("vector", "reciprocal", G.colsc[0:1, 7:8], G.colsc[0:1, 6:7])
    ts(b, "vector", G.tokst2[0:1, :], G.tokst2[0:1, :], G.colsc[0:1, 2:3], G.colsc[0:1, 7:8],
       AT.subtract, AT.mult, inc="dv")
    b.wait("tensor", "dv"); b.wait("tensor", "sc")
    for db in range(8):
        mm(b, G.psB[:, db:db + 1], G.tokst2[0:1, db * 128:(db + 1) * 128], G.onec[0:1, 0:1])
    b.wait("vector", "mm")
    b.op("vector", "tensor_copy", G.xnprevc[:, :, 0], G.psB[:, 0:8], inc="dv")

    for i in range(16):
        buf = G.tokst if i % 2 == 0 else G.tokst2
        b.wait("sync", "mm")
        dma(b, "sync", buf[:, :], d["xs"][i * 128:(i + 1) * 128, :])
        b.wait("vector", "ld")
        b.op("vector", "reduce_sum", G.colsc[:, 8:9], buf[:, :], axis=AX)
        b.wait("scalar", "ld")
        act(b, G.scr[:, 0:4, :].rearrange("p a c -> p (a c)"), buf[:, :], AF.Square)
        b.wait("vector", "sc")
        b.op("vector", "reduce_sum", G.colsc[:, 9:10], G.scr[:, 0:4, :].rearrange("p a c -> p (a c)"), axis=AX)
        ts(b, "vector", G.colsc[:, 10:11], G.colsc[:, 8:9], 1.0 / D, None, AT.mult)
        ts(b, "vector", G.colsc[:, 11:12], G.colsc[:, 9:10], 1.0 / D, None, AT.mult)
        tt(b, "vector", G.colsc[:, 12:13], G.colsc[:, 10:11], G.colsc[:, 10:11], AT.mult)
        tt(b, "vector", G.colsc[:, 13:14], G.colsc[:, 11:12], G.colsc[:, 12:13], AT.subtract)
        ts(b, "vector", G.colsc[:, 13:14], G.colsc[:, 13:14], 1e-5, None, AT.add, inc="dv")
        b.wait("scalar", "dv")
        b.op("scalar", "sqrt", G.colsc[:, 14:15], G.colsc[:, 13:14], inc="sc")
        b.wait("vector", "sc")
        b.op("vector", "reciprocal", G.colsc[:, 15:16], G.colsc[:, 14:15])
        b.op("vector", "tensor_copy", G.stats[:, i, 0:1], G.colsc[:, 10:11])
        b.op("vector", "tensor_copy", G.stats[:, i, 1:2], G.colsc[:, 15:16], inc="dv")
        b.wait("tensor", "ld")
        for db in range(8):
            pst = G.psA if db % 2 == 0 else G.psB
            b.wait("tensor", "dv"); b.wait("tensor", "sc")
            tr(b, pst[:, 256:384] if pst is G.psB else pst[:, 0:128],
               buf[:, db * 128:(db + 1) * 128], G.idf[:, :], inc="mm")
            b.wait("vector", "mm")
            psrc = pst[:, 256:384] if pst is G.psB else pst[:, 0:128]
            b.op("vector", "tensor_copy", G.xres[:, db, i * 128:(i + 1) * 128], psrc, inc="dv")

    b.fence()




def ln_bcast(b, G, statt, ti):
    b.wait("tensor", "dv"); b.wait("tensor", "sc")
    tr(b, G.psCH[0:1, 0, 0:128], statt[:, ti, 0:1], G.idf[:, :])
    tr(b, G.psCH[0:1, 0, 128:256] if False else G.psCH[0:1, 1, 0:128],
       statt[:, ti, 1:2], G.idf[:, :], inc="mm")
    b.wait("vector", "mm")
    b.op("vector", "tensor_copy", G.rowscr[0:1, 0:128], G.psCH[0:1, 0, 0:128])
    b.op("vector", "tensor_copy", G.rowscr[0:1, 128:256], G.psCH[0:1, 1, 0:128], inc="dv")
    b.wait("tensor", "dv"); b.wait("tensor", "sc")
    mm(b, G.psA[:, 0:128], G.oner[:, :], G.rowscr[0:1, 0:128])
    mm(b, G.psB[:, 0:128], G.oner[:, :], G.rowscr[0:1, 128:256], inc="mm")
    b.wait("vector", "mm")

def emit_wconv(b, G, d):
    for pi, p in enumerate("rwkvabg"):
        src = d["wpk"][p]
        for half in range(2):
            sl = slice(half * 512, (half + 1) * 512)
            b.wait("gpsimd", "gld")
            dma(b, "gpsimd", G.wbuf[:, 0:4, :],
                src[sl, :].rearrange("(hb p) e -> p hb e", p=128))
            b.wait("gpsimd", "gld")
            dma(b, "gpsimd", d["sc_wproj"][pi, sl, :].rearrange("(hb p) e -> p hb e", p=128),
                G.wbuf[:, 0:4, :])
    for srci, dst, rows, cols in ((d["wg_i"], d["sc_wg"], D, FF0),
                                  (d["wu_i"], d["sc_wu"], D, FF0),
                                  (d["wd_i"], d["sc_wd"], FF0, D)):
        step = 8192 // cols * 128
        for r0 in range(0, rows, step):
            sl = slice(r0, r0 + step)
            wv = G.wbuf_f[:, 0:(step // 128) * cols].rearrange("p (a c) -> p a c", c=cols)
            b.wait("gpsimd", "gld")
            dma(b, "gpsimd", wv, srci[sl, :].rearrange("(hb p) e -> p hb e", p=128))
            b.wait("gpsimd", "gld")
            dma(b, "gpsimd", dst[sl, :].rearrange("(hb p) e -> p hb e", p=128), wv)
    for h in range(NH):
        b.wait("gpsimd", "gld")
        dma(b, "gpsimd", G.wbuf[:, :, 0:128],
            d["encq"][h].rearrange("(db p) e -> p db e", p=128))
        b.wait("gpsimd", "gld")
        dma(b, "gpsimd", d["sc_encq"][h].rearrange("(db p) e -> p db e", p=128),
            G.wbuf[:, :, 0:128])
    for srci, dst, rows, cols in ((d["f1w"], d["sc_wf1"], D, FF1),
                                  (d["f2w"], d["sc_wf2"], FF1, D),
                                  (d["decw"], d["sc_dec"], D, D)):
        step = 8192 // cols * 128
        for r0 in range(0, rows, step):
            sl = slice(r0, r0 + step)
            wv = G.wbuf_f[:, 0:(step // 128) * cols].rearrange("p (a c) -> p a c", c=cols)
            b.wait("gpsimd", "gld")
            dma(b, "gpsimd", wv, srci[sl, :].rearrange("(hb p) e -> p hb e", p=128))
            b.wait("gpsimd", "gld")
            dma(b, "gpsimd", dst[sl, :].rearrange("(hb p) e -> p hb e", p=128), wv)
    b.fence()


def emit_group(b, G, d, grp):
    gs = grp * GRP
    t128 = gs // 128

    for half in range(2):
        ti = t128 + half
        ln_bcast(b, G, G.stats, ti)
        hs = slice(half * 128, (half + 1) * 128)
        for db in range(8):
            tt(b, "vector", G.scr[:, 0, 0:128], G.xres[:, db, gs + half * 128:gs + (half + 1) * 128],
               G.psA[:, 0:128], AT.subtract, inc=None)
            tt(b, "vector", G.xng[:, db, hs], G.scr[:, 0, 0:128], G.psB[:, 0:128], AT.mult, inc=None)
    tt(b, "vector", G.xxg[:, :, 1:GRP], G.xng[:, :, 0:GRP - 1], G.xng[:, :, 1:GRP], AT.subtract)
    tt(b, "vector", G.xxg[:, :, 0:1], G.xnprevc[:, :, :], G.xng[:, :, 0:1], AT.subtract)
    b.op("vector", "tensor_copy", G.xnprevc[:, :, :], G.xng[:, :, GRP - 1:GRP], inc="dv")

    def xmix_into(tgt, vi):
        b.wait("vector", "mm")
        b.wait("vector", "gld")
        vw = v8(G.scrb, 2, GRP)
        tt(b, "vector", vw, G.xxg[:, :, :],
           G.mixs[:, :, vi:vi + 1].broadcast_to([128, 8, GRP]), AT.mult, inc=None)
        tt(b, "vector", tgt[:, :, :], vw, G.xng[:, :, :], AT.add, inc="dv")

    b.wait("sync", "mm")
    dma(b, "sync", G.wbuf[:, :, :],
        d["sc_wproj"][6].rearrange("(db p) e -> p db e", p=128))
    xmix_into(G.xmx, 6)
    b.wait("tensor", "ld")
    b.wait("tensor", "dv"); b.wait("tensor", "sc")
    for tc in range(2):
        b.wait("tensor", "sc")
        for eh in range(2):
            pst = G.psA if eh == 0 else G.psB
            for db in range(8):
                mm(b, pst[:, 0:512], G.xmx[:, db, tc * 128:(tc + 1) * 128],
                   G.wbuf[:, db, eh * 512:(eh + 1) * 512], start=(db == 0), stop=(db == 7))
        b.wait("scalar", "mm")
        b.wait("scalar", "gld")
        act(b, G.scrb[:, 0, 0:512], G.psA[:, 0:512], AF.Sigmoid)
        act(b, G.scrb[:, 0, 512:1024], G.psB[:, 0:512], AF.Sigmoid, inc="sc")
        b.wait("gpsimd", "sc")
        dma(b, "gpsimd", d["sc_g"][gs + tc * 128:gs + (tc + 1) * 128, :],
            G.scrb[:, 0, :])
    xmix_into(G.xmv, 3)

    for p, vi, stream in (("r", 0, G.s_r), ("w", 1, None), ("k", 2, G.s_k),
                          ("a", 4, G.s_a), ("b", 5, G.s_b)):
        wi = "rwkvabg".index(p)
        b.wait("sync", "mm")
        b.wait("sync", "sc")
        wsrc = d["sc_wproj"][wi].rearrange("(db p) e -> p db e", p=128)
        ldv = {}
        for eb in range(8):
            dma(b, "sync", G.wbuf[:, :, eb * 128:(eb + 1) * 128],
                wsrc[:, :, eb * 128:(eb + 1) * 128])
            ldv[eb] = b.semv["ld"]
            b.wait("sync", "ld", val=ldv[eb])
        xmix_into(G.xmx, vi)
        pev = {}
        ent = (b.semv["dv"], b.semv["sc"])
        b.wait("tensor", "dv"); b.wait("tensor", "sc")
        for eb in range(8):
            pst = G.psA if eb % 2 == 0 else G.psB
            dvv, scv = pev.get(eb - 2, ent)
            b.wait("tensor", "dv", val=dvv)
            b.wait("tensor", "sc", val=scv)
            b.wait("tensor", "ld", val=ldv[eb])
            for db in range(8):
                mm(b, pst[:, 0:GRP], G.wbuf[:, db, eb * 128:(eb + 1) * 128],
                   G.xmx[:, db, :], start=(db == 0), stop=(db == 7))
            if p == "r":
                b.wait("vector", "mm")
                b.op("vector", "tensor_copy", stream[:, eb, :], pst[:, 0:GRP], inc="dv")
            elif p == "k":
                b.wait("scalar", "mm")
                act(b, stream[:, eb, :], pst[:, 0:GRP], AF.Tanh, inc="dv")
            elif p == "w":
                b.wait("scalar", "mm")
                act(b, G.scr[:, 0, :], pst[:, 0:GRP], AF.Sigmoid, scale=-1.0, inc="sc")
                b.wait("vector", "sc")
                b.op("vector", "tensor_scalar_max", G.scr[:, 1, :], G.scr[:, 0, :], 1e-38, inc="dv")
                b.wait("scalar", "dv")
                act(b, G.wcum[:, eb, :], G.scr[:, 1, :], AF.Ln, inc="dv")
            else:
                b.wait("scalar", "mm")
                act(b, G.scr[:, 0, :], pst[:, 0:GRP], AF.Square, inc="sc")
                b.wait("tensor", "sc")
                mm(b, G.psST[0:2, 0, 0:GRP], G.one2[:, :], G.scr[:, 0, :], inc="mm")
                b.wait("vector", "mm")
                b.op("vector", "tensor_copy", G.scr[0:2, 1, :], G.psST[0:2, 0, 0:GRP], inc="dv")
                b.wait("scalar", "dv")
                b.op("scalar", "sqrt", G.scr[0:2, 1, :], G.scr[0:2, 1, :], inc="sc")
                b.wait("vector", "sc")
                b.op("vector", "tensor_scalar_max", G.scr[0:2, 1, :], G.scr[0:2, 1, :], 1e-6)
                b.op("vector", "reciprocal", G.scr[0:2, 1, :], G.scr[0:2, 1, :], inc="dv")
                b.wait("tensor", "dv"); b.wait("tensor", "sc")
                mm(b, G.psST[:, 1, 0:GRP], G.sel2[:, :], G.scr[0:2, 1, :], inc="mm")
                b.wait("vector", "mm")
                b.op("vector", "tensor_copy", G.scr[:, 2, :], G.psST[:, 1, 0:GRP])
                tt(b, "vector", stream[:, eb, :], pst[:, 0:GRP], G.scr[:, 2, :], AT.mult, inc="dv")
            pev[eb] = (b.semv["dv"], b.semv["sc"])

    b.wait("sync", "mm")
    b.wait("sync", "sc")
    b.wait("sync", "dv")
    dma(b, "sync", G.wbuf[:, :, :],
        d["sc_wproj"][3].rearrange("(db p) e -> p db e", p=128))
    b.wait("tensor", "ld")
    b.op("vector", "memset", G.scr[:, 0, 0:GRP], 1.0)
    for hp in range(8):
        b.op("vector", "tensor_tensor_scan", G.wcum[:, hp, :],
             G.scr[:, 0, 0:GRP], G.wcum[:, hp, :], 0.0, AT.mult, AT.add)

    for ci in range(CPG):
        emit_chunk(b, G, d, grp, grp * CPG + ci, ci * C)


def emit_chunk(b, G, d, grp, ch, cs):
    sl = slice(cs, cs + C)

    b.wait("tensor", "dv"); b.wait("tensor", "sc")
    for par in range(2):
        if par:
            b.wait("tensor", "dv")
        tp = (0, 64) if par else None
        po = 64 * par
        for eh in range(2):
            pst = G.psA if eh == 0 else G.psB
            for db in range(8):
                mm(b, pst[po:po + 64, 0:512], G.xmv[:, db, sl],
                   G.wbuf[:, db, eh * 512:(eh + 1) * 512], start=(db == 0), stop=(db == 7), tp=tp)
        b.wait("vector", "mm")
        b.op("vector", "tensor_copy", G.vtok[po:po + 64, 0:512], G.psA[po:po + 64, 0:512])
        b.op("vector", "tensor_copy", G.vtok[po:po + 64, 512:1024], G.psB[po:po + 64, 0:512], inc="dv")

    if cs == 0:
        b.op("vector", "tensor_copy", G.wcc[:, :, 0:C], G.wcum[:, :, sl])
    else:
        tt(b, "vector", G.wcc[:, :, 0:C], G.wcum[:, :, sl],
           G.wcum[:, :, cs - 1:cs].broadcast_to([128, 8, C]), AT.subtract)
    b.op("vector", "memset", G.wcc[:, :, C:C + 1], 0.0)
    b.op("vector", "tensor_copy", G.wcc[:, :, C + 1:2 * C], G.wcc[:, :, 0:C - 1])
    rf = G.wcc[:, :, C // 2 - 1:C // 2]
    tt(b, "vector", G.ftmp[:, :, 0:C], G.wcc[:, :, 0:C], rf.broadcast_to([128, 8, C]), AT.subtract)
    tt(b, "vector", G.ftmp[:, :, C:2 * C], G.wcc[:, :, C:2 * C], rf.broadcast_to([128, 8, C]),
       AT.subtract, inc="dv")
    b.wait("scalar", "dv")
    act(b, G.ftmp[:, :, 0:C], G.ftmp[:, :, 0:C], AF.Exp)
    act(b, G.ftmp[:, :, C:2 * C], G.ftmp[:, :, C:2 * C], AF.Exp)
    act(b, G.fcol[:, :, 0:1], rf, AF.Exp)
    act(b, G.fcol[:, :, 2:3], G.wcc[:, :, C - 1:C], AF.Exp, inc="sc")
    b.wait("vector", "sc")
    b.wait("vector", "gld")
    b.op("vector", "reciprocal", G.ftmp[:, :, 2 * C:3 * C], G.ftmp[:, :, 0:C])
    tt(b, "vector", G.rhsar[:, :, 0:C], G.s_a[:, :, sl], G.ftmp[:, :, C:2 * C], AT.mult)
    tt(b, "vector", G.rhsar[:, :, C:2 * C], G.s_r[:, :, sl], G.ftmp[:, :, 0:C], AT.mult)
    tt(b, "vector", G.lhsbk[:, :, 0:C], G.s_b[:, :, sl], G.ftmp[:, :, 2 * C:3 * C], AT.mult)
    tt(b, "vector", G.lhsbk[:, :, C:2 * C], G.s_k[:, :, sl], G.ftmp[:, :, 2 * C:3 * C], AT.mult)
    tt(b, "vector", G.til[:, :, 0:C], G.rhsar[:, :, 0:C],
       G.fcol[:, :, 0:1].broadcast_to([128, 8, C]), AT.mult)
    tt(b, "vector", G.til[:, :, C:2 * C], G.rhsar[:, :, C:2 * C],
       G.fcol[:, :, 0:1].broadcast_to([128, 8, C]), AT.mult)
    f6 = v8(G.scr, 2, C)
    tt(b, "vector", f6, G.ftmp[:, :, 2 * C:3 * C],
       G.ftmp[:, :, C - 1:C].broadcast_to([128, 8, C]), AT.mult)
    tt(b, "vector", G.til[:, :, 2 * C:3 * C], G.s_b[:, :, sl], f6, AT.mult)
    tt(b, "vector", G.til[:, :, 3 * C:4 * C], G.s_k[:, :, sl], f6, AT.mult, inc="dv")

    b.wait("tensor", "dv"); b.wait("tensor", "sc")
    for h in range(H):
        q = 64 * (h & 1)
        hp = h >> 1
        tp = (q, q) if q else None
        mm(b, G.psST[q:q + 64, hp, 0:128], G.lhsbk[q:q + 64, hp, 0:C],
           G.rhsar[q:q + 64, hp, :], tp=tp)
        mm(b, G.psST[q:q + 64, hp, 128:256], G.lhsbk[q:q + 64, hp, C:2 * C],
           G.rhsar[q:q + 64, hp, :], tp=tp)
        mm(b, G.psCH[q:q + 64, hp, 0:64], G.rhsar[q:q + 64, hp, 0:C],
           G.lhsbk[q:q + 64, hp, 0:C], tp=tp)
    b.wait("vector", "mm")
    tt(b, "vector", G.stsb[:, :, :], G.psST[:, :, :],
       G.msk4[:, :].unsqueeze(1).broadcast_to([128, 8, 256]), AT.mult)
    tt(b, "vector", G.ntb[:, :, :], G.psCH[:, :, 0:64],
       G.msklo[:, :].unsqueeze(1).broadcast_to([128, 8, 64]), AT.mult, inc="dv")

    b.wait("tensor", "dv"); b.wait("tensor", "sc")
    for h in range(H):
        q = 64 * (h & 1)
        hp = h >> 1
        tp = (q, q) if q else None
        mm(b, G.psST[q:q + 64, hp, 0:64], G.til[q:q + 64, hp, 2 * C:3 * C],
           G.idb[q:q + 64, q:q + 64], tp=tp)
        mm(b, G.psST[q:q + 64, hp, 64:128], G.til[q:q + 64, hp, 3 * C:4 * C],
           G.idb[q:q + 64, q:q + 64], tp=tp)
    b.wait("vector", "mm")
    b.op("vector", "tensor_copy", G.trsb[:, :, 0:128], G.psST[:, :, 0:128])
    b.op("vector", "tensor_copy", G.slbf[:, :, :], G.slst[:, :, :], inc="dv")

    b.wait("tensor", "dv"); b.wait("tensor", "sc")
    for h in range(H):
        q = 64 * (h & 1)
        hp = h >> 1
        tp = (q, q) if q else None
        mm(b, G.psCH[q:q + 64, hp, 64:128], G.til[q:q + 64, hp, 0:C],
           G.idb[q:q + 64, q:q + 64], tp=tp)
        mm(b, G.psCH[q:q + 64, hp, 0:64], G.til[q:q + 64, hp, 0:C],
           G.slbf[q:q + 64, hp, :], tp=tp, stop=False)
        mm(b, G.psCH[q:q + 64, hp, 0:64], G.stsb[q:q + 64, hp, 2 * C:3 * C],
           G.vtok[q:q + 64, h * K:(h + 1) * K], tp=tp, start=False)
    b.wait("vector", "mm")
    b.op("vector", "tensor_copy", G.ysbb[:, :, :], G.psCH[:, :, :], inc="dv")

    b.op("vector", "memset", G.pwst[0:64, :, 64:128], 0.0)
    b.op("vector", "memset", G.pwst[64:128, :, 0:64], 0.0)
    b.op("vector", "memset", G.pwts[0:64, :, 64:128], 0.0)
    b.op("vector", "memset", G.pwts[64:128, :, 0:64], 0.0)
    b.op("vector", "tensor_copy", G.pwst[0:64, :, 0:64], G.stsb[0:64, :, 0:C])
    b.op("vector", "tensor_copy", G.pwst[64:128, :, 64:128], G.stsb[64:128, :, 0:C])
    b.op("vector", "tensor_copy", G.pwts[0:64, :, 0:64], G.ntb[0:64, :, :])
    b.op("vector", "tensor_copy", G.pwts[64:128, :, 64:128], G.ntb[64:128, :, :], inc="dv")

    for lev in range(6):
        b.wait("tensor", "dv"); b.wait("tensor", "sc")
        for hp in range(8):
            mm(b, G.psCH[:, hp, 0:128], G.idb[:, :], G.ysbb[:, hp, :],
               start=True, stop=False)
            mm(b, G.psCH[:, hp, 0:128], G.pwst[:, hp, :], G.ysbb[:, hp, :],
               start=False, stop=True)
        b.wait("vector", "mm")
        b.op("vector", "tensor_copy", G.ysbb[:, :, :], G.psCH[:, :, :], inc="dv")
        if lev < 5:
            b.wait("tensor", "dv"); b.wait("tensor", "sc")
            for hp in range(8):
                mm(b, G.psST[:, hp, 0:128], G.pwts[:, hp, :], G.pwst[:, hp, :])
            if lev < 4:
                for hp in range(8):
                    mm(b, G.psST[:, hp, 128:256], G.pwst[:, hp, :], G.pwts[:, hp, :])
            b.wait("vector", "mm")
            b.op("vector", "tensor_copy", G.pwst[:, :, :], G.psST[:, :, 0:128])
            if lev < 4:
                b.op("vector", "tensor_copy", G.pwts[:, :, :], G.psST[:, :, 128:256])


    b.wait("tensor", "dv"); b.wait("tensor", "sc")
    for h in range(H):
        q = 64 * (h & 1)
        hp = h >> 1
        tp = (q, q) if q else None
        mm(b, G.psCH[q:q + 64, hp, 0:64], G.til[q:q + 64, hp, C:2 * C],
           G.slbf[q:q + 64, hp, :], tp=tp, stop=False)
        mm(b, G.psCH[q:q + 64, hp, 0:64], G.stsb[q:q + 64, hp, C:2 * C],
           G.ysbb[q:q + 64, hp, 0:64], tp=tp, start=False, stop=False)
        mm(b, G.psCH[q:q + 64, hp, 0:64], G.stsb[q:q + 64, hp, 3 * C:4 * C],
           G.vtok[q:q + 64, h * K:(h + 1) * K], tp=tp, start=False)
    b.wait("vector", "mm")
    b.op("vector", "tensor_copy", G.ostg[:, :, :], G.psCH[:, :, 0:64], inc="dv")
    olr = d["sc_oloc"].rearrange("tt (hp par e) -> tt par hp e", par=2, e=K)
    for par in range(2):
        b.wait("gpsimd", "dv")
        dma(b, "gpsimd", olr[ch * C:(ch + 1) * C, par], G.ostg[par * 64:(par + 1) * 64, :, :])

    b.wait("tensor", "dv"); b.wait("tensor", "sc")
    for h in range(H):
        q = 64 * (h & 1)
        hp = h >> 1
        tp = (q, q) if q else None
        mm(b, G.psST[q:q + 64, hp, 0:64], G.trsb[q:q + 64, hp, 0:64],
           G.ysbb[q:q + 64, hp, 0:64], tp=tp, stop=False)
        mm(b, G.psST[q:q + 64, hp, 0:64], G.trsb[q:q + 64, hp, 64:128],
           G.vtok[q:q + 64, h * K:(h + 1) * K], tp=tp, start=False)
    b.wait("vector", "mm")
    tt(b, "vector", G.slst[:, :, :], G.slst[:, :, :],
       G.fcol[:, :, 2:3].broadcast_to([128, 8, K]), AT.mult)
    tt(b, "vector", G.slst[:, :, :], G.slst[:, :, :], G.psST[:, :, 0:64], AT.add, inc="dv")

    b.wait("tensor", "dv"); b.wait("tensor", "sc")
    for h in range(H):
        q = 64 * (h & 1)
        hp = h >> 1
        tp = (q, q) if q else None
        mm(b, G.psST[q:q + 64, hp, 64:128], G.ysbb[q:q + 64, hp, 64:128],
           G.trsb[q:q + 64, hp, 0:64], tp=tp)
    b.wait("vector", "mm")
    dsc = v8(G.scr, 2, K)
    tt(b, "vector", dsc, G.idd2[:, :].unsqueeze(1).broadcast_to([128, 8, 64]),
       G.fcol[:, :, 2:3].broadcast_to([128, 8, K]), AT.mult)
    tt(b, "vector", G.psb[:, :, :], G.psST[:, :, 64:128], dsc, AT.add, inc="dv")
    pr = d["sc_p"].rearrange("c (hp par) k k2 -> c par k hp k2", par=2)
    for par in range(2):
        b.wait("gpsimd", "dv")
        dma(b, "gpsimd", pr[ch, par], G.psb[par * 64:(par + 1) * 64, :, :])
    if ch == 0:
        b.op("vector", "tensor_copy", G.accs[:, :, :], G.psb[:, :, :])
        b.op("vector", "tensor_copy", G.accb[:, :, :], G.psb[:, :, :], inc="dv")
    else:
        b.wait("tensor", "dv"); b.wait("tensor", "sc")
        for h in range(H):
            q = 64 * (h & 1)
            hp = h >> 1
            tp = (q, q) if q else None
            mm(b, G.psCH[q:q + 64, hp, 64:128], G.psb[q:q + 64, hp, :], G.accb[q:q + 64, hp, :], tp=tp)
        b.wait("vector", "mm")
        b.op("vector", "tensor_copy", G.accs[:, :, :], G.psCH[:, :, 64:128])
        b.op("vector", "tensor_copy", G.accb[:, :, :], G.psCH[:, :, 64:128], inc="dv")

    b.wait("tensor", "dv"); b.wait("tensor", "sc")
    for h in range(H):
        q = 64 * (h & 1)
        hp = h >> 1
        tp = (q, q) if q else None
        mm(b, G.psCH[q:q + 64, hp, 0:64], G.stsb[q:q + 64, hp, C:2 * C],
           G.ysbb[q:q + 64, hp, 64:128], tp=tp)
    b.wait("vector", "mm")
    psig = G.scrb[:, 0:1, 0:512].rearrange("p a b -> p (a b)").rearrange("p (a c) -> p a c", c=64)
    b.op("vector", "tensor_copy", psig, G.psCH[:, :, 0:64], inc="dv")
    b.wait("tensor", "dv"); b.wait("tensor", "sc")
    for h in range(H):
        q = 64 * (h & 1)
        hp = h >> 1
        tp = (q, q) if q else None
        mm(b, G.psCH[q:q + 64, hp, 64:128], psig[q:q + 64, hp, :],
           G.idb[q:q + 64, q:q + 64], tp=tp)
    b.wait("vector", "mm")
    tt(b, "vector", G.psis[:, :, :], G.psCH[:, :, 64:128],
       G.til[:, :, C:2 * C], AT.add, inc="dv")
    for par in range(2):
        b.wait("gpsimd", "dv")
        dma(b, "gpsimd",
            d["sc_psi"].rearrange("c (hp par) k t -> c par k hp t", par=2)[ch, par],
            G.psis[par * 64:(par + 1) * 64, :, :])


def emit_relay(b, G, d):
    b.fence()
    b.wait("tensor", "dv"); b.wait("tensor", "sc")
    for h in range(H):
        q = 64 * (h & 1)
        hp = h >> 1
        tp = (q, q) if q else None
        mm(b, G.psCH[q:q + 64, hp, 0:64], G.accs[q:q + 64, hp, :],
           G.idf[q:q + 64, q:q + 64], tp=tp)
    b.wait("vector", "mm")
    pct = v8(G.scr2, 2, 64)
    b.op("vector", "tensor_copy", pct, G.psCH[:, :, 0:64], inc="dv")
    b.wait("gpsimd", "dv")
    for h in range(H):
        q = 64 * (h & 1)
        hp = h >> 1
        dma(b, "gpsimd", d["cc1_in"][h, 0:64, :], pct[q:q + 64, hp, :])
        dma(b, "gpsimd", d["cc1_in"][h, 64:128, :], G.slst[q:q + 64, hp, :])
    b.wait("gpsimd", "gld")
    b.op("gpsimd", "collective_compute", "AllGather", AT.bypass,
         replica_groups=[list(range(8))],
         ins=[d["cc1_in"].opt()], outs=[d["cc1_out"].opt()], inc="cc")
    b.wait("gpsimd", "cc")
    for e in ("vector", "tensor", "sync"):
        b.wait(e, "cc")

    b.op("vector", "memset", G.dsb[:, :, :], 0.0)
    b.op("vector", "memset", G.dbf[:, :, :], 0.0, inc="dv")
    for j in range(7):
        b.wait("sync", "dv")
        b.wait("gpsimd", "mm")
        ccr = d["cc1_out"].rearrange("(j hp par) r k -> j par r hp k", j=8, par=2)
        for par in range(2):
            dma(b, "gpsimd", G.accb[par * 64:(par + 1) * 64, :, :], ccr[j, par, 0:64])
            dma(b, "sync", G.accs[par * 64:(par + 1) * 64, :, :], ccr[j, par, 64:128])
        b.wait("tensor", "ld")
        b.wait("tensor", "gld")
        b.wait("tensor", "dv"); b.wait("tensor", "sc")
        for h in range(H):
            q = 64 * (h & 1)
            hp = h >> 1
            tp = (q, q) if q else None
            mm(b, G.psCH[q:q + 64, hp, 0:64], G.accb[q:q + 64, hp, :], G.dbf[q:q + 64, hp, :], tp=tp)
        b.wait("vector", "mm")
        b.wait("vector", "ld")
        b.wait("vector", "gld")
        tmp = v8(G.scr, 2, 64)
        tt(b, "vector", tmp, G.psCH[:, :, 0:64], G.accs[:, :, :], AT.add)
        ts(b, "vector", tmp, tmp, G.flgb[:, j:j + 1], None, AT.mult)
        ts(b, "vector", G.dsb[:, :, :], G.dsb[:, :, :], G.flgb[:, 16 + j:17 + j], None, AT.mult)
        tt(b, "vector", G.dsb[:, :, :], G.dsb[:, :, :], tmp, AT.add)
        b.op("vector", "tensor_copy", G.dbf[:, :, :], G.dsb[:, :, :], inc="dv")
    if DEBUG:
        b.wait("gpsimd", "dv")
        dma(b, "gpsimd", d["sc_dbg"], G.dsb[:, :, :])
    b.fence()


def emit_p3(b, G, d):
    for grp in range(NGRP):
        gs = grp * GRP
        for half in range(2):
            ti = gs // 128 + half
            ln_bcast(b, G, G.stats, ti)
            hs = slice(half * 128, (half + 1) * 128)
            for db in range(8):
                tt(b, "vector", G.scr[:, 0, 0:128], G.xres[:, db, gs + half * 128:gs + (half + 1) * 128],
                   G.psA[:, 0:128], AT.subtract, inc=None)
                tt(b, "vector", G.xng[:, db, hs], G.scr[:, 0, 0:128], G.psB[:, 0:128], AT.mult, inc=None)
        guhist = {}
        guev = {}
        b.wait("tensor", "dv"); b.wait("tensor", "sc")
        for hb in range(FF0 // 128):
            even = hb % 2 == 0
            wt, wt2 = (G.wtile, G.wtile2) if even else (G.wtileB, G.wtile2B)
            pg = G.psA[:, 0:GRP] if even else G.psST[:, 0, 0:GRP]
            pu = G.psB[:, 0:GRP] if even else G.psST[:, 2, 0:GRP]
            scp = G.scr[:, 0, :] if even else G.scr[:, 1, :]
            lag = guhist.get(hb - 2, G._p3gu.get(grp - 1, 0) if hasattr(G, "_p3gu") else 0)
            if lag:
                b.wait("sync", "mm", val=lag)
            dma(b, "sync", wt[:, :, :],
                d["sc_wg"][:, hb * 128:(hb + 1) * 128].rearrange("(db p) e -> p db e", p=128))
            dma(b, "sync", wt2[:, :, :],
                d["sc_wu"][:, hb * 128:(hb + 1) * 128].rearrange("(db p) e -> p db e", p=128))
            b.wait("tensor", "ld")
            if hb >= 2:
                b.wait("tensor", "sc", val=guev[hb - 2][0])
                b.wait("tensor", "dv", val=guev[hb - 2][1])
            for db in range(8):
                mm(b, pg, wt[:, db, :], G.xng[:, db, :], start=(db == 0), stop=(db == 7))
            for db in range(8):
                mm(b, pu, wt2[:, db, :], G.xng[:, db, :], start=(db == 0), stop=(db == 7))
            guhist[hb] = b.semv["mm"]
            b.wait("scalar", "mm")
            if hb >= 2:
                b.wait("scalar", "dv", val=guev[hb - 2][1])
            act(b, scp, pg, AF.Silu, scale=1.0 / 127.0, inc="sc")
            b.wait("vector", "sc")
            stt(b, "vector", G.hstr[:, hb, :], pu, 1.0 / 127.0, scp,
                AT.mult, AT.mult, inc="dv")
            guev[hb] = (b.semv["sc"], b.semv["dv"])
        if not hasattr(G, "_p3gu"):
            G._p3gu = {}
        G._p3gu[grp] = guhist[FF0 // 128 - 1]
        def dpsum(eb):
            return DPSUM(G, eb)
        dnhist = {}
        b.wait("tensor", "dv"); b.wait("tensor", "sc")
        for hb in range(FF0 // 128):
            wdt = G.wdtile if hb % 2 == 0 else G.wdtileB
            lag = dnhist.get(hb - 2, G._p3dn.get(grp - 1, 0) if hasattr(G, "_p3dn") else 0)
            if lag:
                b.wait("sync", "mm", val=lag)
            dma(b, "sync", wdt[:, :], d["sc_wd"][hb * 128:(hb + 1) * 128, :])
            b.wait("tensor", "ld")
            for eb in range(8):
                mm(b, dpsum(eb), wdt[:, eb * 128:(eb + 1) * 128], G.hstr[:, hb, :],
                   start=(hb == 0), stop=(hb == FF0 // 128 - 1))
            dnhist[hb] = b.semv["mm"]
        if not hasattr(G, "_p3dn"):
            G._p3dn = {}
        G._p3dn[grp] = dnhist[FF0 // 128 - 1]
        b.wait("vector", "mm")
        for eb in range(8):
            stt(b, "vector", G.xres[:, eb, gs:gs + GRP], dpsum(eb), 1.0 / 127.0,
                G.xres[:, eb, gs:gs + GRP], AT.mult, AT.add)
    b.fence()

    for i, srcrow, tgt in ((0, d["gnwr"], G.gnwb), (1, d["gnbr"], G.gnbb)):
        b.wait("sync", "mm")
        b.wait("sync", "dv")
        dma(b, "sync", G.rowscr[0:1, 0:512], srcrow[0:1, 0:512])
        b.wait("tensor", "ld")
        mm(b, G.psA[:, 0:512], G.oner[:, :], G.rowscr[0:1, 0:512], inc="mm")
        b.wait("sync", "mm")
        dma(b, "sync", G.rowscr[0:1, 0:512], srcrow[0:1, 512:1024])
        b.wait("vector", "mm")
        b.op("vector", "tensor_copy", tgt[:, 0:512], G.psA[:, 0:512], inc="dv")
        b.wait("tensor", "ld")
        b.wait("tensor", "dv"); b.wait("tensor", "sc")
        mm(b, G.psA[:, 0:512], G.oner[:, :], G.rowscr[0:1, 0:512], inc="mm")
        b.wait("vector", "mm")
        b.op("vector", "tensor_copy", tgt[:, 512:1024], G.psA[:, 0:512], inc="dv")
    dma(b, "sync", G.bocol[:, :, :], d["bor"].unsqueeze(2), slow=True)
    dma(b, "gpsimd", G.wotb[:, :, :],
        d["wot"].rearrange("(db p) e -> p db e", p=128))
    b.fence()

    for ti in range(16):
        b.wait("sync", "dv")
        dma(b, "sync", G.scrb[:, 0, :], d["sc_oloc"][ti * 128:(ti + 1) * 128, :])
        dma(b, "sync", G.scrb[:, 1, :], d["sc_g"][ti * 128:(ti + 1) * 128, :])
        for cpar in range(2):
            ch = 2 * ti + cpar
            b.wait("sync", "mm")
            psir = d["sc_psi"].rearrange("c (hp par) k t -> c par k hp t", par=2)
            pldr = d["sc_p"].rearrange("c (hp par) k k2 -> c par k hp k2", par=2)
            for par in range(2):
                dma(b, "sync", G.psil[par * 64:(par + 1) * 64, :, :], psir[ch, par])
                dma(b, "sync", G.pldl[par * 64:(par + 1) * 64, :, :], pldr[ch, par])
            b.wait("tensor", "ld")
            b.wait("tensor", "dv"); b.wait("tensor", "sc")
            for h in range(H):
                q = 64 * (h & 1)
                hp = h >> 1
                mm(b, G.psST[cpar * 64:(cpar + 1) * 64, h >> 2, (h % 4) * 64:(h % 4) * 64 + 64],
                   G.psil[q:q + 64, hp, :], G.dbf[q:q + 64, hp, :],
                   tp=(q, cpar * 64) if (q or cpar) else None)
            for h in range(H):
                q = 64 * (h & 1)
                hp = h >> 1
                mm(b, G.psCH[q:q + 64, hp, 0:64], G.pldl[q:q + 64, hp, :],
                   G.dbf[q:q + 64, hp, :], tp=(q, q) if q else None)
            b.wait("vector", "mm")
            b.op("vector", "tensor_copy", G.dsb[:, :, :], G.psCH[:, :, 0:64])
            b.op("vector", "tensor_copy", G.dbf[:, :, :], G.dsb[:, :, :], inc="dv")
        b.wait("vector", "ld")
        tt(b, "vector", G.otr_f[:, :], G.scrb[:, 0, :], G.psST[:, 0:4, :].rearrange("p a c -> p (a c)"), AT.add)
        gm = G.otr_f[:, :].rearrange("p (h k) -> p h k", h=16)
        b.op("vector", "reduce_sum", G.gnsc[:, :, 0:1], gm, axis=AX, inc="dv")
        b.wait("scalar", "dv")
        act(b, G.scr[:, 0:4, :].rearrange("p a c -> p (a c)"), G.otr_f[:, :], AF.Square, inc="sc")
        b.wait("vector", "sc")
        b.op("vector", "reduce_sum", G.gnsc[:, :, 1:2],
             G.scr[:, 0:4, :].rearrange("p a c -> p (a c)").rearrange("p (h k) -> p h k", h=16), axis=AX)
        ts(b, "vector", G.gnsc[:, :, 2:3], G.gnsc[:, :, 0:1], 1.0 / 64, None, AT.mult)
        ts(b, "vector", G.gnsc[:, :, 3:4], G.gnsc[:, :, 1:2], 1.0 / 64, None, AT.mult)
        tt(b, "vector", G.gnsc[:, :, 4:5], G.gnsc[:, :, 2:3], G.gnsc[:, :, 2:3], AT.mult)
        tt(b, "vector", G.gnsc[:, :, 5:6], G.gnsc[:, :, 3:4], G.gnsc[:, :, 4:5], AT.subtract)
        ts(b, "vector", G.gnsc[:, :, 5:6], G.gnsc[:, :, 5:6], GN_EPS, None, AT.add, inc="dv")
        b.wait("scalar", "dv")
        b.op("scalar", "sqrt", G.gnsc[:, :, 6:7], G.gnsc[:, :, 5:6], inc="sc")
        b.wait("vector", "sc")
        b.op("vector", "reciprocal", G.gnsc[:, :, 7:8], G.gnsc[:, :, 6:7])
        tt(b, "vector", gm, gm, G.gnsc[:, :, 2:3].broadcast_to([128, 16, 64]), AT.subtract)
        tt(b, "vector", gm, gm, G.gnsc[:, :, 7:8].broadcast_to([128, 16, 64]), AT.mult)
        flat = G.otr_f[:, :]
        tt(b, "vector", flat, flat, G.gnwb[:, :], AT.mult)
        tt(b, "vector", flat, flat, G.gnbb[:, :], AT.add)
        tt(b, "vector", flat, flat, G.scrb[:, 1, :], AT.mult, inc="dv")
        b.wait("tensor", "dv"); b.wait("tensor", "sc")
        for db in range(8):
            tr(b, G.psST[:, db, 0:128], flat[:, db * 128:(db + 1) * 128], G.idf[:, :], inc="mm")
        b.wait("vector", "mm")
        b.op("vector", "tensor_copy", G.ztl[:, :, :], G.psST[:, :, 0:128], inc="dv")
        b.wait("tensor", "dv"); b.wait("tensor", "sc")
        for eb in range(8):
            pst = G.psA if eb % 2 == 0 else G.psB
            for db in range(8):
                mm(b, pst[:, 0:128], G.wotb[:, db, eb * 128:(eb + 1) * 128],
                   G.ztl[:, db, :], start=(db == 0), stop=(db == 7))
            b.wait("vector", "mm")
            ts(b, "vector", G.scr2[:, 0, 0:128], pst[:, 0:128], G.bocol[:, eb, 0:1], None, AT.add)
            tt(b, "vector", G.xres[:, eb, ti * 128:(ti + 1) * 128],
               G.xres[:, eb, ti * 128:(ti + 1) * 128], G.scr2[:, 0, 0:128], AT.add, inc="dv")
            b.wait("tensor", "dv"); b.wait("tensor", "sc")
    b.wait("tensor", "dv"); b.wait("tensor", "sc")
    for db in range(8):
        pst = G.psA if db % 2 == 0 else G.psB
        b.wait("tensor", "dv"); b.wait("tensor", "sc")
        tr(b, pst[0:64, 256:384], G.xres[:, db, TLOC - 64:TLOC], G.idf[:, :], inc="mm")
        b.wait("vector", "mm")
        b.op("vector", "tensor_copy", G.tokst[0:64, db * 128:(db + 1) * 128], pst[0:64, 256:384], inc="dv")
    b.wait("gpsimd", "dv")
    dma(b, "gpsimd", d["cc2_in"][0:64, :], G.tokst[0:64, :])
    b.fence()


def emit_p4(b, G, d):
    for db in range(8):
        dma(b, "gpsimd", G.encvb[:, db, :].rearrange("p (h c) -> p h c", h=8),
            d["encv"][:, db * 128:(db + 1) * 128, :].rearrange("h p e -> p h e"))
    for ti in range(16):
        tsl = slice(ti * 128, (ti + 1) * 128)
        b.wait("scalar", "dv")
        act(b, G.scr[:, 0:4, :].rearrange("p a c -> p (a c)").rearrange("p (a c) -> p a c", a=8)[:, :, 0:128],
            G.xres[:, :, tsl], AF.Square, inc="sc")
        b.wait("tensor", "sc")
        for db in range(8):
            mm(b, G.psCH[0:1, 0, 0:128], G.onec[:, :], G.xres[:, db, tsl],
               start=(db == 0), stop=(db == 7))
        for db in range(8):
            mm(b, G.psCH[0:1, 1, 0:128], G.onec[:, :],
               G.scr[:, 0:4, :].rearrange("p a c -> p (a c)").rearrange("p (a c) -> p a c", a=8)[:, db, 0:128],
               start=(db == 0), stop=(db == 7))
        b.wait("vector", "mm")
        ts(b, "vector", G.rowscr[0:1, 0:128], G.psCH[0:1, 0, 0:128], 1.0 / D, None, AT.mult)
        ts(b, "vector", G.rowscr[0:1, 128:256], G.psCH[0:1, 1, 0:128], 1.0 / D, None, AT.mult)
        tt(b, "vector", G.rowscr[0:1, 256:384], G.rowscr[0:1, 0:128], G.rowscr[0:1, 0:128], AT.mult)
        tt(b, "vector", G.rowscr[0:1, 128:256], G.rowscr[0:1, 128:256], G.rowscr[0:1, 256:384], AT.subtract)
        ts(b, "vector", G.rowscr[0:1, 128:256], G.rowscr[0:1, 128:256], 1e-5, None, AT.add, inc="dv")
        b.wait("scalar", "dv")
        b.op("scalar", "sqrt", G.rowscr[0:1, 128:256], G.rowscr[0:1, 128:256], inc="sc")
        b.wait("vector", "sc")
        b.op("vector", "reciprocal", G.rowscr[0:1, 128:256], G.rowscr[0:1, 128:256], inc="dv")
        b.wait("tensor", "dv"); b.wait("tensor", "sc")
        tr(b, G.psCH[:, 2, 0:1], G.rowscr[0:1, 0:128], G.idf[0:1, 0:1]) if False else None
        mm(b, G.psCH[:, 2, 0:1], G.rowscr[0:1, 0:128], G.onec[0:1, 0:1])
        mm(b, G.psCH[:, 3, 0:1], G.rowscr[0:1, 128:256], G.onec[0:1, 0:1], inc="mm")
        b.wait("vector", "mm")
        b.op("vector", "tensor_copy", G.stats2[:, ti, 0:1], G.psCH[:, 2, 0:1])
        b.op("vector", "tensor_copy", G.stats2[:, ti, 1:2], G.psCH[:, 3, 0:1], inc="dv")
    b.wait("gpsimd", "gld")
    for e in ("tensor", "vector", "scalar"):
        b.wait(e, "gld")
    for grp in range(NGRP):
        gs = grp * GRP
        for half in range(2):
            ti = gs // 128 + half
            ln_bcast(b, G, G.stats2, ti)
            hs = slice(half * 128, (half + 1) * 128)
            for db in range(8):
                tt(b, "vector", G.scr[:, 0, 0:128], G.xres[:, db, gs + half * 128:gs + (half + 1) * 128],
                   G.psA[:, 0:128], AT.subtract, inc=None)
                tt(b, "vector", G.xng[:, db, hs], G.scr[:, 0, 0:128], G.psB[:, 0:128], AT.mult, inc=None)
        b.wait("tensor", "dv"); b.wait("tensor", "sc")
        for h in range(NH):
            pst = G.psA if h % 2 == 0 else G.psB
            b.wait("sync", "mm")
            dma(b, "sync", G.encb[:, :, :],
                d["sc_encq"][h].rearrange("(db p) e -> p db e", p=128))
            b.wait("tensor", "ld")
            for db in range(8):
                mm(b, pst[:, 0:GRP], G.encb[:, db, :],
                   G.xng[:, db, :], start=(db == 0), stop=(db == 7))
            b.wait("scalar", "mm")
            act(b, G.qts[:, h, 64 + gs:64 + gs + GRP], pst[:, 0:GRP], AF.Relu, inc="dv")
            b.wait("tensor", "dv"); b.wait("tensor", "sc")
        for half in range(2):
            tc = gs // 128 + half
            pst = G.psA if half == 0 else G.psB
            b.wait("tensor", "dv"); b.wait("tensor", "sc")
            for db in range(8):
                mm(b, pst[:, 0:512], G.xng[:, db, half * 128:(half + 1) * 128],
                   G.encvb[:, db, 0:512], start=(db == 0), stop=(db == 7))
            b.wait("scalar", "mm")
            act(b, G.vts[:, tc, 0:512], pst[:, 0:512], AF.Relu, inc="dv")
            b.wait("tensor", "dv"); b.wait("tensor", "sc")
            for db in range(8):
                mm(b, pst[:, 0:512], G.xng[:, db, half * 128:(half + 1) * 128],
                   G.encvb[:, db, 512:1024], start=(db == 0), stop=(db == 7))
            b.wait("scalar", "mm")
            act(b, G.vts[:, tc, 512:1024], pst[:, 0:512], AF.Relu, inc="dv")
    b.fence()
    for h in range(NH):
        b.op("vector", "reduce_sum", G.sigc[:, h, 0:1], G.qts[:, h, 64:64 + TLOC], axis=AX)
        b.op("vector", "reduce_sum", G.sigc[:, h, 1:2], G.qts[:, h, 64:64 + TLOC:8], axis=AX)
    b.wait("gpsimd", "dv")
    dma(b, "gpsimd", d["cc2_in"][64:72, 0:128].rearrange("h p -> p h"), G.sigc[:, :, 0], slow=True)
    dma(b, "gpsimd", d["cc2_in"][72:80, 0:128].rearrange("h p -> p h"), G.sigc[:, :, 1], slow=True)
    b.op("vector", "memset", G.tokst2[0:16, 0:896], 0.0, inc="dv")
    b.wait("gpsimd", "dv")
    dma(b, "gpsimd", d["cc2_in"][64:80, 128:1024], G.tokst2[0:16, 0:896])
    b.wait("gpsimd", "gld")
    b.op("gpsimd", "collective_compute", "AllGather", AT.bypass,
         replica_groups=[list(range(8))],
         ins=[d["cc2_in"].opt()], outs=[d["cc2_out"].opt()], inc="cc")
    b.wait("gpsimd", "cc")
    for e in ("tensor", "vector", "scalar", "sync"):
        b.wait(e, "cc")


def emit_p5(b, G, d):
    b.op("vector", "memset", G.tokst[:, :], 0.0, inc="dv")
    for j in range(8):
        b.wait("sync", "dv")
        dma(b, "sync", G.tokst2[0:64, :], d["cc2_out"][j * 80:j * 80 + 64, :])
        b.wait("vector", "ld")
        ts(b, "vector", G.tokst2[0:64, :], G.tokst2[0:64, :], G.flgb[0:64, 24 + j:25 + j], None, AT.mult)
        tt(b, "vector", G.tokst[0:64, :], G.tokst[0:64, :], G.tokst2[0:64, :], AT.add, inc="dv")
    b.op("vector", "reduce_sum", G.colsc[0:64, 0:1], G.tokst[0:64, :], axis=AX, inc="dv")
    b.wait("scalar", "dv")
    act(b, G.scr[0:64, 0:4, :].rearrange("p a c -> p (a c)"), G.tokst[0:64, :], AF.Square, inc="sc")
    b.wait("vector", "sc")
    b.op("vector", "reduce_sum", G.colsc[0:64, 1:2], G.scr[0:64, 0:4, :].rearrange("p a c -> p (a c)"), axis=AX)
    ts(b, "vector", G.colsc[0:64, 2:3], G.colsc[0:64, 0:1], 1.0 / D, None, AT.mult)
    ts(b, "vector", G.colsc[0:64, 3:4], G.colsc[0:64, 1:2], 1.0 / D, None, AT.mult)
    tt(b, "vector", G.colsc[0:64, 4:5], G.colsc[0:64, 2:3], G.colsc[0:64, 2:3], AT.mult)
    tt(b, "vector", G.colsc[0:64, 5:6], G.colsc[0:64, 3:4], G.colsc[0:64, 4:5], AT.subtract)
    ts(b, "vector", G.colsc[0:64, 5:6], G.colsc[0:64, 5:6], 1e-5, None, AT.add, inc="dv")
    b.wait("scalar", "dv")
    b.op("scalar", "sqrt", G.colsc[0:64, 6:7], G.colsc[0:64, 5:6], inc="sc")
    b.wait("vector", "sc")
    b.op("vector", "reciprocal", G.colsc[0:64, 7:8], G.colsc[0:64, 6:7])
    ts(b, "vector", G.tokst[0:64, :], G.tokst[0:64, :], G.colsc[0:64, 2:3], G.colsc[0:64, 7:8],
       AT.subtract, AT.mult, inc="dv")
    b.wait("tensor", "dv"); b.wait("tensor", "sc")
    for db in range(8):
        pst = G.psA if db % 2 == 0 else G.psB
        b.wait("tensor", "dv"); b.wait("tensor", "sc")
        tr(b, pst[:, 0:64], G.tokst[0:64, db * 128:(db + 1) * 128], G.idf[0:64, 0:64], inc="mm")
        b.wait("vector", "mm")
        b.op("vector", "tensor_copy", G.xhalT[:, db, :], pst[:, 0:64], inc="dv")
    b.wait("tensor", "dv"); b.wait("tensor", "sc")
    for h in range(NH):
        pst = G.psA if h % 2 == 0 else G.psB
        b.wait("sync", "mm")
        dma(b, "sync", G.encb[:, :, :],
            d["sc_encq"][h].rearrange("(db p) e -> p db e", p=128))
        b.wait("tensor", "ld")
        for db in range(8):
            mm(b, pst[:, 64:128], G.encb[:, db, :],
               G.xhalT[:, db, :], start=(db == 0), stop=(db == 7))
        b.wait("scalar", "mm")
        act(b, G.qts[:, h, 0:64], pst[:, 64:128], AF.Relu, inc="dv")
        b.wait("tensor", "dv"); b.wait("tensor", "sc")
    for eh in range(2):
        pst = G.psA if eh == 0 else G.psB
        b.wait("tensor", "dv"); b.wait("tensor", "sc")
        for db in range(8):
            mm(b, pst[64:128, 0:512], G.xhalT[:, db, :], G.encvb[:, db, eh * 512:(eh + 1) * 512],
               start=(db == 0), stop=(db == 7), tp=(0, 64))
        b.wait("scalar", "mm")
        act(b, G.vhalo[64:128, eh * 512:(eh + 1) * 512], pst[64:128, 0:512], AF.Relu, inc="dv")
    b.op("vector", "memset", G.sigc[:, :, 2:4], 0.0, inc="dv")
    for j in range(8):
        b.wait("sync", "dv")
        dma(b, "sync", G.mlpc[:, :, 0], d["cc2_out"][j * 80 + 64:j * 80 + 72, 0:128].rearrange(
            "h p -> p h"), slow=True)
        dma(b, "sync", G.mlpc[:, :, 1], d["cc2_out"][j * 80 + 72:j * 80 + 80, 0:128].rearrange(
            "h p -> p h"), slow=True)
        b.wait("vector", "ld")
        ts(b, "vector", G.mlpc[:, :, 0:2], G.mlpc[:, :, 0:2], G.flgb[:, 8 + j:9 + j], None, AT.mult)
        tt(b, "vector", G.sigc[:, :, 2:4], G.sigc[:, :, 2:4], G.mlpc[:, :, 0:2], AT.add, inc="dv")
    ts(b, "vector", G.sigc[:, :, 2:3], G.sigc[:, :, 2:3], 1.0 / 8192.0, None, AT.mult)
    ts(b, "vector", G.sigc[:, :, 3:4], G.sigc[:, :, 3:4], 0.5 / 1024.0, None, AT.mult)
    tt(b, "vector", G.sigc[:, :, 2:3], G.sigc[:, :, 2:3], G.sigc[:, :, 3:4], AT.add)
    b.op("vector", "tensor_copy", G.scrb[:, 0, 0:8], G.sigc[:, :, 2], inc="dv")
    sigb = G.scrb[:, 0, 0:8].rearrange("p (a c) -> p a c", a=8)
    b.wait("gpsimd", "dv")
    dma(b, "gpsimd", G.encvb_f[:, 0:4096].rearrange("p (a c) -> p a c", a=8),
        d["mw1"].rearrange("(db p) e -> p db e", p=128))
    b.wait("sync", "dv")
    dma(b, "sync", G.mlpc[:, 0:4, 0:1], d["mb1"].unsqueeze(2), slow=True)
    dma(b, "sync", G.mlpc[:, :, 2:3], d["mb2"].unsqueeze(2), slow=True)
    dma(b, "sync", G.bmc[:, :, 0:1], d["basem"].unsqueeze(2), slow=True)
    b.wait("tensor", "gld")
    b.wait("tensor", "ld")
    for e4 in range(4):
        for db in range(8):
            mm(b, G.psCH[:, 4, e4:e4 + 1], G.encvb[:, db, e4 * 128:(e4 + 1) * 128],
               sigb[:, db, 0:1], start=(db == 0), stop=(db == 7))
    b.wait("vector", "mm")
    b.wait("vector", "ld")
    ts(b, "vector", G.mlpc[:, 0:4, 1:2].flatten().unsqueeze(1) if False else G.scr[:, 0, 0:4],
       G.psCH[:, 4, 0:4], 1.0, None, AT.mult)
    tt(b, "vector", G.scr[:, 0, 0:4], G.scr[:, 0, 0:4], G.mlpc[:, 0:4, 0], AT.add, inc="dv")
    b.wait("scalar", "dv")
    act(b, G.scr[:, 1, 0:4], G.scr[:, 0, 0:4], AF.Gelu, inc="sc")
    b.wait("vector", "sc")
    b.op("vector", "tensor_copy", G.scrb[:, 0, 16:20], G.scr[:, 1, 0:4], inc="dv")
    h1b = G.scrb[:, 0, 16:20].rearrange("p (a c) -> p a c", a=4)
    b.wait("gpsimd", "dv")
    dma(b, "gpsimd", G.encvb_f[:, 0:4096].rearrange("p (a c) -> p a c", a=4),
        d["mw2"].rearrange("(db p) e -> p db e", p=128))
    b.wait("tensor", "gld")
    for eb in range(8):
        for db in range(4):
            mm(b, G.psCH[:, 5, eb:eb + 1], G.encvb[:, db, eb * 128:(eb + 1) * 128],
               h1b[:, db, 0:1], start=(db == 0), stop=(db == 3))
    b.wait("vector", "mm")
    tt(b, "vector", G.scr[:, 0, 0:8], G.psCH[:, 5, 0:8], G.mlpc[:, :, 2], AT.add)
    ts(b, "vector", G.scr[:, 0, 0:8], G.scr[:, 0, 0:8], 0.1, None, AT.mult)
    tt(b, "vector", G.scr[:, 0, 0:8], G.scr[:, 0, 0:8], G.bmc[:, :, 0], AT.add, inc="dv")
    b.wait("scalar", "dv")
    act(b, G.scr[:, 1, 0:8], G.scr[:, 0, 0:8], AF.Sigmoid, scale=-1.0, inc="sc")
    b.wait("vector", "sc")
    b.op("vector", "tensor_scalar_max", G.scr[:, 1, 0:8], G.scr[:, 1, 0:8], 1e-38, inc="dv")
    b.wait("scalar", "dv")
    act(b, G.scr[:, 2, 0:8], G.scr[:, 1, 0:8], AF.Ln, inc="sc")
    b.wait("vector", "sc")
    ts(b, "vector", G.metr[:, :, 0:1], G.scr[:, 2, 0:8].unsqueeze(2) if False else
       G.scr[:, 2, 0:8].rearrange("p (a c) -> p a c", a=8), -1.0, None, AT.mult, inc="dv")
    for h in range(NH):
        b.wait("gpsimd", "dv")
        b.wait("gpsimd", "mm")
        dma(b, "gpsimd", G.scrb[0:32, 1, 0:128], d["hut"][h])
        dma(b, "gpsimd", G.scrb[0:32, 1, 128:256], d["hvv"][h])
        b.wait("tensor", "gld")
        mm(b, G.psCH[:, 6, 0:128], G.scrb[0:32, 1, 0:128], G.scrb[0:32, 1, 128:256], inc="mm")
        b.wait("vector", "mm")
        b.op("vector", "tensor_copy", G.uvs[:, h * 128:(h + 1) * 128], G.psCH[:, 6, 0:128], inc="dv")
    b.fence()


def emit_p6a(b, G, d):
    for h in range(NH):
        for c in range(17):
            if c == 16:
                qsl = G.qts[:, h, 0:64]
                rows = 64
            else:
                qsl = G.qts[:, h, 64 + c * 128:64 + (c + 1) * 128]
                rows = 128
            b.wait("vector", "mm")
            ts(b, "vector", G.qmb[:, 0:rows], qsl, G.metr[:, h, 0:1], None, AT.mult)
            tt(b, "vector", G.qsq[:, 0:rows], G.qmb[:, 0:rows], qsl, AT.mult)
            tt(b, "vector", G.scr[:, 0, 0:rows], qsl, qsl, AT.mult, inc="dv")
            b.wait("tensor", "dv"); b.wait("tensor", "sc")
            tgt = G.acols[0:rows, h, 16:17] if c == 16 else G.acols[:, h, c:c + 1]
            tg2 = G.sqnc[0:rows, h, 16:17] if c == 16 else G.sqnc[:, h, c:c + 1]
            mm(b, G.psCH[0:rows, 7, 0:1], G.qsq[:, 0:rows], G.onec[:, 0:1])
            mm(b, G.psCH[0:rows, 7, 1:2], G.scr[:, 0, 0:rows], G.onec[:, 0:1], inc="mm")
            b.wait("vector", "mm")
            b.op("vector", "tensor_copy", tgt, G.psCH[0:rows, 7, 0:1])
            b.op("vector", "tensor_copy", tg2, G.psCH[0:rows, 7, 1:2], inc="dv")

    for c in range(16):
        for h in range(NH):
            qsl = G.qts[:, h, 64 + c * 128:64 + (c + 1) * 128]
            b.wait("tensor", "dv"); b.wait("tensor", "sc")
            if c == 0:
                mm(b, G.psCH[0:1, 6, 0:64], G.acols[0:64, h, 16:17], G.idf[0:64, 0:64])
            else:
                mm(b, G.psCH[0:1, 6, 0:64], G.acols[64:128, h, c - 1:c],
                   G.idf[64:128, 64:128], tp=(64, 0))
            mm(b, G.psCH[0:1, 7, 0:128], G.acols[:, h, c:c + 1], G.idf[:, :], inc="mm")
            b.wait("vector", "mm")
            b.op("vector", "tensor_copy", G.rowscr[0:1, 256:320], G.psCH[0:1, 6, 0:64])
            b.op("vector", "tensor_copy", G.rowscr[0:1, 320:448], G.psCH[0:1, 7, 0:128], inc="dv")
            b.wait("tensor", "dv"); b.wait("tensor", "sc")
            mm(b, G.psB[:, 0:192], G.oner[:, :], G.rowscr[0:1, 256:448])
            ts(b, "vector", G.qmb[:, :], qsl, G.metr[:, h, 0:1], None, AT.mult, inc="dv")
            b.wait("tensor", "dv"); b.wait("tensor", "sc")
            mm(b, G.psA[:, 0:192], G.qmb[:, :], G.qts[:, h, c * 128:c * 128 + 192], inc="mm")
            b.wait("vector", "mm")
            ts(b, "vector", G.colsc[:, 16:17], G.acols[:, h, c:c + 1], 1e-8, None, AT.add)
            ts(b, "vector", G.dtl[:, :], G.psA[:, 0:192], -2.0, G.colsc[:, 16:17], AT.mult, AT.add)
            tt(b, "vector", G.dtl[:, :], G.dtl[:, :], G.psB[:, 0:192], AT.add)
            b.op("vector", "tensor_scalar_max", G.dtl[:, :], G.dtl[:, :], 1e-8, inc="dv")
            b.wait("scalar", "dv")
            b.op("scalar", "sqrt", G.dtl[:, :], G.dtl[:, :], inc="sc")
            b.wait("vector", "sc")
            if c == 0:
                tt(b, "vector", G.colsc[:, 17:18], G.dtl[:, 64:65], G.flgb[:, 7:8], AT.mult)
                ts(b, "vector", G.dtl[:, 0:64], G.dtl[:, 0:64], G.flgb[:, 23:24], None, AT.mult)
                tt(b, "vector", G.dtl[:, 0:64], G.dtl[:, 0:64],
                   G.colsc[:, 17:18].broadcast_to([128, 64]), AT.add)
            stt(b, "vector", G.neg1[:, :], G.dtl[:, :], -1.0, G.bndb[:, :], AT.mult, AT.subtract, inc="dv")
            b.wait("scalar", "dv")
            act(b, G.expd[:, :], G.neg1[:, :], AF.Exp, inc="sc")
            b.wait("vector", "sc")
            cur, nxt = G.neg1, G.neg2
            for it in range(4):
                b.op("vector", "max", G.m8[:, :], cur[:, :])
                b.op("vector", "match_replace", nxt[:, :], G.m8[:, :], cur[:, :], MARK)
                cur, nxt = nxt, cur
            ts(b, "vector", G.wse[:, :], cur[:, :], -0.5e20, None, AT.is_le)
            tt(b, "vector", G.wse[:, :], G.wse[:, :], G.expd[:, :], AT.mult, inc="dv")
            if c == 0:
                b.op("vector", "reduce_sum", G.rsum[:, 0:1], G.wse[:, 0:64], axis=AX)
                tt(b, "vector", G.rsum[:, 1:2], G.rsum[:, 0:1], G.flgb[:, 7:8], AT.mult)
                ts(b, "vector", G.wse[:, 0:64], G.wse[:, 0:64], G.flgb[:, 23:24], None, AT.mult)
                tt(b, "vector", G.wse[:, 64:65], G.wse[:, 64:65], G.rsum[:, 1:2], AT.add, inc="dv")
            b.op("vector", "reduce_sum", G.rsum[:, 2:3], G.wse[:, :], axis=AX)
            ts(b, "vector", G.rsum[:, 2:3], G.rsum[:, 2:3], 1e-8, None, AT.add)
            b.op("vector", "reciprocal", G.rsum[:, 3:4], G.rsum[:, 2:3])
            ts(b, "vector", G.wse[:, :], G.wse[:, :], G.rsum[:, 3:4], None, AT.mult)
            b.op("vector", "tensor_copy", G.wsb[:, :], G.wse[:, :], inc="dv")
            b.wait("tensor", "dv"); b.wait("tensor", "sc")
            mm(b, G.psCH[64:128, 0, 0:128], G.wsb[:, 0:64], G.idb[:, :], tp=(0, 64))
            mm(b, G.psCH[0:64, 1, 0:128], G.wsb[:, 64:128], G.idb[:, :])
            mm(b, G.psCH[64:128, 2, 0:128], G.wsb[:, 128:192], G.idb[:, :],
               tp=(0, 64), inc="mm")
            b.wait("vector", "mm")
            b.op("vector", "tensor_copy", G.wtb[64:128, 0:128], G.psCH[64:128, 0, 0:128])
            b.op("vector", "tensor_copy", G.wtb[0:64, 128:256], G.psCH[0:64, 1, 0:128])
            b.op("vector", "tensor_copy", G.wtb[64:128, 256:384],
                 G.psCH[64:128, 2, 0:128], inc="dv")
            b.wait("tensor", "dv"); b.wait("tensor", "sc")
            vprev = G.vhalo[64:128, h * 128:(h + 1) * 128] if c == 0 else                 G.vts[64:128, c - 1, h * 128:(h + 1) * 128]
            mm(b, G.psCH[:, 3, 0:128], G.wtb[64:128, 0:128], vprev, tp=(64, 0), stop=False)
            mm(b, G.psCH[:, 3, 0:128], G.wtb[0:64, 128:256],
               G.vts[0:64, c, h * 128:(h + 1) * 128], start=False, stop=False)
            mm(b, G.psCH[:, 3, 0:128], G.wtb[64:128, 256:384],
               G.vts[64:128, c, h * 128:(h + 1) * 128], tp=(64, 0), start=False)
            mm(b, G.psCH[:, 4, 0:128], qsl, G.uvs[:, h * 128:(h + 1) * 128])
            mm(b, G.psCH[:, 5, 0:128], qsl, G.idb[:, :], inc="mm")
            b.wait("vector", "mm")
            ts(b, "vector", G.colsc[:, 18:19], G.sqnc[:, h, c:c + 1], 0.04, None, AT.is_gt, inc="dv")
            b.wait("scalar", "dv")
            b.op("scalar", "sqrt", G.colsc[:, 19:20], G.sqnc[:, h, c:c + 1], inc="sc")
            b.wait("vector", "sc")
            b.op("vector", "tensor_scalar_max", G.colsc[:, 19:20], G.colsc[:, 19:20], 1e-12)
            b.op("vector", "reciprocal", G.colsc[:, 20:21], G.colsc[:, 19:20])
            tt(b, "vector", G.colsc[:, 21:22], G.colsc[:, 18:19], G.colsc[:, 20:21], AT.mult)
            ts(b, "vector", G.colsc[:, 21:22], G.colsc[:, 21:22], 0.1, None, AT.mult)
            b.op("vector", "tensor_copy", G.qsq[:, :], G.psCH[:, 4, 0:128])
            stt(b, "vector", G.ctmp[:, :], G.qsq[:, :], G.colsc[:, 21:22],
                G.psCH[:, 3, 0:128], AT.mult, AT.add)
            tt(b, "vector", G.ostok[:, :], G.ctmp[:, :], G.psCH[:, 5, 0:128],
               AT.mult, inc="dv")
            b.wait("tensor", "dv"); b.wait("tensor", "sc")
            mm(b, G.psCH[:, 6, 0:128], G.ostok[:, :], G.idb[:, :], inc="mm")
            b.wait("vector", "mm")
            b.wait("vector", "gld")
            b.op("vector", "tensor_copy", G.ospst[:, h, :], G.psCH[:, 6, 0:128], inc="dv")
        b.wait("gpsimd", "dv")
        dma(b, "gpsimd", d["sc_osp"].rearrange("h p t -> p h t")[:, :, c * 128:(c + 1) * 128],
            G.ospst[:, :, :])
    b.fence()


def emit_p6b(b, G, d):
    MAGIC = 12582912.0
    b.op("vector", "memset", G.colsc[:, 22:23], 0.25, inc="dv")
    dma(b, "sync", G.b1c[:, :, 0:1], d["f1b"].unsqueeze(2), slow=True)
    dma(b, "sync", G.b2c[:, :, 0:1], d["f2b"].unsqueeze(2), slow=True)
    lastw1 = {0: 0, 1: 0}
    lastw2dec = 0
    for grp in range(NGRP):
        gs = grp * GRP
        for half in range(2):
            ti = gs // 128 + half
            ln_bcast(b, G, G.stats2, ti)
            hs = slice(half * 128, (half + 1) * 128)
            for db in range(8):
                tt(b, "vector", G.scr[:, 0, 0:128], G.xres[:, db, gs + half * 128:gs + (half + 1) * 128],
                   G.psA[:, 0:128], AT.subtract, inc=None)
                tt(b, "vector", G.xng[:, db, hs], G.scr[:, 0, 0:128], G.psB[:, 0:128], AT.mult, inc=None)
        w1hist = {}
        b.wait("tensor", "dv"); b.wait("tensor", "sc")
        for it in range(FF1 // 256):
            for half in range(2):
                fb = 2 * it + half
                wt = G.wtile if half == 0 else G.wtileB
                pst = G.psA if half == 0 else G.psB
                lag = w1hist[(it - 1, half)] if it >= 1 else lastw1[half]
                if lag:
                    b.wait("sync", "mm", val=lag)
                dma(b, "sync", wt[:, :, :],
                    d["sc_wf1"][:, fb * 128:(fb + 1) * 128].rearrange("(db p) e -> p db e", p=128))
                b.wait("tensor", "ld")
                for db in range(8):
                    mm(b, pst[:, 0:GRP], wt[:, db, :], G.xng[:, db, :],
                       start=(db == 0), stop=(db == 7))
                w1hist[(it, half)] = b.semv["mm"]
            b.wait("vector", "mm")
            ts(b, "vector", G.pwl[:, 0, 0:GRP], G.psA[:, 0:GRP], G.b1c[:, 2 * it, 0:1],
               256.0, AT.add, AT.mult)
            ts(b, "vector", G.pwl[:, 0, GRP:2 * GRP], G.psB[:, 0:GRP], G.b1c[:, 2 * it + 1, 0:1],
               256.0, AT.add, AT.mult, inc="dv")
            b.wait("tensor", "dv")
            b.op("vector", "tensor_copy", G.hq32[:, :], G.pwl[:, 0, :])
            b.op("vector", "tensor_copy", G.pwl[:, 1, :], G.hq32[:, :], inc="dv")
            hq = G.pwl[:, 1, :]
            ts(b, "vector", G.pwl[:, 2, :], hq, 1.0 / 64.0, 80.5078125, AT.mult, AT.add)
            b.op("vector", "tensor_copy", G.hq32[:, :], G.pwl[:, 2, :])
            b.op("vector", "tensor_copy", G.pwl[:, 2, :], G.hq32[:, :])
            ts(b, "vector", G.pwl[:, 2, :], G.pwl[:, 2, :], 81.0, 0.25, AT.subtract, AT.mult)
            ts(b, "vector", G.pwl[:, 2, :], G.pwl[:, 2, :], -4.0, 4.0, AT.max, AT.min, inc="dv")
            XL = G.pwl[:, 2, :]
            b.wait("scalar", "dv")
            act(b, G.pwl[:, 3, :], XL, AF.Gelu)
            act(b, G.pwl[:, 4, :], XL, AF.Gelu, bias=G.colsc[:, 22:23], inc="sc")
            b.wait("vector", "sc")
            tt(b, "vector", G.pwl[:, 5, :], G.pwl[:, 4, :], G.pwl[:, 3, :], AT.subtract)
            stt(b, "vector", G.pwl[:, 6, :], hq, 1.0 / 256.0, XL, AT.mult, AT.subtract)
            stt(b, "vector", G.pwl[:, 6, :], G.pwl[:, 6, :], 4.0, G.pwl[:, 5, :], AT.mult, AT.mult)
            tt(b, "vector", G.pwl[:, 6, :], G.pwl[:, 6, :], G.pwl[:, 3, :], AT.add)
            ts(b, "vector", G.pwl[:, 7, :], hq, 1024.0, None, AT.is_gt)
            ts(b, "vector", G.pwl[:, 5, :], hq, 1.0 / 256.0, None, AT.mult)
            tt(b, "vector", G.pwl[:, 5, :], G.pwl[:, 5, :], G.pwl[:, 6, :], AT.subtract)
            tt(b, "vector", G.pwl[:, 5, :], G.pwl[:, 5, :], G.pwl[:, 7, :], AT.mult)
            tt(b, "vector", G.pwl[:, 6, :], G.pwl[:, 6, :], G.pwl[:, 5, :], AT.add)
            ts(b, "vector", G.pwl[:, 7, :], hq, -1024.0, None, AT.is_ge)
            tt(b, "vector", G.pwl[:, 6, :], G.pwl[:, 6, :], G.pwl[:, 7, :], AT.mult)
            ts(b, "vector", G.pwl[:, 0, :], G.pwl[:, 6, :], 256.0, None, AT.mult)
            b.op("vector", "tensor_copy", G.hq32[:, :], G.pwl[:, 0, :])
            b.op("vector", "tensor_copy", G.pwl[:, 0, :], G.hq32[:, :])
            ts(b, "vector", G.hstr[:, 2 * it:2 * it + 2, :].rearrange("p a c -> p (a c)"),
               G.pwl[:, 0, :], 1.0 / 256.0, None, AT.mult, inc="dv")
        def dpsum(eb):
            return DPSUM(G, eb)
        w2hist = {}
        b.wait("tensor", "dv"); b.wait("tensor", "sc")
        for fb in range(FF1 // 128):
            wt2 = G.wtile2 if fb % 2 == 0 else G.wtile2B
            lag = w2hist[fb - 2] if fb >= 2 else lastw2dec
            if lag:
                b.wait("sync", "mm", val=lag)
            dma(b, "sync", wt2[:, :], d["sc_wf2"][fb * 128:(fb + 1) * 128, :])
            b.wait("tensor", "ld")
            for eb in range(8):
                mm(b, dpsum(eb), wt2[:, eb * 128:(eb + 1) * 128],
                   G.hstr[:, fb, :], start=(fb == 0), stop=(fb == FF1 // 128 - 1))
            w2hist[fb] = b.semv["mm"]
        b.wait("vector", "mm")
        for eb in range(8):
            ts(b, "vector", G.pwl[:, 0, 0:GRP], dpsum(eb), G.b2c[:, eb, 0:1], 256.0, AT.add, AT.mult)
            b.op("vector", "tensor_copy", G.hq32[:, 0:GRP], G.pwl[:, 0, 0:GRP])
            b.op("vector", "tensor_copy", G.pwl[:, 0, 0:GRP], G.hq32[:, 0:GRP])
            stt(b, "vector", G.xres[:, eb, gs:gs + GRP], G.pwl[:, 0, 0:GRP], 1.0 / 256.0,
                G.xres[:, eb, gs:gs + GRP], AT.mult, AT.add)
        b.wait("tensor", "dv"); b.wait("tensor", "sc")
        dchist = {}
        for hb in range(8):
            wt2 = G.wtile2 if hb % 2 == 0 else G.wtile2B
            b.wait("sync", "mm", val=dchist[hb - 2] if hb >= 2 else w2hist[30 + hb])
            dma(b, "sync", wt2[:, :], d["sc_dec"][hb * 128:(hb + 1) * 128, :])
            dma(b, "sync", G.ospl[:, hb, :], d["sc_osp"][hb, :, gs:gs + GRP])
            b.wait("tensor", "ld")
            for eb in range(8):
                mm(b, dpsum(eb), wt2[:, eb * 128:(eb + 1) * 128],
                   G.ospl[:, hb, :], start=(hb == 0), stop=(hb == 7))
            dchist[hb] = b.semv["mm"]
        lastw1 = {0: w1hist[(FF1 // 256 - 1, 0)], 1: w1hist[(FF1 // 256 - 1, 1)]}
        lastw2dec = dchist[7]
        b.wait("vector", "mm")
        for eb in range(8):
            tt(b, "vector", G.xres[:, eb, gs:gs + GRP], G.xres[:, eb, gs:gs + GRP],
               dpsum(eb), AT.add)
        for half in range(2):
            ti = gs // 128 + half
            b.wait("tensor", "dv"); b.wait("tensor", "sc")
            b.wait("vector", "st")
            for db in range(8):
                tr(b, G.psST[:, db, 0:128], G.xres[:, db, ti * 128:(ti + 1) * 128],
                   G.idf[:, :], inc="mm")
            b.wait("vector", "mm")
            b.op("vector", "tensor_copy",
                 G.tokst[:, :].rearrange("p (a c) -> p a c", c=128),
                 G.psST[:, :, 0:128], inc="dv")
            b.wait("sync", "dv")
            dma(b, "sync", d["out"][ti * 128:(ti + 1) * 128, :], G.tokst[:, :], sem="st")
    b.wait("sync", "st")
    b.wait("gpsimd", "st")


def kernel(**inputs):
    ref_dt = np.float32
    x = np.ascontiguousarray(inputs["L0_attn_x"], dtype=np.float32)
    B, T, _ = x.shape
    nc = build_program()

    wT = {p: np.ascontiguousarray(inputs[f"L0_attn_W_{p}"].T) for p in "rwkvabg"}
    wot = np.ascontiguousarray(inputs["L0_attn_W_o"].T)
    mixv8 = np.stack([inputs[f"L0_attn_x_{p}"] for p in "rwkvabg"] + [np.zeros(D, np.float32)])
    mixv = np.ascontiguousarray(mixv8.reshape(8, 8, 128).transpose(2, 1, 0)).astype(np.float32)
    def dimmaj(vec):
        n = vec.shape[0] // 128
        return np.ascontiguousarray(np.asarray(vec, np.float32).reshape(n, 128).T)
    wg_i = np.ascontiguousarray(inputs["L0_ffn_w_gate"].T)
    wu_i = np.ascontiguousarray(inputs["L0_ffn_w_up"].T)
    wd_i = np.ascontiguousarray(inputs["L0_ffn_w_down"].T)
    hut = np.ascontiguousarray(np.transpose(inputs["L1_attn_hebb_U"], (0, 2, 1)))

    ident = np.eye(128, dtype=np.float32)
    up_s = np.triu(np.ones((64, 64), np.float32), 1)
    up_i = np.triu(np.ones((64, 64), np.float32), 0)
    m4 = np.concatenate([up_s, up_i, up_s, up_i], axis=1)
    mask4 = np.concatenate([m4, m4], axis=0)
    lo_s = np.tril(np.ones((64, 64), np.float32), -1)
    masklo = np.concatenate([lo_s, lo_s], axis=0)
    idd2 = np.concatenate([np.eye(64, dtype=np.float32)] * 2, axis=0)
    ttt = np.arange(128)[:, None]
    jjj = np.arange(192)[None, :]
    bm = ((jjj >= ttt + 1) & (jjj <= ttt + 64)).astype(np.float32)
    bandb = (1.0 - bm) * 1.0e30
    ones2 = np.zeros((128, 2), np.float32)
    ones2[0:64, 0] = 1.0
    ones2[64:128, 1] = 1.0
    sel2 = np.zeros((2, 128), np.float32)
    sel2[0, 0:64] = 1.0
    sel2[1, 64:128] = 1.0

    in_maps = []
    for c in range(8):
        bb, gp = c // 4, c % 4
        t0 = gp * TLOC
        xsl = np.ascontiguousarray(x[bb, t0:t0 + TLOC])
        if gp == 0:
            xp = np.zeros((1, D), np.float32)
        else:
            xp = np.ascontiguousarray(x[bb, t0 - 1:t0])
        fl = np.zeros((1, 32), np.float32)
        for j in range(7):
            if (j // 4 == bb) and (j < c):
                fl[0, j] = 1.0
                fl[0, 16 + j] = 0.0
            else:
                fl[0, j] = 0.0
                fl[0, 16 + j] = 1.0
        for j in range(8):
            fl[0, 24 + j] = 1.0 if (gp > 0 and j == c - 1) else 0.0
        fl[0, 7] = 1.0 if gp == 0 else 0.0
        fl[0, 23] = 0.0 if gp == 0 else 1.0
        for j in range(8):
            fl[0, 8 + j] = 1.0 if (j // 4 == bb) else 0.0
        im = dict(
            xs=xsl, xprev=xp, flags=fl,
            mixv=mixv, gnwr=inputs["L0_attn_gn_w"][None], gnbr=inputs["L0_attn_gn_b"][None],
            bor=dimmaj(inputs["L0_attn_b_o"]),
            wot=wot, wg_i=wg_i, wu_i=wu_i, wd_i=wd_i,
            encq=np.ascontiguousarray(inputs["L1_attn_enc_q"]),
            encv=np.ascontiguousarray(inputs["L1_attn_enc_v"]),
            mw1=np.ascontiguousarray(inputs["L1_attn_metric_w1"]),
            mb1=dimmaj(inputs["L1_attn_metric_b1"]),
            mw2=np.ascontiguousarray(inputs["L1_attn_metric_w2"]),
            mb2=dimmaj(inputs["L1_attn_metric_b2"]),
            basem=dimmaj(inputs["L1_attn_base_metric"].reshape(-1)),
            hut=hut, hvv=np.ascontiguousarray(inputs["L1_attn_hebb_V"]),
            decw=np.ascontiguousarray(inputs["L1_attn_decoder"]),
            f1w=np.ascontiguousarray(inputs["L1_ffn_w1"]), f1b=dimmaj(inputs["L1_ffn_b1"]),
            f2w=np.ascontiguousarray(inputs["L1_ffn_w2"]), f2b=dimmaj(inputs["L1_ffn_b2"]),
            identf=ident, identb=ident.astype(np.float32),
            mask4=mask4, masklo=masklo, idd2=idd2, bandm=bm, bandb=bandb,
            ones2=ones2, sel2=sel2, onesr=np.ones((1, 128), np.float32),
            onescol=np.ones((128, 1), np.float32),
        )
        for p in "rwkvabg":
            im[f"w{p}t"] = wT[p]
        in_maps.append(im)

    import ml_dtypes
    in_maps2 = []
    for im in in_maps:
        im2 = dict(im)
        im2["identb"] = ident.astype(ml_dtypes.bfloat16)
        in_maps2.append(im2)

    res = run_bass_kernel_spmd(nc, in_maps2, core_ids=list(range(8)))
    global LAST_RES
    LAST_RES = res
    outs = np.zeros((B, T, D), dtype=ref_dt)
    for c in range(8):
        bb, gp = c // 4, c % 4
        outs[bb, gp * TLOC:(gp + 1) * TLOC] = res.results[c]["out"]
    return outs
